# version 17
# baseline (speedup 1.0000x reference)
import numpy as np
import ml_dtypes
import concourse.bass as bass
import concourse.mybir as mybir
from concourse.bass_utils import run_bass_kernel_spmd

F32 = mybir.dt.float32
BF16 = mybir.dt.bfloat16
AF = mybir.ActivationFunctionType
ALU = mybir.AluOpType
AX = mybir.AxisListType
bf16 = ml_dtypes.bfloat16

D = 1024
T = 4096
NT = T // 128
OH = 16
IH = 8
DH = 64
HID = 2730
NMC = 22
EPS = 1e-5

ENGS = ("t", "s", "v", "g", "y")


class Prog:

    def __init__(self, nc):
        self.nc = nc
        self.ops = []
        self.counts = {}
        for e in ENGS:
            self.counts[e] = 0

    def rec(self, eng, fn, deps=(), channel=None, inc=None, tag="", self_deps=()):
        sem_name = channel if channel is not None else eng
        if inc is None:
            inc = 16 if channel is not None else 1
        if sem_name not in self.counts:
            self.counts[sem_name] = 0
        deps = tuple(d for d in deps if d is not None)
        self_deps = tuple(d for d in self_deps if d is not None)
        self.ops.append((eng, fn, sem_name, inc, deps, self_deps, tag))
        if inc:
            self.counts[sem_name] += inc
            return (sem_name, self.counts[sem_name])
        return None

    def token(self, sem_name):
        c = self.counts.get(sem_name, 0)
        return (sem_name, c) if c else None

    def emit(self, block, final_waiter="y", serialize=False):
        finals = dict(self.counts)
        sems = {}
        guards = []
        for name in self.counts:
            g = self.nc.semaphore("sem_" + name)
            sems[name] = g.__enter__()
            guards.append(g)

        if serialize:
            run_counts = {n: 0 for n in self.counts}
            ser_deps = []
            for (k, fn, sem_name, inc, deps, self_deps, tag) in self.ops:
                ser_deps.append([(n, v) for n, v in run_counts.items() if v])
                if inc:
                    run_counts[sem_name] += inc
        else:
            ser_deps = [list(deps) for (_, _, _, _, deps, _, _) in self.ops]

        def emitter(key):
            def run(eng):
                seen = {}
                for (k, fn, sem_name, inc, deps, self_deps, tag), sdeps in zip(self.ops, ser_deps):
                    if k != key:
                        continue
                    nwait = 0
                    for (dn, dv) in sdeps:
                        if dn == key:
                            continue
                        if seen.get(dn, 0) < dv:
                            if nwait:
                                eng.nop()
                            eng.wait_ge(sems[dn], dv)
                            seen[dn] = dv
                            nwait += 1
                    for (dn, dv) in self_deps:
                        if nwait:
                            eng.nop()
                        eng.wait_ge(sems[dn], dv)
                        nwait += 1
                    instr = fn(eng)
                    if inc:
                        instr.then_inc(sems[sem_name], inc)
                if key == final_waiter:
                    for n2, v2 in finals.items():
                        if v2:
                            eng.wait_ge(sems[n2], v2)
            return run

        block.tensor(emitter("t"))
        block.scalar(emitter("s"))
        block.vector(emitter("v"))
        block.gpsimd(emitter("g"))
        block.sync(emitter("y"))
        return guards



ST = 256
NSUB = ST // 128
NST = T // ST


def alloc_ffn_bufs(nc, tagp, with_final):
    gs = []
    bufs = {}

    def sb(name, shape, dt):
        g = nc.sbuf_tensor(tagp + name, shape, dt)
        t = g.__enter__()
        gs.append(g)
        bufs[name] = t
        return t

    sb("w1", [128, 8 * 5460], BF16)
    sb("w2", [128, NMC * 1024], BF16)
    sb("b1a", [128, NMC], F32)
    sb("b1g", [128, NMC], F32)
    sb("b2row", [1, 1024], BF16)
    sb("grep", [128, 1024], F32)
    sb("brep", [128, 1024], F32)
    sb("xin", [128, 5 * 1024], F32)
    sb("hsh", [128, 2 * 512], F32)
    sb("stats", [128, 4 * 16], F32)
    sb("hln", [128, 2 * 1024], BF16)
    sb("hlnT", [128, 2 * (8 * ST)], BF16)
    sb("tmp", [128, 2 * 1024], F32)
    sb("gel", [128, 2 * ST], BF16)
    sb("prod", [128, 2 * ST], BF16)
    sb("outb", [128, 2 * 1024], F32)
    if with_final:
        sb("x2ring", [128, 3 * 1024], F32)
    return gs, bufs


def alloc_ffn_psums(nc, tagp):
    gs = []
    ps = {}
    pT = []
    for i in range(2):
        g = nc.psum_tensor(tagp + f"pT{i}", [128, 1024], BF16)
        pT.append(g.__enter__()); gs.append(g)
    pag = []
    for i in range(2):
        g = nc.psum_tensor(tagp + f"pag{i}", [128, 512], F32)
        pag.append(g.__enter__()); gs.append(g)
    pout = []
    for i in range(4):
        g = nc.psum_tensor(tagp + f"pout{i}", [128, 512], F32)
        pout.append(g.__enter__()); gs.append(g)
    ps["pT"], ps["pag"], ps["pout"] = pT, pag, pout
    return gs, ps


def record_ffn_phase(P, nc, glob, psums, bufs, wdram, in_dram, out_dram,
                     final=None, weights_dep=None, cp="f0", glob_tok=(), phase_dep=()):
    b = bufs
    pT, pag, pout = psums["pT"], psums["pag"], psums["pout"]
    ident, epsv, ones1 = glob["ident"], glob["eps"], glob["ones1"]

    for k in range(8):
        P.rec("g", lambda g, k=k: g.dma_start(
            b["w1"][:, k * 5460:(k + 1) * 5460], wdram["w1"][k * 128:(k + 1) * 128, :]),
            deps=(([weights_dep] if weights_dep else []) + list(phase_dep)) if k < 4 else (),
            channel=cp + f"w{k % 4}", tag="w1ld")
    for ni, name in enumerate(("b1a", "b1g", "b2row", "grep", "brep")):
        P.rec("g", lambda g, name=name: g.dma_start(b[name][:, :], wdram[name][:, :]),
              deps=list(phase_dep) if ni < 4 else (),
              channel=cp + f"u{ni % 4}", tag=name + "ld")
    for mc in range(NMC):
        rows = 42 if mc == NMC - 1 else 128
        P.rec("g", lambda g, mc=mc, rows=rows: g.dma_start(
            b["w2"][0:rows, mc * 1024:(mc + 1) * 1024], wdram["w2"][mc * 128:mc * 128 + rows, :]),
            channel=cp + f"u{mc % 4}", tag="w2ld")
    wtok4 = [P.token(cp + f"w{c}") for c in range(4)]
    utok4 = [P.token(cp + f"u{c}") for c in range(4)]
    wtok = wtok4[0]

    xin_tok = [None] * (NT + 8)
    shift_tok = [None] * (NT + 8)
    tr_tok = [None] * NT
    lnv_tok = [None] * NT
    srstd_tok = [None] * NT
    trev_tok = [None] * NT
    resid_tok = [None] * NT
    out_tok = [None] * NT
    st_pe_tok = [None] * NST
    x2_tok = [None] * NT
    dump_tok = [[] for _ in range(NT)]

    def load_tile(j):
        if j >= NT:
            return
        dep = [resid_tok[j - 5], shift_tok[j - 4]] if j >= 5 else list(phase_dep)
        xin_tok[j] = P.rec("y", lambda y, j=j: y.dma_start(
            b["xin"][:, (j % 5) * 1024:(j % 5 + 1) * 1024], in_dram[j * 128:(j + 1) * 128, :]),
            deps=dep, channel=cp + f"xi{j % 5}", tag=f"xin{j}")

    def load_x2(j):
        if final is None or "x2_dram" not in final or j >= NT:
            return
        dep = [resid_tok[j - 3]] if j >= 3 else list(phase_dep)
        x2_tok[j] = P.rec("y", lambda y, j=j: y.dma_start(
            b["x2ring"][:, (j % 3) * 1024:(j % 3 + 1) * 1024],
            final["x2_dram"][j * 128:(j + 1) * 128, :]),
            deps=dep, channel=cp + f"x2{j % 3}", tag=f"x2in{j}")

    def ln_tile(j):
        p = j % 2
        xsl = b["xin"][:, (j % 5) * 1024:(j % 5 + 1) * 1024]
        hsh = b["hsh"][:, p * 512:(p + 1) * 512]
        dshift = [xin_tok[j], lnv_tok[j - 2] if j >= 2 else None] + \
            (dump_tok[j - 2] if j >= 2 else [])
        t1 = P.rec("g", lambda g, xsl=xsl, hsh=hsh: g.dma_start(
            hsh[1:128, :], xsl[0:127, 512:1024]), deps=dshift,
            channel=cp + f"sa{j % 2}", tag=f"sh{j}")
        if j == 0:
            t2 = P.rec("g", lambda g, hsh=hsh: g.memset(hsh[0:1, :], 0.0), tag="sh0z")
        else:
            xprev = b["xin"][:, ((j - 1) % 5) * 1024:((j - 1) % 5 + 1) * 1024]
            t2 = P.rec("g", lambda g, xprev=xprev, hsh=hsh: g.dma_start(
                hsh[0:1, :], xprev[127:128, 512:1024]),
                deps=[xin_tok[j - 1]], channel=cp + f"sb{j % 2}", tag=f"shp{j}")
        shift_tok[j] = t2
        stt = b["stats"][:, p * 16:(p + 1) * 16]

        def bn_pair(v, xsl=xsl, hsh=hsh, stt=stt):
            v.bn_stats(stt[:, 0:6], xsl[:, 0:512])
            return v.bn_stats(stt[:, 6:12], hsh[:, :])

        pair_tok = P.rec("v", bn_pair,
                         deps=[xin_tok[j], t1, t2] + (dump_tok[j - 2] if j >= 2 else []),
                         tag=f"bn{j}")
        vagg = P.rec("v", lambda v, stt=stt: v.bn_aggr(stt[:, 12:14], stt[:, 0:12]),
                     self_deps=[pair_tok], tag=f"bnag{j}")
        P.rec("s", lambda s, stt=stt: s.activation(stt[:, 14:15], stt[:, 13:14], AF.Ln,
                                                   bias=epsv[:, :], scale=1.0),
              deps=[vagg] + (list(glob_tok) if j < 2 else []), tag=f"ln{j}")
        lntk = P.token("s")
        srstd_tok[j] = P.rec("s", lambda s, stt=stt: s.activation(
            stt[:, 14:15], stt[:, 14:15], AF.Exp, bias=0.0, scale=-0.5),
            self_deps=[lntk], tag=f"exp{j}")
        tmp = b["tmp"][:, p * 1024:(p + 1) * 1024]
        hln = b["hln"][:, p * 1024:(p + 1) * 1024]
        P.rec("v", lambda v, xsl=xsl, tmp=tmp, stt=stt: v.tensor_scalar(
            tmp[:, 0:512], xsl[:, 0:512], stt[:, 12:13], stt[:, 14:15], ALU.subtract, ALU.mult),
            deps=[srstd_tok[j]], tag=f"apA{j}")
        P.rec("v", lambda v, hsh=hsh, tmp=tmp, stt=stt: v.tensor_scalar(
            tmp[:, 512:1024], hsh[:, :], stt[:, 12:13], stt[:, 14:15], ALU.subtract, ALU.mult),
            tag=f"apB{j}")
        P.rec("v", lambda v, tmp=tmp: v.tensor_tensor(tmp[:, :], tmp[:, :], b["grep"][:, :], ALU.mult),
              deps=(utok4 if j < 3 else ()), tag=f"apG{j}")
        lnv_tok[j] = P.rec("v", lambda v, tmp=tmp, hln=hln: v.tensor_tensor(
            hln[:, :], tmp[:, :], b["brep"][:, :], ALU.add),
            deps=[tr_tok[j - 2] if j >= 2 else None] +
                 (dump_tok[j - 2] if j >= 2 else []), tag=f"apBt{j}")
        if final is not None and "dbg_hln" in final:
            dump_tok[j] = [
                P.rec("y", lambda y, j=j, hln=hln: y.dma_start(
                    final["dbg_hln"][j * 128:(j + 1) * 128, :], hln[:, :]),
                    deps=[lnv_tok[j]], channel=cp + f"db{j % 2}", tag=f"dbg{j}"),
                P.rec("y", lambda y, j=j, hsh=hsh: y.dma_start(
                    final["dbg_hsh"][j * 128:(j + 1) * 128, :], hsh[:, :]),
                    deps=[lnv_tok[j]], channel=cp + f"dc{j % 2}", tag=f"dbgh{j}"),
                P.rec("y", lambda y, j=j, stt=stt: y.dma_start(
                    final["dbg_st"][j * 128:(j + 1) * 128, :], stt[:, 12:15]),
                    deps=[lnv_tok[j]], channel=cp + f"dd{j % 2}", tag=f"dbgs{j}")]

    def transpose_tile(j):
        p = j % 2
        st = j // NSUB
        stp = st % 2
        hln = b["hln"][:, p * 1024:(p + 1) * 1024]
        ptb = pT[p]
        deps = [lnv_tok[j], trev_tok[j - 2] if j >= 2 else None,
                st_pe_tok[st - 2] if st >= 2 else None] + (list(glob_tok) if j < 2 else [])
        for k in range(8):
            P.rec("t", lambda t, k=k, hln=hln, ptb=ptb: t.transpose(
                ptb[:, k * 128:(k + 1) * 128], hln[:, k * 128:(k + 1) * 128], ident[:, :]),
                deps=(deps if k == 0 else ()), inc=(1 if k == 7 else 0), tag=f"tr{j}")
        ttok = P.token("t")
        tr_tok[j] = ttok
        sub = j % NSUB
        base = stp * (8 * ST)
        dstap = bass.AP(b["hlnT"], base + sub * 128, [[2 * 8 * ST, 128], [ST, 8], [1, 128]])
        trev_tok[j] = P.rec("v", lambda v, ptb=ptb, dstap=dstap: v.tensor_copy(dstap, ptb[:, :]),
                            deps=[ttok], tag=f"trev{j}")

    def mc_loop(st):
        stp = st % 2
        base = stp * (8 * ST)
        prev_resid = resid_tok[st * NSUB - 1] if st >= 1 else None
        gel_tok = [None, None]
        prod_tok = [None, None]

        def emit_w2(mc):
            mp = mc % 2
            rows = 42 if mc == NMC - 1 else 128
            prod = b["prod"][:, mp * ST:(mp + 1) * ST]
            for sub in range(NSUB):
                for nh in range(2):
                    d = [prod_tok[mp]]
                    if st == 0 and sub == 0 and nh == 0:
                        d = d + utok4
                    if mc == 0 and sub == 0 and nh == 0 and prev_resid:
                        d = d + [prev_resid]
                    P.rec("t", lambda t, mc=mc, sub=sub, nh=nh, rows=rows, prod=prod:
                          t.matmul(pout[sub * 2 + nh][:, :],
                                   prod[0:rows, sub * 128:(sub + 1) * 128],
                                   b["w2"][0:rows, mc * 1024 + nh * 512:mc * 1024 + (nh + 1) * 512],
                                   start=(mc == 0), stop=False),
                          deps=(d if (sub == 0 and nh == 0) else ()), inc=0, tag=f"w2{st}_{mc}")

        for mc in range(NMC):
            mp = mc % 2
            rows = 42 if mc == NMC - 1 else 128
            pa = pag[mp]
            deps = [trev_tok[st * NSUB], trev_tok[st * NSUB + 1], prod_tok[mp]]
            if mc < 2:
                deps = deps + wtok4
            for half, cofs in ((0, 0), (1, 2730)):
                for k in range(8):
                    last = (half == 1 and k == 7)
                    P.rec("t", lambda t, k=k, mc=mc, rows=rows, half=half, cofs=cofs, pa=pa:
                          t.matmul(pa[0:rows, half * 256:half * 256 + ST],
                                   b["w1"][:, k * 5460 + cofs + mc * 128:
                                           k * 5460 + cofs + mc * 128 + rows],
                                   bass.AP(b["hlnT"], base + k * ST, [[2 * 8 * ST, 128], [1, ST]]),
                                   start=(k == 0), stop=(k == 7)),
                          deps=(deps if (half == 0 and k == 0) else ()),
                          inc=(1 if last else 0), tag=f"ag{st}_{mc}")
            agtok = P.token("t")
            gel = b["gel"][:, mp * ST:(mp + 1) * ST]
            gel_tok[mp] = P.rec("s", lambda s, pa=pa, gel=gel, mc=mc, rows=rows: s.activation(
                gel[0:rows, :], pa[0:rows, 256:512], AF.Gelu,
                bias=b["b1g"][0:rows, mc:mc + 1], scale=1.0),
                deps=[agtok, prod_tok[mp]] + (utok4 if (st == 0 and mc < 2) else []),
                tag=f"gel{st}_{mc}")
            prod = b["prod"][:, mp * ST:(mp + 1) * ST]
            prod_tok[mp] = P.rec("v", lambda v, pa=pa, prod=prod, gel=gel, mc=mc, rows=rows:
                                 v.scalar_tensor_tensor(prod[0:rows, :], pa[0:rows, 0:256],
                                                        b["b1a"][0:rows, mc:mc + 1], gel[0:rows, :],
                                                        ALU.add, ALU.mult),
                                 deps=[gel_tok[mp]], tag=f"prod{st}_{mc}")
            if mc >= 1:
                emit_w2(mc - 1)
        emit_w2(NMC - 1)
        for sub in range(NSUB):
            for nh in range(2):
                last = (sub == NSUB - 1 and nh == 1)
                P.rec("t", lambda t, sub=sub, nh=nh: t.matmul(
                    pout[sub * 2 + nh][:, :], ones1[:, :],
                    b["b2row"][:, nh * 512:(nh + 1) * 512], start=False, stop=True),
                    deps=(list(glob_tok) if (st == 0 and sub == 0 and nh == 0) else ()),
                    inc=(1 if last else 0), tag=f"b2{st}")
        st_pe_tok[st] = P.token("t")

    def evac_st(st):
        for sub in range(NSUB):
            j = st * NSUB + sub
            p = j % 2
            xsl = b["xin"][:, (j % 5) * 1024:(j % 5 + 1) * 1024]
            ob = b["outb"][:, p * 1024:(p + 1) * 1024]
            deps = [st_pe_tok[st], out_tok[j - 2] if j >= 2 else None]
            for nh in range(2):
                P.rec("v", lambda v, sub=sub, nh=nh, xsl=xsl, ob=ob: v.tensor_tensor(
                    ob[:, nh * 512:(nh + 1) * 512], pout[sub * 2 + nh][:, :],
                    xsl[:, nh * 512:(nh + 1) * 512], ALU.add),
                    deps=(deps if nh == 0 else ()), tag=f"res{j}")
            if final is None or "x2_dram" not in final:
                resid_tok[j] = P.token("v")
            else:
                stt = b["stats"][:, 32 + p * 16:32 + (p + 1) * 16]
                def fbn_pair(v, ob=ob, stt=stt):
                    v.bn_stats(stt[:, 0:6], ob[:, 0:512])
                    return v.bn_stats(stt[:, 6:12], ob[:, 512:1024])
                fpt = P.rec("v", fbn_pair, self_deps=[P.token("v")], tag=f"fbn{j}")
                vt = P.rec("v", lambda v, stt=stt: v.bn_aggr(stt[:, 12:14], stt[:, 0:12]),
                           self_deps=[fpt], tag=f"fba{j}")
                P.rec("s", lambda s, stt=stt: s.activation(stt[:, 14:15], stt[:, 13:14], AF.Ln,
                                                           bias=epsv[:, :], scale=1.0),
                      deps=[vt], tag=f"fln{j}")
                flntk = P.token("s")
                fs = P.rec("s", lambda s, stt=stt: s.activation(stt[:, 14:15], stt[:, 14:15],
                                                                AF.Exp, bias=0.0, scale=-0.5),
                           self_deps=[flntk], tag=f"fex{j}")
                tmp = b["tmp"][:, p * 1024:(p + 1) * 1024]
                P.rec("v", lambda v, ob=ob, tmp=tmp, stt=stt: v.tensor_scalar(
                    tmp[:, :], ob[:, :], stt[:, 12:13], stt[:, 14:15], ALU.subtract, ALU.mult),
                    deps=[fs, x2_tok[j]], tag=f"fap{j}")
                resid_tok[j] = P.rec("v", lambda v, ob=ob, tmp=tmp, j=j: v.tensor_tensor(
                    ob[:, :], tmp[:, :], b["x2ring"][:, (j % 3) * 1024:(j % 3 + 1) * 1024], ALU.add),
                    tag=f"fx2{j}")
            out_tok[j] = P.rec("y", lambda y, j=j, ob=ob: y.dma_start(
                out_dram[j * 128:(j + 1) * 128, :], ob[:, :]),
                deps=[resid_tok[j]], channel=cp + f"ot{j % 2}", tag=f"out{j}")

    for j in range(3):
        load_tile(j)
    load_x2(0)
    for j in range(3):
        ln_tile(j)
        transpose_tile(j)
    for st in range(NST):
        for j in (2 * st + 3, 2 * st + 4):
            load_tile(j)
        for j in (2 * st + 1, 2 * st + 2):
            load_x2(j)
        mc_loop(st)
        for j in (2 * st + 3, 2 * st + 4):
            if j < NT:
                ln_tile(j)
                transpose_tile(j)
        evac_st(st)
    return [out_tok[NT - 2], out_tok[NT - 1]]



def ffn_host_weights(ln_g, ln_b, w1, b1, w2, b2, layer):
    l = layer
    b1l = np.asarray(b1[l], np.float32)
    b1a = np.zeros((128, NMC), np.float32)
    b1g = np.zeros((128, NMC), np.float32)
    for mc in range(NMC):
        rows = 42 if mc == NMC - 1 else 128
        b1a[0:rows, mc] = b1l[mc * 128:mc * 128 + rows]
        b1g[0:rows, mc] = b1l[2730 + mc * 128:2730 + mc * 128 + rows]
    return {
        "w1": np.ascontiguousarray(np.asarray(w1[l], np.float32)),
        "w2": np.ascontiguousarray(np.asarray(w2[l], np.float32)),
        "b1a": b1a,
        "b1g": b1g,
        "b2row": np.ascontiguousarray(np.asarray(b2[l], np.float32)[None, :]),
        "grep": np.ascontiguousarray(np.broadcast_to(np.asarray(ln_g[l], np.float32)[None, :], (128, 1024))),
        "brep": np.ascontiguousarray(np.broadcast_to(np.asarray(ln_b[l], np.float32)[None, :], (128, 1024))),
    }


def alloc_globals(nc):
    gs = []
    glob = {}
    g1 = nc.sbuf_tensor("g_ident", [128, 128], BF16)
    glob["ident"] = g1.__enter__(); gs.append(g1)
    g2 = nc.sbuf_tensor("g_eps", [128, 1], F32)
    glob["eps"] = g2.__enter__(); gs.append(g2)
    g3 = nc.sbuf_tensor("g_ones1", [1, 128], BF16)
    glob["ones1"] = g3.__enter__(); gs.append(g3)
    g4 = nc.sbuf_tensor("g_ones1f", [1, 128], F32)
    glob["ones1f"] = g4.__enter__(); gs.append(g4)
    g5 = nc.sbuf_tensor("g_dum", [128, 512], BF16)
    glob["dum"] = g5.__enter__(); gs.append(g5)
    return gs, glob


def record_globals(P, nc, glob, ident_dram):
    t1 = P.rec("g", lambda g: g.dma_start(glob["ident"][:, :], ident_dram[:, :]),
               channel="gw", tag="identld")
    P.rec("g", lambda g: g.memset(glob["eps"][:, :], EPS), tag="epsset")
    P.rec("g", lambda g: g.memset(glob["ones1"][:, :], 1.0), tag="onesset")
    P.rec("g", lambda g: g.memset(glob["dum"][:, :], 0.0), tag="dumset")
    t3 = P.rec("g", lambda g: g.memset(glob["ones1f"][:, :], 1.0), tag="onesfset")
    return [t1, t3]


def record_ham_filler(P, nc, glob, psum_ap, n, deps):
    for q in range(n):
        P.rec("t", lambda t: t.matmul(psum_ap, glob["dum"][:, 0:128],
                                      glob["dum"][:, :], start=True, stop=True),
              deps=(list(deps) if q == 0 else ()), inc=0, tag="hamfill")


def host_ident():
    return np.eye(128, dtype=np.float32)


def build_core_program():
    nc = bass.Bass("TRN2", target_bir_lowering=False, debug=False)
    x_in = nc.dram_tensor("x", [T, D], F32, kind="ExternalInput")
    ident = nc.dram_tensor("cident", [128, 128], F32, kind="ExternalInput")
    out = nc.dram_tensor("out", [T, D], F32, kind="ExternalOutput")
    x2 = nc.dram_tensor("x2s", [T, D], F32)
    m1 = nc.dram_tensor("m1s", [T, D], F32)
    a1 = nc.dram_tensor("a1s", [T, D], F32)
    m2 = nc.dram_tensor("m2s", [T, D], F32)

    shapes_attn = {"wqkvT": (1024, 3072), "woT": (1024, 1024),
                   "trimL": (128, 2048), "trimH": (128, 2048)}
    shapes_iatt = {"wqkv": (1024, 1536), "wo": (512, 1024), "qs": (128, 512),
                   "ks": (128, 512), "rot": (4096, 2048),
                   "trimL": (128, 1024), "trimH": (128, 1024)}
    shapes_ffn = {"w1": (1024, 5460), "w2": (2730, 1024), "b1a": (128, NMC),
                  "b1g": (128, NMC), "b2row": (1, 1024), "grep": (128, 1024),
                  "brep": (128, 1024)}
    wd = {}
    for pfx, shapes in (("p1", shapes_attn), ("p2", shapes_iatt), ("p3", shapes_ffn),
                        ("p4", shapes_iatt), ("p5", shapes_ffn)):
        wd[pfx] = {name: nc.dram_tensor(f"{pfx}_{name}", list(sh), F32, kind="ExternalInput")
                   for name, sh in shapes.items()}

    P = Prog(nc)
    gg, glob = alloc_globals(nc)
    gtok = record_globals(P, nc, glob, ident)

    bg1, b1_ = alloc_attn_bufs(nc, "a1")
    pg1, ps1 = alloc_attn_psums(nc, "a1")
    record_ham_filler(P, nc, glob, ps1["pS0"][:, 0:512], 400, gtok)
    end1 = record_attn_outer(P, nc, glob, ps1, b1_, wd["p1"], x_in, x2, cp="a1",
                             glob_tok=gtok)
    for g in reversed(bg1 + pg1):
        g.__exit__(None, None, None)
    bg2, b2_ = alloc_iattn_bufs(nc, "i2")
    pg2, ps2 = alloc_iattn_psums(nc, "i2")
    record_ham_filler(P, nc, glob, ps2["pS0"][:, 0:512], 450, end1)
    end2 = record_attn_inner(P, nc, glob, ps2, b2_, wd["p2"], x2, m1, cp="i2",
                             glob_tok=gtok, with_ln=True, phase_dep=end1)
    for g in reversed(bg2 + pg2):
        g.__exit__(None, None, None)
    bg3, b3_ = alloc_ffn_bufs(nc, "f3", with_final=False)
    pg3, ps3 = alloc_ffn_psums(nc, "f3")
    record_ham_filler(P, nc, glob, ps3["pag"][0][:, 0:512], 700, end2)
    end3 = record_ffn_phase(P, nc, glob, ps3, b3_, wd["p3"], m1, a1,
                            cp="f3", glob_tok=gtok, phase_dep=end2)
    for g in reversed(bg3 + pg3):
        g.__exit__(None, None, None)
    bg4, b4_ = alloc_iattn_bufs(nc, "i4")
    pg4, ps4 = alloc_iattn_psums(nc, "i4")
    record_ham_filler(P, nc, glob, ps4["pS0"][:, 0:512], 450, end3)
    end4 = record_attn_inner(P, nc, glob, ps4, b4_, wd["p4"], a1, m2, cp="i4",
                             glob_tok=gtok, with_ln=False, phase_dep=end3)
    for g in reversed(bg4 + pg4):
        g.__exit__(None, None, None)
    bg5, b5_ = alloc_ffn_bufs(nc, "f5", with_final=True)
    pg5, ps5 = alloc_ffn_psums(nc, "f5")
    record_ham_filler(P, nc, glob, ps5["pag"][0][:, 0:512], 700, end4)
    record_ffn_phase(P, nc, glob, ps5, b5_, wd["p5"], m2, out,
                     final={"x2_dram": x2}, cp="f5", glob_tok=gtok, phase_dep=end4)
    for g in reversed(bg5 + pg5):
        g.__exit__(None, None, None)

    with nc.Block() as block:
        semguards = P.emit(block)
    for g in reversed(semguards + gg):
        g.__exit__(None, None, None)
    return nc, len(P.ops)


def host_weight_maps(L0_attn_W_qkv, L0_attn_W_o, L0_ffn_wqkv, L0_ffn_q_scale,
                     L0_ffn_k_scale, L0_ffn_wo, L0_ffn_ln_g, L0_ffn_ln_b,
                     L0_ffn_w1, L0_ffn_b1, L0_ffn_w2, L0_ffn_b2):
    m = {}
    for name, arr in attn_host_weights(L0_attn_W_qkv, L0_attn_W_o).items():
        m[f"p1_{name}"] = arr
    for l, pfx in ((0, "p2"), (1, "p4")):
        for name, arr in iattn_host_weights(L0_ffn_wqkv, L0_ffn_q_scale,
                                            L0_ffn_k_scale, L0_ffn_wo, l).items():
            m[f"{pfx}_{name}"] = arr
    for l, pfx in ((0, "p3"), (1, "p5")):
        for name, arr in ffn_host_weights(L0_ffn_ln_g, L0_ffn_ln_b, L0_ffn_w1,
                                          L0_ffn_b1, L0_ffn_w2, L0_ffn_b2, l).items():
            m[f"{pfx}_{name}"] = arr
    m["cident"] = host_ident()
    return m


def kernel(L0_attn_x, L0_attn_W_qkv, L0_attn_W_o, L0_ffn_wqkv, L0_ffn_q_scale,
           L0_ffn_k_scale, L0_ffn_wo, L0_ffn_ln_g, L0_ffn_ln_b, L0_ffn_w1,
           L0_ffn_b1, L0_ffn_w2, L0_ffn_b2):
    x = np.ascontiguousarray(np.asarray(L0_attn_x, np.float32))
    wm = host_weight_maps(L0_attn_W_qkv, L0_attn_W_o, L0_ffn_wqkv, L0_ffn_q_scale,
                          L0_ffn_k_scale, L0_ffn_wo, L0_ffn_ln_g, L0_ffn_ln_b,
                          L0_ffn_w1, L0_ffn_b1, L0_ffn_w2, L0_ffn_b2)
    nc, nops = build_core_program()
    in_maps = []
    for c in range(8):
        im = dict(wm)
        im["x"] = np.ascontiguousarray(x[c])
        in_maps.append(im)
    res = run_bass_kernel_spmd(nc, in_maps, core_ids=list(range(8)))
    out = np.stack([np.asarray(res.results[c]["out"], np.float32) for c in range(8)], axis=0)
    return out



RING = 6


def alloc_attn_bufs(nc, tagp):
    gs = []
    bufs = {}

    def sb(name, shape, dt):
        g = nc.sbuf_tensor(tagp + name, shape, dt)
        t = g.__enter__()
        gs.append(g)
        bufs[name] = t
        return t

    sb("wqkv", [128, 8 * 3072], BF16)
    sb("wo", [128, 8 * 1024], BF16)
    sb("xin", [128, RING * 1024], F32)
    sb("stats", [128, 4 * 16], F32)
    sb("h1", [128, 2 * 1024], BF16)
    sb("h1T", [128, 2 * 1024], BF16)
    sb("qkt", [128, 2 * 2048], BF16)
    sb("qT", [128, 4 * 1024], BF16)
    sb("kT", [128, RING * 1024], BF16)
    sb("vr", [128, RING * 1040], BF16)
    sb("probs", [128, 16 * 640], BF16)
    sb("trimL", [128, 16 * 128], BF16)
    sb("trimH", [128, 16 * 128], BF16)
    sb("oTf", [128, 1024], F32)
    sb("sums", [1, 4096], F32)
    
    sb("oTn", [128, 1024], BF16)
    sb("abuf", [128, 2 * 1024], F32)
    sb("outb", [128, 2 * 1024], F32)
    return gs, bufs


def alloc_attn_psums(nc, tagp):
    gs = []
    ps = {}

    def palloc(name, shape, dt):
        g = nc.psum_tensor(tagp + name, shape, dt)
        t = g.__enter__()
        gs.append(g)
        ps[name] = t
        return t

    palloc("pS0", [128, 640], F32)
    palloc("pS1", [128, 640], F32)
    palloc("pQ", [128, 1024], F32)
    palloc("pOW", [128, 512], F32)
    palloc("pOW2", [128, 512], F32)
    ps["pT"] = ps["pQ"].bitcast(BF16)
    return gs, ps


def record_attn_outer(P, nc, glob, ps, b, wdram, in_dram, out_dram, cp="a1",
                      glob_tok=(), weights_dep=None, dbg=None, phase_dep=()):
    ident, epsv, ones1 = glob["ident"], glob["eps"], glob["ones1f"]
    pS2, pQ, pOW, pT = [ps["pS0"], ps["pS1"]], ps["pQ"], ps["pOW"], ps["pT"]
    pOW2 = ps["pOW2"]
    pows = [pOW, pOW2]

    for k in range(8):
        P.rec("g", lambda g, k=k: g.dma_start(
            b["wqkv"][:, k * 3072:(k + 1) * 3072], wdram["wqkvT"][k * 128:(k + 1) * 128, :]),
            deps=(([weights_dep] if weights_dep else []) + list(phase_dep)) if k < 4 else (),
            channel=cp + f"w{k % 4}", tag="wqkvld")
    for k in range(8):
        P.rec("g", lambda g, k=k: g.dma_start(
            b["wo"][:, k * 1024:(k + 1) * 1024], wdram["woT"][k * 128:(k + 1) * 128, :]),
            channel=cp + f"w{k % 4}", tag="wold")
    for ni, name in enumerate(("trimL", "trimH")):
        P.rec("g", lambda g, name=name: g.dma_start(b[name][:, :], wdram[name][:, :]),
              channel=cp + f"w{ni % 4}", tag=name)
    vzt = P.rec("g", lambda g: g.memset(b["vr"][:, :], 0.0), tag="vz")
    P.rec("g", lambda g: g.memset(
        bass.AP(b["vr"], 64, [[RING * 1040, 128], [65, 16 * RING], [1, 1]]), 1.0),
        self_deps=[vzt], tag="vones")
    P.rec("g", lambda g: g.memset(b["kT"][:, 4 * 1024:6 * 1024], 0.0), tag="kz")
    wtok4 = [P.token(cp + f"w{c}") for c in range(4)]
    wtok = wtok4[0]
    init_tok = P.token("g")

    xin_tok = [None] * (NT + 8)
    vmm_tok = [None] * NT
    h1_tok = [None] * NT
    trh_tok = [None] * NT
    h1T_tok = [None] * NT
    qkv_pe_tok = [None] * NT
    qkev_tok = [None] * NT
    trqk_tok = [None] * NT
    ring_tok = [None] * NT
    exp_tok = [[None] * OH for _ in range(NT)]
    zero_tok = [None] * NT
    zgrp_tok = [[None] * 4 for _ in range(NT)]
    oevv_tok = [None] * NT
    aevh_tok = [[None, None] for _ in range(NT)]
    rex_tok = [None] * NT
    av_tok = [None] * NT
    oev_tok = [None] * NT
    norm_tok = [None] * NT
    attn_pe_tok = [None] * NT
    res_tok = [None] * NT
    out_tok = [None] * NT

    def load_tile(i):
        if i >= NT:
            return
        dep = [res_tok[i - RING]] if i >= RING else list(phase_dep)
        xin_tok[i] = P.rec("y", lambda y, i=i: y.dma_start(
            b["xin"][:, (i % RING) * 1024:(i % RING + 1) * 1024],
            in_dram[i * 128:(i + 1) * 128, :]),
            deps=dep, channel=cp + f"xi{i % RING}", tag=f"xin{i}")

    def produce_ln(i):
        if i >= NT:
            return
        p = i % 2
        xsl = b["xin"][:, (i % RING) * 1024:(i % RING + 1) * 1024]
        stt = b["stats"][:, p * 16:(p + 1) * 16]

        def bn_pair(v, xsl=xsl, stt=stt):
            v.bn_stats(stt[:, 0:6], xsl[:, 0:512])
            return v.bn_stats(stt[:, 6:12], xsl[:, 512:1024])
        pr = P.rec("v", bn_pair, deps=[xin_tok[i]], tag=f"bn{i}")
        vagg = P.rec("v", lambda v, stt=stt: v.bn_aggr(stt[:, 12:14], stt[:, 0:12]),
                     self_deps=[pr], tag=f"bnag{i}")
        P.rec("s", lambda s, stt=stt: s.activation(stt[:, 14:15], stt[:, 13:14], AF.Ln,
                                                   bias=epsv[:, :], scale=1.0),
              deps=[vagg] + (list(glob_tok) if i < 2 else []), tag=f"ln{i}")
        lntk = P.token("s")
        srstd = P.rec("s", lambda s, stt=stt: s.activation(
            stt[:, 14:15], stt[:, 14:15], AF.Exp, bias=0.0, scale=-0.5),
            self_deps=[lntk], tag=f"exp{i}")
        h1 = b["h1"][:, p * 1024:(p + 1) * 1024]
        h1_tok[i] = P.rec("v", lambda v, xsl=xsl, h1=h1, stt=stt: v.tensor_scalar(
            h1[:, :], xsl[:, :], stt[:, 12:13], stt[:, 14:15], ALU.subtract, ALU.mult),
            deps=[srstd, trh_tok[i - 2] if i >= 2 else None], tag=f"h1{i}")

    def produce(i):
        if i >= NT:
            return
        p = i % 2
        h1 = b["h1"][:, p * 1024:(p + 1) * 1024]
        deps = [h1_tok[i], trqk_tok[i - 1] if i >= 1 else None,
                ring_tok[i - 1] if i >= 1 else None,
                qkev_tok[i - 1] if i >= 1 else None]
        for k in range(8):
            P.rec("t", lambda t, k=k, h1=h1: t.transpose(
                pT[:, k * 128:(k + 1) * 128], h1[:, k * 128:(k + 1) * 128], ident[:, :]),
                deps=(deps if k == 0 else ()), inc=(1 if k == 7 else 0), tag=f"trh{i}")
        trh_tok[i] = P.token("t")
        h1T = b["h1T"][:, p * 1024:(p + 1) * 1024]
        h1T_tok[i] = P.rec("v", lambda v, h1T=h1T: v.tensor_copy(h1T[:, :], pT[:, 0:1024]),
                           deps=[trh_tok[i], qkv_pe_tok[i - 2] if i >= 2 else None],
                           tag=f"h1Tev{i}")
        qk = b["qkt"][:, p * 2048:(p + 1) * 2048]
        ev = {}
        for ph in range(3):
            for ncol in range(2):
                for k in range(8):
                    d = []
                    if ph == 0 and k == 0 and ncol == 0:
                        d = [h1T_tok[i], qkev_tok[i - 1] if i >= 1 else None] + wtok4 + [
                             ring_tok[i - 1] if i >= 1 else None]
                    if ph > 0 and k == 0:
                        d = [ev[(ph - 1, ncol)]]
                    c0 = ph * 1024 + ncol * 512
                    P.rec("t", lambda t, k=k, c0=c0, ncol=ncol, h1T=h1T: t.matmul(
                        pQ[:, ncol * 512:ncol * 512 + 512],
                        h1T[:, k * 128:(k + 1) * 128],
                        b["wqkv"][:, k * 3072 + c0:k * 3072 + c0 + 512],
                        start=(k == 0), stop=(k == 7)),
                        deps=d, inc=(1 if k == 7 else 0), tag=f"qkv{i}_{ph}{ncol}")
                mmtok = P.token("t")
                if ph < 2:
                    ev[(ph, ncol)] = P.rec("v", lambda v, qk=qk, ph=ph, ncol=ncol: v.tensor_copy(
                        qk[:, ph * 1024 + ncol * 512:ph * 1024 + ncol * 512 + 512],
                        pQ[:, ncol * 512:ncol * 512 + 512]),
                        deps=[mmtok, trqk_tok[i - 1] if (ph == 0 and ncol == 0 and i >= 1)
                              else None],
                        tag=f"qkev{i}_{ph}{ncol}")
                else:
                    vmm_tok[i] = mmtok
        qkv_pe_tok[i] = P.token("t")
        if dbg is not None and "qk" in dbg:
            P.rec("y", lambda y, i=i, qk=qk: y.dma_start(
                dbg["qk"][i * 128:(i + 1) * 128, :], qk[:, :]),
                deps=[qkev_tok[i]], channel=cp + f"dq{i % 2}", tag=f"dbgqk{i}")
        if dbg is not None and "h1" in dbg:
            h1d = b["h1"][:, (i % 2) * 1024:(i % 2 + 1) * 1024]
            P.rec("y", lambda y, i=i, h1d=h1d: y.dma_start(
                dbg["h1"][i * 128:(i + 1) * 128, :], h1d[:, :]),
                deps=[h1_tok[i]], channel=cp + f"dh{i % 2}", tag=f"dbgh1{i}")

    def produce_b(i):
        if i >= NT:
            return
        p = i % 2
        qk = b["qkt"][:, p * 2048:(p + 1) * 2048]
        slot = i % RING
        dstv = bass.AP(b["vr"], slot * 1040, [[RING * 1040, 128], [65, 16], [1, 64]])
        qkev_tok[i] = P.rec("v", lambda v, dstv=dstv: v.tensor_copy(dstv, pQ[:, :]),
                            deps=[vmm_tok[i], avl_dep(i)], tag=f"vev{i}")
        for k in range(16):
            P.rec("t", lambda t, k=k, qk=qk: t.transpose(
                pT[:, (k % 8) * 128:(k % 8) * 128 + 128],
                qk[:, k * 128:(k + 1) * 128], ident[:, :]),
                deps=([qkev_tok[i], h1T_tok[i], avl_dep(i)] if k == 0 else
                      ([] if k != 8 else [qT_ev_dep(i)])),
                inc=(1 if k in (7, 15) else 0), tag=f"trqk{i}")
            if k == 7:
                t7 = P.token("t")
                qT = b["qT"][:, (i % 4) * 1024:(i % 4 + 1) * 1024]
                qT_ev = P.rec("v", lambda v, qT=qT: v.tensor_copy(qT[:, :], pT[:, 0:1024]),
                              deps=[t7, attn_pe_tok[i - 4] if i >= 4 else None],
                              tag=f"qTev{i}")
                qT_ev_holder[0] = qT_ev
                qT_ev_tok[i] = qT_ev
        trqk_tok[i] = P.token("t")
        ring_tok[i] = P.rec("v", lambda v, slot=slot: v.tensor_copy(
            b["kT"][:, slot * 1024:(slot + 1) * 1024], pT[:, 0:1024]),
            deps=[trqk_tok[i], avl_dep(i)], tag=f"kTev{i}")

    qT_ev_holder = [None]
    qT_ev_tok = [None] * (NT + 8)

    def qT_ev_dep(i):
        return qT_ev_holder[0]

    def avl_dep(i):
        t = i - RING + 2
        return attn_pe_tok[t] if t >= 0 else init_tok

    def attn_qk(j):
        qT = b["qT"][:, (j % 4) * 1024:(j % 4 + 1) * 1024]
        kdep = [ring_tok[min(j + 2, NT - 1)], qT_ev_tok[min(j, NT - 1)]]
        for h in range(OH):
            hh = h % 2
            c = h // 2
            pS = pS2[h % 2]
            if h >= 2:
                war = exp_tok[j][h - 2]
            elif j >= 1:
                war = exp_tok[j - 1][14 + h]
            else:
                war = None
            for tau in range(5):
                tt = j - 2 + tau
                slot = (tt % RING) if tt >= 0 else ((tt + RING) % RING)
                d = []
                if tau == 0:
                    d = (list(kdep) if h == 0 else []) + [war]
                P.rec("t", lambda t, slot=slot, c=c, hh=hh, tau=tau, qT=qT, pS=pS: t.matmul(
                    pS[:, tau * 128:(tau + 1) * 128],
                    b["kT"][hh * 64:(hh + 1) * 64, slot * 1024 + c * 128:slot * 1024 + c * 128 + 128],
                    qT[hh * 64:(hh + 1) * 64, c * 128:(c + 1) * 128],
                    start=True, stop=True),
                    deps=d, inc=(1 if tau == 4 else 0), tag=f"qk{j}_{h}")
            qktok = P.token("t")
            exp_tok[j][h] = P.rec("s", lambda s, h=h, pS=pS: s.activation(
                b["probs"][:, h * 640:(h + 1) * 640], pS[:, :], AF.Exp, bias=0.0, scale=0.125),
                deps=[qktok, av_tok[j - 1] if (h <= 1 and j >= 1) else None], tag=f"pexp{j}_{h}")
        for r in range(4):
            c0 = r * 4 * 640
            apL = bass.AP(b["probs"], c0, [[16 * 640, 128], [640, 4], [1, 128]])
            apH = bass.AP(b["probs"], c0 + 512, [[16 * 640, 128], [640, 4], [1, 128]])
            P.rec("v", lambda v, apL=apL, r=r: v.tensor_tensor(
                apL, apL, b["trimL"][:, r * 512:(r + 1) * 512], ALU.mult),
                deps=[exp_tok[j][r * 4 + 3]], tag=f"zl{j}_{r}")
            ztok = P.rec("v", lambda v, apH=apH, r=r: v.tensor_tensor(
                apH, apH, b["trimH"][:, r * 512:(r + 1) * 512], ALU.mult),
                tag=f"zh{j}_{r}")
            if j <= 1:
                width = 256 if j == 0 else 128
                apE = bass.AP(b["probs"], c0, [[16 * 640, 128], [640, 4], [1, width]])
                ztok = P.rec("v", lambda v, apE=apE: v.memset(apE, 0.0),
                             self_deps=[ztok], tag=f"ze{j}_{r}")
            if j >= NT - 2:
                start = 512 if j == NT - 2 else 384
                apE = bass.AP(b["probs"], c0 + start,
                              [[16 * 640, 128], [640, 4], [1, 640 - start]])
                ztok = P.rec("v", lambda v, apE=apE: v.memset(apE, 0.0),
                             self_deps=[ztok], tag=f"ze{j}_{r}")
            zgrp_tok[j][r] = ztok

    def attn_av(j):
        oevg = [None] * 4
        for r in range(4):
            pw = pows[r % 2]
            for hq in range(4):
                h = r * 4 + hq
                for tau in range(5):
                    tt = j - 2 + tau
                    slot = (tt % RING) if tt >= 0 else ((tt + RING) % RING)
                    d = []
                    if hq == 0 and tau == 0:
                        if r < 2:
                            d = [zgrp_tok[j][r],
                                 aevh_tok[j - 1][r] if j >= 1 else None]
                        else:
                            d = [zgrp_tok[j][r], oevg[r - 2]]
                        if dbg is not None and "_prtok" in dbg and j == 10:
                            d = d + [dbg["_prtok"]]
                    P.rec("t", lambda t, slot=slot, h=h, hq=hq, tau=tau, pw=pw: t.matmul(
                        pw[0:65, hq * 128:(hq + 1) * 128],
                        b["vr"][:, slot * 1040 + h * 65:slot * 1040 + h * 65 + 65],
                        b["probs"][:, h * 640 + tau * 128:h * 640 + (tau + 1) * 128],
                        start=(tau == 0), stop=(tau == 4)),
                        deps=d, inc=(1 if (hq == 3 and tau == 4) else 0), tag=f"av{j}_{r}")
            avtok = P.token("t")
            if r == 3:
                av_tok[j] = avtok
            ap_even_src = bass.AP(pw, 0, [[512, 64], [256, 2], [1, 128]])
            ap_odd_src = bass.AP(pw, 128, [[512, 64], [256, 2], [1, 128]])
            P.rec("s", lambda s, r=r, ap=ap_even_src: s.copy(
                b["oTf"][0:64, r * 256:(r + 1) * 256], ap), deps=[avtok], tag=f"oev{j}_{r}")
            P.rec("s", lambda s, r=r, ap=ap_odd_src: s.copy(
                b["oTf"][64:128, r * 256:(r + 1) * 256], ap), tag=f"oev2{j}_{r}")
            oevg[r] = P.rec("s", lambda s, r=r, pw=pw: s.copy(
                b["sums"][0:1, r * 512:(r + 1) * 512], pw[64:65, 0:512]), tag=f"sev{j}_{r}")
            oev_tok[j] = oevg[r]
            oevv_tok[j] = oevg[r]
        if dbg is not None and "oT" in dbg:
            P.rec("y", lambda y, j=j: y.dma_start(
                dbg["oT"][j * 128:(j + 1) * 128, :], b["oTf"][:, :]),
                deps=[oev_tok[j]], channel=cp + f"do{j % 2}", tag=f"dbgoT{j}")
            P.rec("y", lambda y, j=j: y.dma_start(
                dbg["sums"][j:j + 1, :], b["sums"][:, :]),
                deps=[oev_tok[j]], channel=cp + f"ds{j % 2}", tag=f"dbgsm{j}")
        rlntk = P.rec("s", lambda s: s.activation(b["sums"][0:1, 2048:4096], b["sums"][0:1, 0:2048],
                                          AF.Ln, bias=0.0, scale=1.0),
              self_deps=[oev_tok[j]], tag=f"rln{j}")
        rex_tok[j] = P.rec("s", lambda s: s.activation(b["sums"][0:1, 2048:4096],
                                                       b["sums"][0:1, 2048:4096],
                                                       AF.Exp, bias=0.0, scale=-1.0),
                           self_deps=[rlntk], tag=f"rex{j}")

    def attn_fin(j):
        if j < 0:
            return
        p = j % 2
        ntoks = [None, None]
        for half in range(2):
            pw = pows[half]
            for rh in range(2):
                rhs = bass.AP(b["sums"], 2048 + half * 1024 + rh * 128,
                              [[4096, 1], [256, 4], [1, 128]])
                dd = ([rex_tok[j], oevv_tok[j]] +
                      (list(glob_tok) if j == 0 else [])) if (rh == 0) else ()
                P.rec("t", lambda t, rh=rh, rhs=rhs, pw=pw: t.matmul(
                    pw[rh * 64:(rh + 1) * 64, 0:512],
                    ones1[0:1, 0:64], rhs, start=True, stop=True),
                    deps=dd, inc=(1 if rh == 1 else 0), tag=f"sb{j}_{half}")
            btok = P.token("t")
            ntoks[half] = P.rec("v", lambda v, half=half, pw=pw: v.tensor_tensor(
                b["oTn"][:, half * 512:(half + 1) * 512],
                b["oTf"][:, half * 512:(half + 1) * 512], pw[:, :], ALU.mult),
                deps=[btok], tag=f"on{j}_{half}")
        norm_tok[j] = ntoks[1]
        aevh = [None, None]
        for half in range(2):
            pw = pows[half]
            for k in range(8):
                d = [ntoks[half]] if k == 0 else ()
                P.rec("t", lambda t, k=k, half=half, pw=pw: t.matmul(
                    pw[:, :], b["oTn"][:, k * 128:(k + 1) * 128],
                    b["wo"][:, k * 1024 + half * 512:k * 1024 + (half + 1) * 512],
                    start=(k == 0), stop=(k == 7)),
                    deps=d, inc=(1 if k == 7 else 0), tag=f"wo{j}_{half}")
            wotok = P.token("t")
            ab = b["abuf"][:, p * 1024:(p + 1) * 1024]
            aevh[half] = P.rec("v", lambda v, ab=ab, half=half, pw=pw: v.tensor_copy(
                ab[:, half * 512:(half + 1) * 512], pw[:, :]),
                deps=[wotok, res_tok[j - 2] if (half == 0 and j >= 2) else None],
                tag=f"aev{j}_{half}")
        aevh_tok[j] = aevh
        attn_pe_tok[j] = P.token("t")
        aevt = P.token("v")
        stt = b["stats"][:, 32 + p * 16:32 + (p + 1) * 16]
        ab = b["abuf"][:, p * 1024:(p + 1) * 1024]

        def fbn_pair(v, ab=ab, stt=stt):
            v.bn_stats(stt[:, 0:6], ab[:, 0:512])
            return v.bn_stats(stt[:, 6:12], ab[:, 512:1024])
        fpt = P.rec("v", fbn_pair, self_deps=[aevt], tag=f"fbn{j}")
        vt = P.rec("v", lambda v, stt=stt: v.bn_aggr(stt[:, 12:14], stt[:, 0:12]),
                   self_deps=[fpt], tag=f"fba{j}")
        P.rec("s", lambda s, stt=stt: s.activation(stt[:, 14:15], stt[:, 13:14], AF.Ln,
                                                   bias=epsv[:, :], scale=1.0),
              deps=[vt], tag=f"fln{j}")
        flntk = P.token("s")
        fs = P.rec("s", lambda s, stt=stt: s.activation(stt[:, 14:15], stt[:, 14:15],
                                                        AF.Exp, bias=0.0, scale=-0.5),
                   self_deps=[flntk], tag=f"fex{j}")
        ob = b["outb"][:, p * 1024:(p + 1) * 1024]
        xsl = b["xin"][:, (j % RING) * 1024:(j % RING + 1) * 1024]
        P.rec("v", lambda v, ab=ab, ob=ob, stt=stt: v.tensor_scalar(
            ob[:, :], ab[:, :], stt[:, 12:13], stt[:, 14:15], ALU.subtract, ALU.mult),
            deps=[fs, out_tok[j - 2] if j >= 2 else None], tag=f"fap{j}")
        res_tok[j] = P.rec("v", lambda v, ob=ob, xsl=xsl: v.tensor_tensor(
            ob[:, :], ob[:, :], xsl[:, :], ALU.add), tag=f"fx2{j}")
        out_tok[j] = P.rec("y", lambda y, j=j, ob=ob: y.dma_start(
            out_dram[j * 128:(j + 1) * 128, :], ob[:, :]),
            deps=[res_tok[j]], channel=cp + f"ot{j % 2}", tag=f"out{j}")

    for i in range(3):
        load_tile(i)
    produce_ln(0)
    produce(0)
    produce_b(0)
    produce_ln(1)
    produce(1)
    produce_b(1)
    produce_ln(2)
    produce(2)
    for j in range(NT):
        load_tile(j + 3)
        produce_b(j + 2)
        produce_ln(j + 3)
        attn_qk(j)
        produce(j + 3)
        attn_fin(j - 1)
        attn_av(j)
    attn_fin(NT - 1)
    return [out_tok[NT - 2], out_tok[NT - 1],
            P.token("t"), P.token("v"), P.token("s"), P.token("g")]


def attn_host_weights(W_qkv, W_o):
    p = np.arange(128)
    cL = np.arange(128)
    triL = (cL[None, :] <= p[:, None]).astype(np.float32)
    triH = (cL[None, :] >= p[:, None]).astype(np.float32)
    return {
        "wqkvT": np.ascontiguousarray(np.asarray(W_qkv, np.float32).T),
        "woT": np.ascontiguousarray(np.asarray(W_o, np.float32).T),
        "trimL": np.ascontiguousarray(np.tile(triL, (1, 16))),
        "trimH": np.ascontiguousarray(np.tile(triH, (1, 16))),
    }


IRING = 7


def alloc_iattn_bufs(nc, tagp):
    gs = []
    bufs = {}

    def sb(name, shape, dt):
        g = nc.sbuf_tensor(tagp + name, shape, dt)
        t = g.__enter__()
        gs.append(g)
        bufs[name] = t
        return t

    sb("wqkv", [128, 8 * 1536], BF16)
    sb("wo", [128, 4 * 1024], BF16)
    sb("qs", [128, 512], F32)
    sb("ks", [128, 512], F32)
    sb("xin", [128, IRING * 1024], F32)
    sb("x0", [128, 4 * 1024], F32)
    sb("stats", [128, 4 * 16], F32)
    sb("hT", [128, 2 * 1040], BF16)
    sb("rot", [128, 2 * 2048], F32)
    sb("qb", [128, 2 * 512], F32)
    sb("kb", [128, 2 * 512], F32)
    sb("qTb", [128, 2 * 512], BF16)
    sb("kTb", [128, 2 * 512], BF16)
    sb("x0b", [128, 2 * 1024], BF16)
    sb("qsc", [128, 2 * 512], F32)
    sb("qsc2", [128, 2 * 512], F32)
    sb("ssq", [128, 32], F32)
    sb("qT", [128, 4 * 512], BF16)
    sb("kT", [128, IRING * 512], BF16)
    sb("vr", [128, IRING * 520], BF16)
    sb("probs", [128, 8 * 640], BF16)
    sb("trimL", [128, 8 * 128], BF16)
    sb("trimH", [128, 8 * 128], BF16)
    sb("oTf", [128, 512], F32)
    sb("sums", [1, 2048], F32)
    sb("oTn", [128, 512], BF16)
    sb("outb", [128, 2 * 1024], F32)
    return gs, bufs


def alloc_iattn_psums(nc, tagp):
    gs = []
    ps = {}

    def palloc(name, shape, dt):
        g = nc.psum_tensor(tagp + name, shape, dt)
        t = g.__enter__()
        gs.append(g)
        ps[name] = t
        return t

    palloc("pS0", [128, 640], F32)
    palloc("pS1", [128, 640], F32)
    palloc("pQ", [128, 512], F32)
    palloc("pQ2", [128, 512], F32)
    palloc("pOW", [128, 512], F32)
    palloc("pT", [128, 1024], BF16)
    return gs, ps


def record_attn_inner(P, nc, glob, ps, b, wdram, in_dram, out_dram, cp="i2",
                      glob_tok=(), weights_dep=None, with_ln=True, dbg=None, phase_dep=()):
    ident, epsv, ones1 = glob["ident"], glob["eps"], glob["ones1"]
    pS2 = [ps["pS0"], ps["pS1"]]
    pQ, pOW, pT = ps["pQ"], ps["pOW"], ps["pT"]
    pQ2 = ps["pQ2"]

    for k in range(8):
        P.rec("g", lambda g, k=k: g.dma_start(
            b["wqkv"][:, k * 1536:(k + 1) * 1536], wdram["wqkv"][k * 128:(k + 1) * 128, :]),
            deps=(([weights_dep] if weights_dep else []) + list(phase_dep)) if k < 4 else (),
            channel=cp + f"w{k % 4}", tag="wqkvld")
    for k in range(4):
        P.rec("g", lambda g, k=k: g.dma_start(
            b["wo"][:, k * 1024:(k + 1) * 1024], wdram["wo"][k * 128:(k + 1) * 128, :]),
            channel=cp + f"w{k % 4}", tag="wold")
    for ni, name in enumerate(("qs", "ks", "trimL", "trimH")):
        P.rec("g", lambda g, name=name: g.dma_start(b[name][:, :], wdram[name][:, :]),
              channel=cp + f"w{ni % 4}", tag=name)
    vzt = P.rec("g", lambda g: g.memset(b["vr"][:, :], 0.0), tag="vz")
    P.rec("g", lambda g: g.memset(
        bass.AP(b["vr"], 64, [[IRING * 520, 128], [65, 8 * IRING], [1, 1]]), 1.0),
        self_deps=[vzt], tag="vones")
    P.rec("g", lambda g: g.memset(b["kT"][:, 3 * 512:7 * 512], 0.0), tag="kz")
    wtok4 = [P.token(cp + f"w{c}") for c in range(4)]
    wtok = wtok4[0]
    init_tok = P.token("g")

    xin_tok = [None] * (NT + 8)
    rot_tok = [None] * (NT + 8)
    cast_tok_arr = [None] * NT
    dqt_arr = [None] * NT
    x0_tok = [None] * NT
    trx_tok = [None] * NT
    hT_tok = [None] * NT
    qkv_pe_tok = [None] * NT
    qkproc_tok = [None] * NT
    trqk_tok = [None] * NT
    qT_ev_tok = [None] * NT
    ring_tok = [None] * NT
    exp_tok = [[None] * IH for _ in range(NT)]
    zero_tok = [None] * NT
    av_tok = [None] * NT
    oev_tok = [None] * NT
    rex_tok = [None] * NT
    norm_tok = [None] * NT
    attn_pe_tok = [None] * NT
    res_tok = [None] * NT
    out_tok = [None] * NT

    def load_tile(i):
        if i >= NT:
            return
        dep = [res_tok[i - IRING]] if i >= IRING else list(phase_dep)
        xin_tok[i] = P.rec("y", lambda y, i=i: y.dma_start(
            b["xin"][:, (i % IRING) * 1024:(i % IRING + 1) * 1024],
            in_dram[i * 128:(i + 1) * 128, :]),
            deps=dep, channel=cp + f"xi{i % IRING}", tag=f"xin{i}")


    def produce_ln(i):
        if i >= NT:
            return
        p = i % 2
        dep2 = [qkproc_tok[i - 2]] if i >= 2 else []
        rot_tok[i] = P.rec("y", lambda y, i=i: y.dma_start(
            b["rot"][:, (i % 2) * 2048:(i % 2 + 1) * 2048],
            wdram["rot"][i * 128:(i + 1) * 128, :]),
            deps=dep2, channel=cp + f"ro{i % 2}", tag=f"rot{i}")
        xsl = b["xin"][:, (i % IRING) * 1024:(i % IRING + 1) * 1024]
        x0 = b["x0"][:, (i % 4) * 1024:(i % 4 + 1) * 1024]
        x0war = res_tok[i - 4] if i >= 4 else None
        if with_ln:
            stt = b["stats"][:, p * 16:(p + 1) * 16]

            def bn_pair(v, xsl=xsl, stt=stt):
                v.bn_stats(stt[:, 0:6], xsl[:, 0:512])
                return v.bn_stats(stt[:, 6:12], xsl[:, 512:1024])
            pr = P.rec("v", bn_pair, deps=[xin_tok[i]], tag=f"bn{i}")
            vagg = P.rec("v", lambda v, stt=stt: v.bn_aggr(stt[:, 12:14], stt[:, 0:12]),
                         self_deps=[pr], tag=f"bnag{i}")
            P.rec("s", lambda s, stt=stt: s.activation(stt[:, 14:15], stt[:, 13:14], AF.Ln,
                                                       bias=epsv[:, :], scale=1.0),
                  deps=[vagg] + (list(glob_tok) if i < 2 else []), tag=f"ln{i}")
            lntk = P.token("s")
            srstd = P.rec("s", lambda s, stt=stt: s.activation(
                stt[:, 14:15], stt[:, 14:15], AF.Exp, bias=0.0, scale=-0.5),
                self_deps=[lntk], tag=f"exp{i}")
            x0_tok[i] = P.rec("v", lambda v, xsl=xsl, x0=x0, stt=stt: v.tensor_scalar(
                x0[:, :], xsl[:, :], stt[:, 12:13], stt[:, 14:15], ALU.subtract, ALU.mult),
                deps=[srstd, x0war], tag=f"x0{i}")
        else:
            x0_tok[i] = P.rec("v", lambda v, xsl=xsl, x0=x0: v.tensor_copy(x0[:, :], xsl[:, :]),
                              deps=[xin_tok[i], x0war] + (list(glob_tok) if i < 2 else []),
                              tag=f"x0{i}")
        xb = b["qsc"][:, p * 512:(p + 1) * 512]
        x0b = b["x0b"][:, p * 1024:(p + 1) * 1024]
        cast_tok_arr[i] = P.rec("v", lambda v, x0=x0, x0b=x0b: v.tensor_copy(
            x0b[:, :], x0[:, :]),
            deps=[trx_tok[i - 2] if i >= 2 else None] +
                 (list(glob_tok) if i < 2 else []), tag=f"x0b{i}")

    def produce(i):
        if i >= NT:
            return
        p = i % 2
        x0 = b["x0"][:, (i % 4) * 1024:(i % 4 + 1) * 1024]
        x0b = b["x0b"][:, p * 1024:(p + 1) * 1024]
        cast_tok = cast_tok_arr[i]
        rsl = b["rot"][:, (i % 2) * 2048:(i % 2 + 1) * 2048]
        dqt = dqt_arr[i]
        hT = b["hT"][:, p * 1040:(p + 1) * 1040]
        deps = [cast_tok, hT_tok[i - 2] if i >= 2 else None,
                qkv_pe_tok[i - 1] if i >= 1 else None,
                ring_tok[i - 1] if i >= 1 else None]
        for k in range(8):
            P.rec("t", lambda t, k=k, x0b=x0b: t.transpose(
                pT[:, k * 128:(k + 1) * 128], x0b[:, k * 128:(k + 1) * 128], ident[:, :]),
                deps=(deps if k == 0 else ()), inc=(1 if k == 7 else 0), tag=f"trx{i}")
        trx_tok[i] = P.token("t")
        dstap = bass.AP(b["hT"], p * 1040 + 1, [[2 * 1040, 128], [130, 8], [1, 128]])
        ev = P.rec("v", lambda v, dstap=dstap: v.tensor_copy(dstap, pT[:, :]),
                   deps=[trx_tok[i], qkv_pe_tok[i - 2] if i >= 2 else None], tag=f"hTev{i}")
        if i == 0:
            hT_tok[i] = P.rec("v", lambda v: v.memset(
                bass.AP(b["hT"], 0, [[2 * 1040, 128], [130, 8], [1, 1]]), 0.0),
                self_deps=[ev], tag=f"hTz{i}")
        else:
            po = (i - 1) % 2
            src_ap = bass.AP(b["hT"], po * 1040 + 128, [[2 * 1040, 128], [130, 8], [1, 1]])
            dst_ap = bass.AP(b["hT"], p * 1040 + 0, [[2 * 1040, 128], [130, 8], [1, 1]])
            hT_tok[i] = P.rec("v", lambda v, src_ap=src_ap, dst_ap=dst_ap: v.tensor_copy(
                dst_ap, src_ap), self_deps=[ev], tag=f"hTpc{i}")
        evq = None
        for ph in range(3):
            pq = pQ2 if ph == 1 else pQ
            for k in range(8):
                ofs = 1 if k < 4 else 0
                d = []
                if ph == 0 and k == 0:
                    d = [hT_tok[i]] + wtok4 + [
                         (qkproc_tok[i - 1] if i >= 1 else None)]
                if ph == 2 and k == 0:
                    d = [ev_q0]
                P.rec("t", lambda t, k=k, ph=ph, ofs=ofs, p=p, pq=pq: t.matmul(
                    pq[:, :],
                    bass.AP(b["hT"], p * 1040 + k * 130 + ofs, [[2 * 1040, 128], [1, 128]]),
                    b["wqkv"][:, k * 1536 + ph * 512:k * 1536 + ph * 512 + 512],
                    start=(k == 0), stop=(k == 7)),
                    deps=d, inc=(1 if k == 7 else 0), tag=f"qkv{i}_{ph}")
            mmtok = P.token("t")
            if ph == 0:
                evq = P.rec("v", lambda v, p=p: v.tensor_copy(
                    b["qb"][:, p * 512:(p + 1) * 512], pQ[:, :]),
                    deps=[mmtok, qkproc_tok[i - 2] if i >= 2 else None], tag=f"qev{i}")
                ev_q0 = evq
            elif ph == 1:
                evq = P.rec("v", lambda v, p=p: v.tensor_copy(
                    b["kb"][:, p * 512:(p + 1) * 512], pQ2[:, :]),
                    deps=[mmtok], tag=f"kev{i}")
            else:
                slot = i % IRING
                dstv = bass.AP(b["vr"], slot * 520, [[IRING * 520, 128], [65, 8], [1, 64]])
                vev = P.rec("v", lambda v, dstv=dstv: v.tensor_copy(dstv, pQ[:, :]),
                            deps=[mmtok, avl_dep(i)], tag=f"vev{i}")
        qkv_pe_tok[i] = P.token("t")
        if dbg is not None and "x0" in dbg:
            P.rec("y", lambda y, i=i, x0=x0: y.dma_start(
                dbg["x0"][i * 128:(i + 1) * 128, :], x0[:, :]),
                deps=[x0_tok[i]], channel=cp + f"d0{i % 2}", tag=f"dbgx0{i}")
        rsl = b["rot"][:, (i % 2) * 2048:(i % 2 + 1) * 2048]
        if dbg is not None and "rot" in dbg:
            P.rec("y", lambda y, i=i, rsl=rsl: y.dma_start(
                dbg["rot"][i * 128:(i + 1) * 128, :], rsl),
                deps=[rot_tok[i]], channel=cp + f"d6{i % 2}", tag=f"dbgrot{i}")
        dqt = None
        if dbg is not None and "qraw" in dbg:
            dqt = P.rec("y", lambda y, i=i, p=p: y.dma_start(
                dbg["qraw"][i * 128:(i + 1) * 128, :], b["qb"][:, p * 512:(p + 1) * 512]),
                deps=[ev_q0], channel=cp + f"d4{i % 2}", tag=f"dbgqr{i}")
        bq = b["qb"][:, p * 512:(p + 1) * 512]
        bk = b["kb"][:, p * 512:(p + 1) * 512]
        sqq = b["qsc"][:, p * 512:(p + 1) * 512]
        sqk = b["qsc2"][:, p * 512:(p + 1) * 512]
        baseq = p * 16
        basek = p * 16 + 8
        sqt = P.rec("v", lambda v: v.tensor_tensor(sqq[:, :], bq[:, :], bq[:, :], ALU.mult),
                    deps=[rot_tok[i]], tag=f"sqq{i}")
        for h in range(8):
            P.rec("v", lambda v, h=h: v.tensor_reduce(
                b["ssq"][:, baseq + h:baseq + h + 1], sqq[:, h * 64:(h + 1) * 64],
                AX.X, ALU.add), self_deps=([sqt] if h == 0 else ()), tag=f"ssq{i}_{h}")
        sqtk = P.rec("v", lambda v: v.tensor_tensor(sqk[:, :], bk[:, :], bk[:, :], ALU.mult),
                     tag=f"sqk{i}")
        for h in range(8):
            P.rec("v", lambda v, h=h: v.tensor_reduce(
                b["ssq"][:, basek + h:basek + h + 1], sqk[:, h * 64:(h + 1) * 64],
                AX.X, ALU.add), self_deps=([sqtk] if h == 0 else ()), tag=f"ssk{i}_{h}")
        rdt = P.token("v")
        P.rec("v", lambda v: v.tensor_scalar(
            b["ssq"][:, p * 16:p * 16 + 16], b["ssq"][:, p * 16:p * 16 + 16], 1e-24, None,
            ALU.max), self_deps=[rdt], tag=f"cl{i}")
        vtk = P.token("v")
        P.rec("s", lambda s: s.activation(
            b["ssq"][:, p * 16:p * 16 + 16], b["ssq"][:, p * 16:p * 16 + 16], AF.Ln,
            bias=0.0, scale=1.0), deps=[vtk], tag=f"l2l{i}")
        stk = P.token("s")
        rtk = P.rec("s", lambda s: s.activation(
            b["ssq"][:, p * 16:p * 16 + 16], b["ssq"][:, p * 16:p * 16 + 16], AF.Exp,
            bias=0.0, scale=-0.5), self_deps=[stk], tag=f"l2e{i}")
        if dbg is not None and "ssq" in dbg:
            P.rec("y", lambda y, i=i: y.dma_start(
                dbg["ssq"][i * 128:(i + 1) * 128, :], b["ssq"][:, baseq:baseq + 8]),
                deps=[rtk], channel=cp + f"d5{i % 2}", tag=f"dbgss{i}")
        proc_toks = []
        for (which, bb, base, scv, c0, s0) in (("q", bq, baseq, "qs", 0, 512),
                                               ("k", bk, basek, "ks", 1024, 1536)):
            gate = [rtk] + ([dqt] if (dqt is not None and which == "q") else [])
            for h in range(8):
                P.rec("v", lambda v, bb=bb, h=h, base=base: v.tensor_scalar(
                    bb[:, h * 64:(h + 1) * 64], bb[:, h * 64:(h + 1) * 64],
                    b["ssq"][:, base + h:base + h + 1], None, ALU.mult),
                    deps=(gate if (which == "q" and h == 0) else ()), tag=f"l2a{which}{i}_{h}")
            P.rec("v", lambda v, bb=bb, scv=scv: v.tensor_tensor(
                bb[:, :], bb[:, :], b[scv][:, :], ALU.mult), tag=f"qs{which}{i}")
            if dbg is not None and "qu" in dbg and which == "q":
                ut = P.token("v")
                P.rec("y", lambda y, i=i, bb=bb: y.dma_start(
                    dbg["qu"][i * 128:(i + 1) * 128, :], bb[:, :]),
                    deps=[ut], channel=cp + f"d3{i % 2}", tag=f"dbgu{i}")
            cosr = rsl[:, c0:c0 + 512]
            sinr = rsl[:, s0:s0 + 512]
            P.rec("v", lambda v, bb=bb, cosr=cosr: v.tensor_tensor(
                sqq[:, :], bb[:, :], cosr, ALU.mult), tag=f"rc{which}{i}")
            swap = bass.AP(b["qb" if which == "q" else "kb"], p * 512 + 32,
                           [[1024, 128], [64, 8], [-32, 2], [1, 32]])
            rst = P.rec("v", lambda v, swap=swap, sinr=sinr: v.tensor_tensor(
                sqk[:, :], swap, sinr, ALU.mult), tag=f"rs{which}{i}")
            obf = b["qTb" if which == "q" else "kTb"][:, p * 512:(p + 1) * 512]
            tk = P.rec("v", lambda v, obf=obf: v.tensor_tensor(
                obf[:, :], sqq[:, :], sqk[:, :], ALU.add),
                deps=[trqk_tok[i - 2] if i >= 2 else None],
                self_deps=[rst], tag=f"ra{which}{i}")
            proc_toks.append(tk)
        qkproc_tok[i] = proc_toks[-1]
        if dbg is not None and "qk" in dbg:
            P.rec("y", lambda y, i=i, p=p: y.dma_start(
                dbg["qk"][i * 128:(i + 1) * 128, 0:512], b["qTb"][:, p * 512:(p + 1) * 512]),
                deps=[qkproc_tok[i]], channel=cp + f"d1{i % 2}", tag=f"dbgq{i}")
            P.rec("y", lambda y, i=i, p=p: y.dma_start(
                dbg["qk"][i * 128:(i + 1) * 128, 512:1024], b["kTb"][:, p * 512:(p + 1) * 512]),
                deps=[qkproc_tok[i]], channel=cp + f"d2{i % 2}", tag=f"dbgk{i}")

    def produce_b(i):
        if i >= NT:
            return
        p = i % 2
        slot = i % IRING
        for which in ("q", "k"):
            bfb = b["qTb" if which == "q" else "kTb"][:, p * 512:(p + 1) * 512]
            deps0 = [qkproc_tok[i], ring_tok[i - 1] if i >= 1 else None,
                     (qT_ev_tok[i - 1] if i >= 1 else None)]
            for k in range(4):
                kk = k if which == "q" else 4 + k
                P.rec("t", lambda t, k=k, kk=kk, bfb=bfb: t.transpose(
                    pT[:, kk * 128:(kk + 1) * 128], bfb[:, k * 128:(k + 1) * 128],
                    ident[:, :]),
                    deps=(deps0 if (which == "q" and k == 0) else ()),
                    inc=(1 if k == 3 else 0), tag=f"tr{which}{i}")
        trqk_tok[i] = P.token("t")
        qT_ev_tok[i] = P.rec("v", lambda v, i=i: v.tensor_copy(
            b["qT"][:, (i % 4) * 512:(i % 4 + 1) * 512], pT[:, 0:512]),
            deps=[trqk_tok[i], attn_pe_tok[i - 3] if i >= 3 else None], tag=f"qTev{i}")
        ring_tok[i] = P.rec("v", lambda v, slot=slot: v.tensor_copy(
            b["kT"][:, slot * 512:(slot + 1) * 512], pT[:, 512:1024]),
            deps=[avl_dep(i)], self_deps=[qT_ev_tok[i]], tag=f"kTev{i}")

    def avl_dep(i):
        t = i - IRING + 4
        return attn_pe_tok[t] if t >= 0 else init_tok

    def slot_of(tt):
        return (tt % IRING) if tt >= 0 else ((tt + IRING) % IRING)

    def attn_qk(j):
        kdep = [ring_tok[j]]
        for h in range(IH):
            hh = h % 2
            c = h // 2
            pS = pS2[h % 2]
            if h >= 2:
                war = exp_tok[j][h - 2]
            elif j >= 1:
                war = exp_tok[j - 1][IH - 2 + h]
            else:
                war = None
            for tau in range(5):
                tt = j - 4 + tau
                slot = slot_of(tt)
                d = ()
                if tau == 0:
                    d = (list(kdep) if h == 0 else []) + [war, qT_ev_tok[j]]
                P.rec("t", lambda t, slot=slot, c=c, hh=hh, tau=tau, pS=pS, j=j: t.matmul(
                    pS[:, tau * 128:(tau + 1) * 128],
                    b["kT"][hh * 64:(hh + 1) * 64,
                            slot * 512 + c * 128:slot * 512 + c * 128 + 128],
                    b["qT"][hh * 64:(hh + 1) * 64,
                            (j % 4) * 512 + c * 128:(j % 4) * 512 + c * 128 + 128],
                    start=True, stop=True),
                    deps=d, inc=(1 if tau == 4 else 0), tag=f"st{j}_{h}")
            sttok = P.token("t")
            exp_tok[j][h] = P.rec("s", lambda s, h=h, pS=pS: s.activation(
                b["probs"][:, h * 640:(h + 1) * 640], pS[:, :], AF.Exp, bias=0.0, scale=1.0),
                deps=[sttok, av_tok[j - 1] if (h <= 1 and j >= 1) else None],
                tag=f"pexp{j}_{h}")
        zdeps = [exp_tok[j][IH - 1]]
        apL = bass.AP(b["probs"], 0, [[8 * 640, 128], [640, 8], [1, 128]])
        apH = bass.AP(b["probs"], 512, [[8 * 640, 128], [640, 8], [1, 128]])
        P.rec("v", lambda v, apL=apL: v.tensor_tensor(apL, apL, b["trimL"][:, :], ALU.mult),
              deps=zdeps, tag=f"zl{j}")
        ztok = P.rec("v", lambda v, apH=apH: v.tensor_tensor(apH, apH, b["trimH"][:, :], ALU.mult),
                     tag=f"zh{j}")
        if j <= 3:
            width = 512 - j * 128
            apE = bass.AP(b["probs"], 0, [[8 * 640, 128], [640, 8], [1, width]])
            ztok = P.rec("v", lambda v, apE=apE: v.memset(apE, 0.0),
                         self_deps=[ztok], tag=f"ze{j}")
        zero_tok[j] = ztok

    def attn_av(j):
        oev_group = None
        for r in range(2):
            for hq in range(4):
                h = r * 4 + hq
                for tau in range(5):
                    tt = j - 4 + tau
                    slot = slot_of(tt)
                    d = []
                    if hq == 0 and tau == 0:
                        d = [zero_tok[j], oev_group if r >= 1 else
                             (res_tok[j - 1] if j >= 1 else None),
                             norm_tok[j - 1] if (r == 0 and j >= 1) else None]
                        if dbg is not None and "_prtok" in dbg and j == 10:
                            d = d + [dbg["_prtok"]]
                    P.rec("t", lambda t, slot=slot, h=h, hq=hq, tau=tau: t.matmul(
                        pOW[0:65, hq * 128:(hq + 1) * 128],
                        b["vr"][:, slot * 520 + h * 65:slot * 520 + h * 65 + 65],
                        b["probs"][:, h * 640 + tau * 128:h * 640 + (tau + 1) * 128],
                        start=(tau == 0), stop=(tau == 4)),
                        deps=d, inc=(1 if (hq == 3 and tau == 4) else 0), tag=f"av{j}_{r}")
            avtok = P.token("t")
            if r == 1:
                av_tok[j] = avtok
            ap_even_src = bass.AP(pOW, 0, [[512, 64], [256, 2], [1, 128]])
            ap_odd_src = bass.AP(pOW, 128, [[512, 64], [256, 2], [1, 128]])
            P.rec("s", lambda s, r=r, ap=ap_even_src: s.copy(
                b["oTf"][0:64, r * 256:(r + 1) * 256], ap), deps=[avtok], tag=f"oev{j}_{r}")
            P.rec("s", lambda s, r=r, ap=ap_odd_src: s.copy(
                b["oTf"][64:128, r * 256:(r + 1) * 256], ap), tag=f"oev2{j}_{r}")
            oev_group = P.rec("s", lambda s, r=r: s.copy(
                b["sums"][0:1, r * 512:(r + 1) * 512], pOW[64:65, 0:512]), tag=f"sev{j}_{r}")
            oev_tok[j] = oev_group
        if dbg is not None and "oT" in dbg:
            P.rec("y", lambda y, j=j: y.dma_start(
                dbg["oT"][j * 128:(j + 1) * 128, :], b["oTf"][:, :]),
                deps=[oev_tok[j]], channel=cp + f"d7{j % 2}", tag=f"dbgoT{j}")
            P.rec("y", lambda y, j=j: y.dma_start(
                dbg["sm"][j:j + 1, :], b["sums"][0:1, 0:1024]),
                deps=[oev_tok[j]], channel=cp + f"d8{j % 2}", tag=f"dbgsm{j}")
        rlntk = P.rec("s", lambda s: s.activation(b["sums"][0:1, 1024:2048],
                                                  b["sums"][0:1, 0:1024],
                                                  AF.Ln, bias=0.0, scale=1.0),
                      self_deps=[oev_tok[j]], tag=f"rln{j}")
        rex_tok[j] = P.rec("s", lambda s: s.activation(b["sums"][0:1, 1024:2048],
                                                       b["sums"][0:1, 1024:2048],
                                                       AF.Exp, bias=0.0, scale=-1.0),
                           self_deps=[rlntk], tag=f"rex{j}")

    def attn_fin(j):
        if j < 0:
            return
        p = j % 2
        d = [rex_tok[j]]
        if j >= 1:
            d.append(res_tok[j - 1])
        for cc in range(4):
            for rh in range(2):
                h2 = 2 * cc + rh
                dd = (d + (list(glob_tok) if j == 0 else [])) if (cc == 0 and rh == 0) else ()
                P.rec("t", lambda t, cc=cc, rh=rh, h2=h2: t.matmul(
                    pOW[rh * 64:(rh + 1) * 64, cc * 128:(cc + 1) * 128],
                    glob["ones1f"][0:1, 0:64],
                    b["sums"][0:1, 1024 + h2 * 128:1024 + (h2 + 1) * 128],
                    start=True, stop=True),
                    deps=dd, inc=(1 if (cc == 3 and rh == 1) else 0), tag=f"sbc{j}")
        btok = P.token("t")
        norm_tok[j] = P.rec("v", lambda v: v.tensor_tensor(
            b["oTn"][:, :], b["oTf"][:, :], pOW[:, :], ALU.mult),
            deps=[btok], tag=f"on{j}")
        aev_half = None
        ob = b["outb"][:, p * 1024:(p + 1) * 1024]
        x0 = b["x0"][:, (j % 4) * 1024:(j % 4 + 1) * 1024]
        for half in range(2):
            for k in range(4):
                if half == 0 and k == 0:
                    d = [norm_tok[j]]
                elif half == 1 and k == 0:
                    d = [aev_half]
                else:
                    d = ()
                P.rec("t", lambda t, k=k, half=half: t.matmul(
                    pOW[:, :], b["oTn"][:, k * 128:(k + 1) * 128],
                    b["wo"][:, k * 1024 + half * 512:k * 1024 + (half + 1) * 512],
                    start=(k == 0), stop=(k == 3)),
                    deps=d, inc=(1 if k == 3 else 0), tag=f"wo{j}_{half}")
            wotok = P.token("t")
            aev_half = P.rec("v", lambda v, ob=ob, x0=x0, half=half: v.tensor_tensor(
                ob[:, half * 512:(half + 1) * 512], pOW[:, :],
                x0[:, half * 512:(half + 1) * 512], ALU.add),
                deps=[wotok, out_tok[j - 2] if (half == 0 and j >= 2) else None,
                      x0_tok[j]],
                tag=f"aev{j}_{half}")
        attn_pe_tok[j] = P.token("t")
        res_tok[j] = aev_half
        out_tok[j] = P.rec("y", lambda y, j=j, ob=ob: y.dma_start(
            out_dram[j * 128:(j + 1) * 128, :], ob[:, :]),
            deps=[res_tok[j]], channel=cp + f"ot{j % 2}", tag=f"out{j}")

    for i in range(4):
        load_tile(i)
    produce_ln(0)
    produce(0)
    produce_b(0)
    produce_ln(1)
    produce(1)
    for j in range(NT):
        load_tile(j + 4)
        attn_qk(j)
        attn_fin(j - 1)
        attn_av(j)
        produce_b(j + 1)
        produce_ln(j + 2)
        produce(j + 2)
    attn_fin(NT - 1)
    return [out_tok[NT - 2], out_tok[NT - 1],
            P.token("t"), P.token("v"), P.token("s"), P.token("g")]


def iattn_host_weights(wqkv, q_scale, k_scale, wo, layer):
    l = layer
    p = np.arange(128)
    cL = np.arange(128)
    triL = (cL[None, :] <= p[:, None]).astype(np.float32)
    triH = (cL[None, :] >= p[:, None]).astype(np.float32)
    half = np.arange(0, 64, 2, dtype=np.float32)
    inv_freq = (1.0 / (10000.0 ** (half / 64.0))).astype(np.float32)
    t = np.arange(4096, dtype=np.float32)
    freqs = t[:, None] * inv_freq[None, :]
    freqs = np.concatenate([freqs, freqs], 1)
    base = ((half + 0.4 * 64.0) / (1.4 * 64.0)).astype(np.float32)
    power = (t - 2048.0) / 256.0
    scale = base[None, :] ** power[:, None]
    scale = np.concatenate([scale, scale], 1).astype(np.float32)
    cos = np.cos(freqs).astype(np.float32)
    sin = np.sin(freqs).astype(np.float32)
    sgn = np.concatenate([-np.ones(32, np.float32), np.ones(32, np.float32)])
    cosQ8 = 8.0 * cos * scale
    sinQ8s = 8.0 * sin * scale * sgn
    cosK = cos / scale
    sinKs = sin / scale * sgn
    rep = lambda a: np.tile(a, (1, 8))
    rot = np.concatenate([rep(cosQ8), rep(sinQ8s), rep(cosK), rep(sinKs)], axis=1)
    return {
        "wqkv": np.ascontiguousarray(np.asarray(wqkv[l], np.float32)),
        "wo": np.ascontiguousarray(np.asarray(wo[l], np.float32)),
        "qs": np.ascontiguousarray(np.broadcast_to(
            np.tile(np.asarray(q_scale[l], np.float32), 8)[None, :], (128, 512))),
        "ks": np.ascontiguousarray(np.broadcast_to(
            np.tile(np.asarray(k_scale[l], np.float32), 8)[None, :], (128, 512))),
        "rot": np.ascontiguousarray(rot.astype(np.float32)),
        "trimL": np.ascontiguousarray(np.tile(triL, (1, 8))),
        "trimH": np.ascontiguousarray(np.tile(triH, (1, 8))),
    }


# version 18
# speedup vs baseline: 1.2120x; 1.2120x over previous
import numpy as np
import ml_dtypes
import concourse.bass as bass
import concourse.mybir as mybir
from concourse.bass_utils import run_bass_kernel_spmd

F32 = mybir.dt.float32
BF16 = mybir.dt.bfloat16
AF = mybir.ActivationFunctionType
ALU = mybir.AluOpType
AX = mybir.AxisListType
bf16 = ml_dtypes.bfloat16

D = 1024
T = 4096
NT = T // 128
OH = 16
IH = 8
DH = 64
HID = 2730
NMC = 22
EPS = 1e-5

ENGS = ("t", "s", "v", "g", "y")


class Prog:

    def __init__(self, nc):
        self.nc = nc
        self.ops = []
        self.counts = {}
        for e in ENGS:
            self.counts[e] = 0

    def rec(self, eng, fn, deps=(), channel=None, inc=None, tag="", self_deps=()):
        sem_name = channel if channel is not None else eng
        if inc is None:
            inc = 16 if channel is not None else 1
        if sem_name not in self.counts:
            self.counts[sem_name] = 0
        deps = tuple(d for d in deps if d is not None)
        self_deps = tuple(d for d in self_deps if d is not None)
        self.ops.append((eng, fn, sem_name, inc, deps, self_deps, tag))
        if inc:
            self.counts[sem_name] += inc
            return (sem_name, self.counts[sem_name])
        return None

    def token(self, sem_name):
        c = self.counts.get(sem_name, 0)
        return (sem_name, c) if c else None

    def emit(self, block, final_waiter="y", serialize=False):
        finals = dict(self.counts)
        sems = {}
        guards = []
        for name in self.counts:
            g = self.nc.semaphore("sem_" + name)
            sems[name] = g.__enter__()
            guards.append(g)

        if serialize:
            run_counts = {n: 0 for n in self.counts}
            ser_deps = []
            for (k, fn, sem_name, inc, deps, self_deps, tag) in self.ops:
                ser_deps.append([(n, v) for n, v in run_counts.items() if v])
                if inc:
                    run_counts[sem_name] += inc
        else:
            ser_deps = [list(deps) for (_, _, _, _, deps, _, _) in self.ops]

        def emitter(key):
            def run(eng):
                seen = {}
                for (k, fn, sem_name, inc, deps, self_deps, tag), sdeps in zip(self.ops, ser_deps):
                    if k != key:
                        continue
                    nwait = 0
                    for (dn, dv) in sdeps:
                        if dn == key:
                            continue
                        if seen.get(dn, 0) < dv:
                            if nwait:
                                eng.nop()
                            eng.wait_ge(sems[dn], dv)
                            seen[dn] = dv
                            nwait += 1
                    for (dn, dv) in self_deps:
                        if nwait:
                            eng.nop()
                        eng.wait_ge(sems[dn], dv)
                        nwait += 1
                    instr = fn(eng)
                    if inc:
                        instr.then_inc(sems[sem_name], inc)
                if key == final_waiter:
                    for n2, v2 in finals.items():
                        if v2:
                            eng.wait_ge(sems[n2], v2)
            return run

        block.tensor(emitter("t"))
        block.scalar(emitter("s"))
        block.vector(emitter("v"))
        block.gpsimd(emitter("g"))
        block.sync(emitter("y"))
        return guards



ST = 256
NSUB = ST // 128
NST = T // ST


def alloc_ffn_bufs(nc, tagp, with_final):
    gs = []
    bufs = {}

    def sb(name, shape, dt):
        g = nc.sbuf_tensor(tagp + name, shape, dt)
        t = g.__enter__()
        gs.append(g)
        bufs[name] = t
        return t

    sb("w1", [128, 8 * 5460], BF16)
    sb("w2", [128, NMC * 1024], BF16)
    sb("b1a", [128, NMC], F32)
    sb("b1g", [128, NMC], F32)
    sb("b2row", [1, 1024], BF16)
    sb("grep", [128, 1024], F32)
    sb("brep", [128, 1024], F32)
    sb("xin", [128, 5 * 1024], F32)
    sb("hsh", [128, 2 * 512], F32)
    sb("stats", [128, 4 * 16], F32)
    sb("hln", [128, 2 * 1024], BF16)
    sb("hlnT", [128, 2 * (8 * ST)], BF16)
    sb("tmp", [128, 2 * 1024], F32)
    sb("gel", [128, 2 * ST], BF16)
    sb("prod", [128, 2 * ST], BF16)
    sb("outb", [128, 2 * 1024], F32)
    if with_final:
        sb("x2ring", [128, 3 * 1024], F32)
    return gs, bufs


def alloc_ffn_psums(nc, tagp):
    gs = []
    ps = {}
    pT = []
    for i in range(2):
        g = nc.psum_tensor(tagp + f"pT{i}", [128, 1024], BF16)
        pT.append(g.__enter__()); gs.append(g)
    pag = []
    for i in range(2):
        g = nc.psum_tensor(tagp + f"pag{i}", [128, 512], F32)
        pag.append(g.__enter__()); gs.append(g)
    pout = []
    for i in range(4):
        g = nc.psum_tensor(tagp + f"pout{i}", [128, 512], F32)
        pout.append(g.__enter__()); gs.append(g)
    ps["pT"], ps["pag"], ps["pout"] = pT, pag, pout
    return gs, ps


def record_ffn_phase(P, nc, glob, psums, bufs, wdram, in_dram, out_dram,
                     final=None, weights_dep=None, cp="f0", glob_tok=(), phase_dep=()):
    b = bufs
    pT, pag, pout = psums["pT"], psums["pag"], psums["pout"]
    ident, epsv, ones1 = glob["ident"], glob["eps"], glob["ones1"]

    for k in range(8):
        P.rec("g", lambda g, k=k: g.dma_start(
            b["w1"][:, k * 5460:(k + 1) * 5460], wdram["w1"][k * 128:(k + 1) * 128, :]),
            deps=(([weights_dep] if weights_dep else []) + list(phase_dep)) if k < 4 else (),
            channel=cp + f"w{k % 4}", tag="w1ld")
    for ni, name in enumerate(("b1a", "b1g", "b2row", "grep", "brep")):
        P.rec("g", lambda g, name=name: g.dma_start(b[name][:, :], wdram[name][:, :]),
              deps=list(phase_dep) if ni < 4 else (),
              channel=cp + f"u{ni % 4}", tag=name + "ld")
    for mc in range(NMC):
        rows = 42 if mc == NMC - 1 else 128
        P.rec("g", lambda g, mc=mc, rows=rows: g.dma_start(
            b["w2"][0:rows, mc * 1024:(mc + 1) * 1024], wdram["w2"][mc * 128:mc * 128 + rows, :]),
            channel=cp + f"u{mc % 4}", tag="w2ld")
    wtok4 = [P.token(cp + f"w{c}") for c in range(4)]
    utok4 = [P.token(cp + f"u{c}") for c in range(4)]
    wtok = wtok4[0]

    xin_tok = [None] * (NT + 8)
    shift_tok = [None] * (NT + 8)
    tr_tok = [None] * NT
    lnv_tok = [None] * NT
    srstd_tok = [None] * NT
    trev_tok = [None] * NT
    resid_tok = [None] * NT
    out_tok = [None] * NT
    st_pe_tok = [None] * NST
    x2_tok = [None] * NT
    dump_tok = [[] for _ in range(NT)]

    def load_tile(j):
        if j >= NT:
            return
        dep = [resid_tok[j - 5], shift_tok[j - 4]] if j >= 5 else list(phase_dep)
        xin_tok[j] = P.rec("y", lambda y, j=j: y.dma_start(
            b["xin"][:, (j % 5) * 1024:(j % 5 + 1) * 1024], in_dram[j * 128:(j + 1) * 128, :]),
            deps=dep, channel=cp + f"xi{j % 5}", tag=f"xin{j}")

    def load_x2(j):
        if final is None or "x2_dram" not in final or j >= NT:
            return
        dep = [resid_tok[j - 3]] if j >= 3 else list(phase_dep)
        x2_tok[j] = P.rec("y", lambda y, j=j: y.dma_start(
            b["x2ring"][:, (j % 3) * 1024:(j % 3 + 1) * 1024],
            final["x2_dram"][j * 128:(j + 1) * 128, :]),
            deps=dep, channel=cp + f"x2{j % 3}", tag=f"x2in{j}")

    def ln_tile(j):
        p = j % 2
        xsl = b["xin"][:, (j % 5) * 1024:(j % 5 + 1) * 1024]
        hsh = b["hsh"][:, p * 512:(p + 1) * 512]
        dshift = [xin_tok[j], lnv_tok[j - 2] if j >= 2 else None] + \
            (dump_tok[j - 2] if j >= 2 else [])
        t1 = P.rec("g", lambda g, xsl=xsl, hsh=hsh: g.dma_start(
            hsh[1:128, :], xsl[0:127, 512:1024]), deps=dshift,
            channel=cp + f"sa{j % 2}", tag=f"sh{j}")
        if j == 0:
            t2 = P.rec("g", lambda g, hsh=hsh: g.memset(hsh[0:1, :], 0.0), tag="sh0z")
        else:
            xprev = b["xin"][:, ((j - 1) % 5) * 1024:((j - 1) % 5 + 1) * 1024]
            t2 = P.rec("g", lambda g, xprev=xprev, hsh=hsh: g.dma_start(
                hsh[0:1, :], xprev[127:128, 512:1024]),
                deps=[xin_tok[j - 1]], channel=cp + f"sb{j % 2}", tag=f"shp{j}")
        shift_tok[j] = t2
        stt = b["stats"][:, p * 16:(p + 1) * 16]

        def bn_pair(v, xsl=xsl, hsh=hsh, stt=stt):
            v.bn_stats(stt[:, 0:6], xsl[:, 0:512])
            return v.bn_stats(stt[:, 6:12], hsh[:, :])

        pair_tok = P.rec("v", bn_pair,
                         deps=[xin_tok[j], t1, t2] + (dump_tok[j - 2] if j >= 2 else []),
                         tag=f"bn{j}")
        vagg = P.rec("v", lambda v, stt=stt: v.bn_aggr(stt[:, 12:14], stt[:, 0:12]),
                     self_deps=[pair_tok], tag=f"bnag{j}")
        P.rec("s", lambda s, stt=stt: s.activation(stt[:, 14:15], stt[:, 13:14], AF.Ln,
                                                   bias=epsv[:, :], scale=1.0),
              deps=[vagg] + (list(glob_tok) if j < 2 else []), tag=f"ln{j}")
        lntk = P.token("s")
        srstd_tok[j] = P.rec("s", lambda s, stt=stt: s.activation(
            stt[:, 14:15], stt[:, 14:15], AF.Exp, bias=0.0, scale=-0.5),
            self_deps=[lntk], tag=f"exp{j}")
        tmp = b["tmp"][:, p * 1024:(p + 1) * 1024]
        hln = b["hln"][:, p * 1024:(p + 1) * 1024]
        P.rec("v", lambda v, xsl=xsl, tmp=tmp, stt=stt: v.tensor_scalar(
            tmp[:, 0:512], xsl[:, 0:512], stt[:, 12:13], stt[:, 14:15], ALU.subtract, ALU.mult),
            deps=[srstd_tok[j]], tag=f"apA{j}")
        P.rec("v", lambda v, hsh=hsh, tmp=tmp, stt=stt: v.tensor_scalar(
            tmp[:, 512:1024], hsh[:, :], stt[:, 12:13], stt[:, 14:15], ALU.subtract, ALU.mult),
            tag=f"apB{j}")
        P.rec("v", lambda v, tmp=tmp: v.tensor_tensor(tmp[:, :], tmp[:, :], b["grep"][:, :], ALU.mult),
              deps=(utok4 if j < 3 else ()), tag=f"apG{j}")
        lnv_tok[j] = P.rec("v", lambda v, tmp=tmp, hln=hln: v.tensor_tensor(
            hln[:, :], tmp[:, :], b["brep"][:, :], ALU.add),
            deps=[tr_tok[j - 2] if j >= 2 else None] +
                 (dump_tok[j - 2] if j >= 2 else []), tag=f"apBt{j}")
        if final is not None and "dbg_hln" in final:
            dump_tok[j] = [
                P.rec("y", lambda y, j=j, hln=hln: y.dma_start(
                    final["dbg_hln"][j * 128:(j + 1) * 128, :], hln[:, :]),
                    deps=[lnv_tok[j]], channel=cp + f"db{j % 2}", tag=f"dbg{j}"),
                P.rec("y", lambda y, j=j, hsh=hsh: y.dma_start(
                    final["dbg_hsh"][j * 128:(j + 1) * 128, :], hsh[:, :]),
                    deps=[lnv_tok[j]], channel=cp + f"dc{j % 2}", tag=f"dbgh{j}"),
                P.rec("y", lambda y, j=j, stt=stt: y.dma_start(
                    final["dbg_st"][j * 128:(j + 1) * 128, :], stt[:, 12:15]),
                    deps=[lnv_tok[j]], channel=cp + f"dd{j % 2}", tag=f"dbgs{j}")]

    def transpose_tile(j):
        p = j % 2
        st = j // NSUB
        stp = st % 2
        hln = b["hln"][:, p * 1024:(p + 1) * 1024]
        ptb = pT[p]
        deps = [lnv_tok[j], trev_tok[j - 2] if j >= 2 else None,
                st_pe_tok[st - 2] if st >= 2 else None] + (list(glob_tok) if j < 2 else [])
        for k in range(8):
            P.rec("t", lambda t, k=k, hln=hln, ptb=ptb: t.transpose(
                ptb[:, k * 128:(k + 1) * 128], hln[:, k * 128:(k + 1) * 128], ident[:, :]),
                deps=(deps if k == 0 else ()), inc=(1 if k == 7 else 0), tag=f"tr{j}")
        ttok = P.token("t")
        tr_tok[j] = ttok
        sub = j % NSUB
        base = stp * (8 * ST)
        dstap = bass.AP(b["hlnT"], base + sub * 128, [[2 * 8 * ST, 128], [ST, 8], [1, 128]])
        trev_tok[j] = P.rec("v", lambda v, ptb=ptb, dstap=dstap: v.tensor_copy(dstap, ptb[:, :]),
                            deps=[ttok], tag=f"trev{j}")

    def mc_loop(st):
        stp = st % 2
        base = stp * (8 * ST)
        prev_resid = resid_tok[st * NSUB - 1] if st >= 1 else None
        gel_tok = [None, None]
        prod_tok = [None, None]

        def emit_w2(mc):
            mp = mc % 2
            rows = 42 if mc == NMC - 1 else 128
            prod = b["prod"][:, mp * ST:(mp + 1) * ST]
            for sub in range(NSUB):
                for nh in range(2):
                    d = [prod_tok[mp]]
                    if st == 0 and sub == 0 and nh == 0:
                        d = d + utok4
                    if mc == 0 and sub == 0 and nh == 0 and prev_resid:
                        d = d + [prev_resid]
                    P.rec("t", lambda t, mc=mc, sub=sub, nh=nh, rows=rows, prod=prod:
                          t.matmul(pout[sub * 2 + nh][:, :],
                                   prod[0:rows, sub * 128:(sub + 1) * 128],
                                   b["w2"][0:rows, mc * 1024 + nh * 512:mc * 1024 + (nh + 1) * 512],
                                   start=(mc == 0), stop=False),
                          deps=(d if (sub == 0 and nh == 0) else ()), inc=0, tag=f"w2{st}_{mc}")

        for mc in range(NMC):
            mp = mc % 2
            rows = 42 if mc == NMC - 1 else 128
            pa = pag[mp]
            deps = [trev_tok[st * NSUB], trev_tok[st * NSUB + 1], prod_tok[mp]]
            if mc < 2:
                deps = deps + wtok4
            for half, cofs in ((0, 0), (1, 2730)):
                for k in range(8):
                    last = (half == 1 and k == 7)
                    P.rec("t", lambda t, k=k, mc=mc, rows=rows, half=half, cofs=cofs, pa=pa:
                          t.matmul(pa[0:rows, half * 256:half * 256 + ST],
                                   b["w1"][:, k * 5460 + cofs + mc * 128:
                                           k * 5460 + cofs + mc * 128 + rows],
                                   bass.AP(b["hlnT"], base + k * ST, [[2 * 8 * ST, 128], [1, ST]]),
                                   start=(k == 0), stop=(k == 7)),
                          deps=(deps if (half == 0 and k == 0) else ()),
                          inc=(1 if last else 0), tag=f"ag{st}_{mc}")
            agtok = P.token("t")
            gel = b["gel"][:, mp * ST:(mp + 1) * ST]
            gel_tok[mp] = P.rec("s", lambda s, pa=pa, gel=gel, mc=mc, rows=rows: s.activation(
                gel[0:rows, :], pa[0:rows, 256:512], AF.Gelu,
                bias=b["b1g"][0:rows, mc:mc + 1], scale=1.0),
                deps=[agtok, prod_tok[mp]] + (utok4 if (st == 0 and mc < 2) else []),
                tag=f"gel{st}_{mc}")
            prod = b["prod"][:, mp * ST:(mp + 1) * ST]
            prod_tok[mp] = P.rec("v", lambda v, pa=pa, prod=prod, gel=gel, mc=mc, rows=rows:
                                 v.scalar_tensor_tensor(prod[0:rows, :], pa[0:rows, 0:256],
                                                        b["b1a"][0:rows, mc:mc + 1], gel[0:rows, :],
                                                        ALU.add, ALU.mult),
                                 deps=[gel_tok[mp]], tag=f"prod{st}_{mc}")
            if mc >= 1:
                emit_w2(mc - 1)
        emit_w2(NMC - 1)
        for sub in range(NSUB):
            for nh in range(2):
                last = (sub == NSUB - 1 and nh == 1)
                P.rec("t", lambda t, sub=sub, nh=nh: t.matmul(
                    pout[sub * 2 + nh][:, :], ones1[:, :],
                    b["b2row"][:, nh * 512:(nh + 1) * 512], start=False, stop=True),
                    deps=(list(glob_tok) if (st == 0 and sub == 0 and nh == 0) else ()),
                    inc=(1 if last else 0), tag=f"b2{st}")
        st_pe_tok[st] = P.token("t")

    def evac_st(st):
        for sub in range(NSUB):
            j = st * NSUB + sub
            p = j % 2
            xsl = b["xin"][:, (j % 5) * 1024:(j % 5 + 1) * 1024]
            ob = b["outb"][:, p * 1024:(p + 1) * 1024]
            deps = [st_pe_tok[st], out_tok[j - 2] if j >= 2 else None]
            for nh in range(2):
                P.rec("v", lambda v, sub=sub, nh=nh, xsl=xsl, ob=ob: v.tensor_tensor(
                    ob[:, nh * 512:(nh + 1) * 512], pout[sub * 2 + nh][:, :],
                    xsl[:, nh * 512:(nh + 1) * 512], ALU.add),
                    deps=(deps if nh == 0 else ()), tag=f"res{j}")
            if final is None or "x2_dram" not in final:
                resid_tok[j] = P.token("v")
            else:
                stt = b["stats"][:, 32 + p * 16:32 + (p + 1) * 16]
                def fbn_pair(v, ob=ob, stt=stt):
                    v.bn_stats(stt[:, 0:6], ob[:, 0:512])
                    return v.bn_stats(stt[:, 6:12], ob[:, 512:1024])
                fpt = P.rec("v", fbn_pair, self_deps=[P.token("v")], tag=f"fbn{j}")
                vt = P.rec("v", lambda v, stt=stt: v.bn_aggr(stt[:, 12:14], stt[:, 0:12]),
                           self_deps=[fpt], tag=f"fba{j}")
                P.rec("s", lambda s, stt=stt: s.activation(stt[:, 14:15], stt[:, 13:14], AF.Ln,
                                                           bias=epsv[:, :], scale=1.0),
                      deps=[vt], tag=f"fln{j}")
                flntk = P.token("s")
                fs = P.rec("s", lambda s, stt=stt: s.activation(stt[:, 14:15], stt[:, 14:15],
                                                                AF.Exp, bias=0.0, scale=-0.5),
                           self_deps=[flntk], tag=f"fex{j}")
                tmp = b["tmp"][:, p * 1024:(p + 1) * 1024]
                P.rec("v", lambda v, ob=ob, tmp=tmp, stt=stt: v.tensor_scalar(
                    tmp[:, :], ob[:, :], stt[:, 12:13], stt[:, 14:15], ALU.subtract, ALU.mult),
                    deps=[fs, x2_tok[j]], tag=f"fap{j}")
                resid_tok[j] = P.rec("v", lambda v, ob=ob, tmp=tmp, j=j: v.tensor_tensor(
                    ob[:, :], tmp[:, :], b["x2ring"][:, (j % 3) * 1024:(j % 3 + 1) * 1024], ALU.add),
                    tag=f"fx2{j}")
            out_tok[j] = P.rec("y", lambda y, j=j, ob=ob: y.dma_start(
                out_dram[j * 128:(j + 1) * 128, :], ob[:, :]),
                deps=[resid_tok[j]], channel=cp + f"ot{j % 2}", tag=f"out{j}")

    for j in range(3):
        load_tile(j)
    load_x2(0)
    for j in range(3):
        ln_tile(j)
        transpose_tile(j)
    for st in range(NST):
        for j in (2 * st + 3, 2 * st + 4):
            load_tile(j)
        for j in (2 * st + 1, 2 * st + 2):
            load_x2(j)
        mc_loop(st)
        for j in (2 * st + 3, 2 * st + 4):
            if j < NT:
                ln_tile(j)
                transpose_tile(j)
        evac_st(st)
    return [out_tok[NT - 2], out_tok[NT - 1]]



def ffn_host_weights(ln_g, ln_b, w1, b1, w2, b2, layer):
    l = layer
    b1l = np.asarray(b1[l], np.float32)
    b1a = np.zeros((128, NMC), np.float32)
    b1g = np.zeros((128, NMC), np.float32)
    for mc in range(NMC):
        rows = 42 if mc == NMC - 1 else 128
        b1a[0:rows, mc] = b1l[mc * 128:mc * 128 + rows]
        b1g[0:rows, mc] = b1l[2730 + mc * 128:2730 + mc * 128 + rows]
    return {
        "w1": np.ascontiguousarray(np.asarray(w1[l], np.float32)),
        "w2": np.ascontiguousarray(np.asarray(w2[l], np.float32)),
        "b1a": b1a,
        "b1g": b1g,
        "b2row": np.ascontiguousarray(np.asarray(b2[l], np.float32)[None, :]),
        "grep": np.ascontiguousarray(np.broadcast_to(np.asarray(ln_g[l], np.float32)[None, :], (128, 1024))),
        "brep": np.ascontiguousarray(np.broadcast_to(np.asarray(ln_b[l], np.float32)[None, :], (128, 1024))),
    }


def alloc_globals(nc):
    gs = []
    glob = {}
    g1 = nc.sbuf_tensor("g_ident", [128, 128], BF16)
    glob["ident"] = g1.__enter__(); gs.append(g1)
    g2 = nc.sbuf_tensor("g_eps", [128, 1], F32)
    glob["eps"] = g2.__enter__(); gs.append(g2)
    g3 = nc.sbuf_tensor("g_ones1", [1, 128], BF16)
    glob["ones1"] = g3.__enter__(); gs.append(g3)
    g4 = nc.sbuf_tensor("g_ones1f", [1, 128], F32)
    glob["ones1f"] = g4.__enter__(); gs.append(g4)
    g5 = nc.sbuf_tensor("g_dum", [128, 512], BF16)
    glob["dum"] = g5.__enter__(); gs.append(g5)
    return gs, glob


def record_globals(P, nc, glob, ident_dram):
    t1 = P.rec("g", lambda g: g.dma_start(glob["ident"][:, :], ident_dram[:, :]),
               channel="gw", tag="identld")
    P.rec("g", lambda g: g.memset(glob["eps"][:, :], EPS), tag="epsset")
    P.rec("g", lambda g: g.memset(glob["ones1"][:, :], 1.0), tag="onesset")
    P.rec("g", lambda g: g.memset(glob["dum"][:, :], 0.0), tag="dumset")
    t3 = P.rec("g", lambda g: g.memset(glob["ones1f"][:, :], 1.0), tag="onesfset")
    return [t1, t3]


def record_ham_filler(P, nc, glob, psum_ap, n, deps):
    for q in range(n):
        P.rec("t", lambda t: t.matmul(psum_ap, glob["dum"][:, 0:128],
                                      glob["dum"][:, :], start=True, stop=True),
              deps=(list(deps) if q == 0 else ()), inc=0, tag="hamfill")


def host_ident():
    return np.eye(128, dtype=np.float32)


def build_core_program():
    nc = bass.Bass("TRN2", target_bir_lowering=False, debug=False)
    x_in = nc.dram_tensor("x", [T, D], F32, kind="ExternalInput")
    ident = nc.dram_tensor("cident", [128, 128], F32, kind="ExternalInput")
    out = nc.dram_tensor("out", [T, D], F32, kind="ExternalOutput")
    x2 = nc.dram_tensor("x2s", [T, D], F32)
    m1 = nc.dram_tensor("m1s", [T, D], F32)
    a1 = nc.dram_tensor("a1s", [T, D], F32)
    m2 = nc.dram_tensor("m2s", [T, D], F32)

    shapes_attn = {"wqkvT": (1024, 3072), "woT": (1024, 1024),
                   "trimL": (128, 2048), "trimH": (128, 2048)}
    shapes_iatt = {"wqkv": (1024, 1536), "wo": (512, 1024), "qs": (128, 512),
                   "ks": (128, 512), "rot": (4096, 2048),
                   "trimL": (128, 1024), "trimH": (128, 1024)}
    shapes_ffn = {"w1": (1024, 5460), "w2": (2730, 1024), "b1a": (128, NMC),
                  "b1g": (128, NMC), "b2row": (1, 1024), "grep": (128, 1024),
                  "brep": (128, 1024)}
    wd = {}
    for pfx, shapes in (("p1", shapes_attn), ("p2", shapes_iatt), ("p3", shapes_ffn),
                        ("p4", shapes_iatt), ("p5", shapes_ffn)):
        wd[pfx] = {name: nc.dram_tensor(f"{pfx}_{name}", list(sh), F32, kind="ExternalInput")
                   for name, sh in shapes.items()}

    P = Prog(nc)
    gg, glob = alloc_globals(nc)
    gtok = record_globals(P, nc, glob, ident)

    bg1, b1_ = alloc_attn_bufs(nc, "a1")
    pg1, ps1 = alloc_attn_psums(nc, "a1")
    end1 = record_attn_outer(P, nc, glob, ps1, b1_, wd["p1"], x_in, x2, cp="a1",
                             glob_tok=gtok)
    for g in reversed(bg1 + pg1):
        g.__exit__(None, None, None)
    bg2, b2_ = alloc_iattn_bufs(nc, "i2")
    pg2, ps2 = alloc_iattn_psums(nc, "i2")
    record_ham_filler(P, nc, glob, ps2["pS0"][:, 0:512], 450, end1)
    end2 = record_attn_inner(P, nc, glob, ps2, b2_, wd["p2"], x2, m1, cp="i2",
                             glob_tok=gtok, with_ln=True, phase_dep=end1)
    for g in reversed(bg2 + pg2):
        g.__exit__(None, None, None)
    bg3, b3_ = alloc_ffn_bufs(nc, "f3", with_final=False)
    pg3, ps3 = alloc_ffn_psums(nc, "f3")
    record_ham_filler(P, nc, glob, ps3["pag"][0][:, 0:512], 450, end2)
    end3 = record_ffn_phase(P, nc, glob, ps3, b3_, wd["p3"], m1, a1,
                            cp="f3", glob_tok=gtok, phase_dep=end2)
    for g in reversed(bg3 + pg3):
        g.__exit__(None, None, None)
    bg4, b4_ = alloc_iattn_bufs(nc, "i4")
    pg4, ps4 = alloc_iattn_psums(nc, "i4")
    record_ham_filler(P, nc, glob, ps4["pS0"][:, 0:512], 450, end3)
    end4 = record_attn_inner(P, nc, glob, ps4, b4_, wd["p4"], a1, m2, cp="i4",
                             glob_tok=gtok, with_ln=False, phase_dep=end3)
    for g in reversed(bg4 + pg4):
        g.__exit__(None, None, None)
    bg5, b5_ = alloc_ffn_bufs(nc, "f5", with_final=True)
    pg5, ps5 = alloc_ffn_psums(nc, "f5")
    record_ham_filler(P, nc, glob, ps5["pag"][0][:, 0:512], 450, end4)
    record_ffn_phase(P, nc, glob, ps5, b5_, wd["p5"], m2, out,
                     final={"x2_dram": x2}, cp="f5", glob_tok=gtok, phase_dep=end4)
    for g in reversed(bg5 + pg5):
        g.__exit__(None, None, None)

    with nc.Block() as block:
        semguards = P.emit(block)
    for g in reversed(semguards + gg):
        g.__exit__(None, None, None)
    return nc, len(P.ops)


def host_weight_maps(L0_attn_W_qkv, L0_attn_W_o, L0_ffn_wqkv, L0_ffn_q_scale,
                     L0_ffn_k_scale, L0_ffn_wo, L0_ffn_ln_g, L0_ffn_ln_b,
                     L0_ffn_w1, L0_ffn_b1, L0_ffn_w2, L0_ffn_b2):
    m = {}
    for name, arr in attn_host_weights(L0_attn_W_qkv, L0_attn_W_o).items():
        m[f"p1_{name}"] = arr
    for l, pfx in ((0, "p2"), (1, "p4")):
        for name, arr in iattn_host_weights(L0_ffn_wqkv, L0_ffn_q_scale,
                                            L0_ffn_k_scale, L0_ffn_wo, l).items():
            m[f"{pfx}_{name}"] = arr
    for l, pfx in ((0, "p3"), (1, "p5")):
        for name, arr in ffn_host_weights(L0_ffn_ln_g, L0_ffn_ln_b, L0_ffn_w1,
                                          L0_ffn_b1, L0_ffn_w2, L0_ffn_b2, l).items():
            m[f"{pfx}_{name}"] = arr
    m["cident"] = host_ident()
    return m


def kernel(L0_attn_x, L0_attn_W_qkv, L0_attn_W_o, L0_ffn_wqkv, L0_ffn_q_scale,
           L0_ffn_k_scale, L0_ffn_wo, L0_ffn_ln_g, L0_ffn_ln_b, L0_ffn_w1,
           L0_ffn_b1, L0_ffn_w2, L0_ffn_b2):
    x = np.ascontiguousarray(np.asarray(L0_attn_x, np.float32))
    wm = host_weight_maps(L0_attn_W_qkv, L0_attn_W_o, L0_ffn_wqkv, L0_ffn_q_scale,
                          L0_ffn_k_scale, L0_ffn_wo, L0_ffn_ln_g, L0_ffn_ln_b,
                          L0_ffn_w1, L0_ffn_b1, L0_ffn_w2, L0_ffn_b2)
    nc, nops = build_core_program()
    in_maps = []
    for c in range(8):
        im = dict(wm)
        im["x"] = np.ascontiguousarray(x[c])
        in_maps.append(im)
    res = run_bass_kernel_spmd(nc, in_maps, core_ids=list(range(8)))
    out = np.stack([np.asarray(res.results[c]["out"], np.float32) for c in range(8)], axis=0)
    return out



RING = 6


def alloc_attn_bufs(nc, tagp):
    gs = []
    bufs = {}

    def sb(name, shape, dt):
        g = nc.sbuf_tensor(tagp + name, shape, dt)
        t = g.__enter__()
        gs.append(g)
        bufs[name] = t
        return t

    sb("wqkv", [128, 8 * 3072], BF16)
    sb("wo", [128, 8 * 1024], BF16)
    sb("xin", [128, RING * 1024], F32)
    sb("stats", [128, 4 * 16], F32)
    sb("h1", [128, 2 * 1024], BF16)
    sb("h1T", [128, 2 * 1024], BF16)
    sb("qkt", [128, 2 * 2048], BF16)
    sb("qT", [128, 4 * 1024], BF16)
    sb("kT", [128, RING * 1024], BF16)
    sb("vr", [128, RING * 1040], BF16)
    sb("probs", [128, 16 * 640], BF16)
    sb("trimL", [128, 16 * 128], BF16)
    sb("trimH", [128, 16 * 128], BF16)
    sb("oTf", [128, 1024], F32)
    sb("sums", [1, 4096], F32)
    
    sb("oTn", [128, 1024], BF16)
    sb("abuf", [128, 2 * 1024], F32)
    sb("outb", [128, 2 * 1024], F32)
    return gs, bufs


def alloc_attn_psums(nc, tagp):
    gs = []
    ps = {}

    def palloc(name, shape, dt):
        g = nc.psum_tensor(tagp + name, shape, dt)
        t = g.__enter__()
        gs.append(g)
        ps[name] = t
        return t

    palloc("pS0", [128, 640], F32)
    palloc("pS1", [128, 640], F32)
    palloc("pQ", [128, 1024], F32)
    palloc("pOW", [128, 512], F32)
    palloc("pOW2", [128, 512], F32)
    ps["pT"] = ps["pQ"].bitcast(BF16)
    return gs, ps


def record_attn_outer(P, nc, glob, ps, b, wdram, in_dram, out_dram, cp="a1",
                      glob_tok=(), weights_dep=None, dbg=None, phase_dep=()):
    ident, epsv, ones1 = glob["ident"], glob["eps"], glob["ones1f"]
    pS2, pQ, pOW, pT = [ps["pS0"], ps["pS1"]], ps["pQ"], ps["pOW"], ps["pT"]
    pOW2 = ps["pOW2"]
    pows = [pOW, pOW2]

    for k in range(8):
        P.rec("g", lambda g, k=k: g.dma_start(
            b["wqkv"][:, k * 3072:(k + 1) * 3072], wdram["wqkvT"][k * 128:(k + 1) * 128, :]),
            deps=(([weights_dep] if weights_dep else []) + list(phase_dep)) if k < 4 else (),
            channel=cp + f"w{k % 4}", tag="wqkvld")
    for k in range(8):
        P.rec("g", lambda g, k=k: g.dma_start(
            b["wo"][:, k * 1024:(k + 1) * 1024], wdram["woT"][k * 128:(k + 1) * 128, :]),
            channel=cp + f"w{k % 4}", tag="wold")
    for ni, name in enumerate(("trimL", "trimH")):
        P.rec("g", lambda g, name=name: g.dma_start(b[name][:, :], wdram[name][:, :]),
              channel=cp + f"w{ni % 4}", tag=name)
    vzt = P.rec("g", lambda g: g.memset(b["vr"][:, :], 0.0), tag="vz")
    P.rec("g", lambda g: g.memset(
        bass.AP(b["vr"], 64, [[RING * 1040, 128], [65, 16 * RING], [1, 1]]), 1.0),
        self_deps=[vzt], tag="vones")
    P.rec("g", lambda g: g.memset(b["kT"][:, 4 * 1024:6 * 1024], 0.0), tag="kz")
    wtok4 = [P.token(cp + f"w{c}") for c in range(4)]
    wtok = wtok4[0]
    init_tok = P.token("g")

    xin_tok = [None] * (NT + 8)
    vmm_tok = [None] * NT
    h1_tok = [None] * NT
    trh_tok = [None] * NT
    h1T_tok = [None] * NT
    qkv_pe_tok = [None] * NT
    qkev_tok = [None] * NT
    trqk_tok = [None] * NT
    ring_tok = [None] * NT
    exp_tok = [[None] * OH for _ in range(NT)]
    zero_tok = [None] * NT
    zgrp_tok = [[None] * 4 for _ in range(NT)]
    oevv_tok = [None] * NT
    aevh_tok = [[None, None] for _ in range(NT)]
    rex_tok = [None] * NT
    av_tok = [None] * NT
    oev_tok = [None] * NT
    norm_tok = [None] * NT
    attn_pe_tok = [None] * NT
    res_tok = [None] * NT
    out_tok = [None] * NT

    def load_tile(i):
        if i >= NT:
            return
        dep = [res_tok[i - RING]] if i >= RING else list(phase_dep)
        xin_tok[i] = P.rec("y", lambda y, i=i: y.dma_start(
            b["xin"][:, (i % RING) * 1024:(i % RING + 1) * 1024],
            in_dram[i * 128:(i + 1) * 128, :]),
            deps=dep, channel=cp + f"xi{i % RING}", tag=f"xin{i}")

    def produce_ln(i):
        if i >= NT:
            return
        p = i % 2
        xsl = b["xin"][:, (i % RING) * 1024:(i % RING + 1) * 1024]
        stt = b["stats"][:, p * 16:(p + 1) * 16]

        def bn_pair(v, xsl=xsl, stt=stt):
            v.bn_stats(stt[:, 0:6], xsl[:, 0:512])
            return v.bn_stats(stt[:, 6:12], xsl[:, 512:1024])
        pr = P.rec("v", bn_pair, deps=[xin_tok[i]], tag=f"bn{i}")
        vagg = P.rec("v", lambda v, stt=stt: v.bn_aggr(stt[:, 12:14], stt[:, 0:12]),
                     self_deps=[pr], tag=f"bnag{i}")
        P.rec("s", lambda s, stt=stt: s.activation(stt[:, 14:15], stt[:, 13:14], AF.Ln,
                                                   bias=epsv[:, :], scale=1.0),
              deps=[vagg] + (list(glob_tok) if i < 2 else []), tag=f"ln{i}")
        lntk = P.token("s")
        srstd = P.rec("s", lambda s, stt=stt: s.activation(
            stt[:, 14:15], stt[:, 14:15], AF.Exp, bias=0.0, scale=-0.5),
            self_deps=[lntk], tag=f"exp{i}")
        h1 = b["h1"][:, p * 1024:(p + 1) * 1024]
        h1_tok[i] = P.rec("v", lambda v, xsl=xsl, h1=h1, stt=stt: v.tensor_scalar(
            h1[:, :], xsl[:, :], stt[:, 12:13], stt[:, 14:15], ALU.subtract, ALU.mult),
            deps=[srstd, trh_tok[i - 2] if i >= 2 else None], tag=f"h1{i}")

    def produce(i):
        if i >= NT:
            return
        p = i % 2
        h1 = b["h1"][:, p * 1024:(p + 1) * 1024]
        deps = [h1_tok[i], trqk_tok[i - 1] if i >= 1 else None,
                ring_tok[i - 1] if i >= 1 else None,
                qkev_tok[i - 1] if i >= 1 else None]
        for k in range(8):
            P.rec("t", lambda t, k=k, h1=h1: t.transpose(
                pT[:, k * 128:(k + 1) * 128], h1[:, k * 128:(k + 1) * 128], ident[:, :]),
                deps=(deps if k == 0 else ()), inc=(1 if k == 7 else 0), tag=f"trh{i}")
        trh_tok[i] = P.token("t")
        h1T = b["h1T"][:, p * 1024:(p + 1) * 1024]
        h1T_tok[i] = P.rec("v", lambda v, h1T=h1T: v.tensor_copy(h1T[:, :], pT[:, 0:1024]),
                           deps=[trh_tok[i], qkv_pe_tok[i - 2] if i >= 2 else None],
                           tag=f"h1Tev{i}")
        qk = b["qkt"][:, p * 2048:(p + 1) * 2048]
        ev = {}
        for ph in range(3):
            for ncol in range(2):
                for k in range(8):
                    d = []
                    if ph == 0 and k == 0 and ncol == 0:
                        d = [h1T_tok[i], qkev_tok[i - 1] if i >= 1 else None] + wtok4 + [
                             ring_tok[i - 1] if i >= 1 else None]
                    if ph > 0 and k == 0:
                        d = [ev[(ph - 1, ncol)]]
                    c0 = ph * 1024 + ncol * 512
                    P.rec("t", lambda t, k=k, c0=c0, ncol=ncol, h1T=h1T: t.matmul(
                        pQ[:, ncol * 512:ncol * 512 + 512],
                        h1T[:, k * 128:(k + 1) * 128],
                        b["wqkv"][:, k * 3072 + c0:k * 3072 + c0 + 512],
                        start=(k == 0), stop=(k == 7)),
                        deps=d, inc=(1 if k == 7 else 0), tag=f"qkv{i}_{ph}{ncol}")
                mmtok = P.token("t")
                if ph < 2:
                    ev[(ph, ncol)] = P.rec("v", lambda v, qk=qk, ph=ph, ncol=ncol: v.tensor_copy(
                        qk[:, ph * 1024 + ncol * 512:ph * 1024 + ncol * 512 + 512],
                        pQ[:, ncol * 512:ncol * 512 + 512]),
                        deps=[mmtok, trqk_tok[i - 1] if (ph == 0 and ncol == 0 and i >= 1)
                              else None],
                        tag=f"qkev{i}_{ph}{ncol}")
                else:
                    vmm_tok[i] = mmtok
        qkv_pe_tok[i] = P.token("t")
        if dbg is not None and "qk" in dbg:
            P.rec("y", lambda y, i=i, qk=qk: y.dma_start(
                dbg["qk"][i * 128:(i + 1) * 128, :], qk[:, :]),
                deps=[qkev_tok[i]], channel=cp + f"dq{i % 2}", tag=f"dbgqk{i}")
        if dbg is not None and "h1" in dbg:
            h1d = b["h1"][:, (i % 2) * 1024:(i % 2 + 1) * 1024]
            P.rec("y", lambda y, i=i, h1d=h1d: y.dma_start(
                dbg["h1"][i * 128:(i + 1) * 128, :], h1d[:, :]),
                deps=[h1_tok[i]], channel=cp + f"dh{i % 2}", tag=f"dbgh1{i}")

    def produce_b(i):
        if i >= NT:
            return
        p = i % 2
        qk = b["qkt"][:, p * 2048:(p + 1) * 2048]
        slot = i % RING
        dstv = bass.AP(b["vr"], slot * 1040, [[RING * 1040, 128], [65, 16], [1, 64]])
        qkev_tok[i] = P.rec("v", lambda v, dstv=dstv: v.tensor_copy(dstv, pQ[:, :]),
                            deps=[vmm_tok[i], avl_dep(i)], tag=f"vev{i}")
        for k in range(16):
            P.rec("t", lambda t, k=k, qk=qk: t.transpose(
                pT[:, (k % 8) * 128:(k % 8) * 128 + 128],
                qk[:, k * 128:(k + 1) * 128], ident[:, :]),
                deps=([qkev_tok[i], h1T_tok[i], avl_dep(i)] if k == 0 else
                      ([] if k != 8 else [qT_ev_dep(i)])),
                inc=(1 if k in (7, 15) else 0), tag=f"trqk{i}")
            if k == 7:
                t7 = P.token("t")
                qT = b["qT"][:, (i % 4) * 1024:(i % 4 + 1) * 1024]
                qT_ev = P.rec("v", lambda v, qT=qT: v.tensor_copy(qT[:, :], pT[:, 0:1024]),
                              deps=[t7, attn_pe_tok[i - 4] if i >= 4 else None],
                              tag=f"qTev{i}")
                qT_ev_holder[0] = qT_ev
                qT_ev_tok[i] = qT_ev
        trqk_tok[i] = P.token("t")
        ring_tok[i] = P.rec("v", lambda v, slot=slot: v.tensor_copy(
            b["kT"][:, slot * 1024:(slot + 1) * 1024], pT[:, 0:1024]),
            deps=[trqk_tok[i], avl_dep(i)], tag=f"kTev{i}")

    qT_ev_holder = [None]
    qT_ev_tok = [None] * (NT + 8)

    def qT_ev_dep(i):
        return qT_ev_holder[0]

    def avl_dep(i):
        t = i - RING + 2
        return attn_pe_tok[t] if t >= 0 else init_tok

    def attn_qk(j):
        qT = b["qT"][:, (j % 4) * 1024:(j % 4 + 1) * 1024]
        kdep = [ring_tok[min(j + 2, NT - 1)], qT_ev_tok[min(j, NT - 1)]]
        for h in range(OH):
            hh = h % 2
            c = h // 2
            pS = pS2[h % 2]
            if h >= 2:
                war = exp_tok[j][h - 2]
            elif j >= 1:
                war = exp_tok[j - 1][14 + h]
            else:
                war = None
            for tau in range(5):
                tt = j - 2 + tau
                slot = (tt % RING) if tt >= 0 else ((tt + RING) % RING)
                d = []
                if tau == 0:
                    d = (list(kdep) if h == 0 else []) + [war]
                P.rec("t", lambda t, slot=slot, c=c, hh=hh, tau=tau, qT=qT, pS=pS: t.matmul(
                    pS[:, tau * 128:(tau + 1) * 128],
                    b["kT"][hh * 64:(hh + 1) * 64, slot * 1024 + c * 128:slot * 1024 + c * 128 + 128],
                    qT[hh * 64:(hh + 1) * 64, c * 128:(c + 1) * 128],
                    start=True, stop=True),
                    deps=d, inc=(1 if tau == 4 else 0), tag=f"qk{j}_{h}")
            qktok = P.token("t")
            exp_tok[j][h] = P.rec("s", lambda s, h=h, pS=pS: s.activation(
                b["probs"][:, h * 640:(h + 1) * 640], pS[:, :], AF.Exp, bias=0.0, scale=0.125),
                deps=[qktok, av_tok[j - 1] if (h <= 1 and j >= 1) else None], tag=f"pexp{j}_{h}")
        for r in range(4):
            c0 = r * 4 * 640
            apL = bass.AP(b["probs"], c0, [[16 * 640, 128], [640, 4], [1, 128]])
            apH = bass.AP(b["probs"], c0 + 512, [[16 * 640, 128], [640, 4], [1, 128]])
            P.rec("v", lambda v, apL=apL, r=r: v.tensor_tensor(
                apL, apL, b["trimL"][:, r * 512:(r + 1) * 512], ALU.mult),
                deps=[exp_tok[j][r * 4 + 3]], tag=f"zl{j}_{r}")
            ztok = P.rec("v", lambda v, apH=apH, r=r: v.tensor_tensor(
                apH, apH, b["trimH"][:, r * 512:(r + 1) * 512], ALU.mult),
                tag=f"zh{j}_{r}")
            if j <= 1:
                width = 256 if j == 0 else 128
                apE = bass.AP(b["probs"], c0, [[16 * 640, 128], [640, 4], [1, width]])
                ztok = P.rec("v", lambda v, apE=apE: v.memset(apE, 0.0),
                             self_deps=[ztok], tag=f"ze{j}_{r}")
            if j >= NT - 2:
                start = 512 if j == NT - 2 else 384
                apE = bass.AP(b["probs"], c0 + start,
                              [[16 * 640, 128], [640, 4], [1, 640 - start]])
                ztok = P.rec("v", lambda v, apE=apE: v.memset(apE, 0.0),
                             self_deps=[ztok], tag=f"ze{j}_{r}")
            zgrp_tok[j][r] = ztok

    def attn_av(j):
        oevg = [None] * 4
        for r in range(4):
            pw = pows[r % 2]
            for hq in range(4):
                h = r * 4 + hq
                for tau in range(5):
                    tt = j - 2 + tau
                    slot = (tt % RING) if tt >= 0 else ((tt + RING) % RING)
                    d = []
                    if hq == 0 and tau == 0:
                        if r < 2:
                            d = [zgrp_tok[j][r],
                                 aevh_tok[j - 1][r] if j >= 1 else None]
                        else:
                            d = [zgrp_tok[j][r], oevg[r - 2]]
                        if dbg is not None and "_prtok" in dbg and j == 10:
                            d = d + [dbg["_prtok"]]
                    P.rec("t", lambda t, slot=slot, h=h, hq=hq, tau=tau, pw=pw: t.matmul(
                        pw[0:65, hq * 128:(hq + 1) * 128],
                        b["vr"][:, slot * 1040 + h * 65:slot * 1040 + h * 65 + 65],
                        b["probs"][:, h * 640 + tau * 128:h * 640 + (tau + 1) * 128],
                        start=(tau == 0), stop=(tau == 4)),
                        deps=d, inc=(1 if (hq == 3 and tau == 4) else 0), tag=f"av{j}_{r}")
            avtok = P.token("t")
            if r == 3:
                av_tok[j] = avtok
            ap_even_src = bass.AP(pw, 0, [[512, 64], [256, 2], [1, 128]])
            ap_odd_src = bass.AP(pw, 128, [[512, 64], [256, 2], [1, 128]])
            P.rec("s", lambda s, r=r, ap=ap_even_src: s.copy(
                b["oTf"][0:64, r * 256:(r + 1) * 256], ap), deps=[avtok], tag=f"oev{j}_{r}")
            P.rec("s", lambda s, r=r, ap=ap_odd_src: s.copy(
                b["oTf"][64:128, r * 256:(r + 1) * 256], ap), tag=f"oev2{j}_{r}")
            oevg[r] = P.rec("s", lambda s, r=r, pw=pw: s.copy(
                b["sums"][0:1, r * 512:(r + 1) * 512], pw[64:65, 0:512]), tag=f"sev{j}_{r}")
            oev_tok[j] = oevg[r]
            oevv_tok[j] = oevg[r]
        if dbg is not None and "oT" in dbg:
            P.rec("y", lambda y, j=j: y.dma_start(
                dbg["oT"][j * 128:(j + 1) * 128, :], b["oTf"][:, :]),
                deps=[oev_tok[j]], channel=cp + f"do{j % 2}", tag=f"dbgoT{j}")
            P.rec("y", lambda y, j=j: y.dma_start(
                dbg["sums"][j:j + 1, :], b["sums"][:, :]),
                deps=[oev_tok[j]], channel=cp + f"ds{j % 2}", tag=f"dbgsm{j}")
        rlntk = P.rec("s", lambda s: s.activation(b["sums"][0:1, 2048:4096], b["sums"][0:1, 0:2048],
                                          AF.Ln, bias=0.0, scale=1.0),
              self_deps=[oev_tok[j]], tag=f"rln{j}")
        rex_tok[j] = P.rec("s", lambda s: s.activation(b["sums"][0:1, 2048:4096],
                                                       b["sums"][0:1, 2048:4096],
                                                       AF.Exp, bias=0.0, scale=-1.0),
                           self_deps=[rlntk], tag=f"rex{j}")

    def attn_fin(j):
        if j < 0:
            return
        p = j % 2
        ntoks = [None, None]
        for half in range(2):
            pw = pows[half]
            for rh in range(2):
                rhs = bass.AP(b["sums"], 2048 + half * 1024 + rh * 128,
                              [[4096, 1], [256, 4], [1, 128]])
                dd = ([rex_tok[j], oevv_tok[j]] +
                      (list(glob_tok) if j == 0 else [])) if (rh == 0) else ()
                P.rec("t", lambda t, rh=rh, rhs=rhs, pw=pw: t.matmul(
                    pw[rh * 64:(rh + 1) * 64, 0:512],
                    ones1[0:1, 0:64], rhs, start=True, stop=True),
                    deps=dd, inc=(1 if rh == 1 else 0), tag=f"sb{j}_{half}")
            btok = P.token("t")
            ntoks[half] = P.rec("v", lambda v, half=half, pw=pw: v.tensor_tensor(
                b["oTn"][:, half * 512:(half + 1) * 512],
                b["oTf"][:, half * 512:(half + 1) * 512], pw[:, :], ALU.mult),
                deps=[btok], tag=f"on{j}_{half}")
        norm_tok[j] = ntoks[1]
        aevh = [None, None]
        for half in range(2):
            pw = pows[half]
            for k in range(8):
                d = [ntoks[half]] if k == 0 else ()
                P.rec("t", lambda t, k=k, half=half, pw=pw: t.matmul(
                    pw[:, :], b["oTn"][:, k * 128:(k + 1) * 128],
                    b["wo"][:, k * 1024 + half * 512:k * 1024 + (half + 1) * 512],
                    start=(k == 0), stop=(k == 7)),
                    deps=d, inc=(1 if k == 7 else 0), tag=f"wo{j}_{half}")
            wotok = P.token("t")
            ab = b["abuf"][:, p * 1024:(p + 1) * 1024]
            aevh[half] = P.rec("v", lambda v, ab=ab, half=half, pw=pw: v.tensor_copy(
                ab[:, half * 512:(half + 1) * 512], pw[:, :]),
                deps=[wotok, res_tok[j - 2] if (half == 0 and j >= 2) else None],
                tag=f"aev{j}_{half}")
        aevh_tok[j] = aevh
        attn_pe_tok[j] = P.token("t")
        aevt = P.token("v")
        stt = b["stats"][:, 32 + p * 16:32 + (p + 1) * 16]
        ab = b["abuf"][:, p * 1024:(p + 1) * 1024]

        def fbn_pair(v, ab=ab, stt=stt):
            v.bn_stats(stt[:, 0:6], ab[:, 0:512])
            return v.bn_stats(stt[:, 6:12], ab[:, 512:1024])
        fpt = P.rec("v", fbn_pair, self_deps=[aevt], tag=f"fbn{j}")
        vt = P.rec("v", lambda v, stt=stt: v.bn_aggr(stt[:, 12:14], stt[:, 0:12]),
                   self_deps=[fpt], tag=f"fba{j}")
        P.rec("s", lambda s, stt=stt: s.activation(stt[:, 14:15], stt[:, 13:14], AF.Ln,
                                                   bias=epsv[:, :], scale=1.0),
              deps=[vt], tag=f"fln{j}")
        flntk = P.token("s")
        fs = P.rec("s", lambda s, stt=stt: s.activation(stt[:, 14:15], stt[:, 14:15],
                                                        AF.Exp, bias=0.0, scale=-0.5),
                   self_deps=[flntk], tag=f"fex{j}")
        ob = b["outb"][:, p * 1024:(p + 1) * 1024]
        xsl = b["xin"][:, (j % RING) * 1024:(j % RING + 1) * 1024]
        P.rec("v", lambda v, ab=ab, ob=ob, stt=stt: v.tensor_scalar(
            ob[:, :], ab[:, :], stt[:, 12:13], stt[:, 14:15], ALU.subtract, ALU.mult),
            deps=[fs, out_tok[j - 2] if j >= 2 else None], tag=f"fap{j}")
        res_tok[j] = P.rec("v", lambda v, ob=ob, xsl=xsl: v.tensor_tensor(
            ob[:, :], ob[:, :], xsl[:, :], ALU.add), tag=f"fx2{j}")
        out_tok[j] = P.rec("y", lambda y, j=j, ob=ob: y.dma_start(
            out_dram[j * 128:(j + 1) * 128, :], ob[:, :]),
            deps=[res_tok[j]], channel=cp + f"ot{j % 2}", tag=f"out{j}")

    for i in range(3):
        load_tile(i)
    produce_ln(0)
    produce(0)
    produce_b(0)
    produce_ln(1)
    produce(1)
    produce_b(1)
    produce_ln(2)
    produce(2)
    for j in range(NT):
        load_tile(j + 3)
        produce_b(j + 2)
        produce_ln(j + 3)
        attn_qk(j)
        produce(j + 3)
        attn_fin(j - 1)
        attn_av(j)
    attn_fin(NT - 1)
    return [out_tok[NT - 2], out_tok[NT - 1],
            P.token("t"), P.token("v"), P.token("s"), P.token("g")]


def attn_host_weights(W_qkv, W_o):
    p = np.arange(128)
    cL = np.arange(128)
    triL = (cL[None, :] <= p[:, None]).astype(np.float32)
    triH = (cL[None, :] >= p[:, None]).astype(np.float32)
    return {
        "wqkvT": np.ascontiguousarray(np.asarray(W_qkv, np.float32).T),
        "woT": np.ascontiguousarray(np.asarray(W_o, np.float32).T),
        "trimL": np.ascontiguousarray(np.tile(triL, (1, 16))),
        "trimH": np.ascontiguousarray(np.tile(triH, (1, 16))),
    }


IRING = 7


def alloc_iattn_bufs(nc, tagp):
    gs = []
    bufs = {}

    def sb(name, shape, dt):
        g = nc.sbuf_tensor(tagp + name, shape, dt)
        t = g.__enter__()
        gs.append(g)
        bufs[name] = t
        return t

    sb("wqkv", [128, 8 * 1536], BF16)
    sb("wo", [128, 4 * 1024], BF16)
    sb("qs", [128, 512], F32)
    sb("ks", [128, 512], F32)
    sb("xin", [128, IRING * 1024], F32)
    sb("x0", [128, 4 * 1024], F32)
    sb("stats", [128, 4 * 16], F32)
    sb("hT", [128, 2 * 1040], BF16)
    sb("rot", [128, 2 * 2048], F32)
    sb("qb", [128, 2 * 512], F32)
    sb("kb", [128, 2 * 512], F32)
    sb("qTb", [128, 2 * 512], BF16)
    sb("kTb", [128, 2 * 512], BF16)
    sb("x0b", [128, 2 * 1024], BF16)
    sb("qsc", [128, 2 * 512], F32)
    sb("qsc2", [128, 2 * 512], F32)
    sb("ssq", [128, 32], F32)
    sb("qT", [128, 4 * 512], BF16)
    sb("kT", [128, IRING * 512], BF16)
    sb("vr", [128, IRING * 520], BF16)
    sb("probs", [128, 8 * 640], BF16)
    sb("trimL", [128, 8 * 128], BF16)
    sb("trimH", [128, 8 * 128], BF16)
    sb("oTf", [128, 512], F32)
    sb("sums", [1, 2048], F32)
    sb("oTn", [128, 512], BF16)
    sb("outb", [128, 2 * 1024], F32)
    return gs, bufs


def alloc_iattn_psums(nc, tagp):
    gs = []
    ps = {}

    def palloc(name, shape, dt):
        g = nc.psum_tensor(tagp + name, shape, dt)
        t = g.__enter__()
        gs.append(g)
        ps[name] = t
        return t

    palloc("pS0", [128, 640], F32)
    palloc("pS1", [128, 640], F32)
    palloc("pQ", [128, 512], F32)
    palloc("pQ2", [128, 512], F32)
    palloc("pOW", [128, 512], F32)
    palloc("pT", [128, 1024], BF16)
    return gs, ps


def record_attn_inner(P, nc, glob, ps, b, wdram, in_dram, out_dram, cp="i2",
                      glob_tok=(), weights_dep=None, with_ln=True, dbg=None, phase_dep=()):
    ident, epsv, ones1 = glob["ident"], glob["eps"], glob["ones1"]
    pS2 = [ps["pS0"], ps["pS1"]]
    pQ, pOW, pT = ps["pQ"], ps["pOW"], ps["pT"]
    pQ2 = ps["pQ2"]

    for k in range(8):
        P.rec("g", lambda g, k=k: g.dma_start(
            b["wqkv"][:, k * 1536:(k + 1) * 1536], wdram["wqkv"][k * 128:(k + 1) * 128, :]),
            deps=(([weights_dep] if weights_dep else []) + list(phase_dep)) if k < 4 else (),
            channel=cp + f"w{k % 4}", tag="wqkvld")
    for k in range(4):
        P.rec("g", lambda g, k=k: g.dma_start(
            b["wo"][:, k * 1024:(k + 1) * 1024], wdram["wo"][k * 128:(k + 1) * 128, :]),
            channel=cp + f"w{k % 4}", tag="wold")
    for ni, name in enumerate(("qs", "ks", "trimL", "trimH")):
        P.rec("g", lambda g, name=name: g.dma_start(b[name][:, :], wdram[name][:, :]),
              channel=cp + f"w{ni % 4}", tag=name)
    vzt = P.rec("g", lambda g: g.memset(b["vr"][:, :], 0.0), tag="vz")
    P.rec("g", lambda g: g.memset(
        bass.AP(b["vr"], 64, [[IRING * 520, 128], [65, 8 * IRING], [1, 1]]), 1.0),
        self_deps=[vzt], tag="vones")
    P.rec("g", lambda g: g.memset(b["kT"][:, 3 * 512:7 * 512], 0.0), tag="kz")
    wtok4 = [P.token(cp + f"w{c}") for c in range(4)]
    wtok = wtok4[0]
    init_tok = P.token("g")

    xin_tok = [None] * (NT + 8)
    rot_tok = [None] * (NT + 8)
    cast_tok_arr = [None] * NT
    dqt_arr = [None] * NT
    x0_tok = [None] * NT
    trx_tok = [None] * NT
    hT_tok = [None] * NT
    qkv_pe_tok = [None] * NT
    qkproc_tok = [None] * NT
    trqk_tok = [None] * NT
    qT_ev_tok = [None] * NT
    ring_tok = [None] * NT
    exp_tok = [[None] * IH for _ in range(NT)]
    zero_tok = [None] * NT
    av_tok = [None] * NT
    oev_tok = [None] * NT
    rex_tok = [None] * NT
    norm_tok = [None] * NT
    attn_pe_tok = [None] * NT
    res_tok = [None] * NT
    out_tok = [None] * NT

    def load_tile(i):
        if i >= NT:
            return
        dep = [res_tok[i - IRING]] if i >= IRING else list(phase_dep)
        xin_tok[i] = P.rec("y", lambda y, i=i: y.dma_start(
            b["xin"][:, (i % IRING) * 1024:(i % IRING + 1) * 1024],
            in_dram[i * 128:(i + 1) * 128, :]),
            deps=dep, channel=cp + f"xi{i % IRING}", tag=f"xin{i}")


    def produce_ln(i):
        if i >= NT:
            return
        p = i % 2
        dep2 = [qkproc_tok[i - 2]] if i >= 2 else []
        rot_tok[i] = P.rec("y", lambda y, i=i: y.dma_start(
            b["rot"][:, (i % 2) * 2048:(i % 2 + 1) * 2048],
            wdram["rot"][i * 128:(i + 1) * 128, :]),
            deps=dep2, channel=cp + f"ro{i % 2}", tag=f"rot{i}")
        xsl = b["xin"][:, (i % IRING) * 1024:(i % IRING + 1) * 1024]
        x0 = b["x0"][:, (i % 4) * 1024:(i % 4 + 1) * 1024]
        x0war = res_tok[i - 4] if i >= 4 else None
        if with_ln:
            stt = b["stats"][:, p * 16:(p + 1) * 16]

            def bn_pair(v, xsl=xsl, stt=stt):
                v.bn_stats(stt[:, 0:6], xsl[:, 0:512])
                return v.bn_stats(stt[:, 6:12], xsl[:, 512:1024])
            pr = P.rec("v", bn_pair, deps=[xin_tok[i]], tag=f"bn{i}")
            vagg = P.rec("v", lambda v, stt=stt: v.bn_aggr(stt[:, 12:14], stt[:, 0:12]),
                         self_deps=[pr], tag=f"bnag{i}")
            P.rec("s", lambda s, stt=stt: s.activation(stt[:, 14:15], stt[:, 13:14], AF.Ln,
                                                       bias=epsv[:, :], scale=1.0),
                  deps=[vagg] + (list(glob_tok) if i < 2 else []), tag=f"ln{i}")
            lntk = P.token("s")
            srstd = P.rec("s", lambda s, stt=stt: s.activation(
                stt[:, 14:15], stt[:, 14:15], AF.Exp, bias=0.0, scale=-0.5),
                self_deps=[lntk], tag=f"exp{i}")
            x0_tok[i] = P.rec("v", lambda v, xsl=xsl, x0=x0, stt=stt: v.tensor_scalar(
                x0[:, :], xsl[:, :], stt[:, 12:13], stt[:, 14:15], ALU.subtract, ALU.mult),
                deps=[srstd, x0war], tag=f"x0{i}")
        else:
            x0_tok[i] = P.rec("v", lambda v, xsl=xsl, x0=x0: v.tensor_copy(x0[:, :], xsl[:, :]),
                              deps=[xin_tok[i], x0war] + (list(glob_tok) if i < 2 else []),
                              tag=f"x0{i}")
        xb = b["qsc"][:, p * 512:(p + 1) * 512]
        x0b = b["x0b"][:, p * 1024:(p + 1) * 1024]
        cast_tok_arr[i] = P.rec("v", lambda v, x0=x0, x0b=x0b: v.tensor_copy(
            x0b[:, :], x0[:, :]),
            deps=[trx_tok[i - 2] if i >= 2 else None] +
                 (list(glob_tok) if i < 2 else []), tag=f"x0b{i}")

    def produce(i):
        if i >= NT:
            return
        p = i % 2
        x0 = b["x0"][:, (i % 4) * 1024:(i % 4 + 1) * 1024]
        x0b = b["x0b"][:, p * 1024:(p + 1) * 1024]
        cast_tok = cast_tok_arr[i]
        rsl = b["rot"][:, (i % 2) * 2048:(i % 2 + 1) * 2048]
        dqt = dqt_arr[i]
        hT = b["hT"][:, p * 1040:(p + 1) * 1040]
        deps = [cast_tok, hT_tok[i - 2] if i >= 2 else None,
                qkv_pe_tok[i - 1] if i >= 1 else None,
                ring_tok[i - 1] if i >= 1 else None]
        for k in range(8):
            P.rec("t", lambda t, k=k, x0b=x0b: t.transpose(
                pT[:, k * 128:(k + 1) * 128], x0b[:, k * 128:(k + 1) * 128], ident[:, :]),
                deps=(deps if k == 0 else ()), inc=(1 if k == 7 else 0), tag=f"trx{i}")
        trx_tok[i] = P.token("t")
        dstap = bass.AP(b["hT"], p * 1040 + 1, [[2 * 1040, 128], [130, 8], [1, 128]])
        ev = P.rec("v", lambda v, dstap=dstap: v.tensor_copy(dstap, pT[:, :]),
                   deps=[trx_tok[i], qkv_pe_tok[i - 2] if i >= 2 else None], tag=f"hTev{i}")
        if i == 0:
            hT_tok[i] = P.rec("v", lambda v: v.memset(
                bass.AP(b["hT"], 0, [[2 * 1040, 128], [130, 8], [1, 1]]), 0.0),
                self_deps=[ev], tag=f"hTz{i}")
        else:
            po = (i - 1) % 2
            src_ap = bass.AP(b["hT"], po * 1040 + 128, [[2 * 1040, 128], [130, 8], [1, 1]])
            dst_ap = bass.AP(b["hT"], p * 1040 + 0, [[2 * 1040, 128], [130, 8], [1, 1]])
            hT_tok[i] = P.rec("v", lambda v, src_ap=src_ap, dst_ap=dst_ap: v.tensor_copy(
                dst_ap, src_ap), self_deps=[ev], tag=f"hTpc{i}")
        evq = None
        for ph in range(3):
            pq = pQ2 if ph == 1 else pQ
            for k in range(8):
                ofs = 1 if k < 4 else 0
                d = []
                if ph == 0 and k == 0:
                    d = [hT_tok[i]] + wtok4 + [
                         (qkproc_tok[i - 1] if i >= 1 else None)]
                if ph == 2 and k == 0:
                    d = [ev_q0]
                P.rec("t", lambda t, k=k, ph=ph, ofs=ofs, p=p, pq=pq: t.matmul(
                    pq[:, :],
                    bass.AP(b["hT"], p * 1040 + k * 130 + ofs, [[2 * 1040, 128], [1, 128]]),
                    b["wqkv"][:, k * 1536 + ph * 512:k * 1536 + ph * 512 + 512],
                    start=(k == 0), stop=(k == 7)),
                    deps=d, inc=(1 if k == 7 else 0), tag=f"qkv{i}_{ph}")
            mmtok = P.token("t")
            if ph == 0:
                evq = P.rec("v", lambda v, p=p: v.tensor_copy(
                    b["qb"][:, p * 512:(p + 1) * 512], pQ[:, :]),
                    deps=[mmtok, qkproc_tok[i - 2] if i >= 2 else None], tag=f"qev{i}")
                ev_q0 = evq
            elif ph == 1:
                evq = P.rec("v", lambda v, p=p: v.tensor_copy(
                    b["kb"][:, p * 512:(p + 1) * 512], pQ2[:, :]),
                    deps=[mmtok], tag=f"kev{i}")
            else:
                slot = i % IRING
                dstv = bass.AP(b["vr"], slot * 520, [[IRING * 520, 128], [65, 8], [1, 64]])
                vev = P.rec("v", lambda v, dstv=dstv: v.tensor_copy(dstv, pQ[:, :]),
                            deps=[mmtok, avl_dep(i)], tag=f"vev{i}")
        qkv_pe_tok[i] = P.token("t")
        if dbg is not None and "x0" in dbg:
            P.rec("y", lambda y, i=i, x0=x0: y.dma_start(
                dbg["x0"][i * 128:(i + 1) * 128, :], x0[:, :]),
                deps=[x0_tok[i]], channel=cp + f"d0{i % 2}", tag=f"dbgx0{i}")
        rsl = b["rot"][:, (i % 2) * 2048:(i % 2 + 1) * 2048]
        if dbg is not None and "rot" in dbg:
            P.rec("y", lambda y, i=i, rsl=rsl: y.dma_start(
                dbg["rot"][i * 128:(i + 1) * 128, :], rsl),
                deps=[rot_tok[i]], channel=cp + f"d6{i % 2}", tag=f"dbgrot{i}")
        dqt = None
        if dbg is not None and "qraw" in dbg:
            dqt = P.rec("y", lambda y, i=i, p=p: y.dma_start(
                dbg["qraw"][i * 128:(i + 1) * 128, :], b["qb"][:, p * 512:(p + 1) * 512]),
                deps=[ev_q0], channel=cp + f"d4{i % 2}", tag=f"dbgqr{i}")
        bq = b["qb"][:, p * 512:(p + 1) * 512]
        bk = b["kb"][:, p * 512:(p + 1) * 512]
        sqq = b["qsc"][:, p * 512:(p + 1) * 512]
        sqk = b["qsc2"][:, p * 512:(p + 1) * 512]
        baseq = p * 16
        basek = p * 16 + 8
        sqt = P.rec("v", lambda v: v.tensor_tensor(sqq[:, :], bq[:, :], bq[:, :], ALU.mult),
                    deps=[rot_tok[i]], tag=f"sqq{i}")
        for h in range(8):
            P.rec("v", lambda v, h=h: v.tensor_reduce(
                b["ssq"][:, baseq + h:baseq + h + 1], sqq[:, h * 64:(h + 1) * 64],
                AX.X, ALU.add), self_deps=([sqt] if h == 0 else ()), tag=f"ssq{i}_{h}")
        sqtk = P.rec("v", lambda v: v.tensor_tensor(sqk[:, :], bk[:, :], bk[:, :], ALU.mult),
                     tag=f"sqk{i}")
        for h in range(8):
            P.rec("v", lambda v, h=h: v.tensor_reduce(
                b["ssq"][:, basek + h:basek + h + 1], sqk[:, h * 64:(h + 1) * 64],
                AX.X, ALU.add), self_deps=([sqtk] if h == 0 else ()), tag=f"ssk{i}_{h}")
        rdt = P.token("v")
        P.rec("v", lambda v: v.tensor_scalar(
            b["ssq"][:, p * 16:p * 16 + 16], b["ssq"][:, p * 16:p * 16 + 16], 1e-24, None,
            ALU.max), self_deps=[rdt], tag=f"cl{i}")
        vtk = P.token("v")
        P.rec("s", lambda s: s.activation(
            b["ssq"][:, p * 16:p * 16 + 16], b["ssq"][:, p * 16:p * 16 + 16], AF.Ln,
            bias=0.0, scale=1.0), deps=[vtk], tag=f"l2l{i}")
        stk = P.token("s")
        rtk = P.rec("s", lambda s: s.activation(
            b["ssq"][:, p * 16:p * 16 + 16], b["ssq"][:, p * 16:p * 16 + 16], AF.Exp,
            bias=0.0, scale=-0.5), self_deps=[stk], tag=f"l2e{i}")
        if dbg is not None and "ssq" in dbg:
            P.rec("y", lambda y, i=i: y.dma_start(
                dbg["ssq"][i * 128:(i + 1) * 128, :], b["ssq"][:, baseq:baseq + 8]),
                deps=[rtk], channel=cp + f"d5{i % 2}", tag=f"dbgss{i}")
        proc_toks = []
        for (which, bb, base, scv, c0, s0) in (("q", bq, baseq, "qs", 0, 512),
                                               ("k", bk, basek, "ks", 1024, 1536)):
            gate = [rtk] + ([dqt] if (dqt is not None and which == "q") else [])
            for h in range(8):
                P.rec("v", lambda v, bb=bb, h=h, base=base: v.tensor_scalar(
                    bb[:, h * 64:(h + 1) * 64], bb[:, h * 64:(h + 1) * 64],
                    b["ssq"][:, base + h:base + h + 1], None, ALU.mult),
                    deps=(gate if (which == "q" and h == 0) else ()), tag=f"l2a{which}{i}_{h}")
            P.rec("v", lambda v, bb=bb, scv=scv: v.tensor_tensor(
                bb[:, :], bb[:, :], b[scv][:, :], ALU.mult), tag=f"qs{which}{i}")
            if dbg is not None and "qu" in dbg and which == "q":
                ut = P.token("v")
                P.rec("y", lambda y, i=i, bb=bb: y.dma_start(
                    dbg["qu"][i * 128:(i + 1) * 128, :], bb[:, :]),
                    deps=[ut], channel=cp + f"d3{i % 2}", tag=f"dbgu{i}")
            cosr = rsl[:, c0:c0 + 512]
            sinr = rsl[:, s0:s0 + 512]
            P.rec("v", lambda v, bb=bb, cosr=cosr: v.tensor_tensor(
                sqq[:, :], bb[:, :], cosr, ALU.mult), tag=f"rc{which}{i}")
            swap = bass.AP(b["qb" if which == "q" else "kb"], p * 512 + 32,
                           [[1024, 128], [64, 8], [-32, 2], [1, 32]])
            rst = P.rec("v", lambda v, swap=swap, sinr=sinr: v.tensor_tensor(
                sqk[:, :], swap, sinr, ALU.mult), tag=f"rs{which}{i}")
            obf = b["qTb" if which == "q" else "kTb"][:, p * 512:(p + 1) * 512]
            tk = P.rec("v", lambda v, obf=obf: v.tensor_tensor(
                obf[:, :], sqq[:, :], sqk[:, :], ALU.add),
                deps=[trqk_tok[i - 2] if i >= 2 else None],
                self_deps=[rst], tag=f"ra{which}{i}")
            proc_toks.append(tk)
        qkproc_tok[i] = proc_toks[-1]
        if dbg is not None and "qk" in dbg:
            P.rec("y", lambda y, i=i, p=p: y.dma_start(
                dbg["qk"][i * 128:(i + 1) * 128, 0:512], b["qTb"][:, p * 512:(p + 1) * 512]),
                deps=[qkproc_tok[i]], channel=cp + f"d1{i % 2}", tag=f"dbgq{i}")
            P.rec("y", lambda y, i=i, p=p: y.dma_start(
                dbg["qk"][i * 128:(i + 1) * 128, 512:1024], b["kTb"][:, p * 512:(p + 1) * 512]),
                deps=[qkproc_tok[i]], channel=cp + f"d2{i % 2}", tag=f"dbgk{i}")

    def produce_b(i):
        if i >= NT:
            return
        p = i % 2
        slot = i % IRING
        for which in ("q", "k"):
            bfb = b["qTb" if which == "q" else "kTb"][:, p * 512:(p + 1) * 512]
            deps0 = [qkproc_tok[i], ring_tok[i - 1] if i >= 1 else None,
                     (qT_ev_tok[i - 1] if i >= 1 else None)]
            for k in range(4):
                kk = k if which == "q" else 4 + k
                P.rec("t", lambda t, k=k, kk=kk, bfb=bfb: t.transpose(
                    pT[:, kk * 128:(kk + 1) * 128], bfb[:, k * 128:(k + 1) * 128],
                    ident[:, :]),
                    deps=(deps0 if (which == "q" and k == 0) else ()),
                    inc=(1 if k == 3 else 0), tag=f"tr{which}{i}")
        trqk_tok[i] = P.token("t")
        qT_ev_tok[i] = P.rec("v", lambda v, i=i: v.tensor_copy(
            b["qT"][:, (i % 4) * 512:(i % 4 + 1) * 512], pT[:, 0:512]),
            deps=[trqk_tok[i], attn_pe_tok[i - 3] if i >= 3 else None], tag=f"qTev{i}")
        ring_tok[i] = P.rec("v", lambda v, slot=slot: v.tensor_copy(
            b["kT"][:, slot * 512:(slot + 1) * 512], pT[:, 512:1024]),
            deps=[avl_dep(i)], self_deps=[qT_ev_tok[i]], tag=f"kTev{i}")

    def avl_dep(i):
        t = i - IRING + 4
        return attn_pe_tok[t] if t >= 0 else init_tok

    def slot_of(tt):
        return (tt % IRING) if tt >= 0 else ((tt + IRING) % IRING)

    def attn_qk(j):
        kdep = [ring_tok[j]]
        for h in range(IH):
            hh = h % 2
            c = h // 2
            pS = pS2[h % 2]
            if h >= 2:
                war = exp_tok[j][h - 2]
            elif j >= 1:
                war = exp_tok[j - 1][IH - 2 + h]
            else:
                war = None
            for tau in range(5):
                tt = j - 4 + tau
                slot = slot_of(tt)
                d = ()
                if tau == 0:
                    d = (list(kdep) if h == 0 else []) + [war, qT_ev_tok[j]]
                P.rec("t", lambda t, slot=slot, c=c, hh=hh, tau=tau, pS=pS, j=j: t.matmul(
                    pS[:, tau * 128:(tau + 1) * 128],
                    b["kT"][hh * 64:(hh + 1) * 64,
                            slot * 512 + c * 128:slot * 512 + c * 128 + 128],
                    b["qT"][hh * 64:(hh + 1) * 64,
                            (j % 4) * 512 + c * 128:(j % 4) * 512 + c * 128 + 128],
                    start=True, stop=True),
                    deps=d, inc=(1 if tau == 4 else 0), tag=f"st{j}_{h}")
            sttok = P.token("t")
            exp_tok[j][h] = P.rec("s", lambda s, h=h, pS=pS: s.activation(
                b["probs"][:, h * 640:(h + 1) * 640], pS[:, :], AF.Exp, bias=0.0, scale=1.0),
                deps=[sttok, av_tok[j - 1] if (h <= 1 and j >= 1) else None],
                tag=f"pexp{j}_{h}")
        zdeps = [exp_tok[j][IH - 1]]
        apL = bass.AP(b["probs"], 0, [[8 * 640, 128], [640, 8], [1, 128]])
        apH = bass.AP(b["probs"], 512, [[8 * 640, 128], [640, 8], [1, 128]])
        P.rec("v", lambda v, apL=apL: v.tensor_tensor(apL, apL, b["trimL"][:, :], ALU.mult),
              deps=zdeps, tag=f"zl{j}")
        ztok = P.rec("v", lambda v, apH=apH: v.tensor_tensor(apH, apH, b["trimH"][:, :], ALU.mult),
                     tag=f"zh{j}")
        if j <= 3:
            width = 512 - j * 128
            apE = bass.AP(b["probs"], 0, [[8 * 640, 128], [640, 8], [1, width]])
            ztok = P.rec("v", lambda v, apE=apE: v.memset(apE, 0.0),
                         self_deps=[ztok], tag=f"ze{j}")
        zero_tok[j] = ztok

    def attn_av(j):
        oev_group = None
        for r in range(2):
            for hq in range(4):
                h = r * 4 + hq
                for tau in range(5):
                    tt = j - 4 + tau
                    slot = slot_of(tt)
                    d = []
                    if hq == 0 and tau == 0:
                        d = [zero_tok[j], oev_group if r >= 1 else
                             (res_tok[j - 1] if j >= 1 else None),
                             norm_tok[j - 1] if (r == 0 and j >= 1) else None]
                        if dbg is not None and "_prtok" in dbg and j == 10:
                            d = d + [dbg["_prtok"]]
                    P.rec("t", lambda t, slot=slot, h=h, hq=hq, tau=tau: t.matmul(
                        pOW[0:65, hq * 128:(hq + 1) * 128],
                        b["vr"][:, slot * 520 + h * 65:slot * 520 + h * 65 + 65],
                        b["probs"][:, h * 640 + tau * 128:h * 640 + (tau + 1) * 128],
                        start=(tau == 0), stop=(tau == 4)),
                        deps=d, inc=(1 if (hq == 3 and tau == 4) else 0), tag=f"av{j}_{r}")
            avtok = P.token("t")
            if r == 1:
                av_tok[j] = avtok
            ap_even_src = bass.AP(pOW, 0, [[512, 64], [256, 2], [1, 128]])
            ap_odd_src = bass.AP(pOW, 128, [[512, 64], [256, 2], [1, 128]])
            P.rec("s", lambda s, r=r, ap=ap_even_src: s.copy(
                b["oTf"][0:64, r * 256:(r + 1) * 256], ap), deps=[avtok], tag=f"oev{j}_{r}")
            P.rec("s", lambda s, r=r, ap=ap_odd_src: s.copy(
                b["oTf"][64:128, r * 256:(r + 1) * 256], ap), tag=f"oev2{j}_{r}")
            oev_group = P.rec("s", lambda s, r=r: s.copy(
                b["sums"][0:1, r * 512:(r + 1) * 512], pOW[64:65, 0:512]), tag=f"sev{j}_{r}")
            oev_tok[j] = oev_group
        if dbg is not None and "oT" in dbg:
            P.rec("y", lambda y, j=j: y.dma_start(
                dbg["oT"][j * 128:(j + 1) * 128, :], b["oTf"][:, :]),
                deps=[oev_tok[j]], channel=cp + f"d7{j % 2}", tag=f"dbgoT{j}")
            P.rec("y", lambda y, j=j: y.dma_start(
                dbg["sm"][j:j + 1, :], b["sums"][0:1, 0:1024]),
                deps=[oev_tok[j]], channel=cp + f"d8{j % 2}", tag=f"dbgsm{j}")
        rlntk = P.rec("s", lambda s: s.activation(b["sums"][0:1, 1024:2048],
                                                  b["sums"][0:1, 0:1024],
                                                  AF.Ln, bias=0.0, scale=1.0),
                      self_deps=[oev_tok[j]], tag=f"rln{j}")
        rex_tok[j] = P.rec("s", lambda s: s.activation(b["sums"][0:1, 1024:2048],
                                                       b["sums"][0:1, 1024:2048],
                                                       AF.Exp, bias=0.0, scale=-1.0),
                           self_deps=[rlntk], tag=f"rex{j}")

    def attn_fin(j):
        if j < 0:
            return
        p = j % 2
        d = [rex_tok[j]]
        if j >= 1:
            d.append(res_tok[j - 1])
        for rh in range(2):
            rhs = bass.AP(b["sums"], 1024 + rh * 128, [[2048, 1], [256, 4], [1, 128]])
            dd = (d + (list(glob_tok) if j == 0 else [])) if rh == 0 else ()
            P.rec("t", lambda t, rh=rh, rhs=rhs: t.matmul(
                pOW[rh * 64:(rh + 1) * 64, 0:512],
                glob["ones1f"][0:1, 0:64], rhs, start=True, stop=True),
                deps=dd, inc=(1 if rh == 1 else 0), tag=f"sbc{j}")
        btok = P.token("t")
        norm_tok[j] = P.rec("v", lambda v: v.tensor_tensor(
            b["oTn"][:, :], b["oTf"][:, :], pOW[:, :], ALU.mult),
            deps=[btok], tag=f"on{j}")
        aev_half = None
        ob = b["outb"][:, p * 1024:(p + 1) * 1024]
        x0 = b["x0"][:, (j % 4) * 1024:(j % 4 + 1) * 1024]
        for half in range(2):
            for k in range(4):
                if half == 0 and k == 0:
                    d = [norm_tok[j]]
                elif half == 1 and k == 0:
                    d = [aev_half]
                else:
                    d = ()
                P.rec("t", lambda t, k=k, half=half: t.matmul(
                    pOW[:, :], b["oTn"][:, k * 128:(k + 1) * 128],
                    b["wo"][:, k * 1024 + half * 512:k * 1024 + (half + 1) * 512],
                    start=(k == 0), stop=(k == 3)),
                    deps=d, inc=(1 if k == 3 else 0), tag=f"wo{j}_{half}")
            wotok = P.token("t")
            aev_half = P.rec("v", lambda v, ob=ob, x0=x0, half=half: v.tensor_tensor(
                ob[:, half * 512:(half + 1) * 512], pOW[:, :],
                x0[:, half * 512:(half + 1) * 512], ALU.add),
                deps=[wotok, out_tok[j - 2] if (half == 0 and j >= 2) else None,
                      x0_tok[j]],
                tag=f"aev{j}_{half}")
        attn_pe_tok[j] = P.token("t")
        res_tok[j] = aev_half
        out_tok[j] = P.rec("y", lambda y, j=j, ob=ob: y.dma_start(
            out_dram[j * 128:(j + 1) * 128, :], ob[:, :]),
            deps=[res_tok[j]], channel=cp + f"ot{j % 2}", tag=f"out{j}")

    for i in range(4):
        load_tile(i)
    produce_ln(0)
    produce(0)
    produce_b(0)
    produce_ln(1)
    produce(1)
    for j in range(NT):
        load_tile(j + 4)
        attn_qk(j)
        attn_fin(j - 1)
        attn_av(j)
        produce_b(j + 1)
        produce_ln(j + 2)
        produce(j + 2)
    attn_fin(NT - 1)
    return [out_tok[NT - 2], out_tok[NT - 1],
            P.token("t"), P.token("v"), P.token("s"), P.token("g")]


def iattn_host_weights(wqkv, q_scale, k_scale, wo, layer):
    l = layer
    p = np.arange(128)
    cL = np.arange(128)
    triL = (cL[None, :] <= p[:, None]).astype(np.float32)
    triH = (cL[None, :] >= p[:, None]).astype(np.float32)
    half = np.arange(0, 64, 2, dtype=np.float32)
    inv_freq = (1.0 / (10000.0 ** (half / 64.0))).astype(np.float32)
    t = np.arange(4096, dtype=np.float32)
    freqs = t[:, None] * inv_freq[None, :]
    freqs = np.concatenate([freqs, freqs], 1)
    base = ((half + 0.4 * 64.0) / (1.4 * 64.0)).astype(np.float32)
    power = (t - 2048.0) / 256.0
    scale = base[None, :] ** power[:, None]
    scale = np.concatenate([scale, scale], 1).astype(np.float32)
    cos = np.cos(freqs).astype(np.float32)
    sin = np.sin(freqs).astype(np.float32)
    sgn = np.concatenate([-np.ones(32, np.float32), np.ones(32, np.float32)])
    cosQ8 = 8.0 * cos * scale
    sinQ8s = 8.0 * sin * scale * sgn
    cosK = cos / scale
    sinKs = sin / scale * sgn
    rep = lambda a: np.tile(a, (1, 8))
    rot = np.concatenate([rep(cosQ8), rep(sinQ8s), rep(cosK), rep(sinKs)], axis=1)
    return {
        "wqkv": np.ascontiguousarray(np.asarray(wqkv[l], np.float32)),
        "wo": np.ascontiguousarray(np.asarray(wo[l], np.float32)),
        "qs": np.ascontiguousarray(np.broadcast_to(
            np.tile(np.asarray(q_scale[l], np.float32), 8)[None, :], (128, 512))),
        "ks": np.ascontiguousarray(np.broadcast_to(
            np.tile(np.asarray(k_scale[l], np.float32), 8)[None, :], (128, 512))),
        "rot": np.ascontiguousarray(rot.astype(np.float32)),
        "trimL": np.ascontiguousarray(np.tile(triL, (1, 8))),
        "trimH": np.ascontiguousarray(np.tile(triH, (1, 8))),
    }


# version 19
# speedup vs baseline: 1.2536x; 1.0344x over previous
import numpy as np
import ml_dtypes
import concourse.bass as bass
import concourse.mybir as mybir
from concourse.bass_utils import run_bass_kernel_spmd

F32 = mybir.dt.float32
BF16 = mybir.dt.bfloat16
AF = mybir.ActivationFunctionType
ALU = mybir.AluOpType
AX = mybir.AxisListType
bf16 = ml_dtypes.bfloat16

D = 1024
T = 4096
NT = T // 128
OH = 16
IH = 8
DH = 64
HID = 2730
NMC = 22
EPS = 1e-5

ENGS = ("t", "s", "v", "g", "y")


class Prog:

    def __init__(self, nc):
        self.nc = nc
        self.ops = []
        self.counts = {}
        for e in ENGS:
            self.counts[e] = 0

    def rec(self, eng, fn, deps=(), channel=None, inc=None, tag="", self_deps=()):
        sem_name = channel if channel is not None else eng
        if inc is None:
            inc = 16 if channel is not None else 1
        if sem_name not in self.counts:
            self.counts[sem_name] = 0
        deps = tuple(d for d in deps if d is not None)
        self_deps = tuple(d for d in self_deps if d is not None)
        self.ops.append((eng, fn, sem_name, inc, deps, self_deps, tag))
        if inc:
            self.counts[sem_name] += inc
            return (sem_name, self.counts[sem_name])
        return None

    def token(self, sem_name):
        c = self.counts.get(sem_name, 0)
        return (sem_name, c) if c else None

    def emit(self, block, final_waiter="y", serialize=False):
        finals = dict(self.counts)
        sems = {}
        guards = []
        for name in self.counts:
            g = self.nc.semaphore("sem_" + name)
            sems[name] = g.__enter__()
            guards.append(g)

        if serialize:
            run_counts = {n: 0 for n in self.counts}
            ser_deps = []
            for (k, fn, sem_name, inc, deps, self_deps, tag) in self.ops:
                ser_deps.append([(n, v) for n, v in run_counts.items() if v])
                if inc:
                    run_counts[sem_name] += inc
        else:
            ser_deps = [list(deps) for (_, _, _, _, deps, _, _) in self.ops]

        def emitter(key):
            def run(eng):
                seen = {}
                for (k, fn, sem_name, inc, deps, self_deps, tag), sdeps in zip(self.ops, ser_deps):
                    if k != key:
                        continue
                    nwait = 0
                    for (dn, dv) in sdeps:
                        if dn == key:
                            continue
                        if seen.get(dn, 0) < dv:
                            if nwait:
                                eng.nop()
                            eng.wait_ge(sems[dn], dv)
                            seen[dn] = dv
                            nwait += 1
                    for (dn, dv) in self_deps:
                        if nwait:
                            eng.nop()
                        eng.wait_ge(sems[dn], dv)
                        nwait += 1
                    instr = fn(eng)
                    if inc:
                        instr.then_inc(sems[sem_name], inc)
                if key == final_waiter:
                    for n2, v2 in finals.items():
                        if v2:
                            eng.wait_ge(sems[n2], v2)
            return run

        block.tensor(emitter("t"))
        block.scalar(emitter("s"))
        block.vector(emitter("v"))
        block.gpsimd(emitter("g"))
        block.sync(emitter("y"))
        return guards



ST = 256
NSUB = ST // 128
NST = T // ST


def alloc_ffn_bufs(nc, tagp, with_final):
    gs = []
    bufs = {}

    def sb(name, shape, dt):
        g = nc.sbuf_tensor(tagp + name, shape, dt)
        t = g.__enter__()
        gs.append(g)
        bufs[name] = t
        return t

    sb("w1", [128, 8 * 5460], BF16)
    sb("w2", [128, NMC * 1024], BF16)
    sb("b1a", [128, NMC], F32)
    sb("b1g", [128, NMC], F32)
    sb("b2row", [1, 1024], BF16)
    sb("grep", [128, 1024], F32)
    sb("brep", [128, 1024], F32)
    sb("xin", [128, 5 * 1024], F32)
    sb("hsh", [128, 2 * 512], F32)
    sb("stats", [128, 4 * 16], F32)
    sb("hln", [128, 2 * 1024], BF16)
    sb("hlnT", [128, 2 * (8 * ST)], BF16)
    sb("tmp", [128, 2 * 1024], F32)
    sb("gel", [128, 2 * ST], BF16)
    sb("prod", [128, 2 * ST], BF16)
    sb("outb", [128, 2 * 1024], F32)
    if with_final:
        sb("x2ring", [128, 3 * 1024], F32)
    return gs, bufs


def alloc_ffn_psums(nc, tagp):
    gs = []
    ps = {}
    pT = []
    for i in range(2):
        g = nc.psum_tensor(tagp + f"pT{i}", [128, 1024], BF16)
        pT.append(g.__enter__()); gs.append(g)
    pag = []
    for i in range(2):
        g = nc.psum_tensor(tagp + f"pag{i}", [128, 512], F32)
        pag.append(g.__enter__()); gs.append(g)
    pout = []
    for i in range(4):
        g = nc.psum_tensor(tagp + f"pout{i}", [128, 512], F32)
        pout.append(g.__enter__()); gs.append(g)
    ps["pT"], ps["pag"], ps["pout"] = pT, pag, pout
    return gs, ps


def record_ffn_phase(P, nc, glob, psums, bufs, wdram, in_dram, out_dram,
                     final=None, weights_dep=None, cp="f0", glob_tok=(), phase_dep=()):
    b = bufs
    pT, pag, pout = psums["pT"], psums["pag"], psums["pout"]
    ident, epsv, ones1 = glob["ident"], glob["eps"], glob["ones1"]

    for k in range(8):
        P.rec("g", lambda g, k=k: g.dma_start(
            b["w1"][:, k * 5460:(k + 1) * 5460], wdram["w1"][k * 128:(k + 1) * 128, :]),
            deps=(([weights_dep] if weights_dep else []) + list(phase_dep)) if k < 4 else (),
            channel=cp + f"w{k % 4}", tag="w1ld")
    for ni, name in enumerate(("b1a", "b1g", "b2row", "grep", "brep")):
        P.rec("g", lambda g, name=name: g.dma_start(b[name][:, :], wdram[name][:, :]),
              deps=list(phase_dep) if ni < 4 else (),
              channel=cp + f"u{ni % 4}", tag=name + "ld")
    for mc in range(NMC):
        rows = 42 if mc == NMC - 1 else 128
        P.rec("g", lambda g, mc=mc, rows=rows: g.dma_start(
            b["w2"][0:rows, mc * 1024:(mc + 1) * 1024], wdram["w2"][mc * 128:mc * 128 + rows, :]),
            channel=cp + f"u{mc % 4}", tag="w2ld")
    wtok4 = [P.token(cp + f"w{c}") for c in range(4)]
    utok4 = [P.token(cp + f"u{c}") for c in range(4)]
    wtok = wtok4[0]

    xin_tok = [None] * (NT + 8)
    shift_tok = [None] * (NT + 8)
    tr_tok = [None] * NT
    lnv_tok = [None] * NT
    srstd_tok = [None] * NT
    trev_tok = [None] * NT
    resid_tok = [None] * NT
    out_tok = [None] * NT
    st_pe_tok = [None] * NST
    x2_tok = [None] * NT
    dump_tok = [[] for _ in range(NT)]

    def load_tile(j):
        if j >= NT:
            return
        dep = [resid_tok[j - 5], shift_tok[j - 4]] if j >= 5 else list(phase_dep)
        xin_tok[j] = P.rec("y", lambda y, j=j: y.dma_start(
            b["xin"][:, (j % 5) * 1024:(j % 5 + 1) * 1024], in_dram[j * 128:(j + 1) * 128, :]),
            deps=dep, channel=cp + f"xi{j % 5}", tag=f"xin{j}")

    def load_x2(j):
        if final is None or "x2_dram" not in final or j >= NT:
            return
        dep = [resid_tok[j - 3]] if j >= 3 else list(phase_dep)
        x2_tok[j] = P.rec("y", lambda y, j=j: y.dma_start(
            b["x2ring"][:, (j % 3) * 1024:(j % 3 + 1) * 1024],
            final["x2_dram"][j * 128:(j + 1) * 128, :]),
            deps=dep, channel=cp + f"x2{j % 3}", tag=f"x2in{j}")

    def ln_tile(j):
        p = j % 2
        xsl = b["xin"][:, (j % 5) * 1024:(j % 5 + 1) * 1024]
        hsh = b["hsh"][:, p * 512:(p + 1) * 512]
        dshift = [xin_tok[j], lnv_tok[j - 2] if j >= 2 else None] + \
            (dump_tok[j - 2] if j >= 2 else [])
        t1 = P.rec("g", lambda g, xsl=xsl, hsh=hsh: g.dma_start(
            hsh[1:128, :], xsl[0:127, 512:1024]), deps=dshift,
            channel=cp + f"sa{j % 2}", tag=f"sh{j}")
        if j == 0:
            t2 = P.rec("g", lambda g, hsh=hsh: g.memset(hsh[0:1, :], 0.0), tag="sh0z")
        else:
            xprev = b["xin"][:, ((j - 1) % 5) * 1024:((j - 1) % 5 + 1) * 1024]
            t2 = P.rec("g", lambda g, xprev=xprev, hsh=hsh: g.dma_start(
                hsh[0:1, :], xprev[127:128, 512:1024]),
                deps=[xin_tok[j - 1]], channel=cp + f"sb{j % 2}", tag=f"shp{j}")
        shift_tok[j] = t2
        stt = b["stats"][:, p * 16:(p + 1) * 16]

        def bn_pair(v, xsl=xsl, hsh=hsh, stt=stt):
            v.bn_stats(stt[:, 0:6], xsl[:, 0:512])
            return v.bn_stats(stt[:, 6:12], hsh[:, :])

        pair_tok = P.rec("v", bn_pair,
                         deps=[xin_tok[j], t1, t2] + (dump_tok[j - 2] if j >= 2 else []),
                         tag=f"bn{j}")
        vagg = P.rec("v", lambda v, stt=stt: v.bn_aggr(stt[:, 12:14], stt[:, 0:12]),
                     self_deps=[pair_tok], tag=f"bnag{j}")
        P.rec("s", lambda s, stt=stt: s.activation(stt[:, 14:15], stt[:, 13:14], AF.Ln,
                                                   bias=epsv[:, :], scale=1.0),
              deps=[vagg] + (list(glob_tok) if j < 2 else []), tag=f"ln{j}")
        lntk = P.token("s")
        srstd_tok[j] = P.rec("s", lambda s, stt=stt: s.activation(
            stt[:, 14:15], stt[:, 14:15], AF.Exp, bias=0.0, scale=-0.5),
            self_deps=[lntk], tag=f"exp{j}")
        tmp = b["tmp"][:, p * 1024:(p + 1) * 1024]
        hln = b["hln"][:, p * 1024:(p + 1) * 1024]
        P.rec("v", lambda v, xsl=xsl, tmp=tmp, stt=stt: v.tensor_scalar(
            tmp[:, 0:512], xsl[:, 0:512], stt[:, 12:13], stt[:, 14:15], ALU.subtract, ALU.mult),
            deps=[srstd_tok[j]], tag=f"apA{j}")
        P.rec("v", lambda v, hsh=hsh, tmp=tmp, stt=stt: v.tensor_scalar(
            tmp[:, 512:1024], hsh[:, :], stt[:, 12:13], stt[:, 14:15], ALU.subtract, ALU.mult),
            tag=f"apB{j}")
        P.rec("v", lambda v, tmp=tmp: v.tensor_tensor(tmp[:, :], tmp[:, :], b["grep"][:, :], ALU.mult),
              deps=(utok4 if j < 3 else ()), tag=f"apG{j}")
        lnv_tok[j] = P.rec("v", lambda v, tmp=tmp, hln=hln: v.tensor_tensor(
            hln[:, :], tmp[:, :], b["brep"][:, :], ALU.add),
            deps=[tr_tok[j - 2] if j >= 2 else None] +
                 (dump_tok[j - 2] if j >= 2 else []), tag=f"apBt{j}")
        if final is not None and "dbg_hln" in final:
            dump_tok[j] = [
                P.rec("y", lambda y, j=j, hln=hln: y.dma_start(
                    final["dbg_hln"][j * 128:(j + 1) * 128, :], hln[:, :]),
                    deps=[lnv_tok[j]], channel=cp + f"db{j % 2}", tag=f"dbg{j}"),
                P.rec("y", lambda y, j=j, hsh=hsh: y.dma_start(
                    final["dbg_hsh"][j * 128:(j + 1) * 128, :], hsh[:, :]),
                    deps=[lnv_tok[j]], channel=cp + f"dc{j % 2}", tag=f"dbgh{j}"),
                P.rec("y", lambda y, j=j, stt=stt: y.dma_start(
                    final["dbg_st"][j * 128:(j + 1) * 128, :], stt[:, 12:15]),
                    deps=[lnv_tok[j]], channel=cp + f"dd{j % 2}", tag=f"dbgs{j}")]

    def transpose_tile(j):
        p = j % 2
        st = j // NSUB
        stp = st % 2
        hln = b["hln"][:, p * 1024:(p + 1) * 1024]
        ptb = pT[p]
        deps = [lnv_tok[j], trev_tok[j - 2] if j >= 2 else None,
                st_pe_tok[st - 2] if st >= 2 else None] + (list(glob_tok) if j < 2 else [])
        for k in range(8):
            P.rec("t", lambda t, k=k, hln=hln, ptb=ptb: t.transpose(
                ptb[:, k * 128:(k + 1) * 128], hln[:, k * 128:(k + 1) * 128], ident[:, :]),
                deps=(deps if k == 0 else ()), inc=(1 if k == 7 else 0), tag=f"tr{j}")
        ttok = P.token("t")
        tr_tok[j] = ttok
        sub = j % NSUB
        base = stp * (8 * ST)
        dstap = bass.AP(b["hlnT"], base + sub * 128, [[2 * 8 * ST, 128], [ST, 8], [1, 128]])
        trev_tok[j] = P.rec("v", lambda v, ptb=ptb, dstap=dstap: v.tensor_copy(dstap, ptb[:, :]),
                            deps=[ttok], tag=f"trev{j}")

    def mc_loop(st):
        stp = st % 2
        base = stp * (8 * ST)
        prev_resid = resid_tok[st * NSUB - 1] if st >= 1 else None
        gel_tok = [None, None]
        prod_tok = [None, None]

        def emit_w2(mc):
            mp = mc % 2
            rows = 42 if mc == NMC - 1 else 128
            prod = b["prod"][:, mp * ST:(mp + 1) * ST]
            for sub in range(NSUB):
                for nh in range(2):
                    d = [prod_tok[mp]]
                    if st == 0 and sub == 0 and nh == 0:
                        d = d + utok4
                    if mc == 0 and sub == 0 and nh == 0 and prev_resid:
                        d = d + [prev_resid]
                    P.rec("t", lambda t, mc=mc, sub=sub, nh=nh, rows=rows, prod=prod:
                          t.matmul(pout[sub * 2 + nh][:, :],
                                   prod[0:rows, sub * 128:(sub + 1) * 128],
                                   b["w2"][0:rows, mc * 1024 + nh * 512:mc * 1024 + (nh + 1) * 512],
                                   start=(mc == 0), stop=False),
                          deps=(d if (sub == 0 and nh == 0) else ()), inc=0, tag=f"w2{st}_{mc}")

        for mc in range(NMC):
            mp = mc % 2
            rows = 42 if mc == NMC - 1 else 128
            pa = pag[mp]
            deps = [trev_tok[st * NSUB], trev_tok[st * NSUB + 1], prod_tok[mp]]
            if mc < 2:
                deps = deps + wtok4
            for half, cofs in ((0, 0), (1, 2730)):
                for k in range(8):
                    last = (half == 1 and k == 7)
                    P.rec("t", lambda t, k=k, mc=mc, rows=rows, half=half, cofs=cofs, pa=pa:
                          t.matmul(pa[0:rows, half * 256:half * 256 + ST],
                                   b["w1"][:, k * 5460 + cofs + mc * 128:
                                           k * 5460 + cofs + mc * 128 + rows],
                                   bass.AP(b["hlnT"], base + k * ST, [[2 * 8 * ST, 128], [1, ST]]),
                                   start=(k == 0), stop=(k == 7)),
                          deps=(deps if (half == 0 and k == 0) else ()),
                          inc=(1 if last else 0), tag=f"ag{st}_{mc}")
            agtok = P.token("t")
            gel = b["gel"][:, mp * ST:(mp + 1) * ST]
            gel_tok[mp] = P.rec("s", lambda s, pa=pa, gel=gel, mc=mc, rows=rows: s.activation(
                gel[0:rows, :], pa[0:rows, 256:512], AF.Gelu,
                bias=b["b1g"][0:rows, mc:mc + 1], scale=1.0),
                deps=[agtok, prod_tok[mp]] + (utok4 if (st == 0 and mc < 2) else []),
                tag=f"gel{st}_{mc}")
            prod = b["prod"][:, mp * ST:(mp + 1) * ST]
            prod_tok[mp] = P.rec("v", lambda v, pa=pa, prod=prod, gel=gel, mc=mc, rows=rows:
                                 v.scalar_tensor_tensor(prod[0:rows, :], pa[0:rows, 0:256],
                                                        b["b1a"][0:rows, mc:mc + 1], gel[0:rows, :],
                                                        ALU.add, ALU.mult),
                                 deps=[gel_tok[mp]], tag=f"prod{st}_{mc}")
            if mc >= 1:
                emit_w2(mc - 1)
        emit_w2(NMC - 1)
        for sub in range(NSUB):
            for nh in range(2):
                last = (sub == NSUB - 1 and nh == 1)
                P.rec("t", lambda t, sub=sub, nh=nh: t.matmul(
                    pout[sub * 2 + nh][:, :], ones1[:, :],
                    b["b2row"][:, nh * 512:(nh + 1) * 512], start=False, stop=True),
                    deps=(list(glob_tok) if (st == 0 and sub == 0 and nh == 0) else ()),
                    inc=(1 if last else 0), tag=f"b2{st}")
        st_pe_tok[st] = P.token("t")

    def evac_st(st):
        for sub in range(NSUB):
            j = st * NSUB + sub
            p = j % 2
            xsl = b["xin"][:, (j % 5) * 1024:(j % 5 + 1) * 1024]
            ob = b["outb"][:, p * 1024:(p + 1) * 1024]
            deps = [st_pe_tok[st], out_tok[j - 2] if j >= 2 else None]
            for nh in range(2):
                P.rec("v", lambda v, sub=sub, nh=nh, xsl=xsl, ob=ob: v.tensor_tensor(
                    ob[:, nh * 512:(nh + 1) * 512], pout[sub * 2 + nh][:, :],
                    xsl[:, nh * 512:(nh + 1) * 512], ALU.add),
                    deps=(deps if nh == 0 else ()), tag=f"res{j}")
            if final is None or "x2_dram" not in final:
                resid_tok[j] = P.token("v")
            else:
                stt = b["stats"][:, 32 + p * 16:32 + (p + 1) * 16]
                def fbn_pair(v, ob=ob, stt=stt):
                    v.bn_stats(stt[:, 0:6], ob[:, 0:512])
                    return v.bn_stats(stt[:, 6:12], ob[:, 512:1024])
                fpt = P.rec("v", fbn_pair, self_deps=[P.token("v")], tag=f"fbn{j}")
                vt = P.rec("v", lambda v, stt=stt: v.bn_aggr(stt[:, 12:14], stt[:, 0:12]),
                           self_deps=[fpt], tag=f"fba{j}")
                P.rec("s", lambda s, stt=stt: s.activation(stt[:, 14:15], stt[:, 13:14], AF.Ln,
                                                           bias=epsv[:, :], scale=1.0),
                      deps=[vt], tag=f"fln{j}")
                flntk = P.token("s")
                fs = P.rec("s", lambda s, stt=stt: s.activation(stt[:, 14:15], stt[:, 14:15],
                                                                AF.Exp, bias=0.0, scale=-0.5),
                           self_deps=[flntk], tag=f"fex{j}")
                tmp = b["tmp"][:, p * 1024:(p + 1) * 1024]
                P.rec("v", lambda v, ob=ob, tmp=tmp, stt=stt: v.tensor_scalar(
                    tmp[:, :], ob[:, :], stt[:, 12:13], stt[:, 14:15], ALU.subtract, ALU.mult),
                    deps=[fs, x2_tok[j]], tag=f"fap{j}")
                resid_tok[j] = P.rec("v", lambda v, ob=ob, tmp=tmp, j=j: v.tensor_tensor(
                    ob[:, :], tmp[:, :], b["x2ring"][:, (j % 3) * 1024:(j % 3 + 1) * 1024], ALU.add),
                    tag=f"fx2{j}")
            out_tok[j] = P.rec("y", lambda y, j=j, ob=ob: y.dma_start(
                out_dram[j * 128:(j + 1) * 128, :], ob[:, :]),
                deps=[resid_tok[j]], channel=cp + f"ot{j % 2}", tag=f"out{j}")

    for j in range(3):
        load_tile(j)
    load_x2(0)
    for j in range(3):
        ln_tile(j)
        transpose_tile(j)
    for st in range(NST):
        for j in (2 * st + 3, 2 * st + 4):
            load_tile(j)
        for j in (2 * st + 1, 2 * st + 2):
            load_x2(j)
        mc_loop(st)
        for j in (2 * st + 3, 2 * st + 4):
            if j < NT:
                ln_tile(j)
                transpose_tile(j)
        evac_st(st)
    return [out_tok[NT - 2], out_tok[NT - 1]]



def ffn_host_weights(ln_g, ln_b, w1, b1, w2, b2, layer):
    l = layer
    b1l = np.asarray(b1[l], np.float32)
    b1a = np.zeros((128, NMC), np.float32)
    b1g = np.zeros((128, NMC), np.float32)
    for mc in range(NMC):
        rows = 42 if mc == NMC - 1 else 128
        b1a[0:rows, mc] = b1l[mc * 128:mc * 128 + rows]
        b1g[0:rows, mc] = b1l[2730 + mc * 128:2730 + mc * 128 + rows]
    return {
        "w1": np.ascontiguousarray(np.asarray(w1[l], np.float32)),
        "w2": np.ascontiguousarray(np.asarray(w2[l], np.float32)),
        "b1a": b1a,
        "b1g": b1g,
        "b2row": np.ascontiguousarray(np.asarray(b2[l], np.float32)[None, :]),
        "grep": np.ascontiguousarray(np.broadcast_to(np.asarray(ln_g[l], np.float32)[None, :], (128, 1024))),
        "brep": np.ascontiguousarray(np.broadcast_to(np.asarray(ln_b[l], np.float32)[None, :], (128, 1024))),
    }


def alloc_globals(nc):
    gs = []
    glob = {}
    g1 = nc.sbuf_tensor("g_ident", [128, 128], BF16)
    glob["ident"] = g1.__enter__(); gs.append(g1)
    g2 = nc.sbuf_tensor("g_eps", [128, 1], F32)
    glob["eps"] = g2.__enter__(); gs.append(g2)
    g3 = nc.sbuf_tensor("g_ones1", [1, 128], BF16)
    glob["ones1"] = g3.__enter__(); gs.append(g3)
    g4 = nc.sbuf_tensor("g_ones1f", [1, 128], F32)
    glob["ones1f"] = g4.__enter__(); gs.append(g4)
    g5 = nc.sbuf_tensor("g_dum", [128, 512], BF16)
    glob["dum"] = g5.__enter__(); gs.append(g5)
    return gs, glob


def record_globals(P, nc, glob, ident_dram):
    t1 = P.rec("g", lambda g: g.dma_start(glob["ident"][:, :], ident_dram[:, :]),
               channel="gw", tag="identld")
    P.rec("g", lambda g: g.memset(glob["eps"][:, :], EPS), tag="epsset")
    P.rec("g", lambda g: g.memset(glob["ones1"][:, :], 1.0), tag="onesset")
    P.rec("g", lambda g: g.memset(glob["dum"][:, :], 0.0), tag="dumset")
    t3 = P.rec("g", lambda g: g.memset(glob["ones1f"][:, :], 1.0), tag="onesfset")
    return [t1, t3]


def record_ham_filler(P, nc, glob, psum_ap, n, deps):
    for q in range(n):
        P.rec("t", lambda t: t.matmul(psum_ap, glob["dum"][:, 0:128],
                                      glob["dum"][:, :], start=True, stop=True),
              deps=(list(deps) if q == 0 else ()), inc=0, tag="hamfill")


def host_ident():
    return np.eye(128, dtype=np.float32)


def build_core_program():
    nc = bass.Bass("TRN2", target_bir_lowering=False, debug=False)
    x_in = nc.dram_tensor("x", [T, D], F32, kind="ExternalInput")
    ident = nc.dram_tensor("cident", [128, 128], F32, kind="ExternalInput")
    out = nc.dram_tensor("out", [T, D], F32, kind="ExternalOutput")
    x2 = nc.dram_tensor("x2s", [T, D], F32)
    m1 = nc.dram_tensor("m1s", [T, D], F32)
    a1 = nc.dram_tensor("a1s", [T, D], F32)
    m2 = nc.dram_tensor("m2s", [T, D], F32)

    shapes_attn = {"wqkvT": (1024, 3072), "woT": (1024, 1024),
                   "trimL": (128, 2048), "trimH": (128, 2048)}
    shapes_iatt = {"wqkv": (1024, 1536), "wo": (512, 1024), "qs": (128, 512),
                   "ks": (128, 512), "rot": (4096, 2048),
                   "trimL": (128, 1024), "trimH": (128, 1024)}
    shapes_ffn = {"w1": (1024, 5460), "w2": (2730, 1024), "b1a": (128, NMC),
                  "b1g": (128, NMC), "b2row": (1, 1024), "grep": (128, 1024),
                  "brep": (128, 1024)}
    wd = {}
    for pfx, shapes in (("p1", shapes_attn), ("p2", shapes_iatt), ("p3", shapes_ffn),
                        ("p4", shapes_iatt), ("p5", shapes_ffn)):
        wd[pfx] = {name: nc.dram_tensor(f"{pfx}_{name}", list(sh), F32, kind="ExternalInput")
                   for name, sh in shapes.items()}

    P = Prog(nc)
    gg, glob = alloc_globals(nc)
    gtok = record_globals(P, nc, glob, ident)

    bg1, b1_ = alloc_attn_bufs(nc, "a1")
    pg1, ps1 = alloc_attn_psums(nc, "a1")
    end1 = record_attn_outer(P, nc, glob, ps1, b1_, wd["p1"], x_in, x2, cp="a1",
                             glob_tok=gtok)
    for g in reversed(bg1 + pg1):
        g.__exit__(None, None, None)
    bg2, b2_ = alloc_iattn_bufs(nc, "i2")
    pg2, ps2 = alloc_iattn_psums(nc, "i2")
    record_ham_filler(P, nc, glob, ps2["pS0"][:, 0:512], 450, end1)
    end2 = record_attn_inner(P, nc, glob, ps2, b2_, wd["p2"], x2, m1, cp="i2",
                             glob_tok=gtok, with_ln=True, phase_dep=end1)
    for g in reversed(bg2 + pg2):
        g.__exit__(None, None, None)
    bg3, b3_ = alloc_ffn_bufs(nc, "f3", with_final=False)
    pg3, ps3 = alloc_ffn_psums(nc, "f3")
    record_ham_filler(P, nc, glob, ps3["pag"][0][:, 0:512], 450, end2)
    end3 = record_ffn_phase(P, nc, glob, ps3, b3_, wd["p3"], m1, a1,
                            cp="f3", glob_tok=gtok, phase_dep=end2)
    for g in reversed(bg3 + pg3):
        g.__exit__(None, None, None)
    bg4, b4_ = alloc_iattn_bufs(nc, "i4")
    pg4, ps4 = alloc_iattn_psums(nc, "i4")
    record_ham_filler(P, nc, glob, ps4["pS0"][:, 0:512], 450, end3)
    end4 = record_attn_inner(P, nc, glob, ps4, b4_, wd["p4"], a1, m2, cp="i4",
                             glob_tok=gtok, with_ln=False, phase_dep=end3)
    for g in reversed(bg4 + pg4):
        g.__exit__(None, None, None)
    bg5, b5_ = alloc_ffn_bufs(nc, "f5", with_final=True)
    pg5, ps5 = alloc_ffn_psums(nc, "f5")
    record_ham_filler(P, nc, glob, ps5["pag"][0][:, 0:512], 450, end4)
    record_ffn_phase(P, nc, glob, ps5, b5_, wd["p5"], m2, out,
                     final={"x2_dram": x2}, cp="f5", glob_tok=gtok, phase_dep=end4)
    for g in reversed(bg5 + pg5):
        g.__exit__(None, None, None)

    with nc.Block() as block:
        semguards = P.emit(block)
    for g in reversed(semguards + gg):
        g.__exit__(None, None, None)
    return nc, len(P.ops)


def host_weight_maps(L0_attn_W_qkv, L0_attn_W_o, L0_ffn_wqkv, L0_ffn_q_scale,
                     L0_ffn_k_scale, L0_ffn_wo, L0_ffn_ln_g, L0_ffn_ln_b,
                     L0_ffn_w1, L0_ffn_b1, L0_ffn_w2, L0_ffn_b2):
    m = {}
    for name, arr in attn_host_weights(L0_attn_W_qkv, L0_attn_W_o).items():
        m[f"p1_{name}"] = arr
    for l, pfx in ((0, "p2"), (1, "p4")):
        for name, arr in iattn_host_weights(L0_ffn_wqkv, L0_ffn_q_scale,
                                            L0_ffn_k_scale, L0_ffn_wo, l).items():
            m[f"{pfx}_{name}"] = arr
    for l, pfx in ((0, "p3"), (1, "p5")):
        for name, arr in ffn_host_weights(L0_ffn_ln_g, L0_ffn_ln_b, L0_ffn_w1,
                                          L0_ffn_b1, L0_ffn_w2, L0_ffn_b2, l).items():
            m[f"{pfx}_{name}"] = arr
    m["cident"] = host_ident()
    return m


def kernel(L0_attn_x, L0_attn_W_qkv, L0_attn_W_o, L0_ffn_wqkv, L0_ffn_q_scale,
           L0_ffn_k_scale, L0_ffn_wo, L0_ffn_ln_g, L0_ffn_ln_b, L0_ffn_w1,
           L0_ffn_b1, L0_ffn_w2, L0_ffn_b2):
    x = np.ascontiguousarray(np.asarray(L0_attn_x, np.float32))
    wm = host_weight_maps(L0_attn_W_qkv, L0_attn_W_o, L0_ffn_wqkv, L0_ffn_q_scale,
                          L0_ffn_k_scale, L0_ffn_wo, L0_ffn_ln_g, L0_ffn_ln_b,
                          L0_ffn_w1, L0_ffn_b1, L0_ffn_w2, L0_ffn_b2)
    nc, nops = build_core_program()
    in_maps = []
    for c in range(8):
        im = dict(wm)
        im["x"] = np.ascontiguousarray(x[c])
        in_maps.append(im)
    res = run_bass_kernel_spmd(nc, in_maps, core_ids=list(range(8)))
    out = np.stack([np.asarray(res.results[c]["out"], np.float32) for c in range(8)], axis=0)
    return out



RING = 6


def alloc_attn_bufs(nc, tagp):
    gs = []
    bufs = {}

    def sb(name, shape, dt):
        g = nc.sbuf_tensor(tagp + name, shape, dt)
        t = g.__enter__()
        gs.append(g)
        bufs[name] = t
        return t

    sb("wqkv", [128, 8 * 3072], BF16)
    sb("wo", [128, 8 * 1024], BF16)
    sb("xin", [128, RING * 1024], F32)
    sb("stats", [128, 4 * 16], F32)
    sb("h1", [128, 2 * 1024], BF16)
    sb("h1T", [128, 2 * 1024], BF16)
    sb("qkt", [128, 2 * 2048], BF16)
    sb("qT", [128, 4 * 1024], BF16)
    sb("kT", [128, RING * 1024], BF16)
    sb("vr", [128, RING * 1040], BF16)
    sb("probs", [128, 16 * 640], BF16)
    sb("trimL", [128, 16 * 128], BF16)
    sb("trimH", [128, 16 * 128], BF16)
    sb("oTf", [128, 1024], F32)
    sb("sums", [1, 4096], F32)
    
    sb("oTn", [128, 1024], BF16)
    sb("abuf", [128, 2 * 1024], F32)
    sb("outb", [128, 2 * 1024], F32)
    return gs, bufs


def alloc_attn_psums(nc, tagp):
    gs = []
    ps = {}

    def palloc(name, shape, dt):
        g = nc.psum_tensor(tagp + name, shape, dt)
        t = g.__enter__()
        gs.append(g)
        ps[name] = t
        return t

    palloc("pS0", [128, 640], F32)
    palloc("pS1", [128, 640], F32)
    palloc("pQ", [128, 1024], F32)
    palloc("pOW", [128, 512], F32)
    palloc("pOW2", [128, 512], F32)
    ps["pT"] = ps["pQ"].bitcast(BF16)
    return gs, ps


def record_attn_outer(P, nc, glob, ps, b, wdram, in_dram, out_dram, cp="a1",
                      glob_tok=(), weights_dep=None, dbg=None, phase_dep=()):
    ident, epsv, ones1 = glob["ident"], glob["eps"], glob["ones1f"]
    pS2, pQ, pOW, pT = [ps["pS0"], ps["pS1"]], ps["pQ"], ps["pOW"], ps["pT"]
    pOW2 = ps["pOW2"]
    pows = [pOW, pOW2]

    for k in range(8):
        P.rec("g", lambda g, k=k: g.dma_start(
            b["wqkv"][:, k * 3072:(k + 1) * 3072], wdram["wqkvT"][k * 128:(k + 1) * 128, :]),
            deps=(([weights_dep] if weights_dep else []) + list(phase_dep)) if k < 4 else (),
            channel=cp + f"w{k % 4}", tag="wqkvld")
    for k in range(8):
        P.rec("g", lambda g, k=k: g.dma_start(
            b["wo"][:, k * 1024:(k + 1) * 1024], wdram["woT"][k * 128:(k + 1) * 128, :]),
            channel=cp + f"w{k % 4}", tag="wold")
    for ni, name in enumerate(("trimL", "trimH")):
        P.rec("g", lambda g, name=name: g.dma_start(b[name][:, :], wdram[name][:, :]),
              channel=cp + f"w{ni % 4}", tag=name)
    vzt = P.rec("g", lambda g: g.memset(b["vr"][:, :], 0.0), tag="vz")
    P.rec("g", lambda g: g.memset(
        bass.AP(b["vr"], 64, [[RING * 1040, 128], [65, 16 * RING], [1, 1]]), 1.0),
        self_deps=[vzt], tag="vones")
    P.rec("g", lambda g: g.memset(b["kT"][:, 4 * 1024:6 * 1024], 0.0), tag="kz")
    wtok4 = [P.token(cp + f"w{c}") for c in range(4)]
    wtok = wtok4[0]
    init_tok = P.token("g")

    xin_tok = [None] * (NT + 8)
    vmm_tok = [None] * NT
    h1_tok = [None] * NT
    trh_tok = [None] * NT
    h1T_tok = [None] * NT
    qkv_pe_tok = [None] * NT
    qkev_tok = [None] * NT
    trqk_tok = [None] * NT
    ring_tok = [None] * NT
    exp_tok = [[None] * OH for _ in range(NT)]
    zero_tok = [None] * NT
    zgrp_tok = [[None] * 4 for _ in range(NT)]
    oevv_tok = [None] * NT
    aevh_tok = [[None, None] for _ in range(NT)]
    rex_tok = [None] * NT
    av_tok = [None] * NT
    oev_tok = [None] * NT
    norm_tok = [None] * NT
    attn_pe_tok = [None] * NT
    res_tok = [None] * NT
    out_tok = [None] * NT

    def load_tile(i):
        if i >= NT:
            return
        dep = [res_tok[i - RING]] if i >= RING else list(phase_dep)
        xin_tok[i] = P.rec("y", lambda y, i=i: y.dma_start(
            b["xin"][:, (i % RING) * 1024:(i % RING + 1) * 1024],
            in_dram[i * 128:(i + 1) * 128, :]),
            deps=dep, channel=cp + f"xi{i % RING}", tag=f"xin{i}")

    def produce_ln(i):
        if i >= NT:
            return
        p = i % 2
        xsl = b["xin"][:, (i % RING) * 1024:(i % RING + 1) * 1024]
        stt = b["stats"][:, p * 16:(p + 1) * 16]

        def bn_pair(v, xsl=xsl, stt=stt):
            v.bn_stats(stt[:, 0:6], xsl[:, 0:512])
            return v.bn_stats(stt[:, 6:12], xsl[:, 512:1024])
        pr = P.rec("v", bn_pair, deps=[xin_tok[i]], tag=f"bn{i}")
        vagg = P.rec("v", lambda v, stt=stt: v.bn_aggr(stt[:, 12:14], stt[:, 0:12]),
                     self_deps=[pr], tag=f"bnag{i}")
        P.rec("s", lambda s, stt=stt: s.activation(stt[:, 14:15], stt[:, 13:14], AF.Ln,
                                                   bias=epsv[:, :], scale=1.0),
              deps=[vagg] + (list(glob_tok) if i < 2 else []), tag=f"ln{i}")
        lntk = P.token("s")
        srstd = P.rec("s", lambda s, stt=stt: s.activation(
            stt[:, 14:15], stt[:, 14:15], AF.Exp, bias=0.0, scale=-0.5),
            self_deps=[lntk], tag=f"exp{i}")
        h1 = b["h1"][:, p * 1024:(p + 1) * 1024]
        h1_tok[i] = P.rec("v", lambda v, xsl=xsl, h1=h1, stt=stt: v.tensor_scalar(
            h1[:, :], xsl[:, :], stt[:, 12:13], stt[:, 14:15], ALU.subtract, ALU.mult),
            deps=[srstd, trh_tok[i - 2] if i >= 2 else None], tag=f"h1{i}")

    def produce(i):
        if i >= NT:
            return
        p = i % 2
        h1 = b["h1"][:, p * 1024:(p + 1) * 1024]
        deps = [h1_tok[i], trqk_tok[i - 1] if i >= 1 else None,
                ring_tok[i - 1] if i >= 1 else None,
                qkev_tok[i - 1] if i >= 1 else None]
        for k in range(8):
            P.rec("t", lambda t, k=k, h1=h1: t.transpose(
                pT[:, k * 128:(k + 1) * 128], h1[:, k * 128:(k + 1) * 128], ident[:, :]),
                deps=(deps if k == 0 else ()), inc=(1 if k == 7 else 0), tag=f"trh{i}")
        trh_tok[i] = P.token("t")
        h1T = b["h1T"][:, p * 1024:(p + 1) * 1024]
        h1T_tok[i] = P.rec("v", lambda v, h1T=h1T: v.tensor_copy(h1T[:, :], pT[:, 0:1024]),
                           deps=[trh_tok[i], qkv_pe_tok[i - 2] if i >= 2 else None],
                           tag=f"h1Tev{i}")
        qk = b["qkt"][:, p * 2048:(p + 1) * 2048]
        ev = {}
        for ph in range(3):
            for ncol in range(2):
                for k in range(8):
                    d = []
                    if ph == 0 and k == 0 and ncol == 0:
                        d = [h1T_tok[i], qkev_tok[i - 1] if i >= 1 else None] + wtok4 + [
                             ring_tok[i - 1] if i >= 1 else None]
                    if ph > 0 and k == 0:
                        d = [ev[(ph - 1, ncol)]]
                    c0 = ph * 1024 + ncol * 512
                    P.rec("t", lambda t, k=k, c0=c0, ncol=ncol, h1T=h1T: t.matmul(
                        pQ[:, ncol * 512:ncol * 512 + 512],
                        h1T[:, k * 128:(k + 1) * 128],
                        b["wqkv"][:, k * 3072 + c0:k * 3072 + c0 + 512],
                        start=(k == 0), stop=(k == 7)),
                        deps=d, inc=(1 if k == 7 else 0), tag=f"qkv{i}_{ph}{ncol}")
                mmtok = P.token("t")
                if ph < 2:
                    ev[(ph, ncol)] = P.rec("v", lambda v, qk=qk, ph=ph, ncol=ncol: v.tensor_copy(
                        qk[:, ph * 1024 + ncol * 512:ph * 1024 + ncol * 512 + 512],
                        pQ[:, ncol * 512:ncol * 512 + 512]),
                        deps=[mmtok, trqk_tok[i - 1] if (ph == 0 and ncol == 0 and i >= 1)
                              else None],
                        tag=f"qkev{i}_{ph}{ncol}")
                else:
                    vmm_tok[i] = mmtok
        qkv_pe_tok[i] = P.token("t")
        if dbg is not None and "qk" in dbg:
            P.rec("y", lambda y, i=i, qk=qk: y.dma_start(
                dbg["qk"][i * 128:(i + 1) * 128, :], qk[:, :]),
                deps=[qkev_tok[i]], channel=cp + f"dq{i % 2}", tag=f"dbgqk{i}")
        if dbg is not None and "h1" in dbg:
            h1d = b["h1"][:, (i % 2) * 1024:(i % 2 + 1) * 1024]
            P.rec("y", lambda y, i=i, h1d=h1d: y.dma_start(
                dbg["h1"][i * 128:(i + 1) * 128, :], h1d[:, :]),
                deps=[h1_tok[i]], channel=cp + f"dh{i % 2}", tag=f"dbgh1{i}")

    def produce_b(i):
        if i >= NT:
            return
        p = i % 2
        qk = b["qkt"][:, p * 2048:(p + 1) * 2048]
        slot = i % RING
        dstv = bass.AP(b["vr"], slot * 1040, [[RING * 1040, 128], [65, 16], [1, 64]])
        qkev_tok[i] = P.rec("v", lambda v, dstv=dstv: v.tensor_copy(dstv, pQ[:, :]),
                            deps=[vmm_tok[i], avl_dep(i)], tag=f"vev{i}")
        for k in range(16):
            P.rec("t", lambda t, k=k, qk=qk: t.transpose(
                pT[:, (k % 8) * 128:(k % 8) * 128 + 128],
                qk[:, k * 128:(k + 1) * 128], ident[:, :]),
                deps=([qkev_tok[i], h1T_tok[i], avl_dep(i)] if k == 0 else
                      ([] if k != 8 else [qT_ev_dep(i)])),
                inc=(1 if k in (7, 15) else 0), tag=f"trqk{i}")
            if k == 7:
                t7 = P.token("t")
                qT = b["qT"][:, (i % 4) * 1024:(i % 4 + 1) * 1024]
                qT_ev = P.rec("v", lambda v, qT=qT: v.tensor_copy(qT[:, :], pT[:, 0:1024]),
                              deps=[t7, attn_pe_tok[i - 4] if i >= 4 else None],
                              tag=f"qTev{i}")
                qT_ev_holder[0] = qT_ev
                qT_ev_tok[i] = qT_ev
        trqk_tok[i] = P.token("t")
        ring_tok[i] = P.rec("v", lambda v, slot=slot: v.tensor_copy(
            b["kT"][:, slot * 1024:(slot + 1) * 1024], pT[:, 0:1024]),
            deps=[trqk_tok[i], avl_dep(i)], tag=f"kTev{i}")

    qT_ev_holder = [None]
    qT_ev_tok = [None] * (NT + 8)

    def qT_ev_dep(i):
        return qT_ev_holder[0]

    def avl_dep(i):
        t = i - RING + 2
        return attn_pe_tok[t] if t >= 0 else init_tok

    def attn_qk(j):
        qT = b["qT"][:, (j % 4) * 1024:(j % 4 + 1) * 1024]
        kdep = [ring_tok[min(j + 2, NT - 1)], qT_ev_tok[min(j, NT - 1)]]
        for h in range(OH):
            hh = h % 2
            c = h // 2
            pS = pS2[h % 2]
            if h >= 2:
                war = exp_tok[j][h - 2]
            elif j >= 1:
                war = exp_tok[j - 1][14 + h]
            else:
                war = None
            for tau in range(5):
                tt = j - 2 + tau
                slot = (tt % RING) if tt >= 0 else ((tt + RING) % RING)
                d = []
                if tau == 0:
                    d = (list(kdep) if h == 0 else []) + [war]
                P.rec("t", lambda t, slot=slot, c=c, hh=hh, tau=tau, qT=qT, pS=pS: t.matmul(
                    pS[:, tau * 128:(tau + 1) * 128],
                    b["kT"][hh * 64:(hh + 1) * 64, slot * 1024 + c * 128:slot * 1024 + c * 128 + 128],
                    qT[hh * 64:(hh + 1) * 64, c * 128:(c + 1) * 128],
                    start=True, stop=True),
                    deps=d, inc=(1 if tau == 4 else 0), tag=f"qk{j}_{h}")
            qktok = P.token("t")
            exp_tok[j][h] = P.rec("s", lambda s, h=h, pS=pS: s.activation(
                b["probs"][:, h * 640:(h + 1) * 640], pS[:, :], AF.Exp, bias=0.0, scale=0.125),
                deps=[qktok, av_tok[j - 1] if (h <= 1 and j >= 1) else None], tag=f"pexp{j}_{h}")
        for r in range(4):
            c0 = r * 4 * 640
            apL = bass.AP(b["probs"], c0, [[16 * 640, 128], [640, 4], [1, 128]])
            apH = bass.AP(b["probs"], c0 + 512, [[16 * 640, 128], [640, 4], [1, 128]])
            P.rec("v", lambda v, apL=apL, r=r: v.tensor_tensor(
                apL, apL, b["trimL"][:, r * 512:(r + 1) * 512], ALU.mult),
                deps=[exp_tok[j][r * 4 + 3]], tag=f"zl{j}_{r}")
            ztok = P.rec("v", lambda v, apH=apH, r=r: v.tensor_tensor(
                apH, apH, b["trimH"][:, r * 512:(r + 1) * 512], ALU.mult),
                tag=f"zh{j}_{r}")
            if j <= 1:
                width = 256 if j == 0 else 128
                apE = bass.AP(b["probs"], c0, [[16 * 640, 128], [640, 4], [1, width]])
                ztok = P.rec("v", lambda v, apE=apE: v.memset(apE, 0.0),
                             self_deps=[ztok], tag=f"ze{j}_{r}")
            if j >= NT - 2:
                start = 512 if j == NT - 2 else 384
                apE = bass.AP(b["probs"], c0 + start,
                              [[16 * 640, 128], [640, 4], [1, 640 - start]])
                ztok = P.rec("v", lambda v, apE=apE: v.memset(apE, 0.0),
                             self_deps=[ztok], tag=f"ze{j}_{r}")
            zgrp_tok[j][r] = ztok

    def attn_av(j):
        oevg = [None] * 4
        for r in range(4):
            pw = pows[r % 2]
            for hq in range(4):
                h = r * 4 + hq
                for tau in range(5):
                    tt = j - 2 + tau
                    slot = (tt % RING) if tt >= 0 else ((tt + RING) % RING)
                    d = []
                    if hq == 0 and tau == 0:
                        if r < 2:
                            d = [zgrp_tok[j][r],
                                 aevh_tok[j - 1][r] if j >= 1 else None]
                        else:
                            d = [zgrp_tok[j][r], oevg[r - 2]]
                        if dbg is not None and "_prtok" in dbg and j == 10:
                            d = d + [dbg["_prtok"]]
                    P.rec("t", lambda t, slot=slot, h=h, hq=hq, tau=tau, pw=pw: t.matmul(
                        pw[0:65, hq * 128:(hq + 1) * 128],
                        b["vr"][:, slot * 1040 + h * 65:slot * 1040 + h * 65 + 65],
                        b["probs"][:, h * 640 + tau * 128:h * 640 + (tau + 1) * 128],
                        start=(tau == 0), stop=(tau == 4)),
                        deps=d, inc=(1 if (hq == 3 and tau == 4) else 0), tag=f"av{j}_{r}")
            avtok = P.token("t")
            if r == 3:
                av_tok[j] = avtok
            ap_even_src = bass.AP(pw, 0, [[512, 64], [256, 2], [1, 128]])
            ap_odd_src = bass.AP(pw, 128, [[512, 64], [256, 2], [1, 128]])
            P.rec("s", lambda s, r=r, ap=ap_even_src: s.copy(
                b["oTf"][0:64, r * 256:(r + 1) * 256], ap), deps=[avtok], tag=f"oev{j}_{r}")
            P.rec("s", lambda s, r=r, ap=ap_odd_src: s.copy(
                b["oTf"][64:128, r * 256:(r + 1) * 256], ap), tag=f"oev2{j}_{r}")
            oevg[r] = P.rec("s", lambda s, r=r, pw=pw: s.copy(
                b["sums"][0:1, r * 512:(r + 1) * 512], pw[64:65, 0:512]), tag=f"sev{j}_{r}")
            oev_tok[j] = oevg[r]
            oevv_tok[j] = oevg[r]
        if dbg is not None and "oT" in dbg:
            P.rec("y", lambda y, j=j: y.dma_start(
                dbg["oT"][j * 128:(j + 1) * 128, :], b["oTf"][:, :]),
                deps=[oev_tok[j]], channel=cp + f"do{j % 2}", tag=f"dbgoT{j}")
            P.rec("y", lambda y, j=j: y.dma_start(
                dbg["sums"][j:j + 1, :], b["sums"][:, :]),
                deps=[oev_tok[j]], channel=cp + f"ds{j % 2}", tag=f"dbgsm{j}")
        rlntk = P.rec("s", lambda s: s.activation(b["sums"][0:1, 2048:4096], b["sums"][0:1, 0:2048],
                                          AF.Ln, bias=0.0, scale=1.0),
              self_deps=[oev_tok[j]], tag=f"rln{j}")
        rex_tok[j] = P.rec("s", lambda s: s.activation(b["sums"][0:1, 2048:4096],
                                                       b["sums"][0:1, 2048:4096],
                                                       AF.Exp, bias=0.0, scale=-1.0),
                           self_deps=[rlntk], tag=f"rex{j}")

    def attn_fin(j):
        if j < 0:
            return
        p = j % 2
        ntoks = [None, None]
        for half in range(2):
            pw = pows[half]
            for rh in range(2):
                rhs = bass.AP(b["sums"], 2048 + half * 1024 + rh * 128,
                              [[4096, 1], [256, 4], [1, 128]])
                dd = ([rex_tok[j], oevv_tok[j]] +
                      (list(glob_tok) if j == 0 else [])) if (rh == 0) else ()
                P.rec("t", lambda t, rh=rh, rhs=rhs, pw=pw: t.matmul(
                    pw[rh * 64:(rh + 1) * 64, 0:512],
                    ones1[0:1, 0:64], rhs, start=True, stop=True),
                    deps=dd, inc=(1 if rh == 1 else 0), tag=f"sb{j}_{half}")
            btok = P.token("t")
            ntoks[half] = P.rec("v", lambda v, half=half, pw=pw: v.tensor_tensor(
                b["oTn"][:, half * 512:(half + 1) * 512],
                b["oTf"][:, half * 512:(half + 1) * 512], pw[:, :], ALU.mult),
                deps=[btok], tag=f"on{j}_{half}")
        norm_tok[j] = ntoks[1]
        aevh = [None, None]
        for half in range(2):
            pw = pows[half]
            for k in range(8):
                d = [ntoks[half]] if k == 0 else ()
                P.rec("t", lambda t, k=k, half=half, pw=pw: t.matmul(
                    pw[:, :], b["oTn"][:, k * 128:(k + 1) * 128],
                    b["wo"][:, k * 1024 + half * 512:k * 1024 + (half + 1) * 512],
                    start=(k == 0), stop=(k == 7)),
                    deps=d, inc=(1 if k == 7 else 0), tag=f"wo{j}_{half}")
            wotok = P.token("t")
            ab = b["abuf"][:, p * 1024:(p + 1) * 1024]
            aevh[half] = P.rec("v", lambda v, ab=ab, half=half, pw=pw: v.tensor_copy(
                ab[:, half * 512:(half + 1) * 512], pw[:, :]),
                deps=[wotok, res_tok[j - 2] if (half == 0 and j >= 2) else None],
                tag=f"aev{j}_{half}")
        aevh_tok[j] = aevh
        attn_pe_tok[j] = P.token("t")
        aevt = P.token("v")
        stt = b["stats"][:, 32 + p * 16:32 + (p + 1) * 16]
        ab = b["abuf"][:, p * 1024:(p + 1) * 1024]

        def fbn_pair(v, ab=ab, stt=stt):
            v.bn_stats(stt[:, 0:6], ab[:, 0:512])
            return v.bn_stats(stt[:, 6:12], ab[:, 512:1024])
        fpt = P.rec("v", fbn_pair, self_deps=[aevt], tag=f"fbn{j}")
        vt = P.rec("v", lambda v, stt=stt: v.bn_aggr(stt[:, 12:14], stt[:, 0:12]),
                   self_deps=[fpt], tag=f"fba{j}")
        P.rec("s", lambda s, stt=stt: s.activation(stt[:, 14:15], stt[:, 13:14], AF.Ln,
                                                   bias=epsv[:, :], scale=1.0),
              deps=[vt], tag=f"fln{j}")
        flntk = P.token("s")
        fs = P.rec("s", lambda s, stt=stt: s.activation(stt[:, 14:15], stt[:, 14:15],
                                                        AF.Exp, bias=0.0, scale=-0.5),
                   self_deps=[flntk], tag=f"fex{j}")
        ob = b["outb"][:, p * 1024:(p + 1) * 1024]
        xsl = b["xin"][:, (j % RING) * 1024:(j % RING + 1) * 1024]
        P.rec("v", lambda v, ab=ab, ob=ob, stt=stt: v.tensor_scalar(
            ob[:, :], ab[:, :], stt[:, 12:13], stt[:, 14:15], ALU.subtract, ALU.mult),
            deps=[fs, out_tok[j - 2] if j >= 2 else None], tag=f"fap{j}")
        res_tok[j] = P.rec("v", lambda v, ob=ob, xsl=xsl: v.tensor_tensor(
            ob[:, :], ob[:, :], xsl[:, :], ALU.add), tag=f"fx2{j}")
        out_tok[j] = P.rec("y", lambda y, j=j, ob=ob: y.dma_start(
            out_dram[j * 128:(j + 1) * 128, :], ob[:, :]),
            deps=[res_tok[j]], channel=cp + f"ot{j % 2}", tag=f"out{j}")

    for i in range(3):
        load_tile(i)
    produce_ln(0)
    produce(0)
    produce_b(0)
    produce_ln(1)
    produce(1)
    produce_b(1)
    produce_ln(2)
    produce(2)
    for j in range(NT):
        load_tile(j + 3)
        produce_b(j + 2)
        produce_ln(j + 3)
        attn_qk(j)
        produce(j + 3)
        attn_fin(j - 1)
        attn_av(j)
    attn_fin(NT - 1)
    return [out_tok[NT - 2], out_tok[NT - 1],
            P.token("t"), P.token("v"), P.token("s"), P.token("g")]


def attn_host_weights(W_qkv, W_o):
    p = np.arange(128)
    cL = np.arange(128)
    triL = (cL[None, :] <= p[:, None]).astype(np.float32)
    triH = (cL[None, :] >= p[:, None]).astype(np.float32)
    return {
        "wqkvT": np.ascontiguousarray(np.asarray(W_qkv, np.float32).T),
        "woT": np.ascontiguousarray(np.asarray(W_o, np.float32).T),
        "trimL": np.ascontiguousarray(np.tile(triL, (1, 16))),
        "trimH": np.ascontiguousarray(np.tile(triH, (1, 16))),
    }


IRING = 7


def alloc_iattn_bufs(nc, tagp):
    gs = []
    bufs = {}

    def sb(name, shape, dt):
        g = nc.sbuf_tensor(tagp + name, shape, dt)
        t = g.__enter__()
        gs.append(g)
        bufs[name] = t
        return t

    sb("wqkv", [128, 8 * 1536], BF16)
    sb("wo", [128, 4 * 1024], BF16)
    sb("qs", [128, 512], F32)
    sb("ks", [128, 512], F32)
    sb("xin", [128, IRING * 1024], F32)
    sb("x0", [128, 4 * 1024], F32)
    sb("stats", [128, 4 * 16], F32)
    sb("hT", [128, 2 * 1040], BF16)
    sb("rot", [128, 2 * 2048], F32)
    sb("qb", [128, 2 * 512], F32)
    sb("kb", [128, 2 * 512], F32)
    sb("qTb", [128, 2 * 512], BF16)
    sb("kTb", [128, 2 * 512], BF16)
    sb("x0b", [128, 2 * 1024], BF16)
    sb("qsc", [128, 2 * 512], F32)
    sb("qsc2", [128, 2 * 512], F32)
    sb("ssq", [128, 32], F32)
    sb("qT", [128, 4 * 512], BF16)
    sb("kT", [128, IRING * 512], BF16)
    sb("vr", [128, IRING * 520], BF16)
    sb("probs", [128, 8 * 640], BF16)
    sb("trimL", [128, 8 * 128], BF16)
    sb("trimH", [128, 8 * 128], BF16)
    sb("oTf", [128, 512], F32)
    sb("sums", [1, 2048], F32)
    sb("oTn", [128, 512], BF16)
    sb("outb", [128, 2 * 1024], F32)
    return gs, bufs


def alloc_iattn_psums(nc, tagp):
    gs = []
    ps = {}

    def palloc(name, shape, dt):
        g = nc.psum_tensor(tagp + name, shape, dt)
        t = g.__enter__()
        gs.append(g)
        ps[name] = t
        return t

    palloc("pS0", [128, 640], F32)
    palloc("pS1", [128, 640], F32)
    palloc("pQ", [128, 512], F32)
    palloc("pQ2", [128, 512], F32)
    palloc("pOW", [128, 512], F32)
    palloc("pOW2", [128, 512], F32)
    ps["pT"] = ps["pQ"].bitcast(BF16)
    return gs, ps


def record_attn_inner(P, nc, glob, ps, b, wdram, in_dram, out_dram, cp="i2",
                      glob_tok=(), weights_dep=None, with_ln=True, dbg=None, phase_dep=()):
    ident, epsv, ones1 = glob["ident"], glob["eps"], glob["ones1"]
    pS2 = [ps["pS0"], ps["pS1"]]
    pQ, pOW, pT = ps["pQ"], ps["pOW"], ps["pT"]
    pQ2 = ps["pQ2"]
    pOW2 = ps["pOW2"]
    ipows = [pOW, pOW2]

    for k in range(8):
        P.rec("g", lambda g, k=k: g.dma_start(
            b["wqkv"][:, k * 1536:(k + 1) * 1536], wdram["wqkv"][k * 128:(k + 1) * 128, :]),
            deps=(([weights_dep] if weights_dep else []) + list(phase_dep)) if k < 4 else (),
            channel=cp + f"w{k % 4}", tag="wqkvld")
    for k in range(4):
        P.rec("g", lambda g, k=k: g.dma_start(
            b["wo"][:, k * 1024:(k + 1) * 1024], wdram["wo"][k * 128:(k + 1) * 128, :]),
            channel=cp + f"w{k % 4}", tag="wold")
    for ni, name in enumerate(("qs", "ks", "trimL", "trimH")):
        P.rec("g", lambda g, name=name: g.dma_start(b[name][:, :], wdram[name][:, :]),
              channel=cp + f"w{ni % 4}", tag=name)
    vzt = P.rec("g", lambda g: g.memset(b["vr"][:, :], 0.0), tag="vz")
    P.rec("g", lambda g: g.memset(
        bass.AP(b["vr"], 64, [[IRING * 520, 128], [65, 8 * IRING], [1, 1]]), 1.0),
        self_deps=[vzt], tag="vones")
    P.rec("g", lambda g: g.memset(b["kT"][:, 3 * 512:7 * 512], 0.0), tag="kz")
    wtok4 = [P.token(cp + f"w{c}") for c in range(4)]
    wtok = wtok4[0]
    init_tok = P.token("g")

    xin_tok = [None] * (NT + 8)
    rot_tok = [None] * (NT + 8)
    vev_tok = [None] * NT
    aevh_tok = [[None, None] for _ in range(NT)]
    cast_tok_arr = [None] * NT
    dqt_arr = [None] * NT
    x0_tok = [None] * NT
    trx_tok = [None] * NT
    hT_tok = [None] * NT
    qkv_pe_tok = [None] * NT
    qkproc_tok = [None] * NT
    trqk_tok = [None] * NT
    qT_ev_tok = [None] * NT
    ring_tok = [None] * NT
    exp_tok = [[None] * IH for _ in range(NT)]
    zero_tok = [None] * NT
    av_tok = [None] * NT
    oev_tok = [None] * NT
    rex_tok = [None] * NT
    norm_tok = [None] * NT
    attn_pe_tok = [None] * NT
    res_tok = [None] * NT
    out_tok = [None] * NT

    def load_tile(i):
        if i >= NT:
            return
        dep = [res_tok[i - IRING]] if i >= IRING else list(phase_dep)
        xin_tok[i] = P.rec("y", lambda y, i=i: y.dma_start(
            b["xin"][:, (i % IRING) * 1024:(i % IRING + 1) * 1024],
            in_dram[i * 128:(i + 1) * 128, :]),
            deps=dep, channel=cp + f"xi{i % IRING}", tag=f"xin{i}")


    def produce_ln(i):
        if i >= NT:
            return
        p = i % 2
        dep2 = [qkproc_tok[i - 2]] if i >= 2 else []
        rot_tok[i] = P.rec("y", lambda y, i=i: y.dma_start(
            b["rot"][:, (i % 2) * 2048:(i % 2 + 1) * 2048],
            wdram["rot"][i * 128:(i + 1) * 128, :]),
            deps=dep2, channel=cp + f"ro{i % 2}", tag=f"rot{i}")
        xsl = b["xin"][:, (i % IRING) * 1024:(i % IRING + 1) * 1024]
        x0 = b["x0"][:, (i % 4) * 1024:(i % 4 + 1) * 1024]
        x0war = res_tok[i - 4] if i >= 4 else None
        if with_ln:
            stt = b["stats"][:, p * 16:(p + 1) * 16]

            def bn_pair(v, xsl=xsl, stt=stt):
                v.bn_stats(stt[:, 0:6], xsl[:, 0:512])
                return v.bn_stats(stt[:, 6:12], xsl[:, 512:1024])
            pr = P.rec("v", bn_pair, deps=[xin_tok[i]], tag=f"bn{i}")
            vagg = P.rec("v", lambda v, stt=stt: v.bn_aggr(stt[:, 12:14], stt[:, 0:12]),
                         self_deps=[pr], tag=f"bnag{i}")
            P.rec("s", lambda s, stt=stt: s.activation(stt[:, 14:15], stt[:, 13:14], AF.Ln,
                                                       bias=epsv[:, :], scale=1.0),
                  deps=[vagg] + (list(glob_tok) if i < 2 else []), tag=f"ln{i}")
            lntk = P.token("s")
            srstd = P.rec("s", lambda s, stt=stt: s.activation(
                stt[:, 14:15], stt[:, 14:15], AF.Exp, bias=0.0, scale=-0.5),
                self_deps=[lntk], tag=f"exp{i}")
            x0_tok[i] = P.rec("v", lambda v, xsl=xsl, x0=x0, stt=stt: v.tensor_scalar(
                x0[:, :], xsl[:, :], stt[:, 12:13], stt[:, 14:15], ALU.subtract, ALU.mult),
                deps=[srstd, x0war], tag=f"x0{i}")
        else:
            x0_tok[i] = P.rec("v", lambda v, xsl=xsl, x0=x0: v.tensor_copy(x0[:, :], xsl[:, :]),
                              deps=[xin_tok[i], x0war] + (list(glob_tok) if i < 2 else []),
                              tag=f"x0{i}")
        xb = b["qsc"][:, p * 512:(p + 1) * 512]
        x0b = b["x0b"][:, p * 1024:(p + 1) * 1024]
        cast_tok_arr[i] = P.rec("v", lambda v, x0=x0, x0b=x0b: v.tensor_copy(
            x0b[:, :], x0[:, :]),
            deps=[trx_tok[i - 2] if i >= 2 else None] +
                 (list(glob_tok) if i < 2 else []), tag=f"x0b{i}")

    def produce(i):
        if i >= NT:
            return
        p = i % 2
        x0 = b["x0"][:, (i % 4) * 1024:(i % 4 + 1) * 1024]
        x0b = b["x0b"][:, p * 1024:(p + 1) * 1024]
        cast_tok = cast_tok_arr[i]
        rsl = b["rot"][:, (i % 2) * 2048:(i % 2 + 1) * 2048]
        dqt = dqt_arr[i]
        hT = b["hT"][:, p * 1040:(p + 1) * 1040]
        deps = [cast_tok, hT_tok[i - 2] if i >= 2 else None,
                qkv_pe_tok[i - 1] if i >= 1 else None,
                ring_tok[i - 1] if i >= 1 else None,
                vev_tok[i - 1] if i >= 1 else None]
        for k in range(8):
            P.rec("t", lambda t, k=k, x0b=x0b: t.transpose(
                pT[:, k * 128:(k + 1) * 128], x0b[:, k * 128:(k + 1) * 128], ident[:, :]),
                deps=(deps if k == 0 else ()), inc=(1 if k == 7 else 0), tag=f"trx{i}")
        trx_tok[i] = P.token("t")
        dstap = bass.AP(b["hT"], p * 1040 + 1, [[2 * 1040, 128], [130, 8], [1, 128]])
        ev = P.rec("v", lambda v, dstap=dstap: v.tensor_copy(dstap, pT[:, :]),
                   deps=[trx_tok[i], qkv_pe_tok[i - 2] if i >= 2 else None], tag=f"hTev{i}")
        if i == 0:
            hT_tok[i] = P.rec("v", lambda v: v.memset(
                bass.AP(b["hT"], 0, [[2 * 1040, 128], [130, 8], [1, 1]]), 0.0),
                self_deps=[ev], tag=f"hTz{i}")
        else:
            po = (i - 1) % 2
            src_ap = bass.AP(b["hT"], po * 1040 + 128, [[2 * 1040, 128], [130, 8], [1, 1]])
            dst_ap = bass.AP(b["hT"], p * 1040 + 0, [[2 * 1040, 128], [130, 8], [1, 1]])
            hT_tok[i] = P.rec("v", lambda v, src_ap=src_ap, dst_ap=dst_ap: v.tensor_copy(
                dst_ap, src_ap), self_deps=[ev], tag=f"hTpc{i}")
        evq = None
        for ph in range(3):
            pq = pQ2 if ph == 1 else pQ
            for k in range(8):
                ofs = 1 if k < 4 else 0
                d = []
                if ph == 0 and k == 0:
                    d = [hT_tok[i]] + wtok4 + [
                         (qkproc_tok[i - 1] if i >= 1 else None)]
                if ph == 2 and k == 0:
                    d = [ev_q0]
                P.rec("t", lambda t, k=k, ph=ph, ofs=ofs, p=p, pq=pq: t.matmul(
                    pq[:, :],
                    bass.AP(b["hT"], p * 1040 + k * 130 + ofs, [[2 * 1040, 128], [1, 128]]),
                    b["wqkv"][:, k * 1536 + ph * 512:k * 1536 + ph * 512 + 512],
                    start=(k == 0), stop=(k == 7)),
                    deps=d, inc=(1 if k == 7 else 0), tag=f"qkv{i}_{ph}")
            mmtok = P.token("t")
            if ph == 0:
                evq = P.rec("v", lambda v, p=p: v.tensor_copy(
                    b["qb"][:, p * 512:(p + 1) * 512], pQ[:, :]),
                    deps=[mmtok, qkproc_tok[i - 2] if i >= 2 else None], tag=f"qev{i}")
                ev_q0 = evq
            elif ph == 1:
                evq = P.rec("v", lambda v, p=p: v.tensor_copy(
                    b["kb"][:, p * 512:(p + 1) * 512], pQ2[:, :]),
                    deps=[mmtok], tag=f"kev{i}")
            else:
                slot = i % IRING
                dstv = bass.AP(b["vr"], slot * 520, [[IRING * 520, 128], [65, 8], [1, 64]])
                vev_tok[i] = P.rec("v", lambda v, dstv=dstv: v.tensor_copy(dstv, pQ[:, :]),
                                   deps=[mmtok, avl_dep(i)], tag=f"vev{i}")
        qkv_pe_tok[i] = P.token("t")
        if dbg is not None and "x0" in dbg:
            P.rec("y", lambda y, i=i, x0=x0: y.dma_start(
                dbg["x0"][i * 128:(i + 1) * 128, :], x0[:, :]),
                deps=[x0_tok[i]], channel=cp + f"d0{i % 2}", tag=f"dbgx0{i}")
        rsl = b["rot"][:, (i % 2) * 2048:(i % 2 + 1) * 2048]
        if dbg is not None and "rot" in dbg:
            P.rec("y", lambda y, i=i, rsl=rsl: y.dma_start(
                dbg["rot"][i * 128:(i + 1) * 128, :], rsl),
                deps=[rot_tok[i]], channel=cp + f"d6{i % 2}", tag=f"dbgrot{i}")
        dqt = None
        if dbg is not None and "qraw" in dbg:
            dqt = P.rec("y", lambda y, i=i, p=p: y.dma_start(
                dbg["qraw"][i * 128:(i + 1) * 128, :], b["qb"][:, p * 512:(p + 1) * 512]),
                deps=[ev_q0], channel=cp + f"d4{i % 2}", tag=f"dbgqr{i}")
        bq = b["qb"][:, p * 512:(p + 1) * 512]
        bk = b["kb"][:, p * 512:(p + 1) * 512]
        sqq = b["qsc"][:, p * 512:(p + 1) * 512]
        sqk = b["qsc2"][:, p * 512:(p + 1) * 512]
        baseq = p * 16
        basek = p * 16 + 8
        sqt = P.rec("v", lambda v: v.tensor_tensor(sqq[:, :], bq[:, :], bq[:, :], ALU.mult),
                    deps=[rot_tok[i]], tag=f"sqq{i}")
        for h in range(8):
            P.rec("v", lambda v, h=h: v.tensor_reduce(
                b["ssq"][:, baseq + h:baseq + h + 1], sqq[:, h * 64:(h + 1) * 64],
                AX.X, ALU.add), self_deps=([sqt] if h == 0 else ()), tag=f"ssq{i}_{h}")
        sqtk = P.rec("v", lambda v: v.tensor_tensor(sqk[:, :], bk[:, :], bk[:, :], ALU.mult),
                     tag=f"sqk{i}")
        for h in range(8):
            P.rec("v", lambda v, h=h: v.tensor_reduce(
                b["ssq"][:, basek + h:basek + h + 1], sqk[:, h * 64:(h + 1) * 64],
                AX.X, ALU.add), self_deps=([sqtk] if h == 0 else ()), tag=f"ssk{i}_{h}")
        rdt = P.token("v")
        P.rec("v", lambda v: v.tensor_scalar(
            b["ssq"][:, p * 16:p * 16 + 16], b["ssq"][:, p * 16:p * 16 + 16], 1e-24, None,
            ALU.max), self_deps=[rdt], tag=f"cl{i}")
        vtk = P.token("v")
        P.rec("s", lambda s: s.activation(
            b["ssq"][:, p * 16:p * 16 + 16], b["ssq"][:, p * 16:p * 16 + 16], AF.Ln,
            bias=0.0, scale=1.0), deps=[vtk], tag=f"l2l{i}")
        stk = P.token("s")
        rtk = P.rec("s", lambda s: s.activation(
            b["ssq"][:, p * 16:p * 16 + 16], b["ssq"][:, p * 16:p * 16 + 16], AF.Exp,
            bias=0.0, scale=-0.5), self_deps=[stk], tag=f"l2e{i}")
        if dbg is not None and "ssq" in dbg:
            P.rec("y", lambda y, i=i: y.dma_start(
                dbg["ssq"][i * 128:(i + 1) * 128, :], b["ssq"][:, baseq:baseq + 8]),
                deps=[rtk], channel=cp + f"d5{i % 2}", tag=f"dbgss{i}")
        proc_toks = []
        for (which, bb, base, scv, c0, s0) in (("q", bq, baseq, "qs", 0, 512),
                                               ("k", bk, basek, "ks", 1024, 1536)):
            gate = [rtk] + ([dqt] if (dqt is not None and which == "q") else [])
            for h in range(8):
                P.rec("v", lambda v, bb=bb, h=h, base=base: v.tensor_scalar(
                    bb[:, h * 64:(h + 1) * 64], bb[:, h * 64:(h + 1) * 64],
                    b["ssq"][:, base + h:base + h + 1], None, ALU.mult),
                    deps=(gate if (which == "q" and h == 0) else ()), tag=f"l2a{which}{i}_{h}")
            P.rec("v", lambda v, bb=bb, scv=scv: v.tensor_tensor(
                bb[:, :], bb[:, :], b[scv][:, :], ALU.mult), tag=f"qs{which}{i}")
            if dbg is not None and "qu" in dbg and which == "q":
                ut = P.token("v")
                P.rec("y", lambda y, i=i, bb=bb: y.dma_start(
                    dbg["qu"][i * 128:(i + 1) * 128, :], bb[:, :]),
                    deps=[ut], channel=cp + f"d3{i % 2}", tag=f"dbgu{i}")
            cosr = rsl[:, c0:c0 + 512]
            sinr = rsl[:, s0:s0 + 512]
            P.rec("v", lambda v, bb=bb, cosr=cosr: v.tensor_tensor(
                sqq[:, :], bb[:, :], cosr, ALU.mult), tag=f"rc{which}{i}")
            swap = bass.AP(b["qb" if which == "q" else "kb"], p * 512 + 32,
                           [[1024, 128], [64, 8], [-32, 2], [1, 32]])
            rst = P.rec("v", lambda v, swap=swap, sinr=sinr: v.tensor_tensor(
                sqk[:, :], swap, sinr, ALU.mult), tag=f"rs{which}{i}")
            obf = b["qTb" if which == "q" else "kTb"][:, p * 512:(p + 1) * 512]
            tk = P.rec("v", lambda v, obf=obf: v.tensor_tensor(
                obf[:, :], sqq[:, :], sqk[:, :], ALU.add),
                deps=[trqk_tok[i - 2] if i >= 2 else None],
                self_deps=[rst], tag=f"ra{which}{i}")
            proc_toks.append(tk)
        qkproc_tok[i] = proc_toks[-1]
        if dbg is not None and "qk" in dbg:
            P.rec("y", lambda y, i=i, p=p: y.dma_start(
                dbg["qk"][i * 128:(i + 1) * 128, 0:512], b["qTb"][:, p * 512:(p + 1) * 512]),
                deps=[qkproc_tok[i]], channel=cp + f"d1{i % 2}", tag=f"dbgq{i}")
            P.rec("y", lambda y, i=i, p=p: y.dma_start(
                dbg["qk"][i * 128:(i + 1) * 128, 512:1024], b["kTb"][:, p * 512:(p + 1) * 512]),
                deps=[qkproc_tok[i]], channel=cp + f"d2{i % 2}", tag=f"dbgk{i}")

    def produce_b(i):
        if i >= NT:
            return
        p = i % 2
        slot = i % IRING
        for which in ("q", "k"):
            bfb = b["qTb" if which == "q" else "kTb"][:, p * 512:(p + 1) * 512]
            deps0 = [qkproc_tok[i], ring_tok[i - 1] if i >= 1 else None,
                     (qT_ev_tok[i - 1] if i >= 1 else None)]
            for k in range(4):
                kk = k if which == "q" else 4 + k
                P.rec("t", lambda t, k=k, kk=kk, bfb=bfb: t.transpose(
                    pT[:, kk * 128:(kk + 1) * 128], bfb[:, k * 128:(k + 1) * 128],
                    ident[:, :]),
                    deps=(deps0 if (which == "q" and k == 0) else ()),
                    inc=(1 if k == 3 else 0), tag=f"tr{which}{i}")
        trqk_tok[i] = P.token("t")
        qT_ev_tok[i] = P.rec("v", lambda v, i=i: v.tensor_copy(
            b["qT"][:, (i % 4) * 512:(i % 4 + 1) * 512], pT[:, 0:512]),
            deps=[trqk_tok[i], attn_pe_tok[i - 3] if i >= 3 else None], tag=f"qTev{i}")
        ring_tok[i] = P.rec("v", lambda v, slot=slot: v.tensor_copy(
            b["kT"][:, slot * 512:(slot + 1) * 512], pT[:, 512:1024]),
            deps=[avl_dep(i)], self_deps=[qT_ev_tok[i]], tag=f"kTev{i}")

    def avl_dep(i):
        t = i - IRING + 4
        return attn_pe_tok[t] if t >= 0 else init_tok

    def slot_of(tt):
        return (tt % IRING) if tt >= 0 else ((tt + IRING) % IRING)

    def attn_qk(j):
        kdep = [ring_tok[j]]
        for h in range(IH):
            hh = h % 2
            c = h // 2
            pS = pS2[h % 2]
            if h >= 2:
                war = exp_tok[j][h - 2]
            elif j >= 1:
                war = exp_tok[j - 1][IH - 2 + h]
            else:
                war = None
            for tau in range(5):
                tt = j - 4 + tau
                slot = slot_of(tt)
                d = ()
                if tau == 0:
                    d = (list(kdep) if h == 0 else []) + [war, qT_ev_tok[j]]
                P.rec("t", lambda t, slot=slot, c=c, hh=hh, tau=tau, pS=pS, j=j: t.matmul(
                    pS[:, tau * 128:(tau + 1) * 128],
                    b["kT"][hh * 64:(hh + 1) * 64,
                            slot * 512 + c * 128:slot * 512 + c * 128 + 128],
                    b["qT"][hh * 64:(hh + 1) * 64,
                            (j % 4) * 512 + c * 128:(j % 4) * 512 + c * 128 + 128],
                    start=True, stop=True),
                    deps=d, inc=(1 if tau == 4 else 0), tag=f"st{j}_{h}")
            sttok = P.token("t")
            exp_tok[j][h] = P.rec("s", lambda s, h=h, pS=pS: s.activation(
                b["probs"][:, h * 640:(h + 1) * 640], pS[:, :], AF.Exp, bias=0.0, scale=1.0),
                deps=[sttok, av_tok[j - 1] if (h <= 1 and j >= 1) else None],
                tag=f"pexp{j}_{h}")
        zdeps = [exp_tok[j][IH - 1]]
        apL = bass.AP(b["probs"], 0, [[8 * 640, 128], [640, 8], [1, 128]])
        apH = bass.AP(b["probs"], 512, [[8 * 640, 128], [640, 8], [1, 128]])
        P.rec("v", lambda v, apL=apL: v.tensor_tensor(apL, apL, b["trimL"][:, :], ALU.mult),
              deps=zdeps, tag=f"zl{j}")
        ztok = P.rec("v", lambda v, apH=apH: v.tensor_tensor(apH, apH, b["trimH"][:, :], ALU.mult),
                     tag=f"zh{j}")
        if j <= 3:
            width = 512 - j * 128
            apE = bass.AP(b["probs"], 0, [[8 * 640, 128], [640, 8], [1, width]])
            ztok = P.rec("v", lambda v, apE=apE: v.memset(apE, 0.0),
                         self_deps=[ztok], tag=f"ze{j}")
        zero_tok[j] = ztok

    def attn_av(j):
        for r in range(2):
            pw = ipows[r]
            for hq in range(4):
                h = r * 4 + hq
                for tau in range(5):
                    tt = j - 4 + tau
                    slot = slot_of(tt)
                    d = []
                    if hq == 0 and tau == 0:
                        d = [zero_tok[j],
                             aevh_tok[j - 1][1 - r] if j >= 1 else None]
                        if dbg is not None and "_prtok" in dbg and j == 10:
                            d = d + [dbg["_prtok"]]
                    P.rec("t", lambda t, slot=slot, h=h, hq=hq, tau=tau, pw=pw: t.matmul(
                        pw[0:65, hq * 128:(hq + 1) * 128],
                        b["vr"][:, slot * 520 + h * 65:slot * 520 + h * 65 + 65],
                        b["probs"][:, h * 640 + tau * 128:h * 640 + (tau + 1) * 128],
                        start=(tau == 0), stop=(tau == 4)),
                        deps=d, inc=(1 if (hq == 3 and tau == 4) else 0), tag=f"av{j}_{r}")
            avtok = P.token("t")
            if r == 1:
                av_tok[j] = avtok
            ap_even_src = bass.AP(pw, 0, [[512, 64], [256, 2], [1, 128]])
            ap_odd_src = bass.AP(pw, 128, [[512, 64], [256, 2], [1, 128]])
            P.rec("s", lambda s, r=r, ap=ap_even_src: s.copy(
                b["oTf"][0:64, r * 256:(r + 1) * 256], ap), deps=[avtok], tag=f"oev{j}_{r}")
            P.rec("s", lambda s, r=r, ap=ap_odd_src: s.copy(
                b["oTf"][64:128, r * 256:(r + 1) * 256], ap), tag=f"oev2{j}_{r}")
            oev_tok[j] = P.rec("s", lambda s, r=r, pw=pw: s.copy(
                b["sums"][0:1, r * 512:(r + 1) * 512], pw[64:65, 0:512]), tag=f"sev{j}_{r}")
        if dbg is not None and "oT" in dbg:
            P.rec("y", lambda y, j=j: y.dma_start(
                dbg["oT"][j * 128:(j + 1) * 128, :], b["oTf"][:, :]),
                deps=[oev_tok[j]], channel=cp + f"d7{j % 2}", tag=f"dbgoT{j}")
            P.rec("y", lambda y, j=j: y.dma_start(
                dbg["sm"][j:j + 1, :], b["sums"][0:1, 0:1024]),
                deps=[oev_tok[j]], channel=cp + f"d8{j % 2}", tag=f"dbgsm{j}")
        rlntk = P.rec("s", lambda s: s.activation(b["sums"][0:1, 1024:2048],
                                                  b["sums"][0:1, 0:1024],
                                                  AF.Ln, bias=0.0, scale=1.0),
                      self_deps=[oev_tok[j]], tag=f"rln{j}")
        rex_tok[j] = P.rec("s", lambda s: s.activation(b["sums"][0:1, 1024:2048],
                                                       b["sums"][0:1, 1024:2048],
                                                       AF.Exp, bias=0.0, scale=-1.0),
                           self_deps=[rlntk], tag=f"rex{j}")

    def attn_fin(j):
        if j < 0:
            return
        p = j % 2
        d = [rex_tok[j]]
        if j >= 1:
            d.append(res_tok[j - 1])
        for rh in range(2):
            rhs = bass.AP(b["sums"], 1024 + rh * 128, [[2048, 1], [256, 4], [1, 128]])
            dd = (d + (list(glob_tok) if j == 0 else [])) if rh == 0 else ()
            P.rec("t", lambda t, rh=rh, rhs=rhs: t.matmul(
                pOW[rh * 64:(rh + 1) * 64, 0:512],
                glob["ones1f"][0:1, 0:64], rhs, start=True, stop=True),
                deps=dd, inc=(1 if rh == 1 else 0), tag=f"sbc{j}")
        btok = P.token("t")
        norm_tok[j] = P.rec("v", lambda v: v.tensor_tensor(
            b["oTn"][:, :], b["oTf"][:, :], pOW[:, :], ALU.mult),
            deps=[btok], tag=f"on{j}")
        ob = b["outb"][:, p * 1024:(p + 1) * 1024]
        x0 = b["x0"][:, (j % 4) * 1024:(j % 4 + 1) * 1024]
        aevh = [None, None]
        for half in range(2):
            pw = ipows[1 - half]
            for k in range(4):
                d = [norm_tok[j]] if (half == 0 and k == 0) else ()
                P.rec("t", lambda t, k=k, half=half, pw=pw: t.matmul(
                    pw[:, :], b["oTn"][:, k * 128:(k + 1) * 128],
                    b["wo"][:, k * 1024 + half * 512:k * 1024 + (half + 1) * 512],
                    start=(k == 0), stop=(k == 3)),
                    deps=d, inc=(1 if k == 3 else 0), tag=f"wo{j}_{half}")
            wotok = P.token("t")
            aevh[half] = P.rec("v", lambda v, ob=ob, x0=x0, half=half, pw=pw: v.tensor_tensor(
                ob[:, half * 512:(half + 1) * 512], pw[:, :],
                x0[:, half * 512:(half + 1) * 512], ALU.add),
                deps=[wotok, out_tok[j - 2] if (half == 0 and j >= 2) else None,
                      x0_tok[j]],
                tag=f"aev{j}_{half}")
        aevh_tok[j] = aevh
        attn_pe_tok[j] = P.token("t")
        res_tok[j] = aevh[1]
        out_tok[j] = P.rec("y", lambda y, j=j, ob=ob: y.dma_start(
            out_dram[j * 128:(j + 1) * 128, :], ob[:, :]),
            deps=[res_tok[j]], channel=cp + f"ot{j % 2}", tag=f"out{j}")

    for i in range(4):
        load_tile(i)
    produce_ln(0)
    produce(0)
    produce_b(0)
    produce_ln(1)
    produce(1)
    for j in range(NT):
        load_tile(j + 4)
        attn_qk(j)
        attn_fin(j - 1)
        attn_av(j)
        produce_b(j + 1)
        produce_ln(j + 2)
        produce(j + 2)
    attn_fin(NT - 1)
    return [out_tok[NT - 2], out_tok[NT - 1],
            P.token("t"), P.token("v"), P.token("s"), P.token("g")]


def iattn_host_weights(wqkv, q_scale, k_scale, wo, layer):
    l = layer
    p = np.arange(128)
    cL = np.arange(128)
    triL = (cL[None, :] <= p[:, None]).astype(np.float32)
    triH = (cL[None, :] >= p[:, None]).astype(np.float32)
    half = np.arange(0, 64, 2, dtype=np.float32)
    inv_freq = (1.0 / (10000.0 ** (half / 64.0))).astype(np.float32)
    t = np.arange(4096, dtype=np.float32)
    freqs = t[:, None] * inv_freq[None, :]
    freqs = np.concatenate([freqs, freqs], 1)
    base = ((half + 0.4 * 64.0) / (1.4 * 64.0)).astype(np.float32)
    power = (t - 2048.0) / 256.0
    scale = base[None, :] ** power[:, None]
    scale = np.concatenate([scale, scale], 1).astype(np.float32)
    cos = np.cos(freqs).astype(np.float32)
    sin = np.sin(freqs).astype(np.float32)
    sgn = np.concatenate([-np.ones(32, np.float32), np.ones(32, np.float32)])
    cosQ8 = 8.0 * cos * scale
    sinQ8s = 8.0 * sin * scale * sgn
    cosK = cos / scale
    sinKs = sin / scale * sgn
    rep = lambda a: np.tile(a, (1, 8))
    rot = np.concatenate([rep(cosQ8), rep(sinQ8s), rep(cosK), rep(sinKs)], axis=1)
    return {
        "wqkv": np.ascontiguousarray(np.asarray(wqkv[l], np.float32)),
        "wo": np.ascontiguousarray(np.asarray(wo[l], np.float32)),
        "qs": np.ascontiguousarray(np.broadcast_to(
            np.tile(np.asarray(q_scale[l], np.float32), 8)[None, :], (128, 512))),
        "ks": np.ascontiguousarray(np.broadcast_to(
            np.tile(np.asarray(k_scale[l], np.float32), 8)[None, :], (128, 512))),
        "rot": np.ascontiguousarray(rot.astype(np.float32)),
        "trimL": np.ascontiguousarray(np.tile(triL, (1, 8))),
        "trimH": np.ascontiguousarray(np.tile(triH, (1, 8))),
    }


# version 20
# speedup vs baseline: 1.2733x; 1.0157x over previous
import numpy as np
import ml_dtypes
import concourse.bass as bass
import concourse.mybir as mybir
from concourse.bass_utils import run_bass_kernel_spmd

F32 = mybir.dt.float32
BF16 = mybir.dt.bfloat16
AF = mybir.ActivationFunctionType
ALU = mybir.AluOpType
AX = mybir.AxisListType
bf16 = ml_dtypes.bfloat16

D = 1024
T = 4096
NT = T // 128
OH = 16
IH = 8
DH = 64
HID = 2730
NMC = 22
EPS = 1e-5

ENGS = ("t", "s", "v", "g", "y")


class Prog:

    def __init__(self, nc):
        self.nc = nc
        self.ops = []
        self.counts = {}
        for e in ENGS:
            self.counts[e] = 0

    def rec(self, eng, fn, deps=(), channel=None, inc=None, tag="", self_deps=()):
        sem_name = channel if channel is not None else eng
        if inc is None:
            inc = 16 if channel is not None else 1
        if sem_name not in self.counts:
            self.counts[sem_name] = 0
        deps = tuple(d for d in deps if d is not None)
        self_deps = tuple(d for d in self_deps if d is not None)
        self.ops.append((eng, fn, sem_name, inc, deps, self_deps, tag))
        if inc:
            self.counts[sem_name] += inc
            return (sem_name, self.counts[sem_name])
        return None

    def token(self, sem_name):
        c = self.counts.get(sem_name, 0)
        return (sem_name, c) if c else None

    def emit(self, block, final_waiter="y", serialize=False):
        finals = dict(self.counts)
        sems = {}
        guards = []
        for name in self.counts:
            g = self.nc.semaphore("sem_" + name)
            sems[name] = g.__enter__()
            guards.append(g)

        if serialize:
            run_counts = {n: 0 for n in self.counts}
            ser_deps = []
            for (k, fn, sem_name, inc, deps, self_deps, tag) in self.ops:
                ser_deps.append([(n, v) for n, v in run_counts.items() if v])
                if inc:
                    run_counts[sem_name] += inc
        else:
            ser_deps = [list(deps) for (_, _, _, _, deps, _, _) in self.ops]

        def emitter(key):
            def run(eng):
                seen = {}
                for (k, fn, sem_name, inc, deps, self_deps, tag), sdeps in zip(self.ops, ser_deps):
                    if k != key:
                        continue
                    nwait = 0
                    for (dn, dv) in sdeps:
                        if dn == key:
                            continue
                        if seen.get(dn, 0) < dv:
                            if nwait:
                                eng.nop()
                            eng.wait_ge(sems[dn], dv)
                            seen[dn] = dv
                            nwait += 1
                    for (dn, dv) in self_deps:
                        if nwait:
                            eng.nop()
                        eng.wait_ge(sems[dn], dv)
                        nwait += 1
                    instr = fn(eng)
                    if inc:
                        instr.then_inc(sems[sem_name], inc)
                if key == final_waiter:
                    for n2, v2 in finals.items():
                        if v2:
                            eng.wait_ge(sems[n2], v2)
            return run

        block.tensor(emitter("t"))
        block.scalar(emitter("s"))
        block.vector(emitter("v"))
        block.gpsimd(emitter("g"))
        block.sync(emitter("y"))
        return guards



ST = 256
NSUB = ST // 128
NST = T // ST


def alloc_ffn_bufs(nc, tagp, with_final):
    gs = []
    bufs = {}

    def sb(name, shape, dt):
        g = nc.sbuf_tensor(tagp + name, shape, dt)
        t = g.__enter__()
        gs.append(g)
        bufs[name] = t
        return t

    sb("w1", [128, 8 * 5460], BF16)
    sb("w2", [128, NMC * 1024], BF16)
    sb("b1a", [128, NMC], F32)
    sb("b1g", [128, NMC], F32)
    sb("b2row", [1, 1024], BF16)
    sb("grep", [128, 1024], F32)
    sb("brep", [128, 1024], F32)
    sb("xin", [128, 5 * 1024], F32)
    sb("hsh", [128, 2 * 512], F32)
    sb("stats", [128, 4 * 16], F32)
    sb("hln", [128, 2 * 1024], BF16)
    sb("hlnT", [128, 2 * (8 * ST)], BF16)
    sb("tmp", [128, 2 * 1024], F32)
    sb("gel", [128, 2 * ST], BF16)
    sb("prod", [128, 2 * ST], BF16)
    sb("outb", [128, 2 * 1024], F32)
    if with_final:
        sb("x2ring", [128, 3 * 1024], F32)
    return gs, bufs


def alloc_ffn_psums(nc, tagp):
    gs = []
    ps = {}
    pT = []
    for i in range(2):
        g = nc.psum_tensor(tagp + f"pT{i}", [128, 1024], BF16)
        pT.append(g.__enter__()); gs.append(g)
    pag = []
    for i in range(2):
        g = nc.psum_tensor(tagp + f"pag{i}", [128, 512], F32)
        pag.append(g.__enter__()); gs.append(g)
    pout = []
    for i in range(4):
        g = nc.psum_tensor(tagp + f"pout{i}", [128, 512], F32)
        pout.append(g.__enter__()); gs.append(g)
    ps["pT"], ps["pag"], ps["pout"] = pT, pag, pout
    return gs, ps


def record_ffn_phase(P, nc, glob, psums, bufs, wdram, in_dram, out_dram,
                     final=None, weights_dep=None, cp="f0", glob_tok=(), phase_dep=()):
    b = bufs
    pT, pag, pout = psums["pT"], psums["pag"], psums["pout"]
    ident, epsv, ones1 = glob["ident"], glob["eps"], glob["ones1"]

    for k in range(8):
        P.rec("g", lambda g, k=k: g.dma_start(
            b["w1"][:, k * 5460:(k + 1) * 5460], wdram["w1"][k * 128:(k + 1) * 128, :]),
            deps=(([weights_dep] if weights_dep else []) + list(phase_dep)) if k < 4 else (),
            channel=cp + f"w{k % 4}", tag="w1ld")
    for ni, name in enumerate(("b1a", "b1g", "b2row", "grep", "brep")):
        P.rec("g", lambda g, name=name: g.dma_start(b[name][:, :], wdram[name][:, :]),
              deps=list(phase_dep) if ni < 4 else (),
              channel=cp + f"u{ni % 4}", tag=name + "ld")
    for mc in range(NMC):
        rows = 42 if mc == NMC - 1 else 128
        P.rec("g", lambda g, mc=mc, rows=rows: g.dma_start(
            b["w2"][0:rows, mc * 1024:(mc + 1) * 1024], wdram["w2"][mc * 128:mc * 128 + rows, :]),
            channel=cp + f"u{mc % 4}", tag="w2ld")
    wtok4 = [P.token(cp + f"w{c}") for c in range(4)]
    utok4 = [P.token(cp + f"u{c}") for c in range(4)]
    wtok = wtok4[0]

    xin_tok = [None] * (NT + 8)
    shift_tok = [None] * (NT + 8)
    tr_tok = [None] * NT
    lnv_tok = [None] * NT
    srstd_tok = [None] * NT
    trev_tok = [None] * NT
    resid_tok = [None] * NT
    out_tok = [None] * NT
    st_pe_tok = [None] * NST
    x2_tok = [None] * NT
    dump_tok = [[] for _ in range(NT)]

    def load_tile(j):
        if j >= NT:
            return
        dep = [resid_tok[j - 5], shift_tok[j - 4]] if j >= 5 else list(phase_dep)
        xin_tok[j] = P.rec("y", lambda y, j=j: y.dma_start(
            b["xin"][:, (j % 5) * 1024:(j % 5 + 1) * 1024], in_dram[j * 128:(j + 1) * 128, :]),
            deps=dep, channel=cp + f"xi{j % 5}", tag=f"xin{j}")

    def load_x2(j):
        if final is None or "x2_dram" not in final or j >= NT:
            return
        dep = [resid_tok[j - 3]] if j >= 3 else list(phase_dep)
        x2_tok[j] = P.rec("y", lambda y, j=j: y.dma_start(
            b["x2ring"][:, (j % 3) * 1024:(j % 3 + 1) * 1024],
            final["x2_dram"][j * 128:(j + 1) * 128, :]),
            deps=dep, channel=cp + f"x2{j % 3}", tag=f"x2in{j}")

    def ln_tile(j):
        p = j % 2
        xsl = b["xin"][:, (j % 5) * 1024:(j % 5 + 1) * 1024]
        hsh = b["hsh"][:, p * 512:(p + 1) * 512]
        dshift = [xin_tok[j], lnv_tok[j - 2] if j >= 2 else None] + \
            (dump_tok[j - 2] if j >= 2 else [])
        t1 = P.rec("g", lambda g, xsl=xsl, hsh=hsh: g.dma_start(
            hsh[1:128, :], xsl[0:127, 512:1024]), deps=dshift,
            channel=cp + f"sa{j % 2}", tag=f"sh{j}")
        if j == 0:
            t2 = P.rec("g", lambda g, hsh=hsh: g.memset(hsh[0:1, :], 0.0), tag="sh0z")
        else:
            xprev = b["xin"][:, ((j - 1) % 5) * 1024:((j - 1) % 5 + 1) * 1024]
            t2 = P.rec("g", lambda g, xprev=xprev, hsh=hsh: g.dma_start(
                hsh[0:1, :], xprev[127:128, 512:1024]),
                deps=[xin_tok[j - 1]], channel=cp + f"sb{j % 2}", tag=f"shp{j}")
        shift_tok[j] = t2
        stt = b["stats"][:, p * 16:(p + 1) * 16]

        def bn_pair(v, xsl=xsl, hsh=hsh, stt=stt):
            v.bn_stats(stt[:, 0:6], xsl[:, 0:512])
            return v.bn_stats(stt[:, 6:12], hsh[:, :])

        pair_tok = P.rec("v", bn_pair,
                         deps=[xin_tok[j], t1, t2] + (dump_tok[j - 2] if j >= 2 else []),
                         tag=f"bn{j}")
        vagg = P.rec("v", lambda v, stt=stt: v.bn_aggr(stt[:, 12:14], stt[:, 0:12]),
                     self_deps=[pair_tok], tag=f"bnag{j}")
        P.rec("s", lambda s, stt=stt: s.activation(stt[:, 14:15], stt[:, 13:14], AF.Ln,
                                                   bias=epsv[:, :], scale=1.0),
              deps=[vagg] + (list(glob_tok) if j < 2 else []), tag=f"ln{j}")
        lntk = P.token("s")
        srstd_tok[j] = P.rec("s", lambda s, stt=stt: s.activation(
            stt[:, 14:15], stt[:, 14:15], AF.Exp, bias=0.0, scale=-0.5),
            self_deps=[lntk], tag=f"exp{j}")
        tmp = b["tmp"][:, p * 1024:(p + 1) * 1024]
        hln = b["hln"][:, p * 1024:(p + 1) * 1024]
        P.rec("v", lambda v, xsl=xsl, tmp=tmp, stt=stt: v.tensor_scalar(
            tmp[:, 0:512], xsl[:, 0:512], stt[:, 12:13], stt[:, 14:15], ALU.subtract, ALU.mult),
            deps=[srstd_tok[j]], tag=f"apA{j}")
        P.rec("v", lambda v, hsh=hsh, tmp=tmp, stt=stt: v.tensor_scalar(
            tmp[:, 512:1024], hsh[:, :], stt[:, 12:13], stt[:, 14:15], ALU.subtract, ALU.mult),
            tag=f"apB{j}")
        P.rec("v", lambda v, tmp=tmp: v.tensor_tensor(tmp[:, :], tmp[:, :], b["grep"][:, :], ALU.mult),
              deps=(utok4 if j < 3 else ()), tag=f"apG{j}")
        lnv_tok[j] = P.rec("v", lambda v, tmp=tmp, hln=hln: v.tensor_tensor(
            hln[:, :], tmp[:, :], b["brep"][:, :], ALU.add),
            deps=[tr_tok[j - 2] if j >= 2 else None] +
                 (dump_tok[j - 2] if j >= 2 else []), tag=f"apBt{j}")
        if final is not None and "dbg_hln" in final:
            dump_tok[j] = [
                P.rec("y", lambda y, j=j, hln=hln: y.dma_start(
                    final["dbg_hln"][j * 128:(j + 1) * 128, :], hln[:, :]),
                    deps=[lnv_tok[j]], channel=cp + f"db{j % 2}", tag=f"dbg{j}"),
                P.rec("y", lambda y, j=j, hsh=hsh: y.dma_start(
                    final["dbg_hsh"][j * 128:(j + 1) * 128, :], hsh[:, :]),
                    deps=[lnv_tok[j]], channel=cp + f"dc{j % 2}", tag=f"dbgh{j}"),
                P.rec("y", lambda y, j=j, stt=stt: y.dma_start(
                    final["dbg_st"][j * 128:(j + 1) * 128, :], stt[:, 12:15]),
                    deps=[lnv_tok[j]], channel=cp + f"dd{j % 2}", tag=f"dbgs{j}")]

    def transpose_tile(j):
        p = j % 2
        st = j // NSUB
        stp = st % 2
        hln = b["hln"][:, p * 1024:(p + 1) * 1024]
        ptb = pT[p]
        deps = [lnv_tok[j], trev_tok[j - 2] if j >= 2 else None,
                st_pe_tok[st - 2] if st >= 2 else None] + (list(glob_tok) if j < 2 else [])
        for k in range(8):
            P.rec("t", lambda t, k=k, hln=hln, ptb=ptb: t.transpose(
                ptb[:, k * 128:(k + 1) * 128], hln[:, k * 128:(k + 1) * 128], ident[:, :]),
                deps=(deps if k == 0 else ()), inc=(1 if k == 7 else 0), tag=f"tr{j}")
        ttok = P.token("t")
        tr_tok[j] = ttok
        sub = j % NSUB
        base = stp * (8 * ST)
        dstap = bass.AP(b["hlnT"], base + sub * 128, [[2 * 8 * ST, 128], [ST, 8], [1, 128]])
        trev_tok[j] = P.rec("v", lambda v, ptb=ptb, dstap=dstap: v.tensor_copy(dstap, ptb[:, :]),
                            deps=[ttok], tag=f"trev{j}")

    def mc_loop(st):
        stp = st % 2
        base = stp * (8 * ST)
        prev_resid = resid_tok[st * NSUB - 1] if st >= 1 else None
        gel_tok = [None, None]
        prod_tok = [None, None]

        def emit_w2(mc):
            mp = mc % 2
            rows = 42 if mc == NMC - 1 else 128
            prod = b["prod"][:, mp * ST:(mp + 1) * ST]
            for sub in range(NSUB):
                for nh in range(2):
                    d = [prod_tok[mp]]
                    if st == 0 and sub == 0 and nh == 0:
                        d = d + utok4
                    if mc == 0 and sub == 0 and nh == 0 and prev_resid:
                        d = d + [prev_resid]
                    P.rec("t", lambda t, mc=mc, sub=sub, nh=nh, rows=rows, prod=prod:
                          t.matmul(pout[sub * 2 + nh][:, :],
                                   prod[0:rows, sub * 128:(sub + 1) * 128],
                                   b["w2"][0:rows, mc * 1024 + nh * 512:mc * 1024 + (nh + 1) * 512],
                                   start=(mc == 0), stop=False),
                          deps=(d if (sub == 0 and nh == 0) else ()), inc=0, tag=f"w2{st}_{mc}")

        for mc in range(NMC):
            mp = mc % 2
            rows = 42 if mc == NMC - 1 else 128
            pa = pag[mp]
            deps = [trev_tok[st * NSUB], trev_tok[st * NSUB + 1], prod_tok[mp]]
            if mc < 2:
                deps = deps + wtok4
            for half, cofs in ((0, 0), (1, 2730)):
                for k in range(8):
                    last = (half == 1 and k == 7)
                    P.rec("t", lambda t, k=k, mc=mc, rows=rows, half=half, cofs=cofs, pa=pa:
                          t.matmul(pa[0:rows, half * 256:half * 256 + ST],
                                   b["w1"][:, k * 5460 + cofs + mc * 128:
                                           k * 5460 + cofs + mc * 128 + rows],
                                   bass.AP(b["hlnT"], base + k * ST, [[2 * 8 * ST, 128], [1, ST]]),
                                   start=(k == 0), stop=(k == 7)),
                          deps=(deps if (half == 0 and k == 0) else ()),
                          inc=(1 if last else 0), tag=f"ag{st}_{mc}")
            agtok = P.token("t")
            gel = b["gel"][:, mp * ST:(mp + 1) * ST]
            gel_tok[mp] = P.rec("s", lambda s, pa=pa, gel=gel, mc=mc, rows=rows: s.activation(
                gel[0:rows, :], pa[0:rows, 256:512], AF.Gelu,
                bias=b["b1g"][0:rows, mc:mc + 1], scale=1.0),
                deps=[agtok, prod_tok[mp]] + (utok4 if (st == 0 and mc < 2) else []),
                tag=f"gel{st}_{mc}")
            prod = b["prod"][:, mp * ST:(mp + 1) * ST]
            prod_tok[mp] = P.rec("v", lambda v, pa=pa, prod=prod, gel=gel, mc=mc, rows=rows:
                                 v.scalar_tensor_tensor(prod[0:rows, :], pa[0:rows, 0:256],
                                                        b["b1a"][0:rows, mc:mc + 1], gel[0:rows, :],
                                                        ALU.add, ALU.mult),
                                 deps=[gel_tok[mp]], tag=f"prod{st}_{mc}")
            if mc >= 1:
                emit_w2(mc - 1)
        emit_w2(NMC - 1)
        for sub in range(NSUB):
            for nh in range(2):
                last = (sub == NSUB - 1 and nh == 1)
                P.rec("t", lambda t, sub=sub, nh=nh: t.matmul(
                    pout[sub * 2 + nh][:, :], ones1[:, :],
                    b["b2row"][:, nh * 512:(nh + 1) * 512], start=False, stop=True),
                    deps=(list(glob_tok) if (st == 0 and sub == 0 and nh == 0) else ()),
                    inc=(1 if last else 0), tag=f"b2{st}")
        st_pe_tok[st] = P.token("t")

    def evac_st(st):
        for sub in range(NSUB):
            j = st * NSUB + sub
            p = j % 2
            xsl = b["xin"][:, (j % 5) * 1024:(j % 5 + 1) * 1024]
            ob = b["outb"][:, p * 1024:(p + 1) * 1024]
            deps = [st_pe_tok[st], out_tok[j - 2] if j >= 2 else None]
            for nh in range(2):
                P.rec("v", lambda v, sub=sub, nh=nh, xsl=xsl, ob=ob: v.tensor_tensor(
                    ob[:, nh * 512:(nh + 1) * 512], pout[sub * 2 + nh][:, :],
                    xsl[:, nh * 512:(nh + 1) * 512], ALU.add),
                    deps=(deps if nh == 0 else ()), tag=f"res{j}")
            if final is None or "x2_dram" not in final:
                resid_tok[j] = P.token("v")
            else:
                stt = b["stats"][:, 32 + p * 16:32 + (p + 1) * 16]
                def fbn_pair(v, ob=ob, stt=stt):
                    v.bn_stats(stt[:, 0:6], ob[:, 0:512])
                    return v.bn_stats(stt[:, 6:12], ob[:, 512:1024])
                fpt = P.rec("v", fbn_pair, self_deps=[P.token("v")], tag=f"fbn{j}")
                vt = P.rec("v", lambda v, stt=stt: v.bn_aggr(stt[:, 12:14], stt[:, 0:12]),
                           self_deps=[fpt], tag=f"fba{j}")
                P.rec("s", lambda s, stt=stt: s.activation(stt[:, 14:15], stt[:, 13:14], AF.Ln,
                                                           bias=epsv[:, :], scale=1.0),
                      deps=[vt], tag=f"fln{j}")
                flntk = P.token("s")
                fs = P.rec("s", lambda s, stt=stt: s.activation(stt[:, 14:15], stt[:, 14:15],
                                                                AF.Exp, bias=0.0, scale=-0.5),
                           self_deps=[flntk], tag=f"fex{j}")
                tmp = b["tmp"][:, p * 1024:(p + 1) * 1024]
                P.rec("v", lambda v, ob=ob, tmp=tmp, stt=stt: v.tensor_scalar(
                    tmp[:, :], ob[:, :], stt[:, 12:13], stt[:, 14:15], ALU.subtract, ALU.mult),
                    deps=[fs, x2_tok[j]], tag=f"fap{j}")
                resid_tok[j] = P.rec("v", lambda v, ob=ob, tmp=tmp, j=j: v.tensor_tensor(
                    ob[:, :], tmp[:, :], b["x2ring"][:, (j % 3) * 1024:(j % 3 + 1) * 1024], ALU.add),
                    tag=f"fx2{j}")
            out_tok[j] = P.rec("y", lambda y, j=j, ob=ob: y.dma_start(
                out_dram[j * 128:(j + 1) * 128, :], ob[:, :]),
                deps=[resid_tok[j]], channel=cp + f"ot{j % 2}", tag=f"out{j}")

    for j in range(3):
        load_tile(j)
    load_x2(0)
    for j in range(3):
        ln_tile(j)
        transpose_tile(j)
    for st in range(NST):
        for j in (2 * st + 3, 2 * st + 4):
            load_tile(j)
        for j in (2 * st + 1, 2 * st + 2):
            load_x2(j)
        mc_loop(st)
        for j in (2 * st + 3, 2 * st + 4):
            if j < NT:
                ln_tile(j)
                transpose_tile(j)
        evac_st(st)
    return [out_tok[NT - 2], out_tok[NT - 1]]



def ffn_host_weights(ln_g, ln_b, w1, b1, w2, b2, layer):
    l = layer
    b1l = np.asarray(b1[l], np.float32)
    b1a = np.zeros((128, NMC), np.float32)
    b1g = np.zeros((128, NMC), np.float32)
    for mc in range(NMC):
        rows = 42 if mc == NMC - 1 else 128
        b1a[0:rows, mc] = b1l[mc * 128:mc * 128 + rows]
        b1g[0:rows, mc] = b1l[2730 + mc * 128:2730 + mc * 128 + rows]
    return {
        "w1": np.ascontiguousarray(np.asarray(w1[l], np.float32)),
        "w2": np.ascontiguousarray(np.asarray(w2[l], np.float32)),
        "b1a": b1a,
        "b1g": b1g,
        "b2row": np.ascontiguousarray(np.asarray(b2[l], np.float32)[None, :]),
        "grep": np.ascontiguousarray(np.broadcast_to(np.asarray(ln_g[l], np.float32)[None, :], (128, 1024))),
        "brep": np.ascontiguousarray(np.broadcast_to(np.asarray(ln_b[l], np.float32)[None, :], (128, 1024))),
    }


def alloc_globals(nc):
    gs = []
    glob = {}
    g1 = nc.sbuf_tensor("g_ident", [128, 128], BF16)
    glob["ident"] = g1.__enter__(); gs.append(g1)
    g2 = nc.sbuf_tensor("g_eps", [128, 1], F32)
    glob["eps"] = g2.__enter__(); gs.append(g2)
    g3 = nc.sbuf_tensor("g_ones1", [1, 128], BF16)
    glob["ones1"] = g3.__enter__(); gs.append(g3)
    g4 = nc.sbuf_tensor("g_ones1f", [1, 128], F32)
    glob["ones1f"] = g4.__enter__(); gs.append(g4)
    g5 = nc.sbuf_tensor("g_dum", [128, 512], BF16)
    glob["dum"] = g5.__enter__(); gs.append(g5)
    return gs, glob


def record_globals(P, nc, glob, ident_dram):
    t1 = P.rec("g", lambda g: g.dma_start(glob["ident"][:, :], ident_dram[:, :]),
               channel="gw", tag="identld")
    P.rec("g", lambda g: g.memset(glob["eps"][:, :], EPS), tag="epsset")
    P.rec("g", lambda g: g.memset(glob["ones1"][:, :], 1.0), tag="onesset")
    P.rec("g", lambda g: g.memset(glob["dum"][:, :], 0.0), tag="dumset")
    t3 = P.rec("g", lambda g: g.memset(glob["ones1f"][:, :], 1.0), tag="onesfset")
    return [t1, t3]


def record_ham_filler(P, nc, glob, psum_ap, n, deps):
    for q in range(n):
        P.rec("t", lambda t: t.matmul(psum_ap, glob["dum"][:, 0:128],
                                      glob["dum"][:, :], start=True, stop=True),
              deps=(list(deps) if q == 0 else ()), inc=0, tag="hamfill")


def host_ident():
    return np.eye(128, dtype=np.float32)


def build_core_program():
    nc = bass.Bass("TRN2", target_bir_lowering=False, debug=False)
    x_in = nc.dram_tensor("x", [T, D], F32, kind="ExternalInput")
    ident = nc.dram_tensor("cident", [128, 128], F32, kind="ExternalInput")
    out = nc.dram_tensor("out", [T, D], F32, kind="ExternalOutput")
    x2 = nc.dram_tensor("x2s", [T, D], F32)
    m1 = nc.dram_tensor("m1s", [T, D], F32)
    a1 = nc.dram_tensor("a1s", [T, D], F32)
    m2 = nc.dram_tensor("m2s", [T, D], F32)

    shapes_attn = {"wqkvT": (1024, 3072), "woT": (1024, 1024),
                   "trimL": (128, 2048), "trimH": (128, 2048)}
    shapes_iatt = {"wqkv": (1024, 1536), "wo": (512, 1024), "qs": (128, 512),
                   "ks": (128, 512), "rot": (4096, 2048),
                   "trimL": (128, 1024), "trimH": (128, 1024)}
    shapes_ffn = {"w1": (1024, 5460), "w2": (2730, 1024), "b1a": (128, NMC),
                  "b1g": (128, NMC), "b2row": (1, 1024), "grep": (128, 1024),
                  "brep": (128, 1024)}
    wd = {}
    for pfx, shapes in (("p1", shapes_attn), ("p2", shapes_iatt), ("p3", shapes_ffn),
                        ("p4", shapes_iatt), ("p5", shapes_ffn)):
        wd[pfx] = {name: nc.dram_tensor(f"{pfx}_{name}", list(sh), F32, kind="ExternalInput")
                   for name, sh in shapes.items()}

    P = Prog(nc)
    gg, glob = alloc_globals(nc)
    gtok = record_globals(P, nc, glob, ident)

    bg1, b1_ = alloc_attn_bufs(nc, "a1")
    pg1, ps1 = alloc_attn_psums(nc, "a1")
    end1 = record_attn_outer(P, nc, glob, ps1, b1_, wd["p1"], x_in, x2, cp="a1",
                             glob_tok=gtok)
    for g in reversed(bg1 + pg1):
        g.__exit__(None, None, None)
    bg2, b2_ = alloc_iattn_bufs(nc, "i2")
    pg2, ps2 = alloc_iattn_psums(nc, "i2")
    record_ham_filler(P, nc, glob, ps2["pS0"][:, 0:512], 450, end1)
    end2 = record_attn_inner(P, nc, glob, ps2, b2_, wd["p2"], x2, m1, cp="i2",
                             glob_tok=gtok, with_ln=True, phase_dep=end1)
    for g in reversed(bg2 + pg2):
        g.__exit__(None, None, None)
    bg3, b3_ = alloc_ffn_bufs(nc, "f3", with_final=False)
    pg3, ps3 = alloc_ffn_psums(nc, "f3")
    record_ham_filler(P, nc, glob, ps3["pag"][0][:, 0:512], 450, end2)
    end3 = record_ffn_phase(P, nc, glob, ps3, b3_, wd["p3"], m1, a1,
                            cp="f3", glob_tok=gtok, phase_dep=end2)
    for g in reversed(bg3 + pg3):
        g.__exit__(None, None, None)
    bg4, b4_ = alloc_iattn_bufs(nc, "i4")
    pg4, ps4 = alloc_iattn_psums(nc, "i4")
    record_ham_filler(P, nc, glob, ps4["pS0"][:, 0:512], 450, end3)
    end4 = record_attn_inner(P, nc, glob, ps4, b4_, wd["p4"], a1, m2, cp="i4",
                             glob_tok=gtok, with_ln=False, phase_dep=end3)
    for g in reversed(bg4 + pg4):
        g.__exit__(None, None, None)
    bg5, b5_ = alloc_ffn_bufs(nc, "f5", with_final=True)
    pg5, ps5 = alloc_ffn_psums(nc, "f5")
    record_ham_filler(P, nc, glob, ps5["pag"][0][:, 0:512], 450, end4)
    record_ffn_phase(P, nc, glob, ps5, b5_, wd["p5"], m2, out,
                     final={"x2_dram": x2}, cp="f5", glob_tok=gtok, phase_dep=end4)
    for g in reversed(bg5 + pg5):
        g.__exit__(None, None, None)

    with nc.Block() as block:
        semguards = P.emit(block)
    for g in reversed(semguards + gg):
        g.__exit__(None, None, None)
    return nc, len(P.ops)


def host_weight_maps(L0_attn_W_qkv, L0_attn_W_o, L0_ffn_wqkv, L0_ffn_q_scale,
                     L0_ffn_k_scale, L0_ffn_wo, L0_ffn_ln_g, L0_ffn_ln_b,
                     L0_ffn_w1, L0_ffn_b1, L0_ffn_w2, L0_ffn_b2):
    m = {}
    for name, arr in attn_host_weights(L0_attn_W_qkv, L0_attn_W_o).items():
        m[f"p1_{name}"] = arr
    for l, pfx in ((0, "p2"), (1, "p4")):
        for name, arr in iattn_host_weights(L0_ffn_wqkv, L0_ffn_q_scale,
                                            L0_ffn_k_scale, L0_ffn_wo, l).items():
            m[f"{pfx}_{name}"] = arr
    for l, pfx in ((0, "p3"), (1, "p5")):
        for name, arr in ffn_host_weights(L0_ffn_ln_g, L0_ffn_ln_b, L0_ffn_w1,
                                          L0_ffn_b1, L0_ffn_w2, L0_ffn_b2, l).items():
            m[f"{pfx}_{name}"] = arr
    m["cident"] = host_ident()
    return m


def kernel(L0_attn_x, L0_attn_W_qkv, L0_attn_W_o, L0_ffn_wqkv, L0_ffn_q_scale,
           L0_ffn_k_scale, L0_ffn_wo, L0_ffn_ln_g, L0_ffn_ln_b, L0_ffn_w1,
           L0_ffn_b1, L0_ffn_w2, L0_ffn_b2):
    x = np.ascontiguousarray(np.asarray(L0_attn_x, np.float32))
    wm = host_weight_maps(L0_attn_W_qkv, L0_attn_W_o, L0_ffn_wqkv, L0_ffn_q_scale,
                          L0_ffn_k_scale, L0_ffn_wo, L0_ffn_ln_g, L0_ffn_ln_b,
                          L0_ffn_w1, L0_ffn_b1, L0_ffn_w2, L0_ffn_b2)
    nc, nops = build_core_program()
    in_maps = []
    for c in range(8):
        im = dict(wm)
        im["x"] = np.ascontiguousarray(x[c])
        in_maps.append(im)
    res = run_bass_kernel_spmd(nc, in_maps, core_ids=list(range(8)))
    out = np.stack([np.asarray(res.results[c]["out"], np.float32) for c in range(8)], axis=0)
    return out



RING = 6


def alloc_attn_bufs(nc, tagp):
    gs = []
    bufs = {}

    def sb(name, shape, dt):
        g = nc.sbuf_tensor(tagp + name, shape, dt)
        t = g.__enter__()
        gs.append(g)
        bufs[name] = t
        return t

    sb("wqkv", [128, 8 * 3072], BF16)
    sb("wo", [128, 8 * 1024], BF16)
    sb("xin", [128, RING * 1024], F32)
    sb("stats", [128, 4 * 16], F32)
    sb("h1", [128, 2 * 1024], BF16)
    sb("h1T", [128, 2 * 1024], BF16)
    sb("qkt", [128, 2 * 2048], BF16)
    sb("qT", [128, 4 * 1024], BF16)
    sb("kT", [128, RING * 1024], BF16)
    sb("vr", [128, RING * 1040], BF16)
    sb("probs", [128, 16 * 640], BF16)
    sb("trimL", [128, 16 * 128], BF16)
    sb("trimH", [128, 16 * 128], BF16)
    sb("oTf", [128, 1024], F32)
    sb("sums", [1, 4096], F32)
    
    sb("oTn", [128, 1024], BF16)
    sb("abuf", [128, 2 * 1024], F32)
    sb("outb", [128, 2 * 1024], F32)
    return gs, bufs


def alloc_attn_psums(nc, tagp):
    gs = []
    ps = {}

    def palloc(name, shape, dt):
        g = nc.psum_tensor(tagp + name, shape, dt)
        t = g.__enter__()
        gs.append(g)
        ps[name] = t
        return t

    palloc("pS0", [128, 640], F32)
    palloc("pS1", [128, 640], F32)
    palloc("pQ", [128, 1024], F32)
    palloc("pOW", [128, 512], F32)
    palloc("pOW2", [128, 512], F32)
    ps["pT"] = ps["pQ"].bitcast(BF16)
    return gs, ps


def record_attn_outer(P, nc, glob, ps, b, wdram, in_dram, out_dram, cp="a1",
                      glob_tok=(), weights_dep=None, dbg=None, phase_dep=()):
    ident, epsv, ones1 = glob["ident"], glob["eps"], glob["ones1f"]
    pS2, pQ, pOW, pT = [ps["pS0"], ps["pS1"]], ps["pQ"], ps["pOW"], ps["pT"]
    pOW2 = ps["pOW2"]
    pows = [pOW, pOW2]

    for k in range(8):
        P.rec("g", lambda g, k=k: g.dma_start(
            b["wqkv"][:, k * 3072:(k + 1) * 3072], wdram["wqkvT"][k * 128:(k + 1) * 128, :]),
            deps=(([weights_dep] if weights_dep else []) + list(phase_dep)) if k < 4 else (),
            channel=cp + f"w{k % 4}", tag="wqkvld")
    for k in range(8):
        P.rec("g", lambda g, k=k: g.dma_start(
            b["wo"][:, k * 1024:(k + 1) * 1024], wdram["woT"][k * 128:(k + 1) * 128, :]),
            channel=cp + f"w{k % 4}", tag="wold")
    for ni, name in enumerate(("trimL", "trimH")):
        P.rec("g", lambda g, name=name: g.dma_start(b[name][:, :], wdram[name][:, :]),
              channel=cp + f"w{ni % 4}", tag=name)
    vzt = P.rec("g", lambda g: g.memset(b["vr"][:, :], 0.0), tag="vz")
    P.rec("g", lambda g: g.memset(
        bass.AP(b["vr"], 64, [[RING * 1040, 128], [65, 16 * RING], [1, 1]]), 1.0),
        self_deps=[vzt], tag="vones")
    P.rec("g", lambda g: g.memset(b["kT"][:, 4 * 1024:6 * 1024], 0.0), tag="kz")
    wtok4 = [P.token(cp + f"w{c}") for c in range(4)]
    wtok = wtok4[0]
    init_tok = P.token("g")

    xin_tok = [None] * (NT + 8)
    vmm_tok = [None] * NT
    h1_tok = [None] * NT
    trh_tok = [None] * NT
    h1T_tok = [None] * NT
    qkv_pe_tok = [None] * NT
    qkev_tok = [None] * NT
    trqk_tok = [None] * NT
    ring_tok = [None] * NT
    exp_tok = [[None] * OH for _ in range(NT)]
    zero_tok = [None] * NT
    zgrp_tok = [[None] * 4 for _ in range(NT)]
    oevv_tok = [None] * NT
    aevh_tok = [[None, None] for _ in range(NT)]
    rex_tok = [None] * NT
    av_tok = [None] * NT
    oev_tok = [None] * NT
    norm_tok = [None] * NT
    attn_pe_tok = [None] * NT
    res_tok = [None] * NT
    out_tok = [None] * NT

    def load_tile(i):
        if i >= NT:
            return
        dep = [res_tok[i - RING]] if i >= RING else list(phase_dep)
        xin_tok[i] = P.rec("y", lambda y, i=i: y.dma_start(
            b["xin"][:, (i % RING) * 1024:(i % RING + 1) * 1024],
            in_dram[i * 128:(i + 1) * 128, :]),
            deps=dep, channel=cp + f"xi{i % RING}", tag=f"xin{i}")

    def produce_ln(i):
        if i >= NT:
            return
        p = i % 2
        xsl = b["xin"][:, (i % RING) * 1024:(i % RING + 1) * 1024]
        stt = b["stats"][:, p * 16:(p + 1) * 16]

        def bn_pair(v, xsl=xsl, stt=stt):
            v.bn_stats(stt[:, 0:6], xsl[:, 0:512])
            return v.bn_stats(stt[:, 6:12], xsl[:, 512:1024])
        pr = P.rec("v", bn_pair, deps=[xin_tok[i]], tag=f"bn{i}")
        vagg = P.rec("v", lambda v, stt=stt: v.bn_aggr(stt[:, 12:14], stt[:, 0:12]),
                     self_deps=[pr], tag=f"bnag{i}")
        P.rec("s", lambda s, stt=stt: s.activation(stt[:, 14:15], stt[:, 13:14], AF.Ln,
                                                   bias=epsv[:, :], scale=1.0),
              deps=[vagg] + (list(glob_tok) if i < 2 else []), tag=f"ln{i}")
        lntk = P.token("s")
        srstd = P.rec("s", lambda s, stt=stt: s.activation(
            stt[:, 14:15], stt[:, 14:15], AF.Exp, bias=0.0, scale=-0.5),
            self_deps=[lntk], tag=f"exp{i}")
        h1 = b["h1"][:, p * 1024:(p + 1) * 1024]
        h1_tok[i] = P.rec("v", lambda v, xsl=xsl, h1=h1, stt=stt: v.tensor_scalar(
            h1[:, :], xsl[:, :], stt[:, 12:13], stt[:, 14:15], ALU.subtract, ALU.mult),
            deps=[srstd, trh_tok[i - 2] if i >= 2 else None], tag=f"h1{i}")

    def produce(i):
        if i >= NT:
            return
        p = i % 2
        h1 = b["h1"][:, p * 1024:(p + 1) * 1024]
        deps = [h1_tok[i], trqk_tok[i - 1] if i >= 1 else None,
                ring_tok[i - 1] if i >= 1 else None,
                qkev_tok[i - 1] if i >= 1 else None]
        for k in range(8):
            P.rec("t", lambda t, k=k, h1=h1: t.transpose(
                pT[:, k * 128:(k + 1) * 128], h1[:, k * 128:(k + 1) * 128], ident[:, :]),
                deps=(deps if k == 0 else ()), inc=(1 if k == 7 else 0), tag=f"trh{i}")
        trh_tok[i] = P.token("t")
        h1T = b["h1T"][:, p * 1024:(p + 1) * 1024]
        h1T_tok[i] = P.rec("v", lambda v, h1T=h1T: v.tensor_copy(h1T[:, :], pT[:, 0:1024]),
                           deps=[trh_tok[i], qkv_pe_tok[i - 2] if i >= 2 else None],
                           tag=f"h1Tev{i}")
        qk = b["qkt"][:, p * 2048:(p + 1) * 2048]
        ev = {}
        for ph in range(3):
            for ncol in range(2):
                for k in range(8):
                    d = []
                    if ph == 0 and k == 0 and ncol == 0:
                        d = [h1T_tok[i], qkev_tok[i - 1] if i >= 1 else None] + wtok4 + [
                             ring_tok[i - 1] if i >= 1 else None]
                    if ph > 0 and k == 0:
                        d = [ev[(ph - 1, ncol)]]
                    c0 = ph * 1024 + ncol * 512
                    P.rec("t", lambda t, k=k, c0=c0, ncol=ncol, h1T=h1T: t.matmul(
                        pQ[:, ncol * 512:ncol * 512 + 512],
                        h1T[:, k * 128:(k + 1) * 128],
                        b["wqkv"][:, k * 3072 + c0:k * 3072 + c0 + 512],
                        start=(k == 0), stop=(k == 7)),
                        deps=d, inc=(1 if k == 7 else 0), tag=f"qkv{i}_{ph}{ncol}")
                mmtok = P.token("t")
                if ph < 2:
                    ev[(ph, ncol)] = P.rec("v", lambda v, qk=qk, ph=ph, ncol=ncol: v.tensor_copy(
                        qk[:, ph * 1024 + ncol * 512:ph * 1024 + ncol * 512 + 512],
                        pQ[:, ncol * 512:ncol * 512 + 512]),
                        deps=[mmtok, trqk_tok[i - 1] if (ph == 0 and ncol == 0 and i >= 1)
                              else None],
                        tag=f"qkev{i}_{ph}{ncol}")
                else:
                    vmm_tok[i] = mmtok
        qkv_pe_tok[i] = P.token("t")
        if dbg is not None and "qk" in dbg:
            P.rec("y", lambda y, i=i, qk=qk: y.dma_start(
                dbg["qk"][i * 128:(i + 1) * 128, :], qk[:, :]),
                deps=[qkev_tok[i]], channel=cp + f"dq{i % 2}", tag=f"dbgqk{i}")
        if dbg is not None and "h1" in dbg:
            h1d = b["h1"][:, (i % 2) * 1024:(i % 2 + 1) * 1024]
            P.rec("y", lambda y, i=i, h1d=h1d: y.dma_start(
                dbg["h1"][i * 128:(i + 1) * 128, :], h1d[:, :]),
                deps=[h1_tok[i]], channel=cp + f"dh{i % 2}", tag=f"dbgh1{i}")

    def produce_b(i):
        if i >= NT:
            return
        p = i % 2
        qk = b["qkt"][:, p * 2048:(p + 1) * 2048]
        slot = i % RING
        dstv = bass.AP(b["vr"], slot * 1040, [[RING * 1040, 128], [65, 16], [1, 64]])
        qkev_tok[i] = P.rec("v", lambda v, dstv=dstv: v.tensor_copy(dstv, pQ[:, :]),
                            deps=[vmm_tok[i], avl_dep(i)], tag=f"vev{i}")
        for k in range(16):
            P.rec("t", lambda t, k=k, qk=qk: t.transpose(
                pT[:, (k % 8) * 128:(k % 8) * 128 + 128],
                qk[:, k * 128:(k + 1) * 128], ident[:, :]),
                deps=([qkev_tok[i], h1T_tok[i], avl_dep(i)] if k == 0 else
                      ([] if k != 8 else [qT_ev_dep(i)])),
                inc=(1 if k in (7, 15) else 0), tag=f"trqk{i}")
            if k == 7:
                t7 = P.token("t")
                qT = b["qT"][:, (i % 4) * 1024:(i % 4 + 1) * 1024]
                qT_ev = P.rec("v", lambda v, qT=qT: v.tensor_copy(qT[:, :], pT[:, 0:1024]),
                              deps=[t7, attn_pe_tok[i - 4] if i >= 4 else None],
                              tag=f"qTev{i}")
                qT_ev_holder[0] = qT_ev
                qT_ev_tok[i] = qT_ev
        trqk_tok[i] = P.token("t")
        ring_tok[i] = P.rec("v", lambda v, slot=slot: v.tensor_copy(
            b["kT"][:, slot * 1024:(slot + 1) * 1024], pT[:, 0:1024]),
            deps=[trqk_tok[i], avl_dep(i)], tag=f"kTev{i}")

    qT_ev_holder = [None]
    qT_ev_tok = [None] * (NT + 8)

    def qT_ev_dep(i):
        return qT_ev_holder[0]

    def avl_dep(i):
        t = i - RING + 2
        return attn_pe_tok[t] if t >= 0 else init_tok

    def attn_qk(j):
        qT = b["qT"][:, (j % 4) * 1024:(j % 4 + 1) * 1024]
        kdep = [ring_tok[min(j + 2, NT - 1)], qT_ev_tok[min(j, NT - 1)]]
        for h in range(OH):
            hh = h % 2
            c = h // 2
            pS = pS2[h % 2]
            if h >= 2:
                war = exp_tok[j][h - 2]
            elif j >= 1:
                war = exp_tok[j - 1][14 + h]
            else:
                war = None
            for tau in range(5):
                tt = j - 2 + tau
                slot = (tt % RING) if tt >= 0 else ((tt + RING) % RING)
                d = []
                if tau == 0:
                    d = (list(kdep) if h == 0 else []) + [war]
                P.rec("t", lambda t, slot=slot, c=c, hh=hh, tau=tau, qT=qT, pS=pS: t.matmul(
                    pS[:, tau * 128:(tau + 1) * 128],
                    b["kT"][hh * 64:(hh + 1) * 64, slot * 1024 + c * 128:slot * 1024 + c * 128 + 128],
                    qT[hh * 64:(hh + 1) * 64, c * 128:(c + 1) * 128],
                    start=True, stop=True),
                    deps=d, inc=(1 if tau == 4 else 0), tag=f"qk{j}_{h}")
            qktok = P.token("t")
            exp_tok[j][h] = P.rec("s", lambda s, h=h, pS=pS: s.activation(
                b["probs"][:, h * 640:(h + 1) * 640], pS[:, :], AF.Exp, bias=0.0, scale=0.125),
                deps=[qktok, av_tok[j - 1] if (h <= 1 and j >= 1) else None], tag=f"pexp{j}_{h}")
        for r in range(4):
            c0 = r * 4 * 640
            apL = bass.AP(b["probs"], c0, [[16 * 640, 128], [640, 4], [1, 128]])
            apH = bass.AP(b["probs"], c0 + 512, [[16 * 640, 128], [640, 4], [1, 128]])
            P.rec("v", lambda v, apL=apL, r=r: v.tensor_tensor(
                apL, apL, b["trimL"][:, r * 512:(r + 1) * 512], ALU.mult),
                deps=[exp_tok[j][r * 4 + 3]], tag=f"zl{j}_{r}")
            ztok = P.rec("v", lambda v, apH=apH, r=r: v.tensor_tensor(
                apH, apH, b["trimH"][:, r * 512:(r + 1) * 512], ALU.mult),
                tag=f"zh{j}_{r}")
            if j <= 1:
                width = 256 if j == 0 else 128
                apE = bass.AP(b["probs"], c0, [[16 * 640, 128], [640, 4], [1, width]])
                ztok = P.rec("v", lambda v, apE=apE: v.memset(apE, 0.0),
                             self_deps=[ztok], tag=f"ze{j}_{r}")
            if j >= NT - 2:
                start = 512 if j == NT - 2 else 384
                apE = bass.AP(b["probs"], c0 + start,
                              [[16 * 640, 128], [640, 4], [1, 640 - start]])
                ztok = P.rec("v", lambda v, apE=apE: v.memset(apE, 0.0),
                             self_deps=[ztok], tag=f"ze{j}_{r}")
            zgrp_tok[j][r] = ztok

    def attn_av(j):
        oevg = [None] * 4
        for r in range(4):
            pw = pows[r % 2]
            for hq in range(4):
                h = r * 4 + hq
                for tau in range(5):
                    tt = j - 2 + tau
                    slot = (tt % RING) if tt >= 0 else ((tt + RING) % RING)
                    d = []
                    if hq == 0 and tau == 0:
                        if r < 2:
                            d = [zgrp_tok[j][r],
                                 aevh_tok[j - 1][r] if j >= 1 else None]
                        else:
                            d = [zgrp_tok[j][r], oevg[r - 2]]
                        if dbg is not None and "_prtok" in dbg and j == 10:
                            d = d + [dbg["_prtok"]]
                    P.rec("t", lambda t, slot=slot, h=h, hq=hq, tau=tau, pw=pw: t.matmul(
                        pw[0:65, hq * 128:(hq + 1) * 128],
                        b["vr"][:, slot * 1040 + h * 65:slot * 1040 + h * 65 + 65],
                        b["probs"][:, h * 640 + tau * 128:h * 640 + (tau + 1) * 128],
                        start=(tau == 0), stop=(tau == 4)),
                        deps=d, inc=(1 if (hq == 3 and tau == 4) else 0), tag=f"av{j}_{r}")
            avtok = P.token("t")
            if r == 3:
                av_tok[j] = avtok
            ap_even_src = bass.AP(pw, 0, [[512, 64], [256, 2], [1, 128]])
            ap_odd_src = bass.AP(pw, 128, [[512, 64], [256, 2], [1, 128]])
            P.rec("s", lambda s, r=r, ap=ap_even_src: s.copy(
                b["oTf"][0:64, r * 256:(r + 1) * 256], ap), deps=[avtok], tag=f"oev{j}_{r}")
            P.rec("s", lambda s, r=r, ap=ap_odd_src: s.copy(
                b["oTf"][64:128, r * 256:(r + 1) * 256], ap), tag=f"oev2{j}_{r}")
            oevg[r] = P.rec("s", lambda s, r=r, pw=pw: s.copy(
                b["sums"][0:1, r * 512:(r + 1) * 512], pw[64:65, 0:512]), tag=f"sev{j}_{r}")
            oev_tok[j] = oevg[r]
            oevv_tok[j] = oevg[r]
        if dbg is not None and "oT" in dbg:
            P.rec("y", lambda y, j=j: y.dma_start(
                dbg["oT"][j * 128:(j + 1) * 128, :], b["oTf"][:, :]),
                deps=[oev_tok[j]], channel=cp + f"do{j % 2}", tag=f"dbgoT{j}")
            P.rec("y", lambda y, j=j: y.dma_start(
                dbg["sums"][j:j + 1, :], b["sums"][:, :]),
                deps=[oev_tok[j]], channel=cp + f"ds{j % 2}", tag=f"dbgsm{j}")
        rlntk = P.rec("s", lambda s: s.activation(b["sums"][0:1, 2048:4096], b["sums"][0:1, 0:2048],
                                          AF.Ln, bias=0.0, scale=1.0),
              self_deps=[oev_tok[j]], tag=f"rln{j}")
        rex_tok[j] = P.rec("s", lambda s: s.activation(b["sums"][0:1, 2048:4096],
                                                       b["sums"][0:1, 2048:4096],
                                                       AF.Exp, bias=0.0, scale=-1.0),
                           self_deps=[rlntk], tag=f"rex{j}")

    def attn_fin(j):
        if j < 0:
            return
        p = j % 2
        ntoks = [None, None]
        for half in range(2):
            pw = pows[half]
            for rh in range(2):
                rhs = bass.AP(b["sums"], 2048 + half * 1024 + rh * 128,
                              [[4096, 1], [256, 4], [1, 128]])
                dd = ([rex_tok[j], oevv_tok[j]] +
                      (list(glob_tok) if j == 0 else [])) if (rh == 0) else ()
                P.rec("t", lambda t, rh=rh, rhs=rhs, pw=pw: t.matmul(
                    pw[rh * 64:(rh + 1) * 64, 0:512],
                    ones1[0:1, 0:64], rhs, start=True, stop=True),
                    deps=dd, inc=(1 if rh == 1 else 0), tag=f"sb{j}_{half}")
            btok = P.token("t")
            ntoks[half] = P.rec("v", lambda v, half=half, pw=pw: v.tensor_tensor(
                b["oTn"][:, half * 512:(half + 1) * 512],
                b["oTf"][:, half * 512:(half + 1) * 512], pw[:, :], ALU.mult),
                deps=[btok], tag=f"on{j}_{half}")
        norm_tok[j] = ntoks[1]
        aevh = [None, None]
        for half in range(2):
            pw = pows[half]
            for k in range(8):
                d = [ntoks[half]] if k == 0 else ()
                P.rec("t", lambda t, k=k, half=half, pw=pw: t.matmul(
                    pw[:, :], b["oTn"][:, k * 128:(k + 1) * 128],
                    b["wo"][:, k * 1024 + half * 512:k * 1024 + (half + 1) * 512],
                    start=(k == 0), stop=(k == 7)),
                    deps=d, inc=(1 if k == 7 else 0), tag=f"wo{j}_{half}")
            wotok = P.token("t")
            ab = b["abuf"][:, p * 1024:(p + 1) * 1024]
            aevh[half] = P.rec("v", lambda v, ab=ab, half=half, pw=pw: v.tensor_copy(
                ab[:, half * 512:(half + 1) * 512], pw[:, :]),
                deps=[wotok, res_tok[j - 2] if (half == 0 and j >= 2) else None],
                tag=f"aev{j}_{half}")
        aevh_tok[j] = aevh
        attn_pe_tok[j] = P.token("t")
        aevt = P.token("v")
        stt = b["stats"][:, 32 + p * 16:32 + (p + 1) * 16]
        ab = b["abuf"][:, p * 1024:(p + 1) * 1024]

        def fbn_pair(v, ab=ab, stt=stt):
            v.bn_stats(stt[:, 0:6], ab[:, 0:512])
            return v.bn_stats(stt[:, 6:12], ab[:, 512:1024])
        fpt = P.rec("v", fbn_pair, self_deps=[aevt], tag=f"fbn{j}")
        vt = P.rec("v", lambda v, stt=stt: v.bn_aggr(stt[:, 12:14], stt[:, 0:12]),
                   self_deps=[fpt], tag=f"fba{j}")
        P.rec("s", lambda s, stt=stt: s.activation(stt[:, 14:15], stt[:, 13:14], AF.Ln,
                                                   bias=epsv[:, :], scale=1.0),
              deps=[vt], tag=f"fln{j}")
        flntk = P.token("s")
        fs = P.rec("s", lambda s, stt=stt: s.activation(stt[:, 14:15], stt[:, 14:15],
                                                        AF.Exp, bias=0.0, scale=-0.5),
                   self_deps=[flntk], tag=f"fex{j}")
        ob = b["outb"][:, p * 1024:(p + 1) * 1024]
        xsl = b["xin"][:, (j % RING) * 1024:(j % RING + 1) * 1024]
        P.rec("v", lambda v, ab=ab, ob=ob, stt=stt: v.tensor_scalar(
            ob[:, :], ab[:, :], stt[:, 12:13], stt[:, 14:15], ALU.subtract, ALU.mult),
            deps=[fs, out_tok[j - 2] if j >= 2 else None], tag=f"fap{j}")
        res_tok[j] = P.rec("v", lambda v, ob=ob, xsl=xsl: v.tensor_tensor(
            ob[:, :], ob[:, :], xsl[:, :], ALU.add), tag=f"fx2{j}")
        out_tok[j] = P.rec("y", lambda y, j=j, ob=ob: y.dma_start(
            out_dram[j * 128:(j + 1) * 128, :], ob[:, :]),
            deps=[res_tok[j]], channel=cp + f"ot{j % 2}", tag=f"out{j}")

    for i in range(3):
        load_tile(i)
    produce_ln(0)
    produce(0)
    produce_b(0)
    produce_ln(1)
    produce(1)
    produce_b(1)
    produce_ln(2)
    produce(2)
    for j in range(NT):
        load_tile(j + 3)
        produce_b(j + 2)
        produce_ln(j + 3)
        attn_qk(j)
        produce(j + 3)
        attn_fin(j - 1)
        attn_av(j)
    attn_fin(NT - 1)
    return [out_tok[NT - 2], out_tok[NT - 1],
            P.token("t"), P.token("v"), P.token("s"), P.token("g")]


def attn_host_weights(W_qkv, W_o):
    p = np.arange(128)
    cL = np.arange(128)
    triL = (cL[None, :] <= p[:, None]).astype(np.float32)
    triH = (cL[None, :] >= p[:, None]).astype(np.float32)
    return {
        "wqkvT": np.ascontiguousarray(np.asarray(W_qkv, np.float32).T),
        "woT": np.ascontiguousarray(np.asarray(W_o, np.float32).T),
        "trimL": np.ascontiguousarray(np.tile(triL, (1, 16))),
        "trimH": np.ascontiguousarray(np.tile(triH, (1, 16))),
    }


IRING = 7


def alloc_iattn_bufs(nc, tagp):
    gs = []
    bufs = {}

    def sb(name, shape, dt):
        g = nc.sbuf_tensor(tagp + name, shape, dt)
        t = g.__enter__()
        gs.append(g)
        bufs[name] = t
        return t

    sb("wqkv", [128, 8 * 1536], BF16)
    sb("wo", [128, 4 * 1024], BF16)
    sb("qs", [128, 512], F32)
    sb("ks", [128, 512], F32)
    sb("xin", [128, IRING * 1024], F32)
    sb("x0", [128, 4 * 1024], F32)
    sb("stats", [128, 4 * 16], F32)
    sb("hT", [128, 2 * 1040], BF16)
    sb("rot", [128, 2 * 2048], F32)
    sb("qb", [128, 2 * 512], F32)
    sb("kb", [128, 2 * 512], F32)
    sb("qTb", [128, 2 * 512], BF16)
    sb("kTb", [128, 2 * 512], BF16)
    sb("x0b", [128, 2 * 1024], BF16)
    sb("qsc", [128, 2 * 512], F32)
    sb("qsc2", [128, 2 * 512], F32)
    sb("ssq", [128, 32], F32)
    sb("qT", [128, 4 * 512], BF16)
    sb("kT", [128, IRING * 512], BF16)
    sb("vr", [128, IRING * 520], BF16)
    sb("probs", [128, 8 * 640], BF16)
    sb("trimL", [128, 8 * 128], BF16)
    sb("trimH", [128, 8 * 128], BF16)
    sb("oTf", [128, 512], F32)
    sb("sums", [1, 2048], F32)
    sb("oTn", [128, 512], BF16)
    sb("outb", [128, 2 * 1024], F32)
    return gs, bufs


def alloc_iattn_psums(nc, tagp):
    gs = []
    ps = {}

    def palloc(name, shape, dt):
        g = nc.psum_tensor(tagp + name, shape, dt)
        t = g.__enter__()
        gs.append(g)
        ps[name] = t
        return t

    palloc("pS0", [128, 640], F32)
    palloc("pS1", [128, 640], F32)
    palloc("pQ", [128, 512], F32)
    palloc("pQ2", [128, 512], F32)
    palloc("pOW", [128, 512], F32)
    palloc("pOW2", [128, 512], F32)
    ps["pT"] = ps["pQ"].bitcast(BF16)
    return gs, ps


def record_attn_inner(P, nc, glob, ps, b, wdram, in_dram, out_dram, cp="i2",
                      glob_tok=(), weights_dep=None, with_ln=True, dbg=None, phase_dep=()):
    ident, epsv, ones1 = glob["ident"], glob["eps"], glob["ones1"]
    pS2 = [ps["pS0"], ps["pS1"]]
    pQ, pOW, pT = ps["pQ"], ps["pOW"], ps["pT"]
    pQ2 = ps["pQ2"]
    pOW2 = ps["pOW2"]
    ipows = [pOW, pOW2]

    for k in range(8):
        P.rec("g", lambda g, k=k: g.dma_start(
            b["wqkv"][:, k * 1536:(k + 1) * 1536], wdram["wqkv"][k * 128:(k + 1) * 128, :]),
            deps=(([weights_dep] if weights_dep else []) + list(phase_dep)) if k < 4 else (),
            channel=cp + f"w{k % 4}", tag="wqkvld")
    for k in range(4):
        P.rec("g", lambda g, k=k: g.dma_start(
            b["wo"][:, k * 1024:(k + 1) * 1024], wdram["wo"][k * 128:(k + 1) * 128, :]),
            channel=cp + f"w{k % 4}", tag="wold")
    for ni, name in enumerate(("qs", "ks", "trimL", "trimH")):
        P.rec("g", lambda g, name=name: g.dma_start(b[name][:, :], wdram[name][:, :]),
              channel=cp + f"w{ni % 4}", tag=name)
    vzt = P.rec("g", lambda g: g.memset(b["vr"][:, :], 0.0), tag="vz")
    P.rec("g", lambda g: g.memset(
        bass.AP(b["vr"], 64, [[IRING * 520, 128], [65, 8 * IRING], [1, 1]]), 1.0),
        self_deps=[vzt], tag="vones")
    P.rec("g", lambda g: g.memset(b["kT"][:, 3 * 512:7 * 512], 0.0), tag="kz")
    wtok4 = [P.token(cp + f"w{c}") for c in range(4)]
    wtok = wtok4[0]
    init_tok = P.token("g")

    xin_tok = [None] * (NT + 8)
    rot_tok = [None] * (NT + 8)
    vev_tok = [None] * NT
    aevh_tok = [[None, None] for _ in range(NT)]
    cast_tok_arr = [None] * NT
    dqt_arr = [None] * NT
    x0_tok = [None] * NT
    trx_tok = [None] * NT
    hT_tok = [None] * NT
    qkv_pe_tok = [None] * NT
    qkproc_tok = [None] * NT
    trqk_tok = [None] * NT
    qT_ev_tok = [None] * NT
    ring_tok = [None] * NT
    exp_tok = [[None] * IH for _ in range(NT)]
    zero_tok = [None] * NT
    av_tok = [None] * NT
    oev_tok = [None] * NT
    rex_tok = [None] * NT
    norm_tok = [None] * NT
    attn_pe_tok = [None] * NT
    res_tok = [None] * NT
    out_tok = [None] * NT

    def load_tile(i):
        if i >= NT:
            return
        dep = [res_tok[i - IRING]] if i >= IRING else list(phase_dep)
        xin_tok[i] = P.rec("y", lambda y, i=i: y.dma_start(
            b["xin"][:, (i % IRING) * 1024:(i % IRING + 1) * 1024],
            in_dram[i * 128:(i + 1) * 128, :]),
            deps=dep, channel=cp + f"xi{i % IRING}", tag=f"xin{i}")


    def produce_ln(i):
        if i >= NT:
            return
        p = i % 2
        dep2 = [qkproc_tok[i - 2]] if i >= 2 else []
        rot_tok[i] = P.rec("y", lambda y, i=i: y.dma_start(
            b["rot"][:, (i % 2) * 2048:(i % 2 + 1) * 2048],
            wdram["rot"][i * 128:(i + 1) * 128, :]),
            deps=dep2, channel=cp + f"ro{i % 2}", tag=f"rot{i}")
        xsl = b["xin"][:, (i % IRING) * 1024:(i % IRING + 1) * 1024]
        x0 = b["x0"][:, (i % 4) * 1024:(i % 4 + 1) * 1024]
        x0war = res_tok[i - 4] if i >= 4 else None
        if with_ln:
            stt = b["stats"][:, p * 16:(p + 1) * 16]

            def bn_pair(v, xsl=xsl, stt=stt):
                v.bn_stats(stt[:, 0:6], xsl[:, 0:512])
                return v.bn_stats(stt[:, 6:12], xsl[:, 512:1024])
            pr = P.rec("v", bn_pair, deps=[xin_tok[i]], tag=f"bn{i}")
            vagg = P.rec("v", lambda v, stt=stt: v.bn_aggr(stt[:, 12:14], stt[:, 0:12]),
                         self_deps=[pr], tag=f"bnag{i}")
            P.rec("s", lambda s, stt=stt: s.activation(stt[:, 14:15], stt[:, 13:14], AF.Ln,
                                                       bias=epsv[:, :], scale=1.0),
                  deps=[vagg] + (list(glob_tok) if i < 2 else []), tag=f"ln{i}")
            lntk = P.token("s")
            srstd = P.rec("s", lambda s, stt=stt: s.activation(
                stt[:, 14:15], stt[:, 14:15], AF.Exp, bias=0.0, scale=-0.5),
                self_deps=[lntk], tag=f"exp{i}")
            x0_tok[i] = P.rec("v", lambda v, xsl=xsl, x0=x0, stt=stt: v.tensor_scalar(
                x0[:, :], xsl[:, :], stt[:, 12:13], stt[:, 14:15], ALU.subtract, ALU.mult),
                deps=[srstd, x0war], tag=f"x0{i}")
        else:
            x0_tok[i] = P.rec("v", lambda v, xsl=xsl, x0=x0: v.tensor_copy(x0[:, :], xsl[:, :]),
                              deps=[xin_tok[i], x0war] + (list(glob_tok) if i < 2 else []),
                              tag=f"x0{i}")
        xb = b["qsc"][:, p * 512:(p + 1) * 512]
        x0b = b["x0b"][:, p * 1024:(p + 1) * 1024]
        cast_tok_arr[i] = P.rec("v", lambda v, x0=x0, x0b=x0b: v.tensor_copy(
            x0b[:, :], x0[:, :]),
            deps=[trx_tok[i - 2] if i >= 2 else None] +
                 (list(glob_tok) if i < 2 else []), tag=f"x0b{i}")

    def produce(i):
        if i >= NT:
            return
        p = i % 2
        x0 = b["x0"][:, (i % 4) * 1024:(i % 4 + 1) * 1024]
        x0b = b["x0b"][:, p * 1024:(p + 1) * 1024]
        cast_tok = cast_tok_arr[i]
        rsl = b["rot"][:, (i % 2) * 2048:(i % 2 + 1) * 2048]
        dqt = dqt_arr[i]
        hT = b["hT"][:, p * 1040:(p + 1) * 1040]
        deps = [cast_tok, hT_tok[i - 2] if i >= 2 else None,
                qkv_pe_tok[i - 1] if i >= 1 else None,
                ring_tok[i - 1] if i >= 1 else None,
                vev_tok[i - 1] if i >= 1 else None]
        for k in range(8):
            P.rec("t", lambda t, k=k, x0b=x0b: t.transpose(
                pT[:, k * 128:(k + 1) * 128], x0b[:, k * 128:(k + 1) * 128], ident[:, :]),
                deps=(deps if k == 0 else ()), inc=(1 if k == 7 else 0), tag=f"trx{i}")
        trx_tok[i] = P.token("t")
        dstap = bass.AP(b["hT"], p * 1040 + 1, [[2 * 1040, 128], [130, 8], [1, 128]])
        ev = P.rec("v", lambda v, dstap=dstap: v.tensor_copy(dstap, pT[:, :]),
                   deps=[trx_tok[i], qkv_pe_tok[i - 2] if i >= 2 else None], tag=f"hTev{i}")
        if i == 0:
            hT_tok[i] = P.rec("v", lambda v: v.memset(
                bass.AP(b["hT"], 0, [[2 * 1040, 128], [130, 8], [1, 1]]), 0.0),
                self_deps=[ev], tag=f"hTz{i}")
        else:
            po = (i - 1) % 2
            src_ap = bass.AP(b["hT"], po * 1040 + 128, [[2 * 1040, 128], [130, 8], [1, 1]])
            dst_ap = bass.AP(b["hT"], p * 1040 + 0, [[2 * 1040, 128], [130, 8], [1, 1]])
            hT_tok[i] = P.rec("v", lambda v, src_ap=src_ap, dst_ap=dst_ap: v.tensor_copy(
                dst_ap, src_ap), self_deps=[ev], tag=f"hTpc{i}")
        evq = None
        for ph in range(3):
            pq = pQ2 if ph == 1 else pQ
            for k in range(8):
                ofs = 1 if k < 4 else 0
                d = []
                if ph == 0 and k == 0:
                    d = [hT_tok[i]] + wtok4 + [
                         (qkproc_tok[i - 1] if i >= 1 else None)]
                if ph == 2 and k == 0:
                    d = [ev_q0]
                P.rec("t", lambda t, k=k, ph=ph, ofs=ofs, p=p, pq=pq: t.matmul(
                    pq[:, :],
                    bass.AP(b["hT"], p * 1040 + k * 130 + ofs, [[2 * 1040, 128], [1, 128]]),
                    b["wqkv"][:, k * 1536 + ph * 512:k * 1536 + ph * 512 + 512],
                    start=(k == 0), stop=(k == 7)),
                    deps=d, inc=(1 if k == 7 else 0), tag=f"qkv{i}_{ph}")
            mmtok = P.token("t")
            if ph == 0:
                evq = P.rec("v", lambda v, p=p: v.tensor_copy(
                    b["qb"][:, p * 512:(p + 1) * 512], pQ[:, :]),
                    deps=[mmtok, qkproc_tok[i - 2] if i >= 2 else None], tag=f"qev{i}")
                ev_q0 = evq
            elif ph == 1:
                evq = P.rec("v", lambda v, p=p: v.tensor_copy(
                    b["kb"][:, p * 512:(p + 1) * 512], pQ2[:, :]),
                    deps=[mmtok], tag=f"kev{i}")
            else:
                slot = i % IRING
                dstv = bass.AP(b["vr"], slot * 520, [[IRING * 520, 128], [65, 8], [1, 64]])
                vev_tok[i] = P.rec("v", lambda v, dstv=dstv: v.tensor_copy(dstv, pQ[:, :]),
                                   deps=[mmtok, avl_dep(i)], tag=f"vev{i}")
        qkv_pe_tok[i] = P.token("t")
        if dbg is not None and "x0" in dbg:
            P.rec("y", lambda y, i=i, x0=x0: y.dma_start(
                dbg["x0"][i * 128:(i + 1) * 128, :], x0[:, :]),
                deps=[x0_tok[i]], channel=cp + f"d0{i % 2}", tag=f"dbgx0{i}")
        rsl = b["rot"][:, (i % 2) * 2048:(i % 2 + 1) * 2048]
        if dbg is not None and "rot" in dbg:
            P.rec("y", lambda y, i=i, rsl=rsl: y.dma_start(
                dbg["rot"][i * 128:(i + 1) * 128, :], rsl),
                deps=[rot_tok[i]], channel=cp + f"d6{i % 2}", tag=f"dbgrot{i}")
        dqt = None
        if dbg is not None and "qraw" in dbg:
            dqt = P.rec("y", lambda y, i=i, p=p: y.dma_start(
                dbg["qraw"][i * 128:(i + 1) * 128, :], b["qb"][:, p * 512:(p + 1) * 512]),
                deps=[ev_q0], channel=cp + f"d4{i % 2}", tag=f"dbgqr{i}")
        bq = b["qb"][:, p * 512:(p + 1) * 512]
        bk = b["kb"][:, p * 512:(p + 1) * 512]
        sqq = b["qsc"][:, p * 512:(p + 1) * 512]
        sqk = b["qsc2"][:, p * 512:(p + 1) * 512]
        baseq = p * 16
        basek = p * 16 + 8
        sqt = P.rec("v", lambda v: v.tensor_tensor(sqq[:, :], bq[:, :], bq[:, :], ALU.mult),
                    deps=[rot_tok[i]], tag=f"sqq{i}")
        for h in range(8):
            P.rec("v", lambda v, h=h: v.tensor_reduce(
                b["ssq"][:, baseq + h:baseq + h + 1], sqq[:, h * 64:(h + 1) * 64],
                AX.X, ALU.add), self_deps=([sqt] if h == 0 else ()), tag=f"ssq{i}_{h}")
        sqtk = P.rec("v", lambda v: v.tensor_tensor(sqk[:, :], bk[:, :], bk[:, :], ALU.mult),
                     tag=f"sqk{i}")
        for h in range(8):
            P.rec("v", lambda v, h=h: v.tensor_reduce(
                b["ssq"][:, basek + h:basek + h + 1], sqk[:, h * 64:(h + 1) * 64],
                AX.X, ALU.add), self_deps=([sqtk] if h == 0 else ()), tag=f"ssk{i}_{h}")
        rdt = P.token("v")
        P.rec("v", lambda v: v.tensor_scalar(
            b["ssq"][:, p * 16:p * 16 + 16], b["ssq"][:, p * 16:p * 16 + 16], 1e-24, None,
            ALU.max), self_deps=[rdt], tag=f"cl{i}")
        vtk = P.token("v")
        P.rec("s", lambda s: s.activation(
            b["ssq"][:, p * 16:p * 16 + 16], b["ssq"][:, p * 16:p * 16 + 16], AF.Ln,
            bias=0.0, scale=1.0), deps=[vtk], tag=f"l2l{i}")
        stk = P.token("s")
        rtk = P.rec("s", lambda s: s.activation(
            b["ssq"][:, p * 16:p * 16 + 16], b["ssq"][:, p * 16:p * 16 + 16], AF.Exp,
            bias=0.0, scale=-0.5), self_deps=[stk], tag=f"l2e{i}")
        if dbg is not None and "ssq" in dbg:
            P.rec("y", lambda y, i=i: y.dma_start(
                dbg["ssq"][i * 128:(i + 1) * 128, :], b["ssq"][:, baseq:baseq + 8]),
                deps=[rtk], channel=cp + f"d5{i % 2}", tag=f"dbgss{i}")
        proc_toks = []
        for (which, bb, base, scv, c0, s0) in (("q", bq, baseq, "qs", 0, 512),
                                               ("k", bk, basek, "ks", 1024, 1536)):
            gate = [rtk] + ([dqt] if (dqt is not None and which == "q") else [])
            for h in range(8):
                P.rec("v", lambda v, bb=bb, h=h, base=base, scv=scv:
                      v.scalar_tensor_tensor(
                    bb[:, h * 64:(h + 1) * 64], bb[:, h * 64:(h + 1) * 64],
                    b["ssq"][:, base + h:base + h + 1],
                    b[scv][:, h * 64:(h + 1) * 64], ALU.mult, ALU.mult),
                    deps=(gate if (which == "q" and h == 0) else ()), tag=f"l2a{which}{i}_{h}")
            if dbg is not None and "qu" in dbg and which == "q":
                ut = P.token("v")
                P.rec("y", lambda y, i=i, bb=bb: y.dma_start(
                    dbg["qu"][i * 128:(i + 1) * 128, :], bb[:, :]),
                    deps=[ut], channel=cp + f"d3{i % 2}", tag=f"dbgu{i}")
            cosr = rsl[:, c0:c0 + 512]
            sinr = rsl[:, s0:s0 + 512]
            P.rec("v", lambda v, bb=bb, cosr=cosr: v.tensor_tensor(
                sqq[:, :], bb[:, :], cosr, ALU.mult), tag=f"rc{which}{i}")
            swap = bass.AP(b["qb" if which == "q" else "kb"], p * 512 + 32,
                           [[1024, 128], [64, 8], [-32, 2], [1, 32]])
            rst = P.rec("v", lambda v, swap=swap, sinr=sinr: v.tensor_tensor(
                sqk[:, :], swap, sinr, ALU.mult), tag=f"rs{which}{i}")
            obf = b["qTb" if which == "q" else "kTb"][:, p * 512:(p + 1) * 512]
            tk = P.rec("v", lambda v, obf=obf: v.tensor_tensor(
                obf[:, :], sqq[:, :], sqk[:, :], ALU.add),
                deps=[trqk_tok[i - 2] if i >= 2 else None],
                self_deps=[rst], tag=f"ra{which}{i}")
            proc_toks.append(tk)
        qkproc_tok[i] = proc_toks[-1]
        if dbg is not None and "qk" in dbg:
            P.rec("y", lambda y, i=i, p=p: y.dma_start(
                dbg["qk"][i * 128:(i + 1) * 128, 0:512], b["qTb"][:, p * 512:(p + 1) * 512]),
                deps=[qkproc_tok[i]], channel=cp + f"d1{i % 2}", tag=f"dbgq{i}")
            P.rec("y", lambda y, i=i, p=p: y.dma_start(
                dbg["qk"][i * 128:(i + 1) * 128, 512:1024], b["kTb"][:, p * 512:(p + 1) * 512]),
                deps=[qkproc_tok[i]], channel=cp + f"d2{i % 2}", tag=f"dbgk{i}")

    def produce_b(i):
        if i >= NT:
            return
        p = i % 2
        slot = i % IRING
        for which in ("q", "k"):
            bfb = b["qTb" if which == "q" else "kTb"][:, p * 512:(p + 1) * 512]
            deps0 = [qkproc_tok[i], ring_tok[i - 1] if i >= 1 else None,
                     (qT_ev_tok[i - 1] if i >= 1 else None)]
            for k in range(4):
                kk = k if which == "q" else 4 + k
                P.rec("t", lambda t, k=k, kk=kk, bfb=bfb: t.transpose(
                    pT[:, kk * 128:(kk + 1) * 128], bfb[:, k * 128:(k + 1) * 128],
                    ident[:, :]),
                    deps=(deps0 if (which == "q" and k == 0) else ()),
                    inc=(1 if k == 3 else 0), tag=f"tr{which}{i}")
        trqk_tok[i] = P.token("t")
        qT_ev_tok[i] = P.rec("v", lambda v, i=i: v.tensor_copy(
            b["qT"][:, (i % 4) * 512:(i % 4 + 1) * 512], pT[:, 0:512]),
            deps=[trqk_tok[i], attn_pe_tok[i - 3] if i >= 3 else None], tag=f"qTev{i}")
        ring_tok[i] = P.rec("v", lambda v, slot=slot: v.tensor_copy(
            b["kT"][:, slot * 512:(slot + 1) * 512], pT[:, 512:1024]),
            deps=[avl_dep(i)], self_deps=[qT_ev_tok[i]], tag=f"kTev{i}")

    def avl_dep(i):
        t = i - IRING + 4
        return attn_pe_tok[t] if t >= 0 else init_tok

    def slot_of(tt):
        return (tt % IRING) if tt >= 0 else ((tt + IRING) % IRING)

    def attn_qk(j):
        kdep = [ring_tok[j]]
        for h in range(IH):
            hh = h % 2
            c = h // 2
            pS = pS2[h % 2]
            if h >= 2:
                war = exp_tok[j][h - 2]
            elif j >= 1:
                war = exp_tok[j - 1][IH - 2 + h]
            else:
                war = None
            for tau in range(5):
                tt = j - 4 + tau
                slot = slot_of(tt)
                d = ()
                if tau == 0:
                    d = (list(kdep) if h == 0 else []) + [war, qT_ev_tok[j]]
                P.rec("t", lambda t, slot=slot, c=c, hh=hh, tau=tau, pS=pS, j=j: t.matmul(
                    pS[:, tau * 128:(tau + 1) * 128],
                    b["kT"][hh * 64:(hh + 1) * 64,
                            slot * 512 + c * 128:slot * 512 + c * 128 + 128],
                    b["qT"][hh * 64:(hh + 1) * 64,
                            (j % 4) * 512 + c * 128:(j % 4) * 512 + c * 128 + 128],
                    start=True, stop=True),
                    deps=d, inc=(1 if tau == 4 else 0), tag=f"st{j}_{h}")
            sttok = P.token("t")
            exp_tok[j][h] = P.rec("s", lambda s, h=h, pS=pS: s.activation(
                b["probs"][:, h * 640:(h + 1) * 640], pS[:, :], AF.Exp, bias=0.0, scale=1.0),
                deps=[sttok, av_tok[j - 1] if (h <= 1 and j >= 1) else None],
                tag=f"pexp{j}_{h}")
        zdeps = [exp_tok[j][IH - 1]]
        apL = bass.AP(b["probs"], 0, [[8 * 640, 128], [640, 8], [1, 128]])
        apH = bass.AP(b["probs"], 512, [[8 * 640, 128], [640, 8], [1, 128]])
        P.rec("v", lambda v, apL=apL: v.tensor_tensor(apL, apL, b["trimL"][:, :], ALU.mult),
              deps=zdeps, tag=f"zl{j}")
        ztok = P.rec("v", lambda v, apH=apH: v.tensor_tensor(apH, apH, b["trimH"][:, :], ALU.mult),
                     tag=f"zh{j}")
        if j <= 3:
            width = 512 - j * 128
            apE = bass.AP(b["probs"], 0, [[8 * 640, 128], [640, 8], [1, width]])
            ztok = P.rec("v", lambda v, apE=apE: v.memset(apE, 0.0),
                         self_deps=[ztok], tag=f"ze{j}")
        zero_tok[j] = ztok

    def attn_av(j):
        for r in range(2):
            pw = ipows[r]
            for hq in range(4):
                h = r * 4 + hq
                for tau in range(5):
                    tt = j - 4 + tau
                    slot = slot_of(tt)
                    d = []
                    if hq == 0 and tau == 0:
                        d = [zero_tok[j],
                             aevh_tok[j - 1][1 - r] if j >= 1 else None]
                        if dbg is not None and "_prtok" in dbg and j == 10:
                            d = d + [dbg["_prtok"]]
                    P.rec("t", lambda t, slot=slot, h=h, hq=hq, tau=tau, pw=pw: t.matmul(
                        pw[0:65, hq * 128:(hq + 1) * 128],
                        b["vr"][:, slot * 520 + h * 65:slot * 520 + h * 65 + 65],
                        b["probs"][:, h * 640 + tau * 128:h * 640 + (tau + 1) * 128],
                        start=(tau == 0), stop=(tau == 4)),
                        deps=d, inc=(1 if (hq == 3 and tau == 4) else 0), tag=f"av{j}_{r}")
            avtok = P.token("t")
            if r == 1:
                av_tok[j] = avtok
            ap_even_src = bass.AP(pw, 0, [[512, 64], [256, 2], [1, 128]])
            ap_odd_src = bass.AP(pw, 128, [[512, 64], [256, 2], [1, 128]])
            P.rec("s", lambda s, r=r, ap=ap_even_src: s.copy(
                b["oTf"][0:64, r * 256:(r + 1) * 256], ap), deps=[avtok], tag=f"oev{j}_{r}")
            P.rec("s", lambda s, r=r, ap=ap_odd_src: s.copy(
                b["oTf"][64:128, r * 256:(r + 1) * 256], ap), tag=f"oev2{j}_{r}")
            oev_tok[j] = P.rec("s", lambda s, r=r, pw=pw: s.copy(
                b["sums"][0:1, r * 512:(r + 1) * 512], pw[64:65, 0:512]), tag=f"sev{j}_{r}")
        if dbg is not None and "oT" in dbg:
            P.rec("y", lambda y, j=j: y.dma_start(
                dbg["oT"][j * 128:(j + 1) * 128, :], b["oTf"][:, :]),
                deps=[oev_tok[j]], channel=cp + f"d7{j % 2}", tag=f"dbgoT{j}")
            P.rec("y", lambda y, j=j: y.dma_start(
                dbg["sm"][j:j + 1, :], b["sums"][0:1, 0:1024]),
                deps=[oev_tok[j]], channel=cp + f"d8{j % 2}", tag=f"dbgsm{j}")
        rlntk = P.rec("s", lambda s: s.activation(b["sums"][0:1, 1024:2048],
                                                  b["sums"][0:1, 0:1024],
                                                  AF.Ln, bias=0.0, scale=1.0),
                      self_deps=[oev_tok[j]], tag=f"rln{j}")
        rex_tok[j] = P.rec("s", lambda s: s.activation(b["sums"][0:1, 1024:2048],
                                                       b["sums"][0:1, 1024:2048],
                                                       AF.Exp, bias=0.0, scale=-1.0),
                           self_deps=[rlntk], tag=f"rex{j}")

    def attn_fin(j):
        if j < 0:
            return
        p = j % 2
        d = [rex_tok[j]]
        if j >= 1:
            d.append(res_tok[j - 1])
        for rh in range(2):
            rhs = bass.AP(b["sums"], 1024 + rh * 128, [[2048, 1], [256, 4], [1, 128]])
            dd = (d + (list(glob_tok) if j == 0 else [])) if rh == 0 else ()
            P.rec("t", lambda t, rh=rh, rhs=rhs: t.matmul(
                pOW[rh * 64:(rh + 1) * 64, 0:512],
                glob["ones1f"][0:1, 0:64], rhs, start=True, stop=True),
                deps=dd, inc=(1 if rh == 1 else 0), tag=f"sbc{j}")
        btok = P.token("t")
        norm_tok[j] = P.rec("v", lambda v: v.tensor_tensor(
            b["oTn"][:, :], b["oTf"][:, :], pOW[:, :], ALU.mult),
            deps=[btok], tag=f"on{j}")
        ob = b["outb"][:, p * 1024:(p + 1) * 1024]
        x0 = b["x0"][:, (j % 4) * 1024:(j % 4 + 1) * 1024]
        aevh = [None, None]
        for half in range(2):
            pw = ipows[1 - half]
            for k in range(4):
                d = [norm_tok[j]] if (half == 0 and k == 0) else ()
                P.rec("t", lambda t, k=k, half=half, pw=pw: t.matmul(
                    pw[:, :], b["oTn"][:, k * 128:(k + 1) * 128],
                    b["wo"][:, k * 1024 + half * 512:k * 1024 + (half + 1) * 512],
                    start=(k == 0), stop=(k == 3)),
                    deps=d, inc=(1 if k == 3 else 0), tag=f"wo{j}_{half}")
            wotok = P.token("t")
            aevh[half] = P.rec("v", lambda v, ob=ob, x0=x0, half=half, pw=pw: v.tensor_tensor(
                ob[:, half * 512:(half + 1) * 512], pw[:, :],
                x0[:, half * 512:(half + 1) * 512], ALU.add),
                deps=[wotok, out_tok[j - 2] if (half == 0 and j >= 2) else None,
                      x0_tok[j]],
                tag=f"aev{j}_{half}")
        aevh_tok[j] = aevh
        attn_pe_tok[j] = P.token("t")
        res_tok[j] = aevh[1]
        out_tok[j] = P.rec("y", lambda y, j=j, ob=ob: y.dma_start(
            out_dram[j * 128:(j + 1) * 128, :], ob[:, :]),
            deps=[res_tok[j]], channel=cp + f"ot{j % 2}", tag=f"out{j}")

    for i in range(4):
        load_tile(i)
    produce_ln(0)
    produce(0)
    produce_b(0)
    produce_ln(1)
    produce(1)
    for j in range(NT):
        load_tile(j + 4)
        attn_qk(j)
        attn_fin(j - 1)
        attn_av(j)
        produce_b(j + 1)
        produce_ln(j + 2)
        produce(j + 2)
    attn_fin(NT - 1)
    return [out_tok[NT - 2], out_tok[NT - 1],
            P.token("t"), P.token("v"), P.token("s"), P.token("g")]


def iattn_host_weights(wqkv, q_scale, k_scale, wo, layer):
    l = layer
    p = np.arange(128)
    cL = np.arange(128)
    triL = (cL[None, :] <= p[:, None]).astype(np.float32)
    triH = (cL[None, :] >= p[:, None]).astype(np.float32)
    half = np.arange(0, 64, 2, dtype=np.float32)
    inv_freq = (1.0 / (10000.0 ** (half / 64.0))).astype(np.float32)
    t = np.arange(4096, dtype=np.float32)
    freqs = t[:, None] * inv_freq[None, :]
    freqs = np.concatenate([freqs, freqs], 1)
    base = ((half + 0.4 * 64.0) / (1.4 * 64.0)).astype(np.float32)
    power = (t - 2048.0) / 256.0
    scale = base[None, :] ** power[:, None]
    scale = np.concatenate([scale, scale], 1).astype(np.float32)
    cos = np.cos(freqs).astype(np.float32)
    sin = np.sin(freqs).astype(np.float32)
    sgn = np.concatenate([-np.ones(32, np.float32), np.ones(32, np.float32)])
    cosQ8 = 8.0 * cos * scale
    sinQ8s = 8.0 * sin * scale * sgn
    cosK = cos / scale
    sinKs = sin / scale * sgn
    rep = lambda a: np.tile(a, (1, 8))
    rot = np.concatenate([rep(cosQ8), rep(sinQ8s), rep(cosK), rep(sinKs)], axis=1)
    return {
        "wqkv": np.ascontiguousarray(np.asarray(wqkv[l], np.float32)),
        "wo": np.ascontiguousarray(np.asarray(wo[l], np.float32)),
        "qs": np.ascontiguousarray(np.broadcast_to(
            np.tile(np.asarray(q_scale[l], np.float32), 8)[None, :], (128, 512))),
        "ks": np.ascontiguousarray(np.broadcast_to(
            np.tile(np.asarray(k_scale[l], np.float32), 8)[None, :], (128, 512))),
        "rot": np.ascontiguousarray(rot.astype(np.float32)),
        "trimL": np.ascontiguousarray(np.tile(triL, (1, 8))),
        "trimH": np.ascontiguousarray(np.tile(triH, (1, 8))),
    }


# version 21 (final)
# speedup vs baseline: 1.2845x; 1.0088x over previous
import numpy as np
import ml_dtypes
import concourse.bass as bass
import concourse.mybir as mybir
from concourse.bass_utils import run_bass_kernel_spmd

F32 = mybir.dt.float32
BF16 = mybir.dt.bfloat16
AF = mybir.ActivationFunctionType
ALU = mybir.AluOpType
AX = mybir.AxisListType
bf16 = ml_dtypes.bfloat16

D = 1024
T = 4096
NT = T // 128
OH = 16
IH = 8
DH = 64
HID = 2730
NMC = 22
EPS = 1e-5

ENGS = ("t", "s", "v", "g", "y")


class Prog:

    def __init__(self, nc):
        self.nc = nc
        self.ops = []
        self.counts = {}
        for e in ENGS:
            self.counts[e] = 0

    def rec(self, eng, fn, deps=(), channel=None, inc=None, tag="", self_deps=()):
        sem_name = channel if channel is not None else eng
        if inc is None:
            inc = 16 if channel is not None else 1
        if sem_name not in self.counts:
            self.counts[sem_name] = 0
        deps = tuple(d for d in deps if d is not None)
        self_deps = tuple(d for d in self_deps if d is not None)
        self.ops.append((eng, fn, sem_name, inc, deps, self_deps, tag))
        if inc:
            self.counts[sem_name] += inc
            return (sem_name, self.counts[sem_name])
        return None

    def token(self, sem_name):
        c = self.counts.get(sem_name, 0)
        return (sem_name, c) if c else None

    def emit(self, block, final_waiter="y", serialize=False):
        finals = dict(self.counts)
        sems = {}
        guards = []
        for name in self.counts:
            g = self.nc.semaphore("sem_" + name)
            sems[name] = g.__enter__()
            guards.append(g)

        if serialize:
            run_counts = {n: 0 for n in self.counts}
            ser_deps = []
            for (k, fn, sem_name, inc, deps, self_deps, tag) in self.ops:
                ser_deps.append([(n, v) for n, v in run_counts.items() if v])
                if inc:
                    run_counts[sem_name] += inc
        else:
            ser_deps = [list(deps) for (_, _, _, _, deps, _, _) in self.ops]

        def emitter(key):
            def run(eng):
                seen = {}
                for (k, fn, sem_name, inc, deps, self_deps, tag), sdeps in zip(self.ops, ser_deps):
                    if k != key:
                        continue
                    nwait = 0
                    for (dn, dv) in sdeps:
                        if dn == key:
                            continue
                        if seen.get(dn, 0) < dv:
                            if nwait:
                                eng.nop()
                            eng.wait_ge(sems[dn], dv)
                            seen[dn] = dv
                            nwait += 1
                    for (dn, dv) in self_deps:
                        if nwait:
                            eng.nop()
                        eng.wait_ge(sems[dn], dv)
                        nwait += 1
                    instr = fn(eng)
                    if inc:
                        instr.then_inc(sems[sem_name], inc)
                if key == final_waiter:
                    for n2, v2 in finals.items():
                        if v2:
                            eng.wait_ge(sems[n2], v2)
            return run

        block.tensor(emitter("t"))
        block.scalar(emitter("s"))
        block.vector(emitter("v"))
        block.gpsimd(emitter("g"))
        block.sync(emitter("y"))
        return guards



ST = 256
NSUB = ST // 128
NST = T // ST


def alloc_ffn_bufs(nc, tagp, with_final):
    gs = []
    bufs = {}

    def sb(name, shape, dt):
        g = nc.sbuf_tensor(tagp + name, shape, dt)
        t = g.__enter__()
        gs.append(g)
        bufs[name] = t
        return t

    sb("w1", [128, 8 * 5460], BF16)
    sb("w2", [128, NMC * 1024], BF16)
    sb("b1a", [128, NMC], F32)
    sb("b1g", [128, NMC], F32)
    sb("b2row", [1, 1024], BF16)
    sb("grep", [128, 1024], F32)
    sb("brep", [128, 1024], F32)
    sb("xin", [128, 5 * 1024], F32)
    sb("hsh", [128, 2 * 512], F32)
    sb("stats", [128, 4 * 16], F32)
    sb("hln", [128, 2 * 1024], BF16)
    sb("hlnT", [128, 2 * (8 * ST)], BF16)
    sb("tmp", [128, 2 * 1024], F32)
    sb("gel", [128, 2 * ST], BF16)
    sb("prod", [128, 2 * ST], BF16)
    sb("outb", [128, 2 * 1024], F32)
    if with_final:
        sb("x2ring", [128, 3 * 1024], F32)
    return gs, bufs


def alloc_ffn_psums(nc, tagp):
    gs = []
    ps = {}
    pT = []
    for i in range(2):
        g = nc.psum_tensor(tagp + f"pT{i}", [128, 1024], BF16)
        pT.append(g.__enter__()); gs.append(g)
    pag = []
    for i in range(2):
        g = nc.psum_tensor(tagp + f"pag{i}", [128, 512], F32)
        pag.append(g.__enter__()); gs.append(g)
    pout = []
    for i in range(4):
        g = nc.psum_tensor(tagp + f"pout{i}", [128, 512], F32)
        pout.append(g.__enter__()); gs.append(g)
    ps["pT"], ps["pag"], ps["pout"] = pT, pag, pout
    return gs, ps


def record_ffn_phase(P, nc, glob, psums, bufs, wdram, in_dram, out_dram,
                     final=None, weights_dep=None, cp="f0", glob_tok=(), phase_dep=()):
    b = bufs
    pT, pag, pout = psums["pT"], psums["pag"], psums["pout"]
    ident, epsv, ones1 = glob["ident"], glob["eps"], glob["ones1"]

    for k in range(8):
        P.rec("g", lambda g, k=k: g.dma_start(
            b["w1"][:, k * 5460:(k + 1) * 5460], wdram["w1"][k * 128:(k + 1) * 128, :]),
            deps=(([weights_dep] if weights_dep else []) + list(phase_dep)) if k < 4 else (),
            channel=cp + f"w{k % 4}", tag="w1ld")
    for ni, name in enumerate(("b1a", "b1g", "b2row", "grep", "brep")):
        P.rec("g", lambda g, name=name: g.dma_start(b[name][:, :], wdram[name][:, :]),
              deps=list(phase_dep) if ni < 4 else (),
              channel=cp + f"u{ni % 4}", tag=name + "ld")
    for mc in range(NMC):
        rows = 42 if mc == NMC - 1 else 128
        P.rec("g", lambda g, mc=mc, rows=rows: g.dma_start(
            b["w2"][0:rows, mc * 1024:(mc + 1) * 1024], wdram["w2"][mc * 128:mc * 128 + rows, :]),
            channel=cp + f"u{mc % 4}", tag="w2ld")
    wtok4 = [P.token(cp + f"w{c}") for c in range(4)]
    utok4 = [P.token(cp + f"u{c}") for c in range(4)]
    wtok = wtok4[0]

    xin_tok = [None] * (NT + 8)
    shift_tok = [None] * (NT + 8)
    tr_tok = [None] * NT
    lnv_tok = [None] * NT
    srstd_tok = [None] * NT
    trev_tok = [None] * NT
    resid_tok = [None] * NT
    out_tok = [None] * NT
    st_pe_tok = [None] * NST
    x2_tok = [None] * NT
    dump_tok = [[] for _ in range(NT)]

    def load_tile(j):
        if j >= NT:
            return
        dep = [resid_tok[j - 5], shift_tok[j - 4]] if j >= 5 else list(phase_dep)
        xin_tok[j] = P.rec("y", lambda y, j=j: y.dma_start(
            b["xin"][:, (j % 5) * 1024:(j % 5 + 1) * 1024], in_dram[j * 128:(j + 1) * 128, :]),
            deps=dep, channel=cp + f"xi{j % 5}", tag=f"xin{j}")

    def load_x2(j):
        if final is None or "x2_dram" not in final or j >= NT:
            return
        dep = [resid_tok[j - 3]] if j >= 3 else list(phase_dep)
        x2_tok[j] = P.rec("y", lambda y, j=j: y.dma_start(
            b["x2ring"][:, (j % 3) * 1024:(j % 3 + 1) * 1024],
            final["x2_dram"][j * 128:(j + 1) * 128, :]),
            deps=dep, channel=cp + f"x2{j % 3}", tag=f"x2in{j}")

    def ln_tile(j):
        p = j % 2
        xsl = b["xin"][:, (j % 5) * 1024:(j % 5 + 1) * 1024]
        hsh = b["hsh"][:, p * 512:(p + 1) * 512]
        dshift = [xin_tok[j], lnv_tok[j - 2] if j >= 2 else None] + \
            (dump_tok[j - 2] if j >= 2 else [])
        t1 = P.rec("g", lambda g, xsl=xsl, hsh=hsh: g.dma_start(
            hsh[1:128, :], xsl[0:127, 512:1024]), deps=dshift,
            channel=cp + f"sa{j % 2}", tag=f"sh{j}")
        if j == 0:
            t2 = P.rec("g", lambda g, hsh=hsh: g.memset(hsh[0:1, :], 0.0), tag="sh0z")
        else:
            xprev = b["xin"][:, ((j - 1) % 5) * 1024:((j - 1) % 5 + 1) * 1024]
            t2 = P.rec("g", lambda g, xprev=xprev, hsh=hsh: g.dma_start(
                hsh[0:1, :], xprev[127:128, 512:1024]),
                deps=[xin_tok[j - 1]], channel=cp + f"sb{j % 2}", tag=f"shp{j}")
        shift_tok[j] = t2
        stt = b["stats"][:, p * 16:(p + 1) * 16]

        def bn_pair(v, xsl=xsl, hsh=hsh, stt=stt):
            v.bn_stats(stt[:, 0:6], xsl[:, 0:512])
            return v.bn_stats(stt[:, 6:12], hsh[:, :])

        pair_tok = P.rec("v", bn_pair,
                         deps=[xin_tok[j], t1, t2] + (dump_tok[j - 2] if j >= 2 else []),
                         tag=f"bn{j}")
        vagg = P.rec("v", lambda v, stt=stt: v.bn_aggr(stt[:, 12:14], stt[:, 0:12]),
                     self_deps=[pair_tok], tag=f"bnag{j}")
        P.rec("s", lambda s, stt=stt: s.activation(stt[:, 14:15], stt[:, 13:14], AF.Ln,
                                                   bias=epsv[:, :], scale=1.0),
              deps=[vagg] + (list(glob_tok) if j < 2 else []), tag=f"ln{j}")
        lntk = P.token("s")
        srstd_tok[j] = P.rec("s", lambda s, stt=stt: s.activation(
            stt[:, 14:15], stt[:, 14:15], AF.Exp, bias=0.0, scale=-0.5),
            self_deps=[lntk], tag=f"exp{j}")
        tmp = b["tmp"][:, p * 1024:(p + 1) * 1024]
        hln = b["hln"][:, p * 1024:(p + 1) * 1024]
        P.rec("v", lambda v, xsl=xsl, tmp=tmp, stt=stt: v.tensor_scalar(
            tmp[:, 0:512], xsl[:, 0:512], stt[:, 12:13], stt[:, 14:15], ALU.subtract, ALU.mult),
            deps=[srstd_tok[j]], tag=f"apA{j}")
        P.rec("v", lambda v, hsh=hsh, tmp=tmp, stt=stt: v.tensor_scalar(
            tmp[:, 512:1024], hsh[:, :], stt[:, 12:13], stt[:, 14:15], ALU.subtract, ALU.mult),
            tag=f"apB{j}")
        P.rec("v", lambda v, tmp=tmp: v.tensor_tensor(tmp[:, :], tmp[:, :], b["grep"][:, :], ALU.mult),
              deps=(utok4 if j < 3 else ()), tag=f"apG{j}")
        lnv_tok[j] = P.rec("v", lambda v, tmp=tmp, hln=hln: v.tensor_tensor(
            hln[:, :], tmp[:, :], b["brep"][:, :], ALU.add),
            deps=[tr_tok[j - 2] if j >= 2 else None] +
                 (dump_tok[j - 2] if j >= 2 else []), tag=f"apBt{j}")
        if final is not None and "dbg_hln" in final:
            dump_tok[j] = [
                P.rec("y", lambda y, j=j, hln=hln: y.dma_start(
                    final["dbg_hln"][j * 128:(j + 1) * 128, :], hln[:, :]),
                    deps=[lnv_tok[j]], channel=cp + f"db{j % 2}", tag=f"dbg{j}"),
                P.rec("y", lambda y, j=j, hsh=hsh: y.dma_start(
                    final["dbg_hsh"][j * 128:(j + 1) * 128, :], hsh[:, :]),
                    deps=[lnv_tok[j]], channel=cp + f"dc{j % 2}", tag=f"dbgh{j}"),
                P.rec("y", lambda y, j=j, stt=stt: y.dma_start(
                    final["dbg_st"][j * 128:(j + 1) * 128, :], stt[:, 12:15]),
                    deps=[lnv_tok[j]], channel=cp + f"dd{j % 2}", tag=f"dbgs{j}")]

    def transpose_tile(j):
        p = j % 2
        st = j // NSUB
        stp = st % 2
        hln = b["hln"][:, p * 1024:(p + 1) * 1024]
        ptb = pT[p]
        deps = [lnv_tok[j], trev_tok[j - 2] if j >= 2 else None,
                st_pe_tok[st - 2] if st >= 2 else None] + (list(glob_tok) if j < 2 else [])
        for k in range(8):
            P.rec("t", lambda t, k=k, hln=hln, ptb=ptb: t.transpose(
                ptb[:, k * 128:(k + 1) * 128], hln[:, k * 128:(k + 1) * 128], ident[:, :]),
                deps=(deps if k == 0 else ()), inc=(1 if k == 7 else 0), tag=f"tr{j}")
        ttok = P.token("t")
        tr_tok[j] = ttok
        sub = j % NSUB
        base = stp * (8 * ST)
        dstap = bass.AP(b["hlnT"], base + sub * 128, [[2 * 8 * ST, 128], [ST, 8], [1, 128]])
        trev_tok[j] = P.rec("v", lambda v, ptb=ptb, dstap=dstap: v.tensor_copy(dstap, ptb[:, :]),
                            deps=[ttok], tag=f"trev{j}")

    def mc_loop(st):
        stp = st % 2
        base = stp * (8 * ST)
        prev_resid = resid_tok[st * NSUB - 1] if st >= 1 else None
        gel_tok = [None, None]
        prod_tok = [None, None]

        def emit_w2(mc):
            mp = mc % 2
            rows = 42 if mc == NMC - 1 else 128
            prod = b["prod"][:, mp * ST:(mp + 1) * ST]
            for sub in range(NSUB):
                for nh in range(2):
                    d = [prod_tok[mp]]
                    if st == 0 and sub == 0 and nh == 0:
                        d = d + utok4
                    if mc == 0 and sub == 0 and nh == 0 and prev_resid:
                        d = d + [prev_resid]
                    P.rec("t", lambda t, mc=mc, sub=sub, nh=nh, rows=rows, prod=prod:
                          t.matmul(pout[sub * 2 + nh][:, :],
                                   prod[0:rows, sub * 128:(sub + 1) * 128],
                                   b["w2"][0:rows, mc * 1024 + nh * 512:mc * 1024 + (nh + 1) * 512],
                                   start=(mc == 0), stop=False),
                          deps=(d if (sub == 0 and nh == 0) else ()), inc=0, tag=f"w2{st}_{mc}")

        for mc in range(NMC):
            mp = mc % 2
            rows = 42 if mc == NMC - 1 else 128
            pa = pag[mp]
            deps = [trev_tok[st * NSUB], trev_tok[st * NSUB + 1], prod_tok[mp]]
            if mc < 2:
                deps = deps + wtok4
            for half, cofs in ((0, 0), (1, 2730)):
                for k in range(8):
                    last = (half == 1 and k == 7)
                    P.rec("t", lambda t, k=k, mc=mc, rows=rows, half=half, cofs=cofs, pa=pa:
                          t.matmul(pa[0:rows, half * 256:half * 256 + ST],
                                   b["w1"][:, k * 5460 + cofs + mc * 128:
                                           k * 5460 + cofs + mc * 128 + rows],
                                   bass.AP(b["hlnT"], base + k * ST, [[2 * 8 * ST, 128], [1, ST]]),
                                   start=(k == 0), stop=(k == 7)),
                          deps=(deps if (half == 0 and k == 0) else ()),
                          inc=(1 if last else 0), tag=f"ag{st}_{mc}")
            agtok = P.token("t")
            gel = b["gel"][:, mp * ST:(mp + 1) * ST]
            gel_tok[mp] = P.rec("s", lambda s, pa=pa, gel=gel, mc=mc, rows=rows: s.activation(
                gel[0:rows, :], pa[0:rows, 256:512], AF.Gelu,
                bias=b["b1g"][0:rows, mc:mc + 1], scale=1.0),
                deps=[agtok, prod_tok[mp]] + (utok4 if (st == 0 and mc < 2) else []),
                tag=f"gel{st}_{mc}")
            prod = b["prod"][:, mp * ST:(mp + 1) * ST]
            prod_tok[mp] = P.rec("v", lambda v, pa=pa, prod=prod, gel=gel, mc=mc, rows=rows:
                                 v.scalar_tensor_tensor(prod[0:rows, :], pa[0:rows, 0:256],
                                                        b["b1a"][0:rows, mc:mc + 1], gel[0:rows, :],
                                                        ALU.add, ALU.mult),
                                 deps=[gel_tok[mp]], tag=f"prod{st}_{mc}")
            if mc >= 1:
                emit_w2(mc - 1)
        emit_w2(NMC - 1)
        for sub in range(NSUB):
            for nh in range(2):
                last = (sub == NSUB - 1 and nh == 1)
                P.rec("t", lambda t, sub=sub, nh=nh: t.matmul(
                    pout[sub * 2 + nh][:, :], ones1[:, :],
                    b["b2row"][:, nh * 512:(nh + 1) * 512], start=False, stop=True),
                    deps=(list(glob_tok) if (st == 0 and sub == 0 and nh == 0) else ()),
                    inc=(1 if last else 0), tag=f"b2{st}")
        st_pe_tok[st] = P.token("t")

    def evac_st(st):
        for sub in range(NSUB):
            j = st * NSUB + sub
            p = j % 2
            xsl = b["xin"][:, (j % 5) * 1024:(j % 5 + 1) * 1024]
            ob = b["outb"][:, p * 1024:(p + 1) * 1024]
            deps = [st_pe_tok[st], out_tok[j - 2] if j >= 2 else None]
            for nh in range(2):
                P.rec("v", lambda v, sub=sub, nh=nh, xsl=xsl, ob=ob: v.tensor_tensor(
                    ob[:, nh * 512:(nh + 1) * 512], pout[sub * 2 + nh][:, :],
                    xsl[:, nh * 512:(nh + 1) * 512], ALU.add),
                    deps=(deps if nh == 0 else ()), tag=f"res{j}")
            if final is None or "x2_dram" not in final:
                resid_tok[j] = P.token("v")
            else:
                stt = b["stats"][:, 32 + p * 16:32 + (p + 1) * 16]
                def fbn_pair(v, ob=ob, stt=stt):
                    v.bn_stats(stt[:, 0:6], ob[:, 0:512])
                    return v.bn_stats(stt[:, 6:12], ob[:, 512:1024])
                fpt = P.rec("v", fbn_pair, self_deps=[P.token("v")], tag=f"fbn{j}")
                vt = P.rec("v", lambda v, stt=stt: v.bn_aggr(stt[:, 12:14], stt[:, 0:12]),
                           self_deps=[fpt], tag=f"fba{j}")
                P.rec("s", lambda s, stt=stt: s.activation(stt[:, 14:15], stt[:, 13:14], AF.Ln,
                                                           bias=epsv[:, :], scale=1.0),
                      deps=[vt], tag=f"fln{j}")
                flntk = P.token("s")
                fs = P.rec("s", lambda s, stt=stt: s.activation(stt[:, 14:15], stt[:, 14:15],
                                                                AF.Exp, bias=0.0, scale=-0.5),
                           self_deps=[flntk], tag=f"fex{j}")
                tmp = b["tmp"][:, p * 1024:(p + 1) * 1024]
                P.rec("v", lambda v, ob=ob, tmp=tmp, stt=stt: v.tensor_scalar(
                    tmp[:, :], ob[:, :], stt[:, 12:13], stt[:, 14:15], ALU.subtract, ALU.mult),
                    deps=[fs, x2_tok[j]], tag=f"fap{j}")
                resid_tok[j] = P.rec("v", lambda v, ob=ob, tmp=tmp, j=j: v.tensor_tensor(
                    ob[:, :], tmp[:, :], b["x2ring"][:, (j % 3) * 1024:(j % 3 + 1) * 1024], ALU.add),
                    tag=f"fx2{j}")
            out_tok[j] = P.rec("y", lambda y, j=j, ob=ob: y.dma_start(
                out_dram[j * 128:(j + 1) * 128, :], ob[:, :]),
                deps=[resid_tok[j]], channel=cp + f"ot{j % 2}", tag=f"out{j}")

    for j in range(3):
        load_tile(j)
    load_x2(0)
    for j in range(3):
        ln_tile(j)
        transpose_tile(j)
    for st in range(NST):
        for j in (2 * st + 3, 2 * st + 4):
            load_tile(j)
        for j in (2 * st + 1, 2 * st + 2):
            load_x2(j)
        mc_loop(st)
        for j in (2 * st + 3, 2 * st + 4):
            if j < NT:
                ln_tile(j)
                transpose_tile(j)
        evac_st(st)
    return [out_tok[NT - 2], out_tok[NT - 1]]



def ffn_host_weights(ln_g, ln_b, w1, b1, w2, b2, layer):
    l = layer
    b1l = np.asarray(b1[l], np.float32)
    b1a = np.zeros((128, NMC), np.float32)
    b1g = np.zeros((128, NMC), np.float32)
    for mc in range(NMC):
        rows = 42 if mc == NMC - 1 else 128
        b1a[0:rows, mc] = b1l[mc * 128:mc * 128 + rows]
        b1g[0:rows, mc] = b1l[2730 + mc * 128:2730 + mc * 128 + rows]
    return {
        "w1": np.ascontiguousarray(np.asarray(w1[l], np.float32)),
        "w2": np.ascontiguousarray(np.asarray(w2[l], np.float32)),
        "b1a": b1a,
        "b1g": b1g,
        "b2row": np.ascontiguousarray(np.asarray(b2[l], np.float32)[None, :]),
        "grep": np.ascontiguousarray(np.broadcast_to(np.asarray(ln_g[l], np.float32)[None, :], (128, 1024))),
        "brep": np.ascontiguousarray(np.broadcast_to(np.asarray(ln_b[l], np.float32)[None, :], (128, 1024))),
    }


def alloc_globals(nc):
    gs = []
    glob = {}
    g1 = nc.sbuf_tensor("g_ident", [128, 128], BF16)
    glob["ident"] = g1.__enter__(); gs.append(g1)
    g2 = nc.sbuf_tensor("g_eps", [128, 1], F32)
    glob["eps"] = g2.__enter__(); gs.append(g2)
    g3 = nc.sbuf_tensor("g_ones1", [1, 128], BF16)
    glob["ones1"] = g3.__enter__(); gs.append(g3)
    g4 = nc.sbuf_tensor("g_ones1f", [1, 128], F32)
    glob["ones1f"] = g4.__enter__(); gs.append(g4)
    g5 = nc.sbuf_tensor("g_dum", [128, 512], BF16)
    glob["dum"] = g5.__enter__(); gs.append(g5)
    return gs, glob


def record_globals(P, nc, glob, ident_dram):
    t1 = P.rec("g", lambda g: g.dma_start(glob["ident"][:, :], ident_dram[:, :]),
               channel="gw", tag="identld")
    P.rec("g", lambda g: g.memset(glob["eps"][:, :], EPS), tag="epsset")
    P.rec("g", lambda g: g.memset(glob["ones1"][:, :], 1.0), tag="onesset")
    P.rec("g", lambda g: g.memset(glob["dum"][:, :], 0.0), tag="dumset")
    t3 = P.rec("g", lambda g: g.memset(glob["ones1f"][:, :], 1.0), tag="onesfset")
    return [t1, t3]


def record_ham_filler(P, nc, glob, psum_ap, n, deps):
    for q in range(n):
        P.rec("t", lambda t: t.matmul(psum_ap, glob["dum"][:, 0:128],
                                      glob["dum"][:, :], start=True, stop=True),
              deps=(list(deps) if q == 0 else ()), inc=0, tag="hamfill")


def host_ident():
    return np.eye(128, dtype=np.float32)


def build_core_program():
    nc = bass.Bass("TRN2", target_bir_lowering=False, debug=False)
    x_in = nc.dram_tensor("x", [T, D], F32, kind="ExternalInput")
    ident = nc.dram_tensor("cident", [128, 128], F32, kind="ExternalInput")
    out = nc.dram_tensor("out", [T, D], F32, kind="ExternalOutput")
    x2 = nc.dram_tensor("x2s", [T, D], F32)
    m1 = nc.dram_tensor("m1s", [T, D], F32)
    a1 = nc.dram_tensor("a1s", [T, D], F32)
    m2 = nc.dram_tensor("m2s", [T, D], F32)

    shapes_attn = {"wqkvT": (1024, 3072), "woT": (1024, 1024),
                   "trimL": (128, 2048), "trimH": (128, 2048)}
    shapes_iatt = {"wqkv": (1024, 1536), "wo": (512, 1024), "qs": (128, 512),
                   "ks": (128, 512), "rot": (4096, 2048),
                   "trimL": (128, 1024), "trimH": (128, 1024)}
    shapes_ffn = {"w1": (1024, 5460), "w2": (2730, 1024), "b1a": (128, NMC),
                  "b1g": (128, NMC), "b2row": (1, 1024), "grep": (128, 1024),
                  "brep": (128, 1024)}
    wd = {}
    for pfx, shapes in (("p1", shapes_attn), ("p2", shapes_iatt), ("p3", shapes_ffn),
                        ("p4", shapes_iatt), ("p5", shapes_ffn)):
        wd[pfx] = {name: nc.dram_tensor(f"{pfx}_{name}", list(sh), F32, kind="ExternalInput")
                   for name, sh in shapes.items()}

    P = Prog(nc)
    gg, glob = alloc_globals(nc)
    gtok = record_globals(P, nc, glob, ident)

    bg1, b1_ = alloc_attn_bufs(nc, "a1")
    pg1, ps1 = alloc_attn_psums(nc, "a1")
    end1 = record_attn_outer(P, nc, glob, ps1, b1_, wd["p1"], x_in, x2, cp="a1",
                             glob_tok=gtok)
    for g in reversed(bg1 + pg1):
        g.__exit__(None, None, None)
    bg2, b2_ = alloc_iattn_bufs(nc, "i2")
    pg2, ps2 = alloc_iattn_psums(nc, "i2")
    record_ham_filler(P, nc, glob, ps2["pS0"][:, 0:512], 450, end1)
    end2 = record_attn_inner(P, nc, glob, ps2, b2_, wd["p2"], x2, m1, cp="i2",
                             glob_tok=gtok, with_ln=True, phase_dep=end1)
    for g in reversed(bg2 + pg2):
        g.__exit__(None, None, None)
    bg3, b3_ = alloc_ffn_bufs(nc, "f3", with_final=False)
    pg3, ps3 = alloc_ffn_psums(nc, "f3")
    record_ham_filler(P, nc, glob, ps3["pag"][0][:, 0:512], 450, end2)
    end3 = record_ffn_phase(P, nc, glob, ps3, b3_, wd["p3"], m1, a1,
                            cp="f3", glob_tok=gtok, phase_dep=end2)
    for g in reversed(bg3 + pg3):
        g.__exit__(None, None, None)
    bg4, b4_ = alloc_iattn_bufs(nc, "i4")
    pg4, ps4 = alloc_iattn_psums(nc, "i4")
    record_ham_filler(P, nc, glob, ps4["pS0"][:, 0:512], 450, end3)
    end4 = record_attn_inner(P, nc, glob, ps4, b4_, wd["p4"], a1, m2, cp="i4",
                             glob_tok=gtok, with_ln=False, phase_dep=end3)
    for g in reversed(bg4 + pg4):
        g.__exit__(None, None, None)
    bg5, b5_ = alloc_ffn_bufs(nc, "f5", with_final=True)
    pg5, ps5 = alloc_ffn_psums(nc, "f5")
    record_ham_filler(P, nc, glob, ps5["pag"][0][:, 0:512], 450, end4)
    record_ffn_phase(P, nc, glob, ps5, b5_, wd["p5"], m2, out,
                     final={"x2_dram": x2}, cp="f5", glob_tok=gtok, phase_dep=end4)
    for g in reversed(bg5 + pg5):
        g.__exit__(None, None, None)

    with nc.Block() as block:
        semguards = P.emit(block)
    for g in reversed(semguards + gg):
        g.__exit__(None, None, None)
    return nc, len(P.ops)


def host_weight_maps(L0_attn_W_qkv, L0_attn_W_o, L0_ffn_wqkv, L0_ffn_q_scale,
                     L0_ffn_k_scale, L0_ffn_wo, L0_ffn_ln_g, L0_ffn_ln_b,
                     L0_ffn_w1, L0_ffn_b1, L0_ffn_w2, L0_ffn_b2):
    m = {}
    for name, arr in attn_host_weights(L0_attn_W_qkv, L0_attn_W_o).items():
        m[f"p1_{name}"] = arr
    for l, pfx in ((0, "p2"), (1, "p4")):
        for name, arr in iattn_host_weights(L0_ffn_wqkv, L0_ffn_q_scale,
                                            L0_ffn_k_scale, L0_ffn_wo, l).items():
            m[f"{pfx}_{name}"] = arr
    for l, pfx in ((0, "p3"), (1, "p5")):
        for name, arr in ffn_host_weights(L0_ffn_ln_g, L0_ffn_ln_b, L0_ffn_w1,
                                          L0_ffn_b1, L0_ffn_w2, L0_ffn_b2, l).items():
            m[f"{pfx}_{name}"] = arr
    m["cident"] = host_ident()
    return m


def kernel(L0_attn_x, L0_attn_W_qkv, L0_attn_W_o, L0_ffn_wqkv, L0_ffn_q_scale,
           L0_ffn_k_scale, L0_ffn_wo, L0_ffn_ln_g, L0_ffn_ln_b, L0_ffn_w1,
           L0_ffn_b1, L0_ffn_w2, L0_ffn_b2):
    x = np.ascontiguousarray(np.asarray(L0_attn_x, np.float32))
    wm = host_weight_maps(L0_attn_W_qkv, L0_attn_W_o, L0_ffn_wqkv, L0_ffn_q_scale,
                          L0_ffn_k_scale, L0_ffn_wo, L0_ffn_ln_g, L0_ffn_ln_b,
                          L0_ffn_w1, L0_ffn_b1, L0_ffn_w2, L0_ffn_b2)
    nc, nops = build_core_program()
    in_maps = []
    for c in range(8):
        im = dict(wm)
        im["x"] = np.ascontiguousarray(x[c])
        in_maps.append(im)
    res = run_bass_kernel_spmd(nc, in_maps, core_ids=list(range(8)))
    out = np.stack([np.asarray(res.results[c]["out"], np.float32) for c in range(8)], axis=0)
    return out



RING = 6


def alloc_attn_bufs(nc, tagp):
    gs = []
    bufs = {}

    def sb(name, shape, dt):
        g = nc.sbuf_tensor(tagp + name, shape, dt)
        t = g.__enter__()
        gs.append(g)
        bufs[name] = t
        return t

    sb("wqkv", [128, 8 * 3072], BF16)
    sb("wo", [128, 8 * 1024], BF16)
    sb("xin", [128, RING * 1024], F32)
    sb("stats", [128, 4 * 16], F32)
    sb("h1", [128, 2 * 1024], BF16)
    sb("h1T", [128, 2 * 1024], BF16)
    sb("qkt", [128, 2 * 2048], BF16)
    sb("qT", [128, 4 * 1024], BF16)
    sb("kT", [128, RING * 1024], BF16)
    sb("vr", [128, RING * 1040], BF16)
    sb("probs", [128, 16 * 640], BF16)
    sb("trimL", [128, 16 * 128], BF16)
    sb("trimH", [128, 16 * 128], BF16)
    sb("oTf", [128, 1024], F32)
    sb("sums", [1, 4096], F32)
    
    sb("oTn", [128, 1024], BF16)
    sb("abuf", [128, 2 * 1024], F32)
    sb("outb", [128, 2 * 1024], F32)
    return gs, bufs


def alloc_attn_psums(nc, tagp):
    gs = []
    ps = {}

    def palloc(name, shape, dt):
        g = nc.psum_tensor(tagp + name, shape, dt)
        t = g.__enter__()
        gs.append(g)
        ps[name] = t
        return t

    palloc("pS0", [128, 640], F32)
    palloc("pS1", [128, 640], F32)
    palloc("pQ", [128, 1024], F32)
    palloc("pOW", [128, 512], F32)
    palloc("pOW2", [128, 512], F32)
    ps["pT"] = ps["pQ"].bitcast(BF16)
    return gs, ps


def record_attn_outer(P, nc, glob, ps, b, wdram, in_dram, out_dram, cp="a1",
                      glob_tok=(), weights_dep=None, dbg=None, phase_dep=()):
    ident, epsv, ones1 = glob["ident"], glob["eps"], glob["ones1f"]
    pS2, pQ, pOW, pT = [ps["pS0"], ps["pS1"]], ps["pQ"], ps["pOW"], ps["pT"]
    pOW2 = ps["pOW2"]
    pows = [pOW, pOW2]

    for k in range(8):
        P.rec("g", lambda g, k=k: g.dma_start(
            b["wqkv"][:, k * 3072:(k + 1) * 3072], wdram["wqkvT"][k * 128:(k + 1) * 128, :]),
            deps=(([weights_dep] if weights_dep else []) + list(phase_dep)) if k < 4 else (),
            channel=cp + f"w{k % 4}", tag="wqkvld")
    for k in range(8):
        P.rec("g", lambda g, k=k: g.dma_start(
            b["wo"][:, k * 1024:(k + 1) * 1024], wdram["woT"][k * 128:(k + 1) * 128, :]),
            channel=cp + f"w{k % 4}", tag="wold")
    for ni, name in enumerate(("trimL", "trimH")):
        P.rec("g", lambda g, name=name: g.dma_start(b[name][:, :], wdram[name][:, :]),
              channel=cp + f"w{ni % 4}", tag=name)
    vzt = P.rec("g", lambda g: g.memset(b["vr"][:, :], 0.0), tag="vz")
    P.rec("g", lambda g: g.memset(
        bass.AP(b["vr"], 64, [[RING * 1040, 128], [65, 16 * RING], [1, 1]]), 1.0),
        self_deps=[vzt], tag="vones")
    P.rec("g", lambda g: g.memset(b["kT"][:, 4 * 1024:6 * 1024], 0.0), tag="kz")
    wtok4 = [P.token(cp + f"w{c}") for c in range(4)]
    wtok = wtok4[0]
    init_tok = P.token("g")

    xin_tok = [None] * (NT + 8)
    vmm_tok = [None] * NT
    h1_tok = [None] * NT
    trh_tok = [None] * NT
    h1T_tok = [None] * NT
    qkv_pe_tok = [None] * NT
    qkev_tok = [None] * NT
    trqk_tok = [None] * NT
    ring_tok = [None] * NT
    exp_tok = [[None] * OH for _ in range(NT)]
    zero_tok = [None] * NT
    zgrp_tok = [[None] * 4 for _ in range(NT)]
    oevv_tok = [None] * NT
    aevh_tok = [[None, None] for _ in range(NT)]
    rex_tok = [None] * NT
    av_tok = [None] * NT
    oev_tok = [None] * NT
    norm_tok = [None] * NT
    attn_pe_tok = [None] * NT
    res_tok = [None] * NT
    out_tok = [None] * NT

    def load_tile(i):
        if i >= NT:
            return
        dep = [res_tok[i - RING]] if i >= RING else list(phase_dep)
        xin_tok[i] = P.rec("y", lambda y, i=i: y.dma_start(
            b["xin"][:, (i % RING) * 1024:(i % RING + 1) * 1024],
            in_dram[i * 128:(i + 1) * 128, :]),
            deps=dep, channel=cp + f"xi{i % RING}", tag=f"xin{i}")

    def produce_ln(i):
        if i >= NT:
            return
        p = i % 2
        xsl = b["xin"][:, (i % RING) * 1024:(i % RING + 1) * 1024]
        stt = b["stats"][:, p * 16:(p + 1) * 16]

        def bn_pair(v, xsl=xsl, stt=stt):
            v.bn_stats(stt[:, 0:6], xsl[:, 0:512])
            return v.bn_stats(stt[:, 6:12], xsl[:, 512:1024])
        pr = P.rec("v", bn_pair, deps=[xin_tok[i]], tag=f"bn{i}")
        vagg = P.rec("v", lambda v, stt=stt: v.bn_aggr(stt[:, 12:14], stt[:, 0:12]),
                     self_deps=[pr], tag=f"bnag{i}")
        P.rec("s", lambda s, stt=stt: s.activation(stt[:, 14:15], stt[:, 13:14], AF.Ln,
                                                   bias=epsv[:, :], scale=1.0),
              deps=[vagg] + (list(glob_tok) if i < 2 else []), tag=f"ln{i}")
        lntk = P.token("s")
        srstd = P.rec("s", lambda s, stt=stt: s.activation(
            stt[:, 14:15], stt[:, 14:15], AF.Exp, bias=0.0, scale=-0.5),
            self_deps=[lntk], tag=f"exp{i}")
        h1 = b["h1"][:, p * 1024:(p + 1) * 1024]
        h1_tok[i] = P.rec("v", lambda v, xsl=xsl, h1=h1, stt=stt: v.tensor_scalar(
            h1[:, :], xsl[:, :], stt[:, 12:13], stt[:, 14:15], ALU.subtract, ALU.mult),
            deps=[srstd, trh_tok[i - 2] if i >= 2 else None], tag=f"h1{i}")

    def produce(i):
        if i >= NT:
            return
        p = i % 2
        h1 = b["h1"][:, p * 1024:(p + 1) * 1024]
        deps = [h1_tok[i], trqk_tok[i - 1] if i >= 1 else None,
                ring_tok[i - 1] if i >= 1 else None,
                qkev_tok[i - 1] if i >= 1 else None]
        for k in range(8):
            P.rec("t", lambda t, k=k, h1=h1: t.transpose(
                pT[:, k * 128:(k + 1) * 128], h1[:, k * 128:(k + 1) * 128], ident[:, :]),
                deps=(deps if k == 0 else ()), inc=(1 if k == 7 else 0), tag=f"trh{i}")
        trh_tok[i] = P.token("t")
        h1T = b["h1T"][:, p * 1024:(p + 1) * 1024]
        h1T_tok[i] = P.rec("v", lambda v, h1T=h1T: v.tensor_copy(h1T[:, :], pT[:, 0:1024]),
                           deps=[trh_tok[i], qkv_pe_tok[i - 2] if i >= 2 else None],
                           tag=f"h1Tev{i}")
        qk = b["qkt"][:, p * 2048:(p + 1) * 2048]
        ev = {}
        for ph in range(3):
            for ncol in range(2):
                for k in range(8):
                    d = []
                    if ph == 0 and k == 0 and ncol == 0:
                        d = [h1T_tok[i], qkev_tok[i - 1] if i >= 1 else None] + wtok4 + [
                             ring_tok[i - 1] if i >= 1 else None]
                    if ph > 0 and k == 0:
                        d = [ev[(ph - 1, ncol)]]
                    c0 = ph * 1024 + ncol * 512
                    P.rec("t", lambda t, k=k, c0=c0, ncol=ncol, h1T=h1T: t.matmul(
                        pQ[:, ncol * 512:ncol * 512 + 512],
                        h1T[:, k * 128:(k + 1) * 128],
                        b["wqkv"][:, k * 3072 + c0:k * 3072 + c0 + 512],
                        start=(k == 0), stop=(k == 7)),
                        deps=d, inc=(1 if k == 7 else 0), tag=f"qkv{i}_{ph}{ncol}")
                mmtok = P.token("t")
                if ph < 2:
                    ev[(ph, ncol)] = P.rec("v", lambda v, qk=qk, ph=ph, ncol=ncol: v.tensor_copy(
                        qk[:, ph * 1024 + ncol * 512:ph * 1024 + ncol * 512 + 512],
                        pQ[:, ncol * 512:ncol * 512 + 512]),
                        deps=[mmtok, trqk_tok[i - 1] if (ph == 0 and ncol == 0 and i >= 1)
                              else None],
                        tag=f"qkev{i}_{ph}{ncol}")
                else:
                    vmm_tok[i] = mmtok
        qkv_pe_tok[i] = P.token("t")
        if dbg is not None and "qk" in dbg:
            P.rec("y", lambda y, i=i, qk=qk: y.dma_start(
                dbg["qk"][i * 128:(i + 1) * 128, :], qk[:, :]),
                deps=[qkev_tok[i]], channel=cp + f"dq{i % 2}", tag=f"dbgqk{i}")
        if dbg is not None and "h1" in dbg:
            h1d = b["h1"][:, (i % 2) * 1024:(i % 2 + 1) * 1024]
            P.rec("y", lambda y, i=i, h1d=h1d: y.dma_start(
                dbg["h1"][i * 128:(i + 1) * 128, :], h1d[:, :]),
                deps=[h1_tok[i]], channel=cp + f"dh{i % 2}", tag=f"dbgh1{i}")

    def produce_b(i):
        if i >= NT:
            return
        p = i % 2
        qk = b["qkt"][:, p * 2048:(p + 1) * 2048]
        slot = i % RING
        dstv = bass.AP(b["vr"], slot * 1040, [[RING * 1040, 128], [65, 16], [1, 64]])
        qkev_tok[i] = P.rec("v", lambda v, dstv=dstv: v.tensor_copy(dstv, pQ[:, :]),
                            deps=[vmm_tok[i], avl_dep(i)], tag=f"vev{i}")
        for k in range(16):
            P.rec("t", lambda t, k=k, qk=qk: t.transpose(
                pT[:, (k % 8) * 128:(k % 8) * 128 + 128],
                qk[:, k * 128:(k + 1) * 128], ident[:, :]),
                deps=([qkev_tok[i], h1T_tok[i], avl_dep(i)] if k == 0 else
                      ([] if k != 8 else [qT_ev_dep(i)])),
                inc=(1 if k in (7, 15) else 0), tag=f"trqk{i}")
            if k == 7:
                t7 = P.token("t")
                qT = b["qT"][:, (i % 4) * 1024:(i % 4 + 1) * 1024]
                qT_ev = P.rec("v", lambda v, qT=qT: v.tensor_copy(qT[:, :], pT[:, 0:1024]),
                              deps=[t7, attn_pe_tok[i - 4] if i >= 4 else None],
                              tag=f"qTev{i}")
                qT_ev_holder[0] = qT_ev
                qT_ev_tok[i] = qT_ev
        trqk_tok[i] = P.token("t")
        ring_tok[i] = P.rec("v", lambda v, slot=slot: v.tensor_copy(
            b["kT"][:, slot * 1024:(slot + 1) * 1024], pT[:, 0:1024]),
            deps=[trqk_tok[i], avl_dep(i)], tag=f"kTev{i}")

    qT_ev_holder = [None]
    qT_ev_tok = [None] * (NT + 8)

    def qT_ev_dep(i):
        return qT_ev_holder[0]

    def avl_dep(i):
        t = i - RING + 2
        return attn_pe_tok[t] if t >= 0 else init_tok

    def attn_qk(j):
        qT = b["qT"][:, (j % 4) * 1024:(j % 4 + 1) * 1024]
        kdep = [ring_tok[min(j + 2, NT - 1)], qT_ev_tok[min(j, NT - 1)]]
        for h in range(OH):
            hh = h % 2
            c = h // 2
            pS = pS2[h % 2]
            if h >= 2:
                war = exp_tok[j][h - 2]
            elif j >= 1:
                war = exp_tok[j - 1][14 + h]
            else:
                war = None
            for tau in range(5):
                tt = j - 2 + tau
                slot = (tt % RING) if tt >= 0 else ((tt + RING) % RING)
                d = []
                if tau == 0:
                    d = (list(kdep) if h == 0 else []) + [war]
                P.rec("t", lambda t, slot=slot, c=c, hh=hh, tau=tau, qT=qT, pS=pS: t.matmul(
                    pS[:, tau * 128:(tau + 1) * 128],
                    b["kT"][hh * 64:(hh + 1) * 64, slot * 1024 + c * 128:slot * 1024 + c * 128 + 128],
                    qT[hh * 64:(hh + 1) * 64, c * 128:(c + 1) * 128],
                    start=True, stop=True),
                    deps=d, inc=(1 if tau == 4 else 0), tag=f"qk{j}_{h}")
            qktok = P.token("t")
            exp_tok[j][h] = P.rec("s", lambda s, h=h, pS=pS: s.activation(
                b["probs"][:, h * 640:(h + 1) * 640], pS[:, :], AF.Exp, bias=0.0, scale=0.125),
                deps=[qktok, av_tok[j - 1] if (h <= 1 and j >= 1) else None], tag=f"pexp{j}_{h}")
        for r in range(4):
            c0 = r * 4 * 640
            apL = bass.AP(b["probs"], c0, [[16 * 640, 128], [640, 4], [1, 128]])
            apH = bass.AP(b["probs"], c0 + 512, [[16 * 640, 128], [640, 4], [1, 128]])
            P.rec("v", lambda v, apL=apL, r=r: v.tensor_tensor(
                apL, apL, b["trimL"][:, r * 512:(r + 1) * 512], ALU.mult),
                deps=[exp_tok[j][r * 4 + 3]], tag=f"zl{j}_{r}")
            ztok = P.rec("v", lambda v, apH=apH, r=r: v.tensor_tensor(
                apH, apH, b["trimH"][:, r * 512:(r + 1) * 512], ALU.mult),
                tag=f"zh{j}_{r}")
            if j <= 1:
                width = 256 if j == 0 else 128
                apE = bass.AP(b["probs"], c0, [[16 * 640, 128], [640, 4], [1, width]])
                ztok = P.rec("v", lambda v, apE=apE: v.memset(apE, 0.0),
                             self_deps=[ztok], tag=f"ze{j}_{r}")
            if j >= NT - 2:
                start = 512 if j == NT - 2 else 384
                apE = bass.AP(b["probs"], c0 + start,
                              [[16 * 640, 128], [640, 4], [1, 640 - start]])
                ztok = P.rec("v", lambda v, apE=apE: v.memset(apE, 0.0),
                             self_deps=[ztok], tag=f"ze{j}_{r}")
            zgrp_tok[j][r] = ztok

    def attn_av(j):
        oevg = [None] * 4
        for r in range(4):
            pw = pows[r % 2]
            for hq in range(4):
                h = r * 4 + hq
                for tau in range(5):
                    tt = j - 2 + tau
                    slot = (tt % RING) if tt >= 0 else ((tt + RING) % RING)
                    d = []
                    if hq == 0 and tau == 0:
                        if r < 2:
                            d = [zgrp_tok[j][r],
                                 aevh_tok[j - 1][r] if j >= 1 else None]
                        else:
                            d = [zgrp_tok[j][r], oevg[r - 2]]
                        if dbg is not None and "_prtok" in dbg and j == 10:
                            d = d + [dbg["_prtok"]]
                    P.rec("t", lambda t, slot=slot, h=h, hq=hq, tau=tau, pw=pw: t.matmul(
                        pw[0:65, hq * 128:(hq + 1) * 128],
                        b["vr"][:, slot * 1040 + h * 65:slot * 1040 + h * 65 + 65],
                        b["probs"][:, h * 640 + tau * 128:h * 640 + (tau + 1) * 128],
                        start=(tau == 0), stop=(tau == 4)),
                        deps=d, inc=(1 if (hq == 3 and tau == 4) else 0), tag=f"av{j}_{r}")
            avtok = P.token("t")
            if r == 3:
                av_tok[j] = avtok
            ap_even_src = bass.AP(pw, 0, [[512, 64], [256, 2], [1, 128]])
            ap_odd_src = bass.AP(pw, 128, [[512, 64], [256, 2], [1, 128]])
            P.rec("s", lambda s, r=r, ap=ap_even_src: s.copy(
                b["oTf"][0:64, r * 256:(r + 1) * 256], ap), deps=[avtok], tag=f"oev{j}_{r}")
            P.rec("s", lambda s, r=r, ap=ap_odd_src: s.copy(
                b["oTf"][64:128, r * 256:(r + 1) * 256], ap), tag=f"oev2{j}_{r}")
            oevg[r] = P.rec("s", lambda s, r=r, pw=pw: s.copy(
                b["sums"][0:1, r * 512:(r + 1) * 512], pw[64:65, 0:512]), tag=f"sev{j}_{r}")
            oev_tok[j] = oevg[r]
            oevv_tok[j] = oevg[r]
        if dbg is not None and "oT" in dbg:
            P.rec("y", lambda y, j=j: y.dma_start(
                dbg["oT"][j * 128:(j + 1) * 128, :], b["oTf"][:, :]),
                deps=[oev_tok[j]], channel=cp + f"do{j % 2}", tag=f"dbgoT{j}")
            P.rec("y", lambda y, j=j: y.dma_start(
                dbg["sums"][j:j + 1, :], b["sums"][:, :]),
                deps=[oev_tok[j]], channel=cp + f"ds{j % 2}", tag=f"dbgsm{j}")
        rlntk = P.rec("s", lambda s: s.activation(b["sums"][0:1, 2048:4096], b["sums"][0:1, 0:2048],
                                          AF.Ln, bias=0.0, scale=1.0),
              self_deps=[oev_tok[j]], tag=f"rln{j}")
        rex_tok[j] = P.rec("s", lambda s: s.activation(b["sums"][0:1, 2048:4096],
                                                       b["sums"][0:1, 2048:4096],
                                                       AF.Exp, bias=0.0, scale=-1.0),
                           self_deps=[rlntk], tag=f"rex{j}")

    def attn_fin(j):
        if j < 0:
            return
        p = j % 2
        ntoks = [None, None]
        for half in range(2):
            pw = pows[half]
            for rh in range(2):
                rhs = bass.AP(b["sums"], 2048 + half * 1024 + rh * 128,
                              [[4096, 1], [256, 4], [1, 128]])
                dd = ([rex_tok[j], oevv_tok[j]] +
                      (list(glob_tok) if j == 0 else [])) if (rh == 0) else ()
                P.rec("t", lambda t, rh=rh, rhs=rhs, pw=pw: t.matmul(
                    pw[rh * 64:(rh + 1) * 64, 0:512],
                    ones1[0:1, 0:64], rhs, start=True, stop=True),
                    deps=dd, inc=(1 if rh == 1 else 0), tag=f"sb{j}_{half}")
            btok = P.token("t")
            ntoks[half] = P.rec("v", lambda v, half=half, pw=pw: v.tensor_tensor(
                b["oTn"][:, half * 512:(half + 1) * 512],
                b["oTf"][:, half * 512:(half + 1) * 512], pw[:, :], ALU.mult),
                deps=[btok], tag=f"on{j}_{half}")
        norm_tok[j] = ntoks[1]
        aevh = [None, None]
        for half in range(2):
            pw = pows[half]
            for k in range(8):
                d = [ntoks[half]] if k == 0 else ()
                P.rec("t", lambda t, k=k, half=half, pw=pw: t.matmul(
                    pw[:, :], b["oTn"][:, k * 128:(k + 1) * 128],
                    b["wo"][:, k * 1024 + half * 512:k * 1024 + (half + 1) * 512],
                    start=(k == 0), stop=(k == 7)),
                    deps=d, inc=(1 if k == 7 else 0), tag=f"wo{j}_{half}")
            wotok = P.token("t")
            ab = b["abuf"][:, p * 1024:(p + 1) * 1024]
            aevh[half] = P.rec("v", lambda v, ab=ab, half=half, pw=pw: v.tensor_copy(
                ab[:, half * 512:(half + 1) * 512], pw[:, :]),
                deps=[wotok, res_tok[j - 2] if (half == 0 and j >= 2) else None],
                tag=f"aev{j}_{half}")
        aevh_tok[j] = aevh
        attn_pe_tok[j] = P.token("t")
        aevt = P.token("v")
        stt = b["stats"][:, 32 + p * 16:32 + (p + 1) * 16]
        ab = b["abuf"][:, p * 1024:(p + 1) * 1024]

        def fbn_pair(v, ab=ab, stt=stt):
            v.bn_stats(stt[:, 0:6], ab[:, 0:512])
            return v.bn_stats(stt[:, 6:12], ab[:, 512:1024])
        fpt = P.rec("v", fbn_pair, self_deps=[aevt], tag=f"fbn{j}")
        vt = P.rec("v", lambda v, stt=stt: v.bn_aggr(stt[:, 12:14], stt[:, 0:12]),
                   self_deps=[fpt], tag=f"fba{j}")
        P.rec("s", lambda s, stt=stt: s.activation(stt[:, 14:15], stt[:, 13:14], AF.Ln,
                                                   bias=epsv[:, :], scale=1.0),
              deps=[vt], tag=f"fln{j}")
        flntk = P.token("s")
        fs = P.rec("s", lambda s, stt=stt: s.activation(stt[:, 14:15], stt[:, 14:15],
                                                        AF.Exp, bias=0.0, scale=-0.5),
                   self_deps=[flntk], tag=f"fex{j}")
        ob = b["outb"][:, p * 1024:(p + 1) * 1024]
        xsl = b["xin"][:, (j % RING) * 1024:(j % RING + 1) * 1024]
        P.rec("v", lambda v, ab=ab, ob=ob, stt=stt: v.tensor_scalar(
            ob[:, :], ab[:, :], stt[:, 12:13], stt[:, 14:15], ALU.subtract, ALU.mult),
            deps=[fs, out_tok[j - 2] if j >= 2 else None], tag=f"fap{j}")
        res_tok[j] = P.rec("v", lambda v, ob=ob, xsl=xsl: v.tensor_tensor(
            ob[:, :], ob[:, :], xsl[:, :], ALU.add), tag=f"fx2{j}")
        out_tok[j] = P.rec("y", lambda y, j=j, ob=ob: y.dma_start(
            out_dram[j * 128:(j + 1) * 128, :], ob[:, :]),
            deps=[res_tok[j]], channel=cp + f"ot{j % 2}", tag=f"out{j}")

    for i in range(3):
        load_tile(i)
    produce_ln(0)
    produce(0)
    produce_b(0)
    produce_ln(1)
    produce(1)
    produce_b(1)
    produce_ln(2)
    produce(2)
    for j in range(NT):
        load_tile(j + 3)
        produce_b(j + 2)
        produce_ln(j + 3)
        attn_qk(j)
        produce(j + 3)
        attn_fin(j - 1)
        attn_av(j)
    attn_fin(NT - 1)
    return [out_tok[NT - 2], out_tok[NT - 1],
            P.token("t"), P.token("v"), P.token("s"), P.token("g")]


def attn_host_weights(W_qkv, W_o):
    p = np.arange(128)
    cL = np.arange(128)
    triL = (cL[None, :] <= p[:, None]).astype(np.float32)
    triH = (cL[None, :] >= p[:, None]).astype(np.float32)
    return {
        "wqkvT": np.ascontiguousarray(np.asarray(W_qkv, np.float32).T),
        "woT": np.ascontiguousarray(np.asarray(W_o, np.float32).T),
        "trimL": np.ascontiguousarray(np.tile(triL, (1, 16))),
        "trimH": np.ascontiguousarray(np.tile(triH, (1, 16))),
    }


IRING = 7


def alloc_iattn_bufs(nc, tagp):
    gs = []
    bufs = {}

    def sb(name, shape, dt):
        g = nc.sbuf_tensor(tagp + name, shape, dt)
        t = g.__enter__()
        gs.append(g)
        bufs[name] = t
        return t

    sb("wqkv", [128, 8 * 1536], BF16)
    sb("wo", [128, 4 * 1024], BF16)
    sb("qs", [128, 512], F32)
    sb("ks", [128, 512], F32)
    sb("xin", [128, IRING * 1024], F32)
    sb("x0", [128, 4 * 1024], F32)
    sb("stats", [128, 4 * 16], F32)
    sb("hT", [128, 2 * 1040], BF16)
    sb("rot", [128, 2 * 2048], F32)
    sb("qb", [128, 2 * 512], F32)
    sb("kb", [128, 2 * 512], F32)
    sb("qTb", [128, 2 * 512], BF16)
    sb("kTb", [128, 2 * 512], BF16)
    sb("x0b", [128, 2 * 1024], BF16)
    sb("qsc", [128, 2 * 512], F32)
    sb("qsc2", [128, 2 * 512], F32)
    sb("ssq", [128, 32], F32)
    sb("qT", [128, 4 * 512], BF16)
    sb("kT", [128, IRING * 512], BF16)
    sb("vr", [128, IRING * 520], BF16)
    sb("probs", [128, 8 * 640], BF16)
    sb("trimL", [128, 8 * 128], BF16)
    sb("trimH", [128, 8 * 128], BF16)
    sb("oTf", [128, 512], F32)
    sb("sums", [1, 2048], F32)
    sb("oTn", [128, 512], BF16)
    sb("outb", [128, 2 * 1024], F32)
    return gs, bufs


def alloc_iattn_psums(nc, tagp):
    gs = []
    ps = {}

    def palloc(name, shape, dt):
        g = nc.psum_tensor(tagp + name, shape, dt)
        t = g.__enter__()
        gs.append(g)
        ps[name] = t
        return t

    palloc("pS0", [128, 640], F32)
    palloc("pS1", [128, 640], F32)
    palloc("pQ", [128, 512], F32)
    palloc("pQ2", [128, 512], F32)
    palloc("pOW", [128, 512], F32)
    palloc("pOW2", [128, 512], F32)
    ps["pT"] = ps["pQ"].bitcast(BF16)
    return gs, ps


def record_attn_inner(P, nc, glob, ps, b, wdram, in_dram, out_dram, cp="i2",
                      glob_tok=(), weights_dep=None, with_ln=True, dbg=None, phase_dep=()):
    ident, epsv, ones1 = glob["ident"], glob["eps"], glob["ones1"]
    pS2 = [ps["pS0"], ps["pS1"]]
    pQ, pOW, pT = ps["pQ"], ps["pOW"], ps["pT"]
    pQ2 = ps["pQ2"]
    pOW2 = ps["pOW2"]
    ipows = [pOW, pOW2]

    for k in range(8):
        P.rec("g", lambda g, k=k: g.dma_start(
            b["wqkv"][:, k * 1536:(k + 1) * 1536], wdram["wqkv"][k * 128:(k + 1) * 128, :]),
            deps=(([weights_dep] if weights_dep else []) + list(phase_dep)) if k < 4 else (),
            channel=cp + f"w{k % 4}", tag="wqkvld")
    for k in range(4):
        P.rec("g", lambda g, k=k: g.dma_start(
            b["wo"][:, k * 1024:(k + 1) * 1024], wdram["wo"][k * 128:(k + 1) * 128, :]),
            channel=cp + f"w{k % 4}", tag="wold")
    for ni, name in enumerate(("qs", "ks", "trimL", "trimH")):
        P.rec("g", lambda g, name=name: g.dma_start(b[name][:, :], wdram[name][:, :]),
              channel=cp + f"w{ni % 4}", tag=name)
    vzt = P.rec("g", lambda g: g.memset(b["vr"][:, :], 0.0), tag="vz")
    P.rec("g", lambda g: g.memset(
        bass.AP(b["vr"], 64, [[IRING * 520, 128], [65, 8 * IRING], [1, 1]]), 1.0),
        self_deps=[vzt], tag="vones")
    P.rec("g", lambda g: g.memset(b["kT"][:, 3 * 512:7 * 512], 0.0), tag="kz")
    wtok4 = [P.token(cp + f"w{c}") for c in range(4)]
    wtok = wtok4[0]
    init_tok = P.token("g")

    xin_tok = [None] * (NT + 8)
    rot_tok = [None] * (NT + 8)
    vev_tok = [None] * NT
    aevh_tok = [[None, None] for _ in range(NT)]
    cast_tok_arr = [None] * NT
    dqt_arr = [None] * NT
    x0_tok = [None] * NT
    trx_tok = [None] * NT
    hT_tok = [None] * NT
    qkv_pe_tok = [None] * NT
    qkproc_tok = [None] * NT
    trqk_tok = [None] * NT
    qT_ev_tok = [None] * NT
    ring_tok = [None] * NT
    exp_tok = [[None] * IH for _ in range(NT)]
    zero_tok = [None] * NT
    av_tok = [None] * NT
    oev_tok = [None] * NT
    rex_tok = [None] * NT
    norm_tok = [None] * NT
    attn_pe_tok = [None] * NT
    res_tok = [None] * NT
    out_tok = [None] * NT

    def load_tile(i):
        if i >= NT:
            return
        dep = [res_tok[i - IRING]] if i >= IRING else list(phase_dep)
        xin_tok[i] = P.rec("y", lambda y, i=i: y.dma_start(
            b["xin"][:, (i % IRING) * 1024:(i % IRING + 1) * 1024],
            in_dram[i * 128:(i + 1) * 128, :]),
            deps=dep, channel=cp + f"xi{i % IRING}", tag=f"xin{i}")


    def produce_ln(i):
        if i >= NT:
            return
        p = i % 2
        dep2 = [qkproc_tok[i - 2]] if i >= 2 else []
        rot_tok[i] = P.rec("y", lambda y, i=i: y.dma_start(
            b["rot"][:, (i % 2) * 2048:(i % 2 + 1) * 2048],
            wdram["rot"][i * 128:(i + 1) * 128, :]),
            deps=dep2, channel=cp + f"ro{i % 2}", tag=f"rot{i}")
        xsl = b["xin"][:, (i % IRING) * 1024:(i % IRING + 1) * 1024]
        x0 = b["x0"][:, (i % 4) * 1024:(i % 4 + 1) * 1024]
        x0war = res_tok[i - 4] if i >= 4 else None
        if with_ln:
            stt = b["stats"][:, p * 16:(p + 1) * 16]

            def bn_pair(v, xsl=xsl, stt=stt):
                v.bn_stats(stt[:, 0:6], xsl[:, 0:512])
                return v.bn_stats(stt[:, 6:12], xsl[:, 512:1024])
            pr = P.rec("v", bn_pair, deps=[xin_tok[i]], tag=f"bn{i}")
            vagg = P.rec("v", lambda v, stt=stt: v.bn_aggr(stt[:, 12:14], stt[:, 0:12]),
                         self_deps=[pr], tag=f"bnag{i}")
            P.rec("s", lambda s, stt=stt: s.activation(stt[:, 14:15], stt[:, 13:14], AF.Ln,
                                                       bias=epsv[:, :], scale=1.0),
                  deps=[vagg] + (list(glob_tok) if i < 2 else []), tag=f"ln{i}")
            lntk = P.token("s")
            srstd = P.rec("s", lambda s, stt=stt: s.activation(
                stt[:, 14:15], stt[:, 14:15], AF.Exp, bias=0.0, scale=-0.5),
                self_deps=[lntk], tag=f"exp{i}")
            x0_tok[i] = P.rec("v", lambda v, xsl=xsl, x0=x0, stt=stt: v.tensor_scalar(
                x0[:, :], xsl[:, :], stt[:, 12:13], stt[:, 14:15], ALU.subtract, ALU.mult),
                deps=[srstd, x0war], tag=f"x0{i}")
        else:
            x0_tok[i] = P.rec("v", lambda v, xsl=xsl, x0=x0: v.tensor_copy(x0[:, :], xsl[:, :]),
                              deps=[xin_tok[i], x0war] + (list(glob_tok) if i < 2 else []),
                              tag=f"x0{i}")
        xb = b["qsc"][:, p * 512:(p + 1) * 512]
        x0b = b["x0b"][:, p * 1024:(p + 1) * 1024]
        cast_tok_arr[i] = P.rec("v", lambda v, x0=x0, x0b=x0b: v.tensor_copy(
            x0b[:, :], x0[:, :]),
            deps=[trx_tok[i - 2] if i >= 2 else None] +
                 (list(glob_tok) if i < 2 else []), tag=f"x0b{i}")

    def produce(i):
        if i >= NT:
            return
        p = i % 2
        x0 = b["x0"][:, (i % 4) * 1024:(i % 4 + 1) * 1024]
        x0b = b["x0b"][:, p * 1024:(p + 1) * 1024]
        cast_tok = cast_tok_arr[i]
        rsl = b["rot"][:, (i % 2) * 2048:(i % 2 + 1) * 2048]
        dqt = dqt_arr[i]
        hT = b["hT"][:, p * 1040:(p + 1) * 1040]
        deps = [cast_tok, hT_tok[i - 2] if i >= 2 else None,
                qkv_pe_tok[i - 1] if i >= 1 else None,
                ring_tok[i - 1] if i >= 1 else None,
                vev_tok[i - 1] if i >= 1 else None]
        for k in range(8):
            P.rec("t", lambda t, k=k, x0b=x0b: t.transpose(
                pT[:, k * 128:(k + 1) * 128], x0b[:, k * 128:(k + 1) * 128], ident[:, :]),
                deps=(deps if k == 0 else ()), inc=(1 if k == 7 else 0), tag=f"trx{i}")
        trx_tok[i] = P.token("t")
        dstap = bass.AP(b["hT"], p * 1040 + 1, [[2 * 1040, 128], [130, 8], [1, 128]])
        ev = P.rec("v", lambda v, dstap=dstap: v.tensor_copy(dstap, pT[:, :]),
                   deps=[trx_tok[i], qkv_pe_tok[i - 2] if i >= 2 else None], tag=f"hTev{i}")
        if i == 0:
            hT_tok[i] = P.rec("v", lambda v: v.memset(
                bass.AP(b["hT"], 0, [[2 * 1040, 128], [130, 8], [1, 1]]), 0.0),
                self_deps=[ev], tag=f"hTz{i}")
        else:
            po = (i - 1) % 2
            src_ap = bass.AP(b["hT"], po * 1040 + 128, [[2 * 1040, 128], [130, 8], [1, 1]])
            dst_ap = bass.AP(b["hT"], p * 1040 + 0, [[2 * 1040, 128], [130, 8], [1, 1]])
            hT_tok[i] = P.rec("v", lambda v, src_ap=src_ap, dst_ap=dst_ap: v.tensor_copy(
                dst_ap, src_ap), self_deps=[ev], tag=f"hTpc{i}")
        evq = None
        for ph in range(3):
            pq = pQ2 if ph == 1 else pQ
            for k in range(8):
                ofs = 1 if k < 4 else 0
                d = []
                if ph == 0 and k == 0:
                    d = [hT_tok[i]] + wtok4 + [
                         (qkproc_tok[i - 1] if i >= 1 else None)]
                if ph == 2 and k == 0:
                    d = [ev_q0]
                P.rec("t", lambda t, k=k, ph=ph, ofs=ofs, p=p, pq=pq: t.matmul(
                    pq[:, :],
                    bass.AP(b["hT"], p * 1040 + k * 130 + ofs, [[2 * 1040, 128], [1, 128]]),
                    b["wqkv"][:, k * 1536 + ph * 512:k * 1536 + ph * 512 + 512],
                    start=(k == 0), stop=(k == 7)),
                    deps=d, inc=(1 if k == 7 else 0), tag=f"qkv{i}_{ph}")
            mmtok = P.token("t")
            if ph == 0:
                evq = P.rec("v", lambda v, p=p: v.tensor_copy(
                    b["qb"][:, p * 512:(p + 1) * 512], pQ[:, :]),
                    deps=[mmtok, qkproc_tok[i - 2] if i >= 2 else None], tag=f"qev{i}")
                ev_q0 = evq
            elif ph == 1:
                evq = P.rec("v", lambda v, p=p: v.tensor_copy(
                    b["kb"][:, p * 512:(p + 1) * 512], pQ2[:, :]),
                    deps=[mmtok], tag=f"kev{i}")
            else:
                slot = i % IRING
                dstv = bass.AP(b["vr"], slot * 520, [[IRING * 520, 128], [65, 8], [1, 64]])
                vev_tok[i] = P.rec("v", lambda v, dstv=dstv: v.tensor_copy(dstv, pQ[:, :]),
                                   deps=[mmtok, avl_dep(i)], tag=f"vev{i}")
        qkv_pe_tok[i] = P.token("t")
        if dbg is not None and "x0" in dbg:
            P.rec("y", lambda y, i=i, x0=x0: y.dma_start(
                dbg["x0"][i * 128:(i + 1) * 128, :], x0[:, :]),
                deps=[x0_tok[i]], channel=cp + f"d0{i % 2}", tag=f"dbgx0{i}")
        rsl = b["rot"][:, (i % 2) * 2048:(i % 2 + 1) * 2048]
        if dbg is not None and "rot" in dbg:
            P.rec("y", lambda y, i=i, rsl=rsl: y.dma_start(
                dbg["rot"][i * 128:(i + 1) * 128, :], rsl),
                deps=[rot_tok[i]], channel=cp + f"d6{i % 2}", tag=f"dbgrot{i}")
        dqt = None
        if dbg is not None and "qraw" in dbg:
            dqt = P.rec("y", lambda y, i=i, p=p: y.dma_start(
                dbg["qraw"][i * 128:(i + 1) * 128, :], b["qb"][:, p * 512:(p + 1) * 512]),
                deps=[ev_q0], channel=cp + f"d4{i % 2}", tag=f"dbgqr{i}")
        bq = b["qb"][:, p * 512:(p + 1) * 512]
        bk = b["kb"][:, p * 512:(p + 1) * 512]
        sqq = b["qsc"][:, p * 512:(p + 1) * 512]
        sqk = b["qsc2"][:, p * 512:(p + 1) * 512]
        baseq = p * 16
        basek = p * 16 + 8
        sqt = P.rec("v", lambda v: v.tensor_tensor(sqq[:, :], bq[:, :], bq[:, :], ALU.mult),
                    deps=[rot_tok[i]], tag=f"sqq{i}")
        for g2 in range(2):
            sap = bass.AP(b["qsc"], p * 512 + g2 * 256, [[1024, 128], [64, 4], [1, 64]])
            P.rec("v", lambda v, g2=g2, sap=sap: v.tensor_reduce(
                b["ssq"][:, baseq + g2 * 4:baseq + g2 * 4 + 4], sap,
                AX.X, ALU.add), self_deps=([sqt] if g2 == 0 else ()), tag=f"ssq{i}_{g2}")
        sqtk = P.rec("v", lambda v: v.tensor_tensor(sqk[:, :], bk[:, :], bk[:, :], ALU.mult),
                     tag=f"sqk{i}")
        for g2 in range(2):
            sap = bass.AP(b["qsc2"], p * 512 + g2 * 256, [[1024, 128], [64, 4], [1, 64]])
            P.rec("v", lambda v, g2=g2, sap=sap: v.tensor_reduce(
                b["ssq"][:, basek + g2 * 4:basek + g2 * 4 + 4], sap,
                AX.X, ALU.add), self_deps=([sqtk] if g2 == 0 else ()), tag=f"ssk{i}_{g2}")
        rdt = P.token("v")
        P.rec("v", lambda v: v.tensor_scalar(
            b["ssq"][:, p * 16:p * 16 + 16], b["ssq"][:, p * 16:p * 16 + 16], 1e-24, None,
            ALU.max), self_deps=[rdt], tag=f"cl{i}")
        vtk = P.token("v")
        P.rec("s", lambda s: s.activation(
            b["ssq"][:, p * 16:p * 16 + 16], b["ssq"][:, p * 16:p * 16 + 16], AF.Ln,
            bias=0.0, scale=1.0), deps=[vtk], tag=f"l2l{i}")
        stk = P.token("s")
        rtk = P.rec("s", lambda s: s.activation(
            b["ssq"][:, p * 16:p * 16 + 16], b["ssq"][:, p * 16:p * 16 + 16], AF.Exp,
            bias=0.0, scale=-0.5), self_deps=[stk], tag=f"l2e{i}")
        if dbg is not None and "ssq" in dbg:
            P.rec("y", lambda y, i=i: y.dma_start(
                dbg["ssq"][i * 128:(i + 1) * 128, :], b["ssq"][:, baseq:baseq + 8]),
                deps=[rtk], channel=cp + f"d5{i % 2}", tag=f"dbgss{i}")
        proc_toks = []
        for (which, bb, base, scv, c0, s0) in (("q", bq, baseq, "qs", 0, 512),
                                               ("k", bk, basek, "ks", 1024, 1536)):
            gate = [rtk] + ([dqt] if (dqt is not None and which == "q") else [])
            for h in range(8):
                P.rec("v", lambda v, bb=bb, h=h, base=base, scv=scv:
                      v.scalar_tensor_tensor(
                    bb[:, h * 64:(h + 1) * 64], bb[:, h * 64:(h + 1) * 64],
                    b["ssq"][:, base + h:base + h + 1],
                    b[scv][:, h * 64:(h + 1) * 64], ALU.mult, ALU.mult),
                    deps=(gate if (which == "q" and h == 0) else ()), tag=f"l2a{which}{i}_{h}")
            if dbg is not None and "qu" in dbg and which == "q":
                ut = P.token("v")
                P.rec("y", lambda y, i=i, bb=bb: y.dma_start(
                    dbg["qu"][i * 128:(i + 1) * 128, :], bb[:, :]),
                    deps=[ut], channel=cp + f"d3{i % 2}", tag=f"dbgu{i}")
            cosr = rsl[:, c0:c0 + 512]
            sinr = rsl[:, s0:s0 + 512]
            P.rec("v", lambda v, bb=bb, cosr=cosr: v.tensor_tensor(
                sqq[:, :], bb[:, :], cosr, ALU.mult), tag=f"rc{which}{i}")
            swap = bass.AP(b["qb" if which == "q" else "kb"], p * 512 + 32,
                           [[1024, 128], [64, 8], [-32, 2], [1, 32]])
            rst = P.rec("v", lambda v, swap=swap, sinr=sinr: v.tensor_tensor(
                sqk[:, :], swap, sinr, ALU.mult), tag=f"rs{which}{i}")
            obf = b["qTb" if which == "q" else "kTb"][:, p * 512:(p + 1) * 512]
            tk = P.rec("v", lambda v, obf=obf: v.tensor_tensor(
                obf[:, :], sqq[:, :], sqk[:, :], ALU.add),
                deps=[trqk_tok[i - 2] if i >= 2 else None],
                self_deps=[rst], tag=f"ra{which}{i}")
            proc_toks.append(tk)
        qkproc_tok[i] = proc_toks[-1]
        if dbg is not None and "qk" in dbg:
            P.rec("y", lambda y, i=i, p=p: y.dma_start(
                dbg["qk"][i * 128:(i + 1) * 128, 0:512], b["qTb"][:, p * 512:(p + 1) * 512]),
                deps=[qkproc_tok[i]], channel=cp + f"d1{i % 2}", tag=f"dbgq{i}")
            P.rec("y", lambda y, i=i, p=p: y.dma_start(
                dbg["qk"][i * 128:(i + 1) * 128, 512:1024], b["kTb"][:, p * 512:(p + 1) * 512]),
                deps=[qkproc_tok[i]], channel=cp + f"d2{i % 2}", tag=f"dbgk{i}")

    def produce_b(i):
        if i >= NT:
            return
        p = i % 2
        slot = i % IRING
        for which in ("q", "k"):
            bfb = b["qTb" if which == "q" else "kTb"][:, p * 512:(p + 1) * 512]
            deps0 = [qkproc_tok[i], ring_tok[i - 1] if i >= 1 else None,
                     (qT_ev_tok[i - 1] if i >= 1 else None)]
            for k in range(4):
                kk = k if which == "q" else 4 + k
                P.rec("t", lambda t, k=k, kk=kk, bfb=bfb: t.transpose(
                    pT[:, kk * 128:(kk + 1) * 128], bfb[:, k * 128:(k + 1) * 128],
                    ident[:, :]),
                    deps=(deps0 if (which == "q" and k == 0) else ()),
                    inc=(1 if k == 3 else 0), tag=f"tr{which}{i}")
        trqk_tok[i] = P.token("t")
        qT_ev_tok[i] = P.rec("v", lambda v, i=i: v.tensor_copy(
            b["qT"][:, (i % 4) * 512:(i % 4 + 1) * 512], pT[:, 0:512]),
            deps=[trqk_tok[i], attn_pe_tok[i - 3] if i >= 3 else None], tag=f"qTev{i}")
        ring_tok[i] = P.rec("v", lambda v, slot=slot: v.tensor_copy(
            b["kT"][:, slot * 512:(slot + 1) * 512], pT[:, 512:1024]),
            deps=[avl_dep(i)], self_deps=[qT_ev_tok[i]], tag=f"kTev{i}")

    def avl_dep(i):
        t = i - IRING + 4
        return attn_pe_tok[t] if t >= 0 else init_tok

    def slot_of(tt):
        return (tt % IRING) if tt >= 0 else ((tt + IRING) % IRING)

    def attn_qk(j):
        kdep = [ring_tok[j]]
        for h in range(IH):
            hh = h % 2
            c = h // 2
            pS = pS2[h % 2]
            if h >= 2:
                war = exp_tok[j][h - 2]
            elif j >= 1:
                war = exp_tok[j - 1][IH - 2 + h]
            else:
                war = None
            for tau in range(5):
                tt = j - 4 + tau
                slot = slot_of(tt)
                d = ()
                if tau == 0:
                    d = (list(kdep) if h == 0 else []) + [war, qT_ev_tok[j]]
                P.rec("t", lambda t, slot=slot, c=c, hh=hh, tau=tau, pS=pS, j=j: t.matmul(
                    pS[:, tau * 128:(tau + 1) * 128],
                    b["kT"][hh * 64:(hh + 1) * 64,
                            slot * 512 + c * 128:slot * 512 + c * 128 + 128],
                    b["qT"][hh * 64:(hh + 1) * 64,
                            (j % 4) * 512 + c * 128:(j % 4) * 512 + c * 128 + 128],
                    start=True, stop=True),
                    deps=d, inc=(1 if tau == 4 else 0), tag=f"st{j}_{h}")
            sttok = P.token("t")
            exp_tok[j][h] = P.rec("s", lambda s, h=h, pS=pS: s.activation(
                b["probs"][:, h * 640:(h + 1) * 640], pS[:, :], AF.Exp, bias=0.0, scale=1.0),
                deps=[sttok, av_tok[j - 1] if (h <= 1 and j >= 1) else None],
                tag=f"pexp{j}_{h}")
        zdeps = [exp_tok[j][IH - 1]]
        apL = bass.AP(b["probs"], 0, [[8 * 640, 128], [640, 8], [1, 128]])
        apH = bass.AP(b["probs"], 512, [[8 * 640, 128], [640, 8], [1, 128]])
        P.rec("v", lambda v, apL=apL: v.tensor_tensor(apL, apL, b["trimL"][:, :], ALU.mult),
              deps=zdeps, tag=f"zl{j}")
        ztok = P.rec("v", lambda v, apH=apH: v.tensor_tensor(apH, apH, b["trimH"][:, :], ALU.mult),
                     tag=f"zh{j}")
        if j <= 3:
            width = 512 - j * 128
            apE = bass.AP(b["probs"], 0, [[8 * 640, 128], [640, 8], [1, width]])
            ztok = P.rec("v", lambda v, apE=apE: v.memset(apE, 0.0),
                         self_deps=[ztok], tag=f"ze{j}")
        zero_tok[j] = ztok

    def attn_av(j):
        for r in range(2):
            pw = ipows[r]
            for hq in range(4):
                h = r * 4 + hq
                for tau in range(5):
                    tt = j - 4 + tau
                    slot = slot_of(tt)
                    d = []
                    if hq == 0 and tau == 0:
                        d = [zero_tok[j],
                             aevh_tok[j - 1][1 - r] if j >= 1 else None]
                        if dbg is not None and "_prtok" in dbg and j == 10:
                            d = d + [dbg["_prtok"]]
                    P.rec("t", lambda t, slot=slot, h=h, hq=hq, tau=tau, pw=pw: t.matmul(
                        pw[0:65, hq * 128:(hq + 1) * 128],
                        b["vr"][:, slot * 520 + h * 65:slot * 520 + h * 65 + 65],
                        b["probs"][:, h * 640 + tau * 128:h * 640 + (tau + 1) * 128],
                        start=(tau == 0), stop=(tau == 4)),
                        deps=d, inc=(1 if (hq == 3 and tau == 4) else 0), tag=f"av{j}_{r}")
            avtok = P.token("t")
            if r == 1:
                av_tok[j] = avtok
            ap_even_src = bass.AP(pw, 0, [[512, 64], [256, 2], [1, 128]])
            ap_odd_src = bass.AP(pw, 128, [[512, 64], [256, 2], [1, 128]])
            P.rec("s", lambda s, r=r, ap=ap_even_src: s.copy(
                b["oTf"][0:64, r * 256:(r + 1) * 256], ap), deps=[avtok], tag=f"oev{j}_{r}")
            P.rec("s", lambda s, r=r, ap=ap_odd_src: s.copy(
                b["oTf"][64:128, r * 256:(r + 1) * 256], ap), tag=f"oev2{j}_{r}")
            oev_tok[j] = P.rec("s", lambda s, r=r, pw=pw: s.copy(
                b["sums"][0:1, r * 512:(r + 1) * 512], pw[64:65, 0:512]), tag=f"sev{j}_{r}")
        if dbg is not None and "oT" in dbg:
            P.rec("y", lambda y, j=j: y.dma_start(
                dbg["oT"][j * 128:(j + 1) * 128, :], b["oTf"][:, :]),
                deps=[oev_tok[j]], channel=cp + f"d7{j % 2}", tag=f"dbgoT{j}")
            P.rec("y", lambda y, j=j: y.dma_start(
                dbg["sm"][j:j + 1, :], b["sums"][0:1, 0:1024]),
                deps=[oev_tok[j]], channel=cp + f"d8{j % 2}", tag=f"dbgsm{j}")
        rlntk = P.rec("s", lambda s: s.activation(b["sums"][0:1, 1024:2048],
                                                  b["sums"][0:1, 0:1024],
                                                  AF.Ln, bias=0.0, scale=1.0),
                      self_deps=[oev_tok[j]], tag=f"rln{j}")
        rex_tok[j] = P.rec("s", lambda s: s.activation(b["sums"][0:1, 1024:2048],
                                                       b["sums"][0:1, 1024:2048],
                                                       AF.Exp, bias=0.0, scale=-1.0),
                           self_deps=[rlntk], tag=f"rex{j}")

    def attn_fin(j):
        if j < 0:
            return
        p = j % 2
        d = [rex_tok[j]]
        if j >= 1:
            d.append(res_tok[j - 1])
        for rh in range(2):
            rhs = bass.AP(b["sums"], 1024 + rh * 128, [[2048, 1], [256, 4], [1, 128]])
            dd = (d + (list(glob_tok) if j == 0 else [])) if rh == 0 else ()
            P.rec("t", lambda t, rh=rh, rhs=rhs: t.matmul(
                pOW[rh * 64:(rh + 1) * 64, 0:512],
                glob["ones1f"][0:1, 0:64], rhs, start=True, stop=True),
                deps=dd, inc=(1 if rh == 1 else 0), tag=f"sbc{j}")
        btok = P.token("t")
        norm_tok[j] = P.rec("v", lambda v: v.tensor_tensor(
            b["oTn"][:, :], b["oTf"][:, :], pOW[:, :], ALU.mult),
            deps=[btok], tag=f"on{j}")
        ob = b["outb"][:, p * 1024:(p + 1) * 1024]
        x0 = b["x0"][:, (j % 4) * 1024:(j % 4 + 1) * 1024]
        aevh = [None, None]
        for half in range(2):
            pw = ipows[1 - half]
            for k in range(4):
                d = [norm_tok[j]] if (half == 0 and k == 0) else ()
                P.rec("t", lambda t, k=k, half=half, pw=pw: t.matmul(
                    pw[:, :], b["oTn"][:, k * 128:(k + 1) * 128],
                    b["wo"][:, k * 1024 + half * 512:k * 1024 + (half + 1) * 512],
                    start=(k == 0), stop=(k == 3)),
                    deps=d, inc=(1 if k == 3 else 0), tag=f"wo{j}_{half}")
            wotok = P.token("t")
            aevh[half] = P.rec("v", lambda v, ob=ob, x0=x0, half=half, pw=pw: v.tensor_tensor(
                ob[:, half * 512:(half + 1) * 512], pw[:, :],
                x0[:, half * 512:(half + 1) * 512], ALU.add),
                deps=[wotok, out_tok[j - 2] if (half == 0 and j >= 2) else None,
                      x0_tok[j]],
                tag=f"aev{j}_{half}")
        aevh_tok[j] = aevh
        attn_pe_tok[j] = P.token("t")
        res_tok[j] = aevh[1]
        out_tok[j] = P.rec("y", lambda y, j=j, ob=ob: y.dma_start(
            out_dram[j * 128:(j + 1) * 128, :], ob[:, :]),
            deps=[res_tok[j]], channel=cp + f"ot{j % 2}", tag=f"out{j}")

    for i in range(4):
        load_tile(i)
    produce_ln(0)
    produce(0)
    produce_b(0)
    produce_ln(1)
    produce(1)
    for j in range(NT):
        load_tile(j + 4)
        attn_qk(j)
        attn_fin(j - 1)
        attn_av(j)
        produce_b(j + 1)
        produce_ln(j + 2)
        produce(j + 2)
    attn_fin(NT - 1)
    return [out_tok[NT - 2], out_tok[NT - 1],
            P.token("t"), P.token("v"), P.token("s"), P.token("g")]


def iattn_host_weights(wqkv, q_scale, k_scale, wo, layer):
    l = layer
    p = np.arange(128)
    cL = np.arange(128)
    triL = (cL[None, :] <= p[:, None]).astype(np.float32)
    triH = (cL[None, :] >= p[:, None]).astype(np.float32)
    half = np.arange(0, 64, 2, dtype=np.float32)
    inv_freq = (1.0 / (10000.0 ** (half / 64.0))).astype(np.float32)
    t = np.arange(4096, dtype=np.float32)
    freqs = t[:, None] * inv_freq[None, :]
    freqs = np.concatenate([freqs, freqs], 1)
    base = ((half + 0.4 * 64.0) / (1.4 * 64.0)).astype(np.float32)
    power = (t - 2048.0) / 256.0
    scale = base[None, :] ** power[:, None]
    scale = np.concatenate([scale, scale], 1).astype(np.float32)
    cos = np.cos(freqs).astype(np.float32)
    sin = np.sin(freqs).astype(np.float32)
    sgn = np.concatenate([-np.ones(32, np.float32), np.ones(32, np.float32)])
    cosQ8 = 8.0 * cos * scale
    sinQ8s = 8.0 * sin * scale * sgn
    cosK = cos / scale
    sinKs = sin / scale * sgn
    rep = lambda a: np.tile(a, (1, 8))
    rot = np.concatenate([rep(cosQ8), rep(sinQ8s), rep(cosK), rep(sinKs)], axis=1)
    return {
        "wqkv": np.ascontiguousarray(np.asarray(wqkv[l], np.float32)),
        "wo": np.ascontiguousarray(np.asarray(wo[l], np.float32)),
        "qs": np.ascontiguousarray(np.broadcast_to(
            np.tile(np.asarray(q_scale[l], np.float32), 8)[None, :], (128, 512))),
        "ks": np.ascontiguousarray(np.broadcast_to(
            np.tile(np.asarray(k_scale[l], np.float32), 8)[None, :], (128, 512))),
        "rot": np.ascontiguousarray(rot.astype(np.float32)),
        "trimL": np.ascontiguousarray(np.tile(triL, (1, 8))),
        "trimH": np.ascontiguousarray(np.tile(triH, (1, 8))),
    }
